# version 05
# baseline (speedup 1.0000x reference)
import numpy as np
import ml_dtypes
from contextlib import ExitStack

import concourse.bass as bass
import concourse.mybir as mybir
from concourse.bass_utils import run_bass_kernel_spmd

F32 = mybir.dt.float32
BF16 = mybir.dt.bfloat16
AF = mybir.ActivationFunctionType
ALU = mybir.AluOpType
AX = mybir.AxisListType
BF = ml_dtypes.bfloat16
MAGIC = 12582912.0
NCORES = 8

C_SX, C_RSX, C_RSUP, C_RSG, C_SG, C_RST, C_RSO, C_SO = range(8)
K_LNEPS, K_EIGHTH = 0, 1

_CHAIN_OPS = frozenset((
    "tensor_copy", "tensor_tensor", "tensor_add", "tensor_sub", "tensor_mul",
    "tensor_max", "tensor_scalar", "tensor_single_scalar", "tensor_scalar_add",
    "tensor_scalar_sub", "tensor_scalar_mul", "tensor_scalar_max",
    "tensor_scalar_min", "tensor_reduce", "reduce_max", "reduce_sum",
    "scalar_tensor_tensor", "reciprocal", "memset", "memzero", "select",
    "activation", "copy", "mul", "add", "square", "sqrt", "sign",
))


class Chained:

    def __init__(self, pr, eng, sem_name):
        self._pr = pr
        self._e = eng
        self._s = sem_name

    def __getattr__(self, n):
        a = getattr(self._e, n)
        if n not in _CHAIN_OPS:
            return a
        pr, e, sn = self._pr, self._e, self._s

        def w(*args, **kw):
            if pr.cnt[sn] > 0:
                e.wait_ge(pr.sems[sn], pr.cnt[sn])
            ins = a(*args, **kw)
            ins.then_inc(pr.sems[sn], 1)
            pr.cnt[sn] += 1
            return ins
        return w


class Stream:

    def __init__(self, nc, name, ctx):
        self.sem = ctx.enter_context(nc.semaphore(name))
        self.cnt = 0

    def dma(self, g, out, in_):
        if self.cnt > 0:
            g.wait_ge(self.sem, self.cnt)
        g.dma_start(out=out, in_=in_).then_inc(self.sem, 16)
        self.cnt += 16
        return self.cnt

    def wait(self, eng, val=None):
        v = self.cnt if val is None else val
        if v > 0:
            e = eng._e if isinstance(eng, Chained) else eng
            e.wait_ge(self.sem, v)


class Prog:
    def __init__(self, nc, block, ctx):
        self.nc = nc
        self.block = block
        self.cnt = {}
        self.sems = {}
        for s in ("te", "ve", "se", "cc"):
            self.sems[s] = ctx.enter_context(nc.semaphore(s))
            self.cnt[s] = 0

    def vec(self, f):
        self.block.vector(lambda v: f(Chained(self, v, "ve")))

    def sca(self, f):
        self.block.scalar(lambda s: f(Chained(self, s, "se")))

    def inc(self, ins, name, n=1):
        ins.then_inc(self.sems[name], n)
        self.cnt[name] += n
        return self.cnt[name]

    def wait(self, eng, name, val=None):
        v = self.cnt[name] if val is None else val
        if v > 0:
            e = eng._e if isinstance(eng, Chained) else eng
            e.wait_ge(self.sems[name], v)


def make_consts(S, E, H):
    D = E // H
    idx = np.arange(S, dtype=np.float64)
    a_ang = 1.0 / 10000.0 ** np.linspace(0.0, 1.0, D // 2)
    angle = np.repeat(a_ang, 2)
    sin_t = np.sin(idx[:, None] * angle[None, :])
    cos_t = np.cos(idx[:, None] * angle[None, :])
    gamma = 1.0 - 2.0 ** (-5.0 - np.arange(H, dtype=np.float64))
    r = (1.0 - gamma[:, None] ** (idx[None, :] + 1.0)) / (1.0 - gamma[:, None])
    a_hq = gamma[:, None] ** idx[None, :] / np.sqrt(r)
    b_hk = np.exp(-idx[None, :] * np.log(gamma)[:, None])
    scaling = float(E) ** (-0.5)

    rows = np.arange(E)
    h_r = 2 * (rows // 128) + (rows % 128) // D
    d_r = rows % D
    QCOS = (cos_t[:, d_r].T * a_hq[h_r]).astype(np.float32)
    QSIN = (sin_t[:, d_r].T * a_hq[h_r]).astype(np.float32)
    KCOS = (cos_t[:, d_r].T * b_hk[h_r] * scaling).astype(np.float32)
    KSIN = (sin_t[:, d_r].T * b_hk[h_r] * scaling).astype(np.float32)

    IDT = np.eye(128, dtype=np.float32).astype(BF)
    PROT = np.zeros((128, 128), np.float32)
    for i in range(64):
        PROT[2 * i + 1, 2 * i] = -1.0
        PROT[2 * i, 2 * i + 1] = 1.0
    PROT = PROT.astype(BF)
    ki = np.arange(128)
    TRI = (ki[:, None] <= ki[None, :]).astype(np.float32)
    ONES1 = np.ones((128, 1), np.float32).astype(BF)
    ONESF = np.ones((1, 128), np.float32)
    CONSTS = np.zeros((128, 8), np.float32)
    CONSTS[:, K_LNEPS] = 1e-5
    CONSTS[:, K_EIGHTH] = 0.125
    return dict(qcos=QCOS, qsin=QSIN, kcos=KCOS, ksin=KSIN, idt=IDT, prot=PROT,
                tri=TRI, ones1=ONES1, onesf=ONESF, consts=CONSTS)


def build_program(S=2048, E=1024, H=16, I=3072, stop_after=99):
    D = E // H
    TT_ = S // 128
    EC = E // 128
    IC = I // 128
    NQ = S // 512
    NP = H // 2
    NT = S // 512
    NO = E // 512
    NBU = E // 32
    NBD = I // 32

    nc = bass.Bass("TRN2", target_bir_lowering=False, debug=False)
    ap_in = {}

    def din(name, shape, dt=F32):
        ap_in[name] = nc.dram_tensor(name, list(shape), dt,
                                     kind="ExternalInput").ap()
        return ap_in[name]

    x_in = din("x", (S, E))
    wq_in = din("wq", (E, E)); wk_in = din("wk", (E, E))
    wv_in = din("wv", (E, E)); wg_in = din("wg", (E, E))
    wup_in = din("wup", (I, E)); wgate_in = din("wgate", (I, E))
    wdown_in = din("wdown", (E, I))
    qcos_in = din("qcos", (E, S)); qsin_in = din("qsin", (E, S))
    kcos_in = din("kcos", (E, S)); ksin_in = din("ksin", (E, S))
    idt_in = din("idt", (128, 128), BF16)
    prot_in = din("prot", (128, 128), BF16)
    tri_in = din("tri", (128, 128))
    ones1_in = din("ones1", (128, 1), BF16)
    onesf_in = din("onesf", (1, 128))
    consts_in = din("consts", (128, 8))

    out_d = nc.dram_tensor("out", [S, E], F32, kind="ExternalOutput").ap()

    g_spill = nc.dram_tensor("g_spill", [128, TT_ * E], BF16)
    x_spill = nc.dram_tensor("x_spill", [128, TT_ * E], F32)
    x2_spill = nc.dram_tensor("x2_spill", [128, TT_ * E], F32)
    up_spill = nc.dram_tensor("up_spill", [128, IC * S], BF16)
    cc_i = [nc.dram_tensor(f"cc_i{k}", [1, n], F32)
            for k, n in enumerate((1, 2, 1, 1))]
    cc_o = [nc.dram_tensor(f"cc_o{k}", [1, n * NCORES], F32)
            for k, n in enumerate((1, 2, 1, 1))]

    ctx0 = ExitStack()
    with ctx0:
        sb0 = lambda *a, **kw: ctx0.enter_context(nc.sbuf_tensor(*a, **kw))
        psa = lambda *a: ctx0.enter_context(nc.psum_tensor(*a))

        IDT = sb0("IDT", [128, 128], BF16)
        PROT = sb0("PROT", [128, 128], BF16)
        TRI = sb0("TRI", [128, 128], F32)
        ONES1 = sb0("ONES1", [128, 1], BF16)
        ONESF = sb0("ONESF", [1, 128], F32)
        CONSTS = sb0("CONSTS", [128, 8], F32)
        s1c = sb0("s1c", [128, TT_], F32)
        s2c = sb0("s2c", [128, TT_], F32)
        meanc = sb0("meanc", [128, TT_], F32)
        m2c = sb0("m2c", [128, TT_], F32)
        varc = sb0("varc", [128, TT_], F32)
        rootc = sb0("rootc", [128, TT_], F32)
        rstdc = sb0("rstdc", [128, TT_], F32)
        den_tm = sb0("den_tm", [128, TT_ * H], F32)
        redcols = sb0("redcols", [128, 64], F32)
        redsm = sb0("redsm", [128, 8], F32)
        redb = sb0("redb", [128, 8], BF16)
        sm = sb0("sm", [1, 16], F32)
        ag_sb = sb0("ag_sb", [1, 2 * NCORES], F32)
        bc = sb0("bc", [128, 16], F32)
        sqscr = sb0("sqscr", [128, E], F32)
        d2scr = sb0("d2scr", [128, 8], F32)
        msqscr = sb0("msqscr", [128, 8], F32)
        argscr = sb0("argscr", [128, 8], F32)
        rootscr = sb0("rootscr", [128, 8], F32)
        sclscr = sb0("sclscr", [128, 8], F32)

        P = [psa(f"P{i}", [128, 512], F32) for i in range(6)]
        P6 = psa("P6", [128, 512], F32)
        PT = psa("PT", [128, 1024], BF16)

        block = ctx0.enter_context(nc.Block())
        pr = Prog(nc, block, ctx0)
        ST = {n: Stream(nc, "st_" + n, ctx0) for n in
              ("cs", "x1", "xsp", "wq", "tA", "tB", "gw", "x2l", "gp",
               "x2sp", "xr2", "ccb", "wde", "upw", "ub", "uc", "xr", "out")}
        outb0 = sb0("outb0", [128, E], F32)
        open_stacks = []

        def early_out():
            def f(v):
                pr.wait(v, "te")
                pr.wait(v, "se")
                v.memset(outb0[:], 0.0)
            pr.vec(f)

            def fg(g):
                pr.wait(g, "ve")
                for t in range(TT_):
                    ST["out"].dma(g, out_d[t * 128:(t + 1) * 128, :], outb0[:])
                ST["out"].wait(g)
            block.gpsimd(fg)
            for st_ in reversed(open_stacks):
                st_.close()

        def s0(sync):
            for dst, src in ((TRI, tri_in), (ONES1, ones1_in),
                             (ONESF, onesf_in), (CONSTS, consts_in),
                             (IDT, idt_in), (PROT, prot_in)):
                ST["cs"].dma(sync, dst[:], src)
        block.sync(s0)

        ctxR1 = ExitStack()
        open_stacks.append(ctxR1)
        X1 = ctxR1.enter_context(nc.sbuf_tensor("X1", [128, TT_ * E], F32,
                                                side="right"))

        def s0b(sync):
            for t in range(TT_):
                ST["x1"].dma(sync, X1[:, t * E:(t + 1) * E],
                             x_in[t * 128:(t + 1) * 128, :])
        block.sync(s0b)

        def ln_stats(Xbuf, wait_specs):
            def f_se(s):
                for wfn in wait_specs:
                    wfn(s)
                for t in range(TT_):
                    s.activation(sqscr[:], Xbuf[:, t * E:(t + 1) * E],
                                 AF.Square, accum_out=s2c[:, t:t + 1])
            pr.sca(f_se)
            se_sq = pr.cnt["se"]

            def f_ve(v):
                for wfn in wait_specs:
                    wfn(v)
                for t in range(TT_):
                    v.tensor_reduce(s1c[:, t:t + 1], Xbuf[:, t * E:(t + 1) * E],
                                    AX.X, ALU.add)
                pr.wait(v, "se", se_sq)
                v.tensor_single_scalar(meanc[:, 0:TT_], s1c[:, 0:TT_],
                                       1.0 / E, ALU.mult)
                v.tensor_tensor(m2c[:, 0:TT_], meanc[:, 0:TT_],
                                meanc[:, 0:TT_], ALU.mult)
                v.scalar_tensor_tensor(varc[:, 0:TT_], s2c[:, 0:TT_],
                                       1.0 / E, m2c[:, 0:TT_],
                                       ALU.mult, ALU.subtract)
            pr.vec(f_ve)
            ve_var = pr.cnt["ve"]

            def f_se2(s):
                pr.wait(s, "ve", ve_var)
                s.activation(rootc[:, 0:TT_], varc[:, 0:TT_], AF.Sqrt,
                             bias=CONSTS[:, K_LNEPS:K_LNEPS + 1])
            pr.sca(f_se2)
            se_root = pr.cnt["se"]

            def f_ve2(v):
                pr.wait(v, "se", se_root)
                v.reciprocal(rstdc[:, 0:TT_], rootc[:, 0:TT_])
            pr.vec(f_ve2)

        ln_stats(X1, [lambda e: ST["cs"].wait(e), lambda e: ST["x1"].wait(e)])

        xln = ctxR1.enter_context(nc.sbuf_tensor("xln", [128, TT_ * E], BF16,
                                                 side="right"))

        def s1_ve(v):
            for t in range(TT_):
                v.tensor_scalar(xln[:, t * E:(t + 1) * E],
                                X1[:, t * E:(t + 1) * E],
                                meanc[:, t:t + 1], rstdc[:, t:t + 1],
                                ALU.subtract, ALU.mult)
        pr.vec(s1_ve)
        ve_ln1 = pr.cnt["ve"]

        def s1_g(g):
            pr.wait(g, "ve", ve_ln1)
            ST["xsp"].dma(g, x_spill.ap(), X1[:])
        block.gpsimd(s1_g)

        if stop_after < 2:
            early_out()
            return nc, list(ap_in.keys())

        ctxQK = ExitStack()
        open_stacks.append(ctxQK)
        xlnT = ctxQK.enter_context(nc.sbuf_tensor("xlnT", [128, EC * S], BF16))

        tr_jobs = [(t, c) for t in range(TT_) for c in range(EC)]
        ngrp = len(tr_jobs) // 8
        te_b2 = pr.cnt["te"]
        ve_b2 = pr.cnt["ve"]

        def s2_te(t_e):
            pr.wait(t_e, "ve", ve_ln1)
            ST["xsp"].wait(t_e)
            for gi in range(ngrp):
                if gi > 0:
                    t_e.wait_ge(pr.sems["ve"], ve_b2 + 8 * gi)
                for k in range(8):
                    t, c = tr_jobs[gi * 8 + k]
                    ins = t_e.transpose(
                        PT[:, k * 128:(k + 1) * 128],
                        xln[:, t * E + c * 128:t * E + (c + 1) * 128], IDT[:])
                pr.inc(ins, "te")
        block.tensor(s2_te)

        def s2_ve(v):
            for gi in range(ngrp):
                pr.wait(v, "te", te_b2 + gi + 1)
                for k in range(8):
                    t, c = tr_jobs[gi * 8 + k]
                    v.tensor_copy(
                        xlnT[:, c * S + t * 128:c * S + (t + 1) * 128],
                        PT[:, k * 128:(k + 1) * 128])
            assert pr.cnt["ve"] == ve_b2 + 8 * ngrp
        pr.vec(s2_ve)
        ve_xlnT = pr.cnt["ve"]

        ctxR1.close()
        open_stacks.remove(ctxR1)

        if stop_after < 3:
            early_out()
            return nc, list(ap_in.keys())

        wbuf = ctxQK.enter_context(nc.sbuf_tensor("wbuf", [128, EC * E], BF16))
        wbuf2 = ctxQK.enter_context(nc.sbuf_tensor("wbuf2", [128, EC * E], BF16))
        tabc = ctxQK.enter_context(nc.sbuf_tensor("tabc", [128, 2 * S], BF16))
        tabc2 = ctxQK.enter_context(nc.sbuf_tensor("tabc2", [128, 2 * S], BF16))
        q_sb = ctxQK.enter_context(nc.sbuf_tensor("q_sb", [128, 4 * 512], BF16))
        t1scr = ctxQK.enter_context(nc.sbuf_tensor("t1scr", [128, 512], F32))
        t2scr = ctxQK.enter_context(nc.sbuf_tensor("t2scr", [128, 512], F32))

        ctxR2 = ExitStack()
        open_stacks.append(ctxR2)
        qr = ctxR2.enter_context(nc.sbuf_tensor("qr", [128, EC * S], BF16,
                                                side="right"))
        kr = ctxR2.enter_context(nc.sbuf_tensor("kr", [128, EC * S], BF16,
                                                side="right"))

        for wi, (w_ap, cos_ap, sin_ap, dst, wb) in enumerate([
                (wq_in, qcos_in, qsin_in, qr, wbuf),
                (wk_in, ksin_in, ksin_in, kr, wbuf2)]) if False else \
                enumerate([(wq_in, qcos_in, qsin_in, qr, wbuf),
                           (wk_in, kcos_in, ksin_in, kr, wbuf2)]):
            niter = EC * NT
            te_b = pr.cnt["te"]
            ve_b = pr.cnt["ve"]
            tab_cnt = {}

            def f_g(g, w_ap=w_ap, cos_ap=cos_ap, sin_ap=sin_ap, wb=wb,
                    ve_b=ve_b, wi=wi, tab_cnt=tab_cnt):
                ST["wq"].dma(g, wb[:].rearrange("p (c o) -> p c o", o=E),
                             w_ap.rearrange("(c p) o -> p c o", p=128))
                for oc in range(EC):
                    st_t = ST["tA"] if oc % 2 == 0 else ST["tB"]
                    buf = tabc if oc % 2 == 0 else tabc2
                    if oc >= 2:
                        g.wait_ge(pr.sems["ve"], ve_b + 4 * NT * (oc - 1))
                    elif wi == 1 and ve_b > 0:
                        g.wait_ge(pr.sems["ve"], ve_b)
                    st_t.dma(g, buf[:, 0:S],
                             cos_ap[oc * 128:(oc + 1) * 128, :])
                    st_t.dma(g, buf[:, S:2 * S],
                             sin_ap[oc * 128:(oc + 1) * 128, :])
                    tab_cnt[oc] = st_t.cnt
            block.gpsimd(f_g)
            wq_cnt = ST["wq"].cnt

            def f_te(t_e, wb=wb, te_b=te_b, ve_b=ve_b, wq_cnt=wq_cnt):
                pr.wait(t_e, "ve", ve_b)
                ST["wq"].wait(t_e, wq_cnt)
                for it in range(niter):
                    oc, ts_ = divmod(it, NT)
                    bank = P[it % 4]
                    if it >= 4:
                        t_e.wait_ge(pr.sems["ve"], ve_b + 4 * (it - 4) + 1)
                    if it >= 2:
                        t_e.wait_ge(pr.sems["ve"], ve_b + 4 * (it - 2) + 3)
                    for kc in range(EC):
                        ins = t_e.matmul(
                            bank[:],
                            wb[:, kc * E + oc * 128:kc * E + (oc + 1) * 128],
                            xlnT[:, kc * S + ts_ * 512:kc * S + (ts_ + 1) * 512],
                            start=(kc == 0), stop=(kc == EC - 1))
                    pr.inc(ins, "te")
                    t_e.wait_ge(pr.sems["ve"], ve_b + 4 * it + 1)
                    pr.inc(t_e.matmul(P[4 + it % 2][:], PROT[:],
                                      q_sb[:, (it % 4) * 512:(it % 4 + 1) * 512],
                                      start=True, stop=True), "te")
                assert pr.cnt["te"] == te_b + 2 * niter
            block.tensor(f_te)

            def f_ve(v, dst=dst, te_b=te_b, ve_b=ve_b, tab_cnt=tab_cnt):
                for it in range(niter):
                    oc, ts_ = divmod(it, NT)
                    v.wait_ge(pr.sems["te"], te_b + 2 * it + 1)
                    v.tensor_copy(q_sb[:, (it % 4) * 512:(it % 4 + 1) * 512],
                                  P[it % 4][:])
                    v.wait_ge(pr.sems["te"], te_b + 2 * it + 2)
                    (ST["tA"] if oc % 2 == 0 else ST["tB"]).wait(
                        v, tab_cnt[oc])
                    buf = tabc if oc % 2 == 0 else tabc2
                    sl = slice(ts_ * 512, (ts_ + 1) * 512)
                    v.tensor_tensor(t1scr[:],
                                    q_sb[:, (it % 4) * 512:(it % 4 + 1) * 512],
                                    buf[:, sl], ALU.mult)
                    v.tensor_tensor(t2scr[:], P[4 + it % 2][:],
                                    buf[:, S + ts_ * 512:S + (ts_ + 1) * 512],
                                    ALU.mult)
                    v.tensor_tensor(
                        dst[:, oc * S + ts_ * 512:oc * S + (ts_ + 1) * 512],
                        t1scr[:], t2scr[:], ALU.add)
                assert pr.cnt["ve"] == ve_b + 4 * niter
            pr.vec(f_ve)

        te_qk = pr.cnt["te"]
        ve_qk = pr.cnt["ve"]

        if stop_after < 4:
            early_out()
            return nc, list(ap_in.keys())

        vtm = ctxR2.enter_context(nc.sbuf_tensor("vtm", [128, TT_ * E], BF16,
                                                 side="right"))
        g_stage = ctxQK.enter_context(nc.sbuf_tensor("g_stage", [128, 2 * E],
                                                     BF16))

        def s3b_g(g):
            pr.wait(g, "te", te_qk)
            ST["wq"].dma(g, wbuf[:].rearrange("p (c o) -> p c o", o=E),
                         wv_in.rearrange("(c p) o -> p c o", p=128))
            ST["wq"].dma(g, wbuf2[:].rearrange("p (c o) -> p c o", o=E),
                         wg_in.rearrange("(c p) o -> p c o", p=128))
        block.gpsimd(s3b_g)
        wq_vg = ST["wq"].cnt

        te_b3 = pr.cnt["te"]
        ve_b3 = pr.cnt["ve"]
        se_b3 = pr.cnt["se"]

        def s3b_te(t_e):
            ST["wq"].wait(t_e, wq_vg)
            pr.wait(t_e, "ve", ve_qk)
            for t in range(TT_):
                if t > 0:
                    t_e.wait_ge(pr.sems["ve"], ve_b3 + 2 * NO * t)
                for kc in range(EC):
                    for no in range(NO):
                        ins = t_e.matmul(
                            P[no][:],
                            xlnT[:, kc * S + t * 128:kc * S + (t + 1) * 128],
                            wbuf[:, kc * E + no * 512:kc * E + (no + 1) * 512],
                            start=(kc == 0), stop=(kc == EC - 1))
                    for no in range(NO):
                        ins = t_e.matmul(
                            P[2 + no][:],
                            xlnT[:, kc * S + t * 128:kc * S + (t + 1) * 128],
                            wbuf2[:, kc * E + no * 512:
                                  kc * E + (no + 1) * 512],
                            start=(kc == 0), stop=(kc == EC - 1))
                pr.inc(ins, "te")
        block.tensor(s3b_te)

        def s3b_ve(v):
            for t in range(TT_):
                v.wait_ge(pr.sems["te"], te_b3 + t + 1)
                if t >= 2:
                    ST["gw"].wait(v, 16 * (t - 1))
                for no in range(NO):
                    v.tensor_copy(
                        vtm[:, t * E + no * 512:t * E + (no + 1) * 512],
                        P[no][:])
                for no in range(NO):
                    v.tensor_copy(
                        g_stage[:, (t % 2) * E + no * 512:
                                (t % 2) * E + (no + 1) * 512],
                        P[2 + no][:])
            assert pr.cnt["ve"] == ve_b3 + 2 * NO * TT_
        pr.vec(s3b_ve)

        def s3b_gs(g):
            for t in range(TT_):
                g.wait_ge(pr.sems["ve"], ve_b3 + 2 * NO * (t + 1))
                ST["gw"].dma(g, g_spill.ap()[:, t * E:(t + 1) * E],
                             g_stage[:, (t % 2) * E:(t % 2 + 1) * E])
        block.gpsimd(s3b_gs)
        te_vg = pr.cnt["te"]
        ve_vg = pr.cnt["ve"]
        se_vg = pr.cnt["se"]

        ctxQK.close()
        open_stacks.remove(ctxQK)

        if stop_after < 5 or stop_after in (41, 42):
            early_out()
            return nc, list(ap_in.keys())

        ctxS4 = ExitStack()
        open_stacks.append(ctxS4)
        SA = ctxS4.enter_context(nc.sbuf_tensor("SA", [128, 512], BF16))
        SB = ctxS4.enter_context(nc.sbuf_tensor("SB", [128, 512], BF16))
        ABSA = ctxS4.enter_context(nc.sbuf_tensor("ABSA", [128, 512], BF16))
        ABSB = ctxS4.enter_context(nc.sbuf_tensor("ABSB", [128, 512], BF16))
        avfm = ctxS4.enter_context(nc.sbuf_tensor("avfm", [128, 512], BF16))
        avtm = ctxS4.enter_context(nc.sbuf_tensor("avtm", [128, 512], BF16))
        den_sb = ctxS4.enter_context(nc.sbuf_tensor("den_sb", [128, 512], BF16))
        gsil = ctxS4.enter_context(nc.sbuf_tensor("gsil", [128, 512], BF16))
        tmpp = ctxS4.enter_context(nc.sbuf_tensor("tmpp", [128, 128], BF16))
        g_pair = ctxS4.enter_context(
            nc.sbuf_tensor("g_pair", [128, TT_ * 128], BF16))
        X2 = ctxS4.enter_context(nc.sbuf_tensor("X2", [128, TT_ * E], F32))

        def s4_x2(g):
            pr.wait(g, "te", te_vg)
            pr.wait(g, "ve", ve_vg)
            pr.wait(g, "se", se_vg)
            ST["x2l"].dma(g, X2[:], x_spill.ap())
        block.gpsimd(s4_x2)

        for p in range(NP):
            def s4_gp(g, p=p):
                if p == 0:
                    ST["gw"].wait(g)
                else:
                    pr.wait(g, "se")
                for q4 in range(TT_ // 4):
                    ST["gp"].dma(
                        g,
                        g_pair[:, q4 * 512:(q4 + 1) * 512].rearrange(
                            "q (t c) -> q t c", c=128),
                        bass.AP(g_spill, p * 128 + q4 * 4 * E,
                                [[TT_ * E, 128], [E, 4], [1, 128]]))
            block.gpsimd(s4_gp)
            gp_cnt = ST["gp"].cnt

            for qc in range(NQ):
                nk = 4 * qc + 4
                te_b4 = pr.cnt["te"]
                ve_b4 = pr.cnt["ve"]
                ve_cum = [0] * (nk + 1)
                for kt in range(nk):
                    j = kt - 4 * qc
                    ve_cum[kt + 1] = ve_cum[kt] + (
                        4 if j < 0 else (4 if j == 3 else 6))

                def f_te(t_e, p=p, qc=qc, nk=nk, te_b4=te_b4,
                         ve_b4=ve_b4, ve_cum=ve_cum):
                    pr.wait(t_e, "ve", ve_vg)
                    for kt in range(nk):
                        j = kt - 4 * qc
                        off = max(j, 0) * 128
                        qs0 = qc * 512
                        t_e.matmul(
                            P[0][:, off:512],
                            kr[0:64, p * S + kt * 128:p * S + (kt + 1) * 128],
                            qr[0:64, p * S + qs0 + off:p * S + qs0 + 512],
                            start=True, stop=True)
                        ins = t_e.matmul(
                            P[1][:, off:512],
                            kr[64:128, p * S + kt * 128:p * S + (kt + 1) * 128],
                            qr[64:128, p * S + qs0 + off:p * S + qs0 + 512],
                            start=True, stop=True, tile_position=(64, 0))
                        pr.inc(ins, "te")
                        t_e.wait_ge(pr.sems["ve"], ve_b4 + ve_cum[kt + 1])
                        first = (kt == 0)
                        last = (kt == nk - 1)
                        t_e.matmul(P[2][0:64, off:512],
                                   vtm[:, kt * E + (2 * p) * D:
                                       kt * E + (2 * p) * D + D],
                                   SA[:, off:512], start=first, stop=last)
                        t_e.matmul(P[3][64:128, off:512],
                                   vtm[:, kt * E + (2 * p + 1) * D:
                                       kt * E + (2 * p + 1) * D + D],
                                   SB[:, off:512], start=first, stop=last,
                                   tile_position=(0, 64), skip_group_check=True)
                        t_e.matmul(P[4][0:1, off:512], ONES1[:],
                                   ABSA[:, off:512], start=first, stop=last)
                        ins = t_e.matmul(P[5][64:65, off:512], ONES1[:],
                                         ABSB[:, off:512], start=first,
                                         stop=last, tile_position=(0, 64),
                                         skip_group_check=True)
                        pr.inc(ins, "te")
                block.tensor(f_te)

                def f_ve(v, p=p, qc=qc, nk=nk, te_b4=te_b4):
                    for kt in range(nk):
                        j = kt - 4 * qc
                        v.wait_ge(pr.sems["te"], te_b4 + 2 * kt + 1)
                        if j >= 0:
                            v.tensor_tensor(SA[:, j * 128:(j + 1) * 128],
                                            P[0][:, j * 128:(j + 1) * 128],
                                            TRI[:], ALU.mult)
                            v.tensor_tensor(SB[:, j * 128:(j + 1) * 128],
                                            P[1][:, j * 128:(j + 1) * 128],
                                            TRI[:], ALU.mult)
                        if j < 3:
                            off2 = (j + 1) * 128 if j >= 0 else 0
                            v.tensor_copy(SA[:, off2:512], P[0][:, off2:512])
                            v.tensor_copy(SB[:, off2:512], P[1][:, off2:512])
                        off = max(j, 0) * 128
                        v.scalar_tensor_tensor(ABSA[:, off:512], SA[:, off:512],
                                               -1.0, SA[:, off:512],
                                               ALU.mult, ALU.max)
                        v.scalar_tensor_tensor(ABSB[:, off:512], SB[:, off:512],
                                               -1.0, SB[:, off:512],
                                               ALU.mult, ALU.max)
                pr.vec(f_ve)

                te_av = pr.cnt["te"]

                def e_ve1(v, te_av=te_av):
                    pr.wait(v, "te", te_av)
                    v.tensor_copy(avfm[0:64, :], P[2][0:64, :])
                    v.tensor_copy(avfm[64:128, :], P[3][64:128, :])
                    v.tensor_copy(den_sb[0:1, :], P[4][0:1, :])
                    v.tensor_copy(den_sb[64:65, :], P[5][64:65, :])
                pr.vec(e_ve1)
                ve_ep1 = pr.cnt["ve"]

                def e_te(t_e, ve_ep1=ve_ep1):
                    pr.wait(t_e, "ve", ve_ep1)
                    for k in range(4):
                        t_e.transpose(PT[:, k * 128:(k + 1) * 128],
                                      avfm[:, k * 128:(k + 1) * 128], IDT[:])
                    for k in range(4):
                        ins = t_e.transpose(
                            PT[:, 512 + k * 128:512 + (k + 1) * 128],
                            den_sb[:, k * 128:(k + 1) * 128], IDT[:])
                    pr.inc(ins, "te")
                block.tensor(e_te)
                te_ep = pr.cnt["te"]

                def e_ve2(v, p=p, qc=qc, te_ep=te_ep):
                    pr.wait(v, "te", te_ep)
                    v.tensor_copy(avtm[:], PT[:, 0:512])
                    for tau in range(4):
                        tg = qc * 4 + tau
                        v.tensor_copy(
                            den_tm[:, tg * H + 2 * p:tg * H + 2 * p + 2],
                            bass.AP(PT, 512 + tau * 128,
                                    [[1024, 128], [64, 2]]))
                    dv = bass.AP(den_tm, (qc * 4) * H + 2 * p,
                                 [[TT_ * H, 128], [H, 4], [1, 2]])
                    v.tensor_scalar(dv, dv, 1.0, 50000.0, ALU.max, ALU.min)
                    v.tensor_tensor(d2scr[:], dv, dv, ALU.mult)
                pr.vec(e_ve2)
                ve_ep2 = pr.cnt["ve"]

                def e_se1(s, ve_ep2=ve_ep2):
                    pr.wait(s, "ve", ve_ep2)
                    s.activation(sqscr[:, 0:512], avtm[:], AF.Square,
                                 scale=CONSTS[:, K_EIGHTH:K_EIGHTH + 1])
                pr.sca(e_se1)
                se_ep1 = pr.cnt["se"]

                def e_ve3(v, se_ep1=se_ep1):
                    pr.wait(v, "se", se_ep1)
                    v.tensor_reduce(
                        msqscr[:],
                        bass.AP(sqscr, 0, [[E, 128], [64, 8], [1, 64]]),
                        AX.X, ALU.add)
                    v.scalar_tensor_tensor(argscr[:], d2scr[:], 1e-6,
                                           msqscr[:], ALU.mult, ALU.add)
                pr.vec(e_ve3)
                ve_ep3 = pr.cnt["ve"]

                def e_se2(s, p=p, qc=qc, ve_ep3=ve_ep3, gp_cnt=gp_cnt):
                    pr.wait(s, "ve", ve_ep3)
                    s.activation(rootscr[:], argscr[:], AF.Sqrt)
                    ST["gp"].wait(s, gp_cnt)
                    s.activation(gsil[:], g_pair[:, qc * 512:(qc + 1) * 512],
                                 AF.Silu)
                pr.sca(e_se2)
                se_ep2 = pr.cnt["se"]

                def e_ve4(v, p=p, qc=qc, se_ep2=se_ep2):
                    pr.wait(v, "se", se_ep2)
                    ST["x2l"].wait(v)
                    v.reciprocal(sclscr[:], rootscr[:])
                    for tau in range(4):
                        tg = qc * 4 + tau
                        for h in range(2):
                            v.scalar_tensor_tensor(
                                tmpp[:, h * D:(h + 1) * D],
                                avtm[:, tau * 128 + h * D:
                                     tau * 128 + (h + 1) * D],
                                sclscr[:, tau * 2 + h:tau * 2 + h + 1],
                                gsil[:, tau * 128 + h * D:
                                     tau * 128 + (h + 1) * D],
                                ALU.mult, ALU.mult)
                        v.tensor_tensor(
                            X2[:, tg * E + p * 128:tg * E + (p + 1) * 128],
                            X2[:, tg * E + p * 128:tg * E + (p + 1) * 128],
                            tmpp[:], ALU.add)
                pr.vec(e_ve4)

        ve_s4 = pr.cnt["ve"]
        te_s4 = pr.cnt["te"]

        if stop_after < 6:
            early_out()
            return nc, list(ap_in.keys())

        ln_stats(X2, [lambda e, _v=ve_s4: pr.wait(e, "ve", _v)])

        def s5_sp(g):
            pr.wait(g, "ve")
            pr.wait(g, "se")
            ST["x2sp"].dma(g, x2_spill.ap(), X2[:])
        block.gpsimd(s5_sp)
        te_preF = pr.cnt["te"]
        ctxS4.close()
        open_stacks.remove(ctxS4)
        ctxR2.close()
        open_stacks.remove(ctxR2)

        ctxF1 = ExitStack()
        open_stacks.append(ctxF1)
        xln2 = ctxF1.enter_context(nc.sbuf_tensor("xln2", [128, TT_ * E],
                                                  BF16))
        xres2 = ctxF1.enter_context(nc.sbuf_tensor("xres2", [128, 2 * E], F32))

        ve_b5 = pr.cnt["ve"]

        def s5_g2(g):
            pr.wait(g, "te", te_preF)
            ST["x2sp"].wait(g)
            for t in range(TT_):
                if t >= 2:
                    g.wait_ge(pr.sems["ve"], ve_b5 + (t - 1))
                ST["xr2"].dma(g, xres2[:, (t % 2) * E:(t % 2 + 1) * E],
                              x2_spill.ap()[:, t * E:(t + 1) * E])
        block.gpsimd(s5_g2)

        def s5_ve(v):
            pr.wait(v, "te", te_preF)
            for t in range(TT_):
                ST["xr2"].wait(v, 16 * (t + 1))
                v.tensor_scalar(xln2[:, t * E:(t + 1) * E],
                                xres2[:, (t % 2) * E:(t % 2 + 1) * E],
                                meanc[:, t:t + 1], rstdc[:, t:t + 1],
                                ALU.subtract, ALU.mult)
            v.tensor_reduce(redcols[:, 0:1], xln2[:], AX.X, ALU.max,
                            apply_absolute_value=True)
            v.tensor_single_scalar(redb[:, 0:1], redcols[:, 0:1], 1.0,
                                   ALU.mult)
        pr.vec(s5_ve)
        ve_ln2 = pr.cnt["ve"]

        def part_max_to_sm(col_redb, sm_col, wait_ve):
            def f_te(t_e):
                pr.wait(t_e, "ve", wait_ve)
                pr.inc(t_e.transpose(PT[0:1, 0:128],
                                     redb[:, col_redb:col_redb + 1], IDT[:]),
                       "te")
            block.tensor(f_te)
            te_t = pr.cnt["te"]

            def f_ve(v):
                pr.wait(v, "te", te_t)
                v.tensor_reduce(sm[:, sm_col:sm_col + 1], PT[0:1, 0:128],
                                AX.X, ALU.max)
            pr.vec(f_ve)
            return pr.cnt["ve"]

        ve_mx = part_max_to_sm(0, 0, ve_ln2)

        def collective_round(k, src_cols, wait_ve):
            n = len(src_cols)

            def f_g(g):
                pr.wait(g, "ve", wait_ve)
                ap_src = sm[0:1, src_cols[0]:src_cols[0] + n]
                ST["ccb"].dma(g, cc_i[k].ap(), ap_src)
                ST["ccb"].wait(g)
                pr.inc(g.collective_compute(
                    "AllGather", ALU.bypass,
                    replica_groups=[list(range(NCORES))],
                    ins=[cc_i[k].ap().opt()], outs=[cc_o[k].ap().opt()]),
                    "cc", 1)
                pr.wait(g, "cc")
                ST["ccb"].dma(g, ag_sb[:, 0:n * NCORES], cc_o[k].ap())
            block.gpsimd(f_g)
            cc_done = ST["ccb"].cnt

            def f_ve(v):
                ST["ccb"].wait(v, cc_done)
                for i in range(n):
                    v.tensor_reduce(
                        sm[:, src_cols[i]:src_cols[i] + 1],
                        bass.AP(ag_sb, i, [[2 * NCORES, 1], [n, NCORES]]),
                        AX.X, ALU.max)
            pr.vec(f_ve)
            return pr.cnt["ve"]

        ve_r1 = collective_round(0, [0], ve_mx)

        def r1_ve(v):
            pr.wait(v, "ve", ve_r1)
            v.tensor_scalar(sm[:, 1:2], sm[:, 0:1], 1.0 / 32767.0, 1e-12,
                            ALU.mult, ALU.max)
            v.reciprocal(sm[:, 2:3], sm[:, 1:2])
        pr.vec(r1_ve)
        ve_r1b = pr.cnt["ve"]

        def bcast(cols_src, cols_dst, wait_ve):
            n = len(cols_src)
            assert cols_src == list(range(cols_src[0], cols_src[0] + n))

            def f_te(t_e):
                pr.wait(t_e, "ve", wait_ve)
                pr.inc(t_e.matmul(P6[:, 0:n], ONESF[:],
                                  sm[0:1, cols_src[0]:cols_src[0] + n],
                                  start=True, stop=True), "te")
            block.tensor(f_te)
            te_t = pr.cnt["te"]

            def f_ve(v):
                pr.wait(v, "te", te_t)
                for i, cd in enumerate(cols_dst):
                    v.tensor_copy(bc[:, cd:cd + 1], P6[:, i:i + 1])
            pr.vec(f_ve)
            return pr.cnt["ve"]

        ve_bc1 = bcast([1, 2], [C_SX, C_RSX], ve_r1b)

        if stop_after < 7:
            early_out()
            return nc, list(ap_in.keys())

        y6 = ctxF1.enter_context(nc.sbuf_tensor("y6", [128, 2 * E], F32))
        Xq = ctxF1.enter_context(nc.sbuf_tensor("Xq", [128, TT_ * E], BF16))

        ctxR3 = ExitStack()
        open_stacks.append(ctxR3)
        XqT = ctxR3.enter_context(nc.sbuf_tensor("XqT", [128, EC * S], BF16,
                                                 side="right"))

        se_b6 = pr.cnt["se"]
        ve_b6 = pr.cnt["ve"]

        def s6_se(s):
            pr.wait(s, "ve", ve_bc1)
            for t in range(TT_):
                if t >= 2:
                    s.wait_ge(pr.sems["ve"], ve_b6 + (t - 1))
                s.activation(y6[:, (t % 2) * E:(t % 2 + 1) * E],
                             xln2[:, t * E:(t + 1) * E], AF.Copy,
                             scale=bc[:, C_RSX:C_RSX + 1])
        pr.sca(s6_se)

        def s6_ve(v):
            for t in range(TT_):
                v.wait_ge(pr.sems["se"], se_b6 + t + 1)
                v.tensor_scalar(Xq[:, t * E:(t + 1) * E],
                                y6[:, (t % 2) * E:(t % 2 + 1) * E],
                                MAGIC, MAGIC, ALU.add, ALU.subtract)
            assert pr.cnt["ve"] == ve_b6 + TT_
        pr.vec(s6_ve)
        ve_xq = pr.cnt["ve"]

        te_b6 = pr.cnt["te"]
        ve_b6b = pr.cnt["ve"]

        def s6_te(t_e):
            pr.wait(t_e, "ve", ve_xq)
            for gi in range(ngrp):
                if gi > 0:
                    t_e.wait_ge(pr.sems["ve"], ve_b6b + 8 * gi)
                for k in range(8):
                    t, c = tr_jobs[gi * 8 + k]
                    ins = t_e.transpose(
                        PT[:, k * 128:(k + 1) * 128],
                        Xq[:, t * E + c * 128:t * E + (c + 1) * 128], IDT[:])
                pr.inc(ins, "te")
        block.tensor(s6_te)

        def s6_ve2(v):
            for gi in range(ngrp):
                v.wait_ge(pr.sems["te"], te_b6 + gi + 1)
                for k in range(8):
                    t, c = tr_jobs[gi * 8 + k]
                    v.tensor_copy(
                        XqT[:, c * S + t * 128:c * S + (t + 1) * 128],
                        PT[:, k * 128:(k + 1) * 128])
        pr.vec(s6_ve2)
        ve_xqT = pr.cnt["ve"]

        ctxF1.close()
        open_stacks.remove(ctxF1)
        ctxF2 = ExitStack()
        open_stacks.append(ctxF2)

        if stop_after < 8:
            early_out()
            return nc, list(ap_in.keys())

        G = ctxF2.enter_context(nc.sbuf_tensor("G", [128, IC * S], BF16))
        ctxPB = ExitStack()
        open_stacks.append(ctxPB)
        yscr = ctxPB.enter_context(nc.sbuf_tensor("yscr", [128, 2 * S], F32))
        sigscr = ctxPB.enter_context(nc.sbuf_tensor("sigscr", [128, S], BF16))
        upc = ctxPB.enter_context(nc.sbuf_tensor("upc", [128, 2 * S], BF16))
        uqscr = ctxPB.enter_context(nc.sbuf_tensor("uqscr", [128, S], BF16))
        t2b = ctxPB.enter_context(nc.sbuf_tensor("t2b", [128, S], BF16))
        ctxUG = ExitStack()
        open_stacks.append(ctxUG)
        wde = ctxUG.enter_context(nc.sbuf_tensor("wde", [128, 2 * E], F32))
        tqscr = ctxUG.enter_context(nc.sbuf_tensor("tqscr", [128, E], F32))
        qwscr = ctxUG.enter_context(nc.sbuf_tensor("qwscr", [128, E], BF16))
        bmscr = ctxUG.enter_context(nc.sbuf_tensor("bmscr", [128, NBU], F32))
        sWscr = ctxUG.enter_context(nc.sbuf_tensor("sWscr", [128, NBU], F32))
        rsWscr = ctxUG.enter_context(nc.sbuf_tensor("rsWscr", [128, NBU], F32))
        wqT = ctxUG.enter_context(nc.sbuf_tensor("wqT", [128, 2 * E], BF16))
        upst = ctxUG.enter_context(nc.sbuf_tensor("upst", [128, S], BF16))

        def dequant_chunk(bufs, nb, width, ld_val, slot):
            wd, tq, qw, bm, sW, rsW, wrow, nbmax = bufs

            def f_ve(v):
                ST["wde"].wait(v, ld_val)
                v.tensor_reduce(
                    bm[:, 0:nb],
                    bass.AP(wd, slot * width, [[wrow, 128], [32, nb], [1, 32]]),
                    AX.X, ALU.max, apply_absolute_value=True)
                v.tensor_scalar(sW[:, 0:nb], bm[:, 0:nb], 1.0 / 7.0,
                                1e-12, ALU.mult, ALU.max)
                v.reciprocal(rsW[:, 0:nb], sW[:, 0:nb])
                v.tensor_tensor(
                    tq[:, 0:width],
                    bass.AP(wd, slot * width, [[wrow, 128], [32, nb], [1, 32]]),
                    bass.AP(rsW, 0, [[nbmax, 128], [1, nb], [0, 32]]),
                    ALU.mult)
                v.tensor_scalar(tq[:, 0:width], tq[:, 0:width],
                                MAGIC, MAGIC, ALU.add, ALU.subtract)
                v.tensor_tensor(
                    qw[:, 0:width],
                    bass.AP(tq, 0, [[width, 128], [32, nb], [1, 32]]),
                    bass.AP(sW, 0, [[nbmax, 128], [1, nb], [0, 32]]),
                    ALU.mult)
            pr.vec(f_ve)
            return pr.cnt["ve"]

        ug_bufs = (wde, tqscr, qwscr, bmscr, sWscr, rsWscr, 2 * E, NBU)

        for mi, (w_src, is_up) in enumerate([(wup_in, True),
                                             (wgate_in, False)]):
            for c in range(IC):
                def f_ld(sync, w_src=w_src, c=c, mi=mi):
                    if mi == 0 and c == 0:
                        ST["x2sp"].wait(sync)
                        pr.wait(sync, "ve")
                        pr.wait(sync, "se")
                        pr.wait(sync, "te")
                    if mi > 0 or c >= 2:
                        pr.wait(sync, "ve")
                    ST["wde"].dma(sync,
                                  wde[:, (c % 2) * E:(c % 2 + 1) * E],
                                  w_src[c * 128:(c + 1) * 128, :])
                block.sync(f_ld)
                ld_w = ST["wde"].cnt

                ve_dq = dequant_chunk(ug_bufs, NBU, E, ld_w, c % 2)

                def f_tr(t_e, ve_dq=ve_dq, c=c, mi=mi):
                    pr.wait(t_e, "ve", ve_dq)
                    if c > 0 or mi > 0:
                        pr.wait(t_e, "ve")
                    for k in range(8):
                        ins = t_e.transpose(PT[:, k * 128:(k + 1) * 128],
                                            qwscr[:, k * 128:(k + 1) * 128],
                                            IDT[:])
                    pr.inc(ins, "te")
                block.tensor(f_tr)
                te_tr = pr.cnt["te"]

                def f_ev(v, te_tr=te_tr, c=c):
                    pr.wait(v, "te", te_tr)
                    v.tensor_copy(wqT[:, (c % 2) * E:(c % 2 + 1) * E], PT[:])
                pr.vec(f_ev)
                ve_wqT = pr.cnt["ve"]

                def f_mm(t_e, ve_wqT=ve_wqT, c=c, mi=mi):
                    pr.wait(t_e, "ve", ve_wqT)
                    if mi == 0 and c == 0:
                        pr.wait(t_e, "ve", ve_xqT)
                    for kc in range(EC):
                        for ts_ in range(NT):
                            ins = t_e.matmul(
                                P[ts_][:],
                                wqT[:, (c % 2) * E + kc * 128:
                                    (c % 2) * E + (kc + 1) * 128],
                                XqT[:, kc * S + ts_ * 512:
                                    kc * S + (ts_ + 1) * 512],
                                start=(kc == 0), stop=(kc == EC - 1))
                    pr.inc(ins, "te")
                block.tensor(f_mm)
                te_mm = pr.cnt["te"]

                def f_evac(v, te_mm=te_mm, c=c, is_up=is_up):
                    pr.wait(v, "te", te_mm)
                    if is_up:
                        ST["upw"].wait(v)
                    for ts_ in range(NT):
                        if is_up:
                            o = upst[:, ts_ * 512:(ts_ + 1) * 512]
                        else:
                            o = G[:, c * S + ts_ * 512:c * S + (ts_ + 1) * 512]
                        v.tensor_scalar(o, P[ts_][:], bc[:, C_SX:C_SX + 1],
                                        None, ALU.mult)
                    src = upst[:] if is_up else G[:, c * S:(c + 1) * S]
                    v.tensor_reduce(redcols[:, mi * IC + c:mi * IC + c + 1],
                                    src, AX.X, ALU.max,
                                    apply_absolute_value=True)
                pr.vec(f_evac)
                ve_ev = pr.cnt["ve"]

                if is_up:
                    def f_sp(g, c=c, ve_ev=ve_ev):
                        pr.wait(g, "ve", ve_ev)
                        ST["upw"].dma(g, up_spill.ap()[:, c * S:(c + 1) * S],
                                      upst[:])
                    block.gpsimd(f_sp)

        ve_s7 = pr.cnt["ve"]

        def s7_red(v):
            pr.wait(v, "ve", ve_s7)
            v.tensor_reduce(redsm[:, 0:1], redcols[:, 0:IC], AX.X, ALU.max)
            v.tensor_reduce(redsm[:, 1:2], redcols[:, IC:2 * IC], AX.X,
                            ALU.max)
            v.tensor_single_scalar(redb[:, 0:2], redsm[:, 0:2], 1.0, ALU.mult)
        pr.vec(s7_red)
        ve_r2a = pr.cnt["ve"]

        def s7_tr(t_e):
            pr.wait(t_e, "ve", ve_r2a)
            t_e.transpose(PT[0:1, 0:128], redb[:, 0:1], IDT[:])
            pr.inc(t_e.transpose(PT[0:1, 128:256], redb[:, 1:2], IDT[:]), "te")
        block.tensor(s7_tr)
        te_r2 = pr.cnt["te"]

        def s7_fin(v):
            pr.wait(v, "te", te_r2)
            v.tensor_reduce(sm[:, 4:5], PT[0:1, 0:128], AX.X, ALU.max)
            v.tensor_reduce(sm[:, 5:6], PT[0:1, 128:256], AX.X, ALU.max)
        pr.vec(s7_fin)
        ve_r2b = pr.cnt["ve"]

        ve_r2 = collective_round(1, [4, 5], ve_r2b)

        def r2_ve(v):
            pr.wait(v, "ve", ve_r2)
            v.tensor_scalar(sm[:, 6:7], sm[:, 4:5], 1.0 / 32767.0, 1e-12,
                            ALU.mult, ALU.max)
            v.tensor_scalar(sm[:, 7:8], sm[:, 5:6], 1.0 / 32767.0, 1e-12,
                            ALU.mult, ALU.max)
            v.reciprocal(sm[:, 8:9], sm[:, 6:7])
            v.reciprocal(sm[:, 9:10], sm[:, 7:8])
        pr.vec(r2_ve)
        ve_r2c = pr.cnt["ve"]
        ve_bc2 = bcast([7, 8, 9], [C_SG, C_RSUP, C_RSG], ve_r2c)

        if stop_after < 9:
            early_out()
            return nc, list(ap_in.keys())

        ctxUG.close()
        open_stacks.remove(ctxUG)
        ctxR3.close()
        open_stacks.remove(ctxR3)

        se_b8 = pr.cnt["se"]
        ve_b8 = pr.cnt["ve"]
        for c in range(IC):
            def a_se1(s, c=c):
                if c == 0:
                    pr.wait(s, "ve", ve_bc2)
                else:
                    s.wait_ge(pr.sems["ve"], ve_b8 + 5 * (c - 1) + 3)
                s.activation(yscr[:, 0:S], G[:, c * S:(c + 1) * S],
                             AF.Copy, scale=bc[:, C_RSG:C_RSG + 1])
            pr.sca(a_se1)

            def a_ve1(v, c=c):
                v.wait_ge(pr.sems["se"], se_b8 + 2 * c + 1)
                v.tensor_scalar(G[:, c * S:(c + 1) * S], yscr[:, 0:S],
                                MAGIC, MAGIC, ALU.add, ALU.subtract)
            pr.vec(a_ve1)

            def a_se2(s, c=c):
                s.wait_ge(pr.sems["ve"], ve_b8 + 5 * c + 1)
                s.activation(yscr[:, 0:S], G[:, c * S:(c + 1) * S],
                             AF.Sigmoid, scale=bc[:, C_SG:C_SG + 1])
            pr.sca(a_se2)

            def a_ve2(v, c=c):
                v.wait_ge(pr.sems["se"], se_b8 + 2 * c + 2)
                v.tensor_scalar(yscr[:, 0:S], yscr[:, 0:S], 65536.0, MAGIC,
                                ALU.mult, ALU.add)
                v.tensor_scalar(sigscr[:], yscr[:, 0:S], MAGIC, 65535.0,
                                ALU.subtract, ALU.min)
                v.tensor_tensor(G[:, c * S:(c + 1) * S],
                                G[:, c * S:(c + 1) * S], sigscr[:], ALU.mult)
                v.tensor_reduce(redcols[:, 32 + c:33 + c],
                                G[:, c * S:(c + 1) * S], AX.X, ALU.max,
                                apply_absolute_value=True)
            pr.vec(a_ve2)
        ve_s8a = pr.cnt["ve"]

        def s8_red(v):
            pr.wait(v, "ve", ve_s8a)
            v.tensor_reduce(redsm[:, 2:3], redcols[:, 32:32 + IC], AX.X,
                            ALU.max)
            v.tensor_single_scalar(redb[:, 2:3], redsm[:, 2:3], 1.0, ALU.mult)
        pr.vec(s8_red)
        ve_r3a = pr.cnt["ve"]

        def s8_tr(t_e):
            pr.wait(t_e, "ve", ve_r3a)
            pr.inc(t_e.transpose(PT[0:1, 0:128], redb[:, 2:3], IDT[:]), "te")
        block.tensor(s8_tr)
        te_r3 = pr.cnt["te"]

        def s8_fin(v):
            pr.wait(v, "te", te_r3)
            v.tensor_reduce(sm[:, 10:11], PT[0:1, 0:128], AX.X, ALU.max)
        pr.vec(s8_fin)
        ve_r3b = pr.cnt["ve"]

        ve_r3 = collective_round(2, [10], ve_r3b)

        def r3_ve(v):
            pr.wait(v, "ve", ve_r3)
            v.tensor_tensor(sm[:, 11:12], sm[:, 10:11], sm[:, 7:8], ALU.mult)
            v.tensor_scalar(sm[:, 11:12], sm[:, 11:12],
                            1.0 / (65536.0 * 32767.0), 1e-12,
                            ALU.mult, ALU.max)
            v.reciprocal(sm[:, 12:13], sm[:, 11:12])
            v.tensor_tensor(sm[:, 12:13], sm[:, 12:13], sm[:, 7:8], ALU.mult)
            v.tensor_single_scalar(sm[:, 12:13], sm[:, 12:13],
                                   1.0 / 65536.0, ALU.mult)
        pr.vec(r3_ve)
        ve_r3c = pr.cnt["ve"]
        ve_bc3 = bcast([12], [C_RST], ve_r3c)

        se_b8b = pr.cnt["se"]
        ve_b8b = pr.cnt["ve"]
        for c in range(IC):
            def b_se(s, c=c):
                if c == 0:
                    pr.wait(s, "ve", ve_bc3)
                else:
                    s.wait_ge(pr.sems["ve"], ve_b8b + c)
                s.activation(yscr[:, 0:S], G[:, c * S:(c + 1) * S],
                             AF.Copy, scale=bc[:, C_RST:C_RST + 1])
            pr.sca(b_se)

            def b_ve(v, c=c):
                v.wait_ge(pr.sems["se"], se_b8b + c + 1)
                v.tensor_scalar(G[:, c * S:(c + 1) * S], yscr[:, 0:S],
                                MAGIC, MAGIC, ALU.add, ALU.subtract)
            pr.vec(b_ve)
        ve_g2 = pr.cnt["ve"]

        if stop_after < 10:
            early_out()
            return nc, list(ap_in.keys())

        se_b8c = pr.cnt["se"]
        ve_b8c = pr.cnt["ve"]

        for c in range(IC):
            def pb_g(g, c=c):
                if c == 0:
                    pr.wait(g, "ve", ve_g2)
                    ST["upw"].wait(g)
                if c >= 2:
                    g.wait_ge(pr.sems["se"], se_b8c + (c - 1))
                ST["ub"].dma(g, upc[:, (c % 2) * S:(c % 2 + 1) * S],
                             up_spill.ap()[:, c * S:(c + 1) * S])
            block.gpsimd(pb_g)

            def pb_se(s, c=c):
                ST["ub"].wait(s, 16 * (c + 1))
                if c > 0:
                    s.wait_ge(pr.sems["ve"], ve_b8c + 3 * (c - 1) + 1)
                s.activation(yscr[:, 0:S], upc[:, (c % 2) * S:(c % 2 + 1) * S],
                             AF.Copy, scale=bc[:, C_RSUP:C_RSUP + 1])
            pr.sca(pb_se)

            def pb_ve(v, c=c):
                v.wait_ge(pr.sems["se"], se_b8c + c + 1)
                v.tensor_scalar(uqscr[:], yscr[:, 0:S], MAGIC, MAGIC,
                                ALU.add, ALU.subtract)
                v.tensor_tensor(t2b[:], G[:, c * S:(c + 1) * S], uqscr[:],
                                ALU.mult)
                v.tensor_reduce(redcols[:, 32 + c:33 + c], t2b[:], AX.X,
                                ALU.max, apply_absolute_value=True)
            pr.vec(pb_ve)
        ve_s8b = pr.cnt["ve"]

        def s8_red2(v):
            pr.wait(v, "ve", ve_s8b)
            v.tensor_reduce(redsm[:, 3:4], redcols[:, 32:32 + IC], AX.X,
                            ALU.max)
            v.tensor_single_scalar(redb[:, 3:4], redsm[:, 3:4], 1.0, ALU.mult)
        pr.vec(s8_red2)
        ve_r4a = pr.cnt["ve"]

        def s8_tr2(t_e):
            pr.wait(t_e, "ve", ve_r4a)
            pr.inc(t_e.transpose(PT[0:1, 0:128], redb[:, 3:4], IDT[:]), "te")
        block.tensor(s8_tr2)
        te_r4 = pr.cnt["te"]

        def s8_fin2(v):
            pr.wait(v, "te", te_r4)
            v.tensor_reduce(sm[:, 13:14], PT[0:1, 0:128], AX.X, ALU.max)
        pr.vec(s8_fin2)
        ve_r4b = pr.cnt["ve"]

        ve_r4 = collective_round(3, [13], ve_r4b)

        def r4_ve(v):
            pr.wait(v, "ve", ve_r4)
            v.tensor_tensor(sm[:, 14:15], sm[:, 13:14], sm[:, 11:12],
                            ALU.mult)
            v.tensor_tensor(sm[:, 14:15], sm[:, 14:15], sm[:, 6:7], ALU.mult)
            v.tensor_scalar(sm[:, 14:15], sm[:, 14:15], 1.0 / 32767.0,
                            1e-12, ALU.mult, ALU.max)
            v.reciprocal(sm[:, 15:16], sm[:, 14:15])
            v.tensor_tensor(sm[:, 15:16], sm[:, 15:16], sm[:, 11:12],
                            ALU.mult)
            v.tensor_tensor(sm[:, 15:16], sm[:, 15:16], sm[:, 6:7], ALU.mult)
        pr.vec(r4_ve)
        ve_r4c = pr.cnt["ve"]
        ve_bc4 = bcast([14, 15], [C_SO, C_RSO], ve_r4c)

        if stop_after < 11:
            early_out()
            return nc, list(ap_in.keys())

        se_b8d = pr.cnt["se"]
        ve_b8d = pr.cnt["ve"]
        for c in range(IC):
            def pc_g(g, c=c):
                if c == 0:
                    pr.wait(g, "ve", ve_bc4)
                if c >= 2:
                    g.wait_ge(pr.sems["se"], se_b8d + 2 * (c - 1) + 1)
                ST["uc"].dma(g, upc[:, (c % 2) * S:(c % 2 + 1) * S],
                             up_spill.ap()[:, c * S:(c + 1) * S])
            block.gpsimd(pc_g)

            def pc_se1(s, c=c):
                ST["uc"].wait(s, 16 * (c + 1))
                if c > 0:
                    s.wait_ge(pr.sems["ve"], ve_b8d + 3 * c)
                s.activation(yscr[:, 0:S], upc[:, (c % 2) * S:(c % 2 + 1) * S],
                             AF.Copy, scale=bc[:, C_RSUP:C_RSUP + 1])
            pr.sca(pc_se1)

            def pc_ve1(v, c=c):
                v.wait_ge(pr.sems["se"], se_b8d + 2 * c + 1)
                v.tensor_scalar(uqscr[:], yscr[:, 0:S], MAGIC, MAGIC,
                                ALU.add, ALU.subtract)
                v.tensor_tensor(t2b[:], G[:, c * S:(c + 1) * S], uqscr[:],
                                ALU.mult)
            pr.vec(pc_ve1)

            def pc_se2(s, c=c):
                s.wait_ge(pr.sems["ve"], ve_b8d + 3 * c + 2)
                s.activation(yscr[:, S:2 * S], t2b[:], AF.Copy,
                             scale=bc[:, C_RSO:C_RSO + 1])
            pr.sca(pc_se2)

            def pc_ve2(v, c=c):
                v.wait_ge(pr.sems["se"], se_b8d + 2 * c + 2)
                v.tensor_scalar(G[:, c * S:(c + 1) * S], yscr[:, S:2 * S],
                                MAGIC, MAGIC, ALU.add, ALU.subtract)
            pr.vec(pc_ve2)
        ve_oint = pr.cnt["ve"]

        if stop_after < 12:
            early_out()
            return nc, list(ap_in.keys())

        ctxPB.close()
        open_stacks.remove(ctxPB)
        wdeD = ctxF2.enter_context(nc.sbuf_tensor("wdeD", [128, I], F32))
        tqD = ctxF2.enter_context(nc.sbuf_tensor("tqD", [128, I], F32))
        qwD = ctxF2.enter_context(nc.sbuf_tensor("qwD", [128, I], BF16))
        bmD = ctxF2.enter_context(nc.sbuf_tensor("bmD", [128, NBD], F32))
        sWD = ctxF2.enter_context(nc.sbuf_tensor("sWD", [128, NBD], F32))
        rsWD = ctxF2.enter_context(nc.sbuf_tensor("rsWD", [128, NBD], F32))
        wqdT = ctxF2.enter_context(nc.sbuf_tensor("wqdT", [128, IC * E],
                                                  BF16))
        d_bufs = (wdeD, tqD, qwD, bmD, sWD, rsWD, I, NBD)

        n_dchunks = 1 if stop_after == 121 else (2 if stop_after == 123 else EC)
        skip_dtr = (stop_after == 122)
        for c in range(n_dchunks):
            def d_ld(sync, c=c):
                pr.wait(sync, "ve", ve_oint) if c < 1 else pr.wait(sync, "ve")
                ST["wde"].dma(sync, wdeD[:],
                              wdown_in[c * 128:(c + 1) * 128, :])
            block.sync(d_ld)
            ld_w = ST["wde"].cnt

            ve_dq = dequant_chunk(d_bufs, NBD, I, ld_w, 0)

            for tg in range(0 if skip_dtr else (IC // 8)):
                def d_tr(t_e, ve_dq=ve_dq, c=c, tg=tg):
                    pr.wait(t_e, "ve", ve_dq)
                    if not (c == 0 and tg == 0):
                        pr.wait(t_e, "ve")
                    for k in range(8):
                        jc = tg * 8 + k
                        ins = t_e.transpose(PT[:, k * 128:(k + 1) * 128],
                                            qwD[:, jc * 128:(jc + 1) * 128],
                                            IDT[:])
                    pr.inc(ins, "te")
                block.tensor(d_tr)
                te_tr = pr.cnt["te"]

                def d_ev(v, te_tr=te_tr, c=c, tg=tg):
                    pr.wait(v, "te", te_tr)
                    for k in range(8):
                        jc = tg * 8 + k
                        v.tensor_copy(
                            wqdT[:, jc * E + c * 128:jc * E + (c + 1) * 128],
                            PT[:, k * 128:(k + 1) * 128])
                pr.vec(d_ev)
        ve_wqd = pr.cnt["ve"]

        if stop_after < 13 or stop_after in (121, 122, 123):
            early_out()
            return nc, list(ap_in.keys())

        xres = ctxF2.enter_context(nc.sbuf_tensor("xres", [128, 2 * E], F32))
        xout = ctxF2.enter_context(nc.sbuf_tensor("xout", [128, 2 * E], F32))

        te_b9 = pr.cnt["te"]
        ve_b9 = pr.cnt["ve"]

        only_g = (stop_after == 126)
        only_gte = (stop_after == 127)

        def s9_g(g):
            pr.wait(g, "ve", ve_wqd)
            ST["x2sp"].wait(g)
            for t in range(TT_):
                if t >= 2 and not (only_g or only_gte):
                    g.wait_ge(pr.sems["ve"], ve_b9 + 2 * NO * (t - 1))
                ST["xr"].dma(g, xres[:, (t % 2) * E:(t % 2 + 1) * E],
                             x2_spill.ap()[:, t * E:(t + 1) * E])
        block.gpsimd(s9_g)

        if only_g:
            early_out()
            return nc, list(ap_in.keys())

        jc_rng = (range(0, 14) if stop_after == 124 else
                  (range(14, IC) if stop_after == 125 else range(IC)))

        def s9_te(t_e):
            pr.wait(t_e, "ve", ve_wqd)
            for t in range(TT_):
                if t >= 2 and not only_gte:
                    t_e.wait_ge(pr.sems["ve"], ve_b9 + 2 * NO * (t - 1))
                for jc in jc_rng:
                    for no in range(NO):
                        ins = t_e.matmul(
                            P[(t % 2) * 2 + no][:],
                            G[:, jc * S + t * 128:jc * S + (t + 1) * 128],
                            wqdT[:, jc * E + no * 512:jc * E + (no + 1) * 512],
                            start=(jc == jc_rng[0] if hasattr(jc_rng, "__getitem__") else jc == 0),
                            stop=(jc == list(jc_rng)[-1]))
                pr.inc(ins, "te")
        block.tensor(s9_te)

        if only_gte:
            early_out()
            return nc, list(ap_in.keys())

        def s9_ve(v):
            for t in range(TT_):
                v.wait_ge(pr.sems["te"], te_b9 + t + 1)
                ST["xr"].wait(v, 16 * (t + 1))
                if t >= 2:
                    ST["out"].wait(v, 16 * (t - 1))
                for no in range(NO):
                    v.tensor_scalar(sqscr[:, no * 512:(no + 1) * 512],
                                    P[(t % 2) * 2 + no][:],
                                    bc[:, C_SO:C_SO + 1], None, ALU.mult)
                for no in range(NO):
                    v.tensor_tensor(
                        xout[:, (t % 2) * E + no * 512:
                             (t % 2) * E + (no + 1) * 512],
                        sqscr[:, no * 512:(no + 1) * 512],
                        xres[:, (t % 2) * E + no * 512:
                             (t % 2) * E + (no + 1) * 512],
                        ALU.add)
        pr.vec(s9_ve)

        if stop_after == 128:
            early_out()
            return nc, list(ap_in.keys())

        def s9_st(g):
            for t in range(TT_):
                g.wait_ge(pr.sems["ve"], ve_b9 + 2 * NO * (t + 1))
                ST["out"].dma(g, out_d[t * 128:(t + 1) * 128, :],
                              xout[:, (t % 2) * E:(t % 2 + 1) * E])
            ST["out"].wait(g)
        block.sync(s9_st)

        ctxF2.close()
        open_stacks.remove(ctxF2)

    return nc, list(ap_in.keys())


_CACHE = {}


def _get_prog(S, E, H, I):
    key = (S, E, H, I)
    if key not in _CACHE:
        _CACHE[key] = build_program(S, E, H, I)
    return _CACHE[key]


def kernel(L0_attn_x, L0_attn_Wq, L0_attn_Wk, L0_attn_Wv, L0_attn_Wg,
           L0_ffn_w_gate, L0_ffn_w_up, L0_ffn_w_down):
    x = np.ascontiguousarray(np.asarray(L0_attn_x, np.float32))
    B, S, E = x.shape
    H = 16
    I = L0_ffn_w_up.shape[0]
    consts = make_consts(S, E, H)
    nc, _ = _get_prog(S, E, H, I)
    common = dict(
        wq=np.ascontiguousarray(np.asarray(L0_attn_Wq, np.float32)),
        wk=np.ascontiguousarray(np.asarray(L0_attn_Wk, np.float32)),
        wv=np.ascontiguousarray(np.asarray(L0_attn_Wv, np.float32)),
        wg=np.ascontiguousarray(np.asarray(L0_attn_Wg, np.float32)),
        wup=np.ascontiguousarray(np.asarray(L0_ffn_w_up, np.float32)),
        wgate=np.ascontiguousarray(np.asarray(L0_ffn_w_gate, np.float32)),
        wdown=np.ascontiguousarray(np.asarray(L0_ffn_w_down, np.float32)),
        **consts)
    in_maps = [dict(x=np.ascontiguousarray(x[c]), **common) for c in range(B)]
    res = run_bass_kernel_spmd(nc, in_maps, core_ids=list(range(NCORES)))
    out = np.stack([np.asarray(res.results[c]["out"]) for c in range(B)], 0)
    return out.astype(np.float32)


# version 06
# speedup vs baseline: 1.1163x; 1.1163x over previous
import numpy as np
import ml_dtypes
from contextlib import ExitStack

import concourse.bass as bass
import concourse.mybir as mybir
from concourse.bass_utils import run_bass_kernel_spmd

F32 = mybir.dt.float32
BF16 = mybir.dt.bfloat16
AF = mybir.ActivationFunctionType
ALU = mybir.AluOpType
AX = mybir.AxisListType
BF = ml_dtypes.bfloat16
MAGIC = 12582912.0
NCORES = 8

C_SX, C_RSX, C_RSUP, C_RSG, C_SG, C_RST, C_RSO, C_SO = range(8)
K_LNEPS, K_EIGHTH = 0, 1

_CHAIN_OPS = frozenset((
    "tensor_copy", "tensor_tensor", "tensor_add", "tensor_sub", "tensor_mul",
    "tensor_max", "tensor_scalar", "tensor_single_scalar", "tensor_scalar_add",
    "tensor_scalar_sub", "tensor_scalar_mul", "tensor_scalar_max",
    "tensor_scalar_min", "tensor_reduce", "reduce_max", "reduce_sum",
    "scalar_tensor_tensor", "reciprocal", "memset", "memzero", "select",
    "activation", "copy", "mul", "add", "square", "sqrt", "sign",
))


class Chained:

    def __init__(self, pr, eng, sem_name):
        self._pr = pr
        self._e = eng
        self._s = sem_name

    def __getattr__(self, n):
        a = getattr(self._e, n)
        if n not in _CHAIN_OPS:
            return a
        pr, e, sn = self._pr, self._e, self._s

        def w(*args, **kw):
            if pr.cnt[sn] > 0:
                e.wait_ge(pr.sems[sn], pr.cnt[sn])
            ins = a(*args, **kw)
            ins.then_inc(pr.sems[sn], 1)
            pr.cnt[sn] += 1
            return ins
        return w


class Stream:

    def __init__(self, nc, name, ctx):
        self.sem = ctx.enter_context(nc.semaphore(name))
        self.cnt = 0

    def dma(self, g, out, in_):
        if self.cnt > 0:
            g.wait_ge(self.sem, self.cnt)
        g.dma_start(out=out, in_=in_).then_inc(self.sem, 16)
        self.cnt += 16
        return self.cnt

    def wait(self, eng, val=None):
        v = self.cnt if val is None else val
        if v > 0:
            e = eng._e if isinstance(eng, Chained) else eng
            e.wait_ge(self.sem, v)


class Prog:
    def __init__(self, nc, block, ctx):
        self.nc = nc
        self.block = block
        self.cnt = {}
        self.sems = {}
        for s in ("te", "ve", "se", "cc"):
            self.sems[s] = ctx.enter_context(nc.semaphore(s))
            self.cnt[s] = 0

    def vec(self, f):
        self.block.vector(lambda v: f(Chained(self, v, "ve")))

    def sca(self, f):
        self.block.scalar(lambda s: f(Chained(self, s, "se")))

    def inc(self, ins, name, n=1):
        ins.then_inc(self.sems[name], n)
        self.cnt[name] += n
        return self.cnt[name]

    def wait(self, eng, name, val=None):
        v = self.cnt[name] if val is None else val
        if v > 0:
            e = eng._e if isinstance(eng, Chained) else eng
            e.wait_ge(self.sems[name], v)


def make_consts(S, E, H):
    D = E // H
    idx = np.arange(S, dtype=np.float64)
    a_ang = 1.0 / 10000.0 ** np.linspace(0.0, 1.0, D // 2)
    angle = np.repeat(a_ang, 2)
    sin_t = np.sin(idx[:, None] * angle[None, :])
    cos_t = np.cos(idx[:, None] * angle[None, :])
    gamma = 1.0 - 2.0 ** (-5.0 - np.arange(H, dtype=np.float64))
    r = (1.0 - gamma[:, None] ** (idx[None, :] + 1.0)) / (1.0 - gamma[:, None])
    a_hq = gamma[:, None] ** idx[None, :] / np.sqrt(r)
    b_hk = np.exp(-idx[None, :] * np.log(gamma)[:, None])
    scaling = float(E) ** (-0.5)

    rows = np.arange(E)
    h_r = 2 * (rows // 128) + (rows % 128) // D
    d_r = rows % D
    QCOS = (cos_t[:, d_r].T * a_hq[h_r]).astype(np.float32)
    QSIN = (sin_t[:, d_r].T * a_hq[h_r]).astype(np.float32)
    KCOS = (cos_t[:, d_r].T * b_hk[h_r] * scaling).astype(np.float32)
    KSIN = (sin_t[:, d_r].T * b_hk[h_r] * scaling).astype(np.float32)

    IDT = np.eye(128, dtype=np.float32).astype(BF)
    PROT = np.zeros((128, 128), np.float32)
    for i in range(64):
        PROT[2 * i + 1, 2 * i] = -1.0
        PROT[2 * i, 2 * i + 1] = 1.0
    PROT = PROT.astype(BF)
    ki = np.arange(128)
    TRI = (ki[:, None] <= ki[None, :]).astype(np.float32)
    ONES1 = np.ones((128, 1), np.float32).astype(BF)
    ONESF = np.ones((1, 128), np.float32)
    CONSTS = np.zeros((128, 8), np.float32)
    CONSTS[:, K_LNEPS] = 1e-5
    CONSTS[:, K_EIGHTH] = 0.125
    return dict(qcos=QCOS, qsin=QSIN, kcos=KCOS, ksin=KSIN, idt=IDT, prot=PROT,
                tri=TRI, ones1=ONES1, onesf=ONESF, consts=CONSTS)


def build_program(S=2048, E=1024, H=16, I=3072, stop_after=99):
    D = E // H
    TT_ = S // 128
    EC = E // 128
    IC = I // 128
    NQ = S // 512
    NP = H // 2
    NT = S // 512
    NO = E // 512
    NBU = E // 32
    NBD = I // 32

    nc = bass.Bass("TRN2", target_bir_lowering=False, debug=False)
    ap_in = {}

    def din(name, shape, dt=F32):
        ap_in[name] = nc.dram_tensor(name, list(shape), dt,
                                     kind="ExternalInput").ap()
        return ap_in[name]

    x_in = din("x", (S, E))
    wq_in = din("wq", (E, E)); wk_in = din("wk", (E, E))
    wv_in = din("wv", (E, E)); wg_in = din("wg", (E, E))
    wup_in = din("wup", (I, E)); wgate_in = din("wgate", (I, E))
    wdown_in = din("wdown", (E, I))
    qcos_in = din("qcos", (E, S)); qsin_in = din("qsin", (E, S))
    kcos_in = din("kcos", (E, S)); ksin_in = din("ksin", (E, S))
    idt_in = din("idt", (128, 128), BF16)
    prot_in = din("prot", (128, 128), BF16)
    tri_in = din("tri", (128, 128))
    ones1_in = din("ones1", (128, 1), BF16)
    onesf_in = din("onesf", (1, 128))
    consts_in = din("consts", (128, 8))

    out_d = nc.dram_tensor("out", [S, E], F32, kind="ExternalOutput").ap()

    g_spill = nc.dram_tensor("g_spill", [128, TT_ * E], BF16)
    x_spill = nc.dram_tensor("x_spill", [128, TT_ * E], F32)
    x2_spill = nc.dram_tensor("x2_spill", [128, TT_ * E], F32)
    up_spill = nc.dram_tensor("up_spill", [128, IC * S], BF16)
    cc_i = [nc.dram_tensor(f"cc_i{k}", [1, n], F32)
            for k, n in enumerate((1, 2, 1, 1))]
    cc_o = [nc.dram_tensor(f"cc_o{k}", [1, n * NCORES], F32)
            for k, n in enumerate((1, 2, 1, 1))]

    ctx0 = ExitStack()
    with ctx0:
        sb0 = lambda *a, **kw: ctx0.enter_context(nc.sbuf_tensor(*a, **kw))
        psa = lambda *a: ctx0.enter_context(nc.psum_tensor(*a))

        IDT = sb0("IDT", [128, 128], BF16)
        PROT = sb0("PROT", [128, 128], BF16)
        TRI = sb0("TRI", [128, 128], F32)
        ONES1 = sb0("ONES1", [128, 1], BF16)
        ONESF = sb0("ONESF", [1, 128], F32)
        CONSTS = sb0("CONSTS", [128, 8], F32)
        s1c = sb0("s1c", [128, TT_], F32)
        s2c = sb0("s2c", [128, TT_], F32)
        meanc = sb0("meanc", [128, TT_], F32)
        m2c = sb0("m2c", [128, TT_], F32)
        varc = sb0("varc", [128, TT_], F32)
        rootc = sb0("rootc", [128, TT_], F32)
        rstdc = sb0("rstdc", [128, TT_], F32)
        den_tm = sb0("den_tm", [128, TT_ * H], F32)
        redcols = sb0("redcols", [128, 64], F32)
        redsm = sb0("redsm", [128, 8], F32)
        redb = sb0("redb", [128, 8], BF16)
        sm = sb0("sm", [1, 16], F32)
        ag_sb = sb0("ag_sb", [1, 2 * NCORES], F32)
        bc = sb0("bc", [128, 16], F32)
        sqscr = sb0("sqscr", [128, E], F32)
        d2scr = sb0("d2scr", [128, 8], F32)
        msqscr = sb0("msqscr", [128, 8], F32)
        argscr = sb0("argscr", [128, 8], F32)
        rootscr = sb0("rootscr", [128, 8], F32)
        sclscr = sb0("sclscr", [128, 8], F32)

        P = [psa(f"P{i}", [128, 512], F32) for i in range(6)]
        P6 = psa("P6", [128, 512], F32)
        PT = psa("PT", [128, 1024], BF16)

        block = ctx0.enter_context(nc.Block())
        pr = Prog(nc, block, ctx0)
        ST = {n: Stream(nc, "st_" + n, ctx0) for n in
              ("cs", "x1", "xsp", "wq", "tA", "tB", "gw", "x2l", "gp",
               "x2sp", "xr2", "ccb", "wde", "upw", "ub", "uc", "xr", "out")}
        outb0 = sb0("outb0", [128, E], F32)
        open_stacks = []

        def early_out():
            def f(v):
                pr.wait(v, "te")
                pr.wait(v, "se")
                v.memset(outb0[:], 0.0)
            pr.vec(f)

            def fg(g):
                pr.wait(g, "ve")
                for t in range(TT_):
                    ST["out"].dma(g, out_d[t * 128:(t + 1) * 128, :], outb0[:])
                ST["out"].wait(g)
            block.gpsimd(fg)
            for st_ in reversed(open_stacks):
                st_.close()

        def s0(sync):
            for dst, src in ((TRI, tri_in), (ONES1, ones1_in),
                             (ONESF, onesf_in), (CONSTS, consts_in),
                             (IDT, idt_in), (PROT, prot_in)):
                ST["cs"].dma(sync, dst[:], src)
        block.sync(s0)

        ctxR1 = ExitStack()
        open_stacks.append(ctxR1)
        X1 = ctxR1.enter_context(nc.sbuf_tensor("X1", [128, TT_ * E], F32,
                                                side="right"))

        def s0b(sync):
            for t in range(TT_):
                ST["x1"].dma(sync, X1[:, t * E:(t + 1) * E],
                             x_in[t * 128:(t + 1) * 128, :])
        block.sync(s0b)

        def ln_stats(Xbuf, wait_specs):
            def f_se(s):
                for wfn in wait_specs:
                    wfn(s)
                for t in range(TT_):
                    s.activation(sqscr[:], Xbuf[:, t * E:(t + 1) * E],
                                 AF.Square, accum_out=s2c[:, t:t + 1])
            pr.sca(f_se)
            se_sq = pr.cnt["se"]

            def f_ve(v):
                for wfn in wait_specs:
                    wfn(v)
                for t in range(TT_):
                    v.tensor_reduce(s1c[:, t:t + 1], Xbuf[:, t * E:(t + 1) * E],
                                    AX.X, ALU.add)
                pr.wait(v, "se", se_sq)
                v.tensor_single_scalar(meanc[:, 0:TT_], s1c[:, 0:TT_],
                                       1.0 / E, ALU.mult)
                v.tensor_tensor(m2c[:, 0:TT_], meanc[:, 0:TT_],
                                meanc[:, 0:TT_], ALU.mult)
                v.scalar_tensor_tensor(varc[:, 0:TT_], s2c[:, 0:TT_],
                                       1.0 / E, m2c[:, 0:TT_],
                                       ALU.mult, ALU.subtract)
            pr.vec(f_ve)
            ve_var = pr.cnt["ve"]

            def f_se2(s):
                pr.wait(s, "ve", ve_var)
                s.activation(rootc[:, 0:TT_], varc[:, 0:TT_], AF.Sqrt,
                             bias=CONSTS[:, K_LNEPS:K_LNEPS + 1])
            pr.sca(f_se2)
            se_root = pr.cnt["se"]

            def f_ve2(v):
                pr.wait(v, "se", se_root)
                v.reciprocal(rstdc[:, 0:TT_], rootc[:, 0:TT_])
            pr.vec(f_ve2)

        ln_stats(X1, [lambda e: ST["cs"].wait(e), lambda e: ST["x1"].wait(e)])

        xln = ctxR1.enter_context(nc.sbuf_tensor("xln", [128, TT_ * E], BF16,
                                                 side="right"))

        def s1_ve(v):
            for t in range(TT_):
                v.tensor_scalar(xln[:, t * E:(t + 1) * E],
                                X1[:, t * E:(t + 1) * E],
                                meanc[:, t:t + 1], rstdc[:, t:t + 1],
                                ALU.subtract, ALU.mult)
        pr.vec(s1_ve)
        ve_ln1 = pr.cnt["ve"]

        def s1_g(g):
            pr.wait(g, "ve", ve_ln1)
            ST["xsp"].dma(g, x_spill.ap(), X1[:])
        block.gpsimd(s1_g)

        if stop_after < 2:
            early_out()
            return nc, list(ap_in.keys())

        ctxQK = ExitStack()
        open_stacks.append(ctxQK)
        xlnT = ctxQK.enter_context(nc.sbuf_tensor("xlnT", [128, EC * S], BF16))

        tr_jobs = [(t, c) for t in range(TT_) for c in range(EC)]
        ngrp = len(tr_jobs) // 8
        te_b2 = pr.cnt["te"]
        ve_b2 = pr.cnt["ve"]

        def s2_te(t_e):
            pr.wait(t_e, "ve", ve_ln1)
            ST["xsp"].wait(t_e)
            for gi in range(ngrp):
                if gi > 0:
                    t_e.wait_ge(pr.sems["ve"], ve_b2 + 8 * gi)
                for k in range(8):
                    t, c = tr_jobs[gi * 8 + k]
                    ins = t_e.transpose(
                        PT[:, k * 128:(k + 1) * 128],
                        xln[:, t * E + c * 128:t * E + (c + 1) * 128], IDT[:])
                pr.inc(ins, "te")
        block.tensor(s2_te)

        def s2_ve(v):
            for gi in range(ngrp):
                pr.wait(v, "te", te_b2 + gi + 1)
                for k in range(8):
                    t, c = tr_jobs[gi * 8 + k]
                    v.tensor_copy(
                        xlnT[:, c * S + t * 128:c * S + (t + 1) * 128],
                        PT[:, k * 128:(k + 1) * 128])
            assert pr.cnt["ve"] == ve_b2 + 8 * ngrp
        pr.vec(s2_ve)
        ve_xlnT = pr.cnt["ve"]

        ctxR1.close()
        open_stacks.remove(ctxR1)

        if stop_after < 3:
            early_out()
            return nc, list(ap_in.keys())

        wbuf = ctxQK.enter_context(nc.sbuf_tensor("wbuf", [128, EC * E], BF16))
        wbuf2 = ctxQK.enter_context(nc.sbuf_tensor("wbuf2", [128, EC * E], BF16))
        tabc = ctxQK.enter_context(nc.sbuf_tensor("tabc", [128, 2 * S], BF16))
        tabc2 = ctxQK.enter_context(nc.sbuf_tensor("tabc2", [128, 2 * S], BF16))
        q_sb = ctxQK.enter_context(nc.sbuf_tensor("q_sb", [128, 4 * 512], BF16))
        t1scr = ctxQK.enter_context(nc.sbuf_tensor("t1scr", [128, 512], F32))
        t2scr = ctxQK.enter_context(nc.sbuf_tensor("t2scr", [128, 512], F32))

        ctxR2 = ExitStack()
        open_stacks.append(ctxR2)
        qr = ctxR2.enter_context(nc.sbuf_tensor("qr", [128, EC * S], BF16,
                                                side="right"))
        kr = ctxR2.enter_context(nc.sbuf_tensor("kr", [128, EC * S], BF16,
                                                side="right"))

        for wi, (w_ap, cos_ap, sin_ap, dst, wb) in enumerate([
                (wq_in, qcos_in, qsin_in, qr, wbuf),
                (wk_in, ksin_in, ksin_in, kr, wbuf2)]) if False else \
                enumerate([(wq_in, qcos_in, qsin_in, qr, wbuf),
                           (wk_in, kcos_in, ksin_in, kr, wbuf2)]):
            niter = EC * NT
            te_b = pr.cnt["te"]
            ve_b = pr.cnt["ve"]
            tab_cnt = {}

            def f_g(g, w_ap=w_ap, cos_ap=cos_ap, sin_ap=sin_ap, wb=wb,
                    ve_b=ve_b, wi=wi, tab_cnt=tab_cnt):
                ST["wq"].dma(g, wb[:].rearrange("p (c o) -> p c o", o=E),
                             w_ap.rearrange("(c p) o -> p c o", p=128))
                for oc in range(EC):
                    st_t = ST["tA"] if oc % 2 == 0 else ST["tB"]
                    buf = tabc if oc % 2 == 0 else tabc2
                    if oc >= 2:
                        g.wait_ge(pr.sems["ve"], ve_b + 4 * NT * (oc - 1))
                    elif wi == 1 and ve_b > 0:
                        g.wait_ge(pr.sems["ve"], ve_b)
                    st_t.dma(g, buf[:, 0:S],
                             cos_ap[oc * 128:(oc + 1) * 128, :])
                    st_t.dma(g, buf[:, S:2 * S],
                             sin_ap[oc * 128:(oc + 1) * 128, :])
                    tab_cnt[oc] = st_t.cnt
            block.gpsimd(f_g)
            wq_cnt = ST["wq"].cnt

            def f_te(t_e, wb=wb, te_b=te_b, ve_b=ve_b, wq_cnt=wq_cnt):
                pr.wait(t_e, "ve", ve_b)
                ST["wq"].wait(t_e, wq_cnt)
                for it in range(niter):
                    oc, ts_ = divmod(it, NT)
                    bank = P[it % 4]
                    if it >= 4:
                        t_e.wait_ge(pr.sems["ve"], ve_b + 4 * (it - 4) + 1)
                    if it >= 2:
                        t_e.wait_ge(pr.sems["ve"], ve_b + 4 * (it - 2) + 3)
                    for kc in range(EC):
                        ins = t_e.matmul(
                            bank[:],
                            wb[:, kc * E + oc * 128:kc * E + (oc + 1) * 128],
                            xlnT[:, kc * S + ts_ * 512:kc * S + (ts_ + 1) * 512],
                            start=(kc == 0), stop=(kc == EC - 1))
                    pr.inc(ins, "te")
                    t_e.wait_ge(pr.sems["ve"], ve_b + 4 * it + 1)
                    pr.inc(t_e.matmul(P[4 + it % 2][:], PROT[:],
                                      q_sb[:, (it % 4) * 512:(it % 4 + 1) * 512],
                                      start=True, stop=True), "te")
                assert pr.cnt["te"] == te_b + 2 * niter
            block.tensor(f_te)

            def f_ve(v, dst=dst, te_b=te_b, ve_b=ve_b, tab_cnt=tab_cnt):
                for it in range(niter):
                    oc, ts_ = divmod(it, NT)
                    v.wait_ge(pr.sems["te"], te_b + 2 * it + 1)
                    v.tensor_copy(q_sb[:, (it % 4) * 512:(it % 4 + 1) * 512],
                                  P[it % 4][:])
                    v.wait_ge(pr.sems["te"], te_b + 2 * it + 2)
                    (ST["tA"] if oc % 2 == 0 else ST["tB"]).wait(
                        v, tab_cnt[oc])
                    buf = tabc if oc % 2 == 0 else tabc2
                    sl = slice(ts_ * 512, (ts_ + 1) * 512)
                    v.tensor_tensor(t1scr[:],
                                    q_sb[:, (it % 4) * 512:(it % 4 + 1) * 512],
                                    buf[:, sl], ALU.mult)
                    v.tensor_tensor(t2scr[:], P[4 + it % 2][:],
                                    buf[:, S + ts_ * 512:S + (ts_ + 1) * 512],
                                    ALU.mult)
                    v.tensor_tensor(
                        dst[:, oc * S + ts_ * 512:oc * S + (ts_ + 1) * 512],
                        t1scr[:], t2scr[:], ALU.add)
                assert pr.cnt["ve"] == ve_b + 4 * niter
            pr.vec(f_ve)

        te_qk = pr.cnt["te"]
        ve_qk = pr.cnt["ve"]

        if stop_after < 4:
            early_out()
            return nc, list(ap_in.keys())

        vtm = ctxR2.enter_context(nc.sbuf_tensor("vtm", [128, TT_ * E], BF16,
                                                 side="right"))
        g_stage = ctxQK.enter_context(nc.sbuf_tensor("g_stage", [128, 2 * E],
                                                     BF16))

        def s3b_g(g):
            pr.wait(g, "te", te_qk)
            ST["wq"].dma(g, wbuf[:].rearrange("p (c o) -> p c o", o=E),
                         wv_in.rearrange("(c p) o -> p c o", p=128))
            ST["wq"].dma(g, wbuf2[:].rearrange("p (c o) -> p c o", o=E),
                         wg_in.rearrange("(c p) o -> p c o", p=128))
        block.gpsimd(s3b_g)
        wq_vg = ST["wq"].cnt

        te_b3 = pr.cnt["te"]
        ve_b3 = pr.cnt["ve"]
        se_b3 = pr.cnt["se"]

        def s3b_te(t_e):
            ST["wq"].wait(t_e, wq_vg)
            pr.wait(t_e, "ve", ve_qk)
            for t in range(TT_):
                if t > 0:
                    t_e.wait_ge(pr.sems["ve"], ve_b3 + 2 * NO * t)
                for kc in range(EC):
                    for no in range(NO):
                        ins = t_e.matmul(
                            P[no][:],
                            xlnT[:, kc * S + t * 128:kc * S + (t + 1) * 128],
                            wbuf[:, kc * E + no * 512:kc * E + (no + 1) * 512],
                            start=(kc == 0), stop=(kc == EC - 1))
                    for no in range(NO):
                        ins = t_e.matmul(
                            P[2 + no][:],
                            xlnT[:, kc * S + t * 128:kc * S + (t + 1) * 128],
                            wbuf2[:, kc * E + no * 512:
                                  kc * E + (no + 1) * 512],
                            start=(kc == 0), stop=(kc == EC - 1))
                pr.inc(ins, "te")
        block.tensor(s3b_te)

        def s3b_ve(v):
            for t in range(TT_):
                v.wait_ge(pr.sems["te"], te_b3 + t + 1)
                if t >= 2:
                    ST["gw"].wait(v, 16 * (t - 1))
                for no in range(NO):
                    v.tensor_copy(
                        vtm[:, t * E + no * 512:t * E + (no + 1) * 512],
                        P[no][:])
                for no in range(NO):
                    v.tensor_copy(
                        g_stage[:, (t % 2) * E + no * 512:
                                (t % 2) * E + (no + 1) * 512],
                        P[2 + no][:])
            assert pr.cnt["ve"] == ve_b3 + 2 * NO * TT_
        pr.vec(s3b_ve)

        def s3b_gs(g):
            for t in range(TT_):
                g.wait_ge(pr.sems["ve"], ve_b3 + 2 * NO * (t + 1))
                ST["gw"].dma(g, g_spill.ap()[:, t * E:(t + 1) * E],
                             g_stage[:, (t % 2) * E:(t % 2 + 1) * E])
        block.gpsimd(s3b_gs)
        te_vg = pr.cnt["te"]
        ve_vg = pr.cnt["ve"]
        se_vg = pr.cnt["se"]

        ctxQK.close()
        open_stacks.remove(ctxQK)

        if stop_after < 5 or stop_after in (41, 42):
            early_out()
            return nc, list(ap_in.keys())

        ctxS4 = ExitStack()
        open_stacks.append(ctxS4)
        SA = ctxS4.enter_context(nc.sbuf_tensor("SA", [128, 512], BF16))
        SB = ctxS4.enter_context(nc.sbuf_tensor("SB", [128, 512], BF16))
        ABSA = ctxS4.enter_context(nc.sbuf_tensor("ABSA", [128, 512], BF16))
        ABSB = ctxS4.enter_context(nc.sbuf_tensor("ABSB", [128, 512], BF16))
        SA2 = ctxS4.enter_context(nc.sbuf_tensor("SA2", [128, 512], BF16))
        SB2 = ctxS4.enter_context(nc.sbuf_tensor("SB2", [128, 512], BF16))
        ABSA2 = ctxS4.enter_context(nc.sbuf_tensor("ABSA2", [128, 512], BF16))
        ABSB2 = ctxS4.enter_context(nc.sbuf_tensor("ABSB2", [128, 512], BF16))
        avfm = ctxS4.enter_context(nc.sbuf_tensor("avfm", [128, 512], BF16))
        avtm = ctxS4.enter_context(nc.sbuf_tensor("avtm", [128, 512], BF16))
        den_sb = ctxS4.enter_context(nc.sbuf_tensor("den_sb", [128, 512], BF16))
        gsil = ctxS4.enter_context(nc.sbuf_tensor("gsil", [128, 512], BF16))
        tmpp = ctxS4.enter_context(nc.sbuf_tensor("tmpp", [128, 128], BF16))
        g_pair = ctxS4.enter_context(
            nc.sbuf_tensor("g_pair", [128, TT_ * 128], BF16))
        X2 = ctxS4.enter_context(nc.sbuf_tensor("X2", [128, TT_ * E], F32))

        def s4_x2(g):
            pr.wait(g, "te", te_vg)
            pr.wait(g, "ve", ve_vg)
            pr.wait(g, "se", se_vg)
            ST["x2l"].dma(g, X2[:], x_spill.ap())
        block.gpsimd(s4_x2)

        for p in range(NP):
            def s4_gp(g, p=p):
                if p == 0:
                    ST["gw"].wait(g)
                else:
                    pr.wait(g, "se")
                for q4 in range(TT_ // 4):
                    ST["gp"].dma(
                        g,
                        g_pair[:, q4 * 512:(q4 + 1) * 512].rearrange(
                            "q (t c) -> q t c", c=128),
                        bass.AP(g_spill, p * 128 + q4 * 4 * E,
                                [[TT_ * E, 128], [E, 4], [1, 128]]))
            block.gpsimd(s4_gp)
            gp_cnt = ST["gp"].cnt

            for qc in range(NQ):
                nk = 4 * qc + 4
                te_b4 = pr.cnt["te"]
                ve_b4 = pr.cnt["ve"]
                ve_cum = [0] * (nk + 1)
                for kt in range(nk):
                    j = kt - 4 * qc
                    ve_cum[kt + 1] = ve_cum[kt] + (
                        4 if j < 0 else (4 if j == 3 else 6))

                def s_banks(kt):
                    return (P[0], P[1]) if kt % 2 == 0 else (P[5], P6)

                def emit_scores(t_e, kt, p, qc):
                    j = kt - 4 * qc
                    off = max(j, 0) * 128
                    qs0 = qc * 512
                    bA, bB = s_banks(kt)
                    if kt >= 2:
                        t_e.wait_ge(pr.sems["ve"], ve_b4 + ve_cum[kt - 1])
                    t_e.matmul(
                        bA[:, off:512],
                        kr[0:64, p * S + kt * 128:p * S + (kt + 1) * 128],
                        qr[0:64, p * S + qs0 + off:p * S + qs0 + 512],
                        start=True, stop=True)
                    ins = t_e.matmul(
                        bB[:, off:512],
                        kr[64:128, p * S + kt * 128:p * S + (kt + 1) * 128],
                        qr[64:128, p * S + qs0 + off:p * S + qs0 + 512],
                        start=True, stop=True, tile_position=(64, 0))
                    pr.inc(ins, "te")

                def emit_avden(t_e, kt, p, qc, nk):
                    j = kt - 4 * qc
                    off = max(j, 0) * 128
                    t_e.wait_ge(pr.sems["ve"], ve_b4 + ve_cum[kt + 1])
                    first = (kt == 0)
                    last = (kt == nk - 1)
                    sa = SA if kt % 2 == 0 else SA2
                    sb2 = SB if kt % 2 == 0 else SB2
                    aa = ABSA if kt % 2 == 0 else ABSA2
                    ab = ABSB if kt % 2 == 0 else ABSB2
                    t_e.matmul(P[2][0:64, off:512],
                               vtm[:, kt * E + (2 * p) * D:
                                   kt * E + (2 * p) * D + D],
                               sa[:, off:512], start=first, stop=last)
                    t_e.matmul(P[3][64:128, off:512],
                               vtm[:, kt * E + (2 * p + 1) * D:
                                   kt * E + (2 * p + 1) * D + D],
                               sb2[:, off:512], start=first, stop=last,
                               tile_position=(0, 64), skip_group_check=True)
                    t_e.matmul(P[4][0:1, off:512], ONES1[:],
                               aa[:, off:512], start=first, stop=last)
                    ins = t_e.matmul(P[4][64:65, off:512], ONES1[:],
                                     ab[:, off:512], start=first,
                                     stop=last, tile_position=(0, 64),
                                     skip_group_check=True)
                    pr.inc(ins, "te")

                def f_te(t_e, p=p, qc=qc, nk=nk, te_b4=te_b4,
                         ve_b4=ve_b4, ve_cum=ve_cum):
                    pr.wait(t_e, "ve", ve_vg)
                    for kt in range(nk):
                        emit_scores(t_e, kt, p, qc)
                        if kt >= 1:
                            emit_avden(t_e, kt - 1, p, qc, nk)
                    emit_avden(t_e, nk - 1, p, qc, nk)
                block.tensor(f_te)

                def f_ve(v, p=p, qc=qc, nk=nk, te_b4=te_b4):
                    for kt in range(nk):
                        j = kt - 4 * qc
                        v.wait_ge(pr.sems["te"],
                                  te_b4 + (1 if kt == 0 else 2 * kt))
                        bA, bB = s_banks(kt)
                        sa = SA if kt % 2 == 0 else SA2
                        sb2 = SB if kt % 2 == 0 else SB2
                        aa = ABSA if kt % 2 == 0 else ABSA2
                        ab = ABSB if kt % 2 == 0 else ABSB2
                        if j >= 0:
                            v.tensor_tensor(sa[:, j * 128:(j + 1) * 128],
                                            bA[:, j * 128:(j + 1) * 128],
                                            TRI[:], ALU.mult)
                            v.tensor_tensor(sb2[:, j * 128:(j + 1) * 128],
                                            bB[:, j * 128:(j + 1) * 128],
                                            TRI[:], ALU.mult)
                        if j < 3:
                            off2 = (j + 1) * 128 if j >= 0 else 0
                            v.tensor_copy(sa[:, off2:512], bA[:, off2:512])
                            v.tensor_copy(sb2[:, off2:512], bB[:, off2:512])
                        off = max(j, 0) * 128
                        v.scalar_tensor_tensor(aa[:, off:512], sa[:, off:512],
                                               -1.0, sa[:, off:512],
                                               ALU.mult, ALU.max)
                        v.scalar_tensor_tensor(ab[:, off:512], sb2[:, off:512],
                                               -1.0, sb2[:, off:512],
                                               ALU.mult, ALU.max)
                pr.vec(f_ve)

                te_av = pr.cnt["te"]

                def e_ve1(v, te_av=te_av):
                    pr.wait(v, "te", te_av)
                    v.tensor_copy(avfm[0:64, :], P[2][0:64, :])
                    v.tensor_copy(avfm[64:128, :], P[3][64:128, :])
                    v.tensor_copy(den_sb[0:1, :], P[4][0:1, :])
                    v.tensor_copy(den_sb[64:65, :], P[4][64:65, :])
                pr.vec(e_ve1)
                ve_ep1 = pr.cnt["ve"]

                def e_te(t_e, ve_ep1=ve_ep1):
                    pr.wait(t_e, "ve", ve_ep1)
                    for k in range(4):
                        t_e.transpose(PT[:, k * 128:(k + 1) * 128],
                                      avfm[:, k * 128:(k + 1) * 128], IDT[:])
                    for k in range(4):
                        ins = t_e.transpose(
                            PT[:, 512 + k * 128:512 + (k + 1) * 128],
                            den_sb[:, k * 128:(k + 1) * 128], IDT[:])
                    pr.inc(ins, "te")
                block.tensor(e_te)
                te_ep = pr.cnt["te"]

                def e_ve2(v, p=p, qc=qc, te_ep=te_ep):
                    pr.wait(v, "te", te_ep)
                    v.tensor_copy(avtm[:], PT[:, 0:512])
                    for tau in range(4):
                        tg = qc * 4 + tau
                        v.tensor_copy(
                            den_tm[:, tg * H + 2 * p:tg * H + 2 * p + 2],
                            bass.AP(PT, 512 + tau * 128,
                                    [[1024, 128], [64, 2]]))
                    dv = bass.AP(den_tm, (qc * 4) * H + 2 * p,
                                 [[TT_ * H, 128], [H, 4], [1, 2]])
                    v.tensor_scalar(dv, dv, 1.0, 50000.0, ALU.max, ALU.min)
                    v.tensor_tensor(d2scr[:], dv, dv, ALU.mult)
                pr.vec(e_ve2)
                ve_ep2 = pr.cnt["ve"]

                def e_se1(s, ve_ep2=ve_ep2):
                    pr.wait(s, "ve", ve_ep2)
                    s.activation(sqscr[:, 0:512], avtm[:], AF.Square,
                                 scale=CONSTS[:, K_EIGHTH:K_EIGHTH + 1])
                pr.sca(e_se1)
                se_ep1 = pr.cnt["se"]

                def e_ve3(v, se_ep1=se_ep1):
                    pr.wait(v, "se", se_ep1)
                    v.tensor_reduce(
                        msqscr[:],
                        bass.AP(sqscr, 0, [[E, 128], [64, 8], [1, 64]]),
                        AX.X, ALU.add)
                    v.scalar_tensor_tensor(argscr[:], d2scr[:], 1e-6,
                                           msqscr[:], ALU.mult, ALU.add)
                pr.vec(e_ve3)
                ve_ep3 = pr.cnt["ve"]

                def e_se2(s, p=p, qc=qc, ve_ep3=ve_ep3, gp_cnt=gp_cnt):
                    pr.wait(s, "ve", ve_ep3)
                    s.activation(rootscr[:], argscr[:], AF.Sqrt)
                    ST["gp"].wait(s, gp_cnt)
                    s.activation(gsil[:], g_pair[:, qc * 512:(qc + 1) * 512],
                                 AF.Silu)
                pr.sca(e_se2)
                se_ep2 = pr.cnt["se"]

                def e_ve4(v, p=p, qc=qc, se_ep2=se_ep2):
                    pr.wait(v, "se", se_ep2)
                    ST["x2l"].wait(v)
                    v.reciprocal(sclscr[:], rootscr[:])
                    for tau in range(4):
                        tg = qc * 4 + tau
                        for h in range(2):
                            v.scalar_tensor_tensor(
                                tmpp[:, h * D:(h + 1) * D],
                                avtm[:, tau * 128 + h * D:
                                     tau * 128 + (h + 1) * D],
                                sclscr[:, tau * 2 + h:tau * 2 + h + 1],
                                gsil[:, tau * 128 + h * D:
                                     tau * 128 + (h + 1) * D],
                                ALU.mult, ALU.mult)
                        v.tensor_tensor(
                            X2[:, tg * E + p * 128:tg * E + (p + 1) * 128],
                            X2[:, tg * E + p * 128:tg * E + (p + 1) * 128],
                            tmpp[:], ALU.add)
                pr.vec(e_ve4)

        ve_s4 = pr.cnt["ve"]
        te_s4 = pr.cnt["te"]

        if stop_after < 6:
            early_out()
            return nc, list(ap_in.keys())

        ln_stats(X2, [lambda e, _v=ve_s4: pr.wait(e, "ve", _v)])

        def s5_sp(g):
            pr.wait(g, "ve")
            pr.wait(g, "se")
            ST["x2sp"].dma(g, x2_spill.ap(), X2[:])
        block.gpsimd(s5_sp)
        te_preF = pr.cnt["te"]
        ctxS4.close()
        open_stacks.remove(ctxS4)
        ctxR2.close()
        open_stacks.remove(ctxR2)

        ctxF1 = ExitStack()
        open_stacks.append(ctxF1)
        xln2 = ctxF1.enter_context(nc.sbuf_tensor("xln2", [128, TT_ * E],
                                                  BF16))
        xres2 = ctxF1.enter_context(nc.sbuf_tensor("xres2", [128, 2 * E], F32))

        ve_b5 = pr.cnt["ve"]

        def s5_g2(g):
            pr.wait(g, "te", te_preF)
            ST["x2sp"].wait(g)
            for t in range(TT_):
                if t >= 2:
                    g.wait_ge(pr.sems["ve"], ve_b5 + (t - 1))
                ST["xr2"].dma(g, xres2[:, (t % 2) * E:(t % 2 + 1) * E],
                              x2_spill.ap()[:, t * E:(t + 1) * E])
        block.gpsimd(s5_g2)

        def s5_ve(v):
            pr.wait(v, "te", te_preF)
            for t in range(TT_):
                ST["xr2"].wait(v, 16 * (t + 1))
                v.tensor_scalar(xln2[:, t * E:(t + 1) * E],
                                xres2[:, (t % 2) * E:(t % 2 + 1) * E],
                                meanc[:, t:t + 1], rstdc[:, t:t + 1],
                                ALU.subtract, ALU.mult)
            v.tensor_reduce(redcols[:, 0:1], xln2[:], AX.X, ALU.max,
                            apply_absolute_value=True)
            v.tensor_single_scalar(redb[:, 0:1], redcols[:, 0:1], 1.0,
                                   ALU.mult)
        pr.vec(s5_ve)
        ve_ln2 = pr.cnt["ve"]

        def part_max_to_sm(col_redb, sm_col, wait_ve):
            def f_te(t_e):
                pr.wait(t_e, "ve", wait_ve)
                pr.inc(t_e.transpose(PT[0:1, 0:128],
                                     redb[:, col_redb:col_redb + 1], IDT[:]),
                       "te")
            block.tensor(f_te)
            te_t = pr.cnt["te"]

            def f_ve(v):
                pr.wait(v, "te", te_t)
                v.tensor_reduce(sm[:, sm_col:sm_col + 1], PT[0:1, 0:128],
                                AX.X, ALU.max)
            pr.vec(f_ve)
            return pr.cnt["ve"]

        ve_mx = part_max_to_sm(0, 0, ve_ln2)

        def collective_round(k, src_cols, wait_ve):
            n = len(src_cols)

            def f_g(g):
                pr.wait(g, "ve", wait_ve)
                ap_src = sm[0:1, src_cols[0]:src_cols[0] + n]
                ST["ccb"].dma(g, cc_i[k].ap(), ap_src)
                ST["ccb"].wait(g)
                pr.inc(g.collective_compute(
                    "AllGather", ALU.bypass,
                    replica_groups=[list(range(NCORES))],
                    ins=[cc_i[k].ap().opt()], outs=[cc_o[k].ap().opt()]),
                    "cc", 1)
                pr.wait(g, "cc")
                ST["ccb"].dma(g, ag_sb[:, 0:n * NCORES], cc_o[k].ap())
            block.gpsimd(f_g)
            cc_done = ST["ccb"].cnt

            def f_ve(v):
                ST["ccb"].wait(v, cc_done)
                for i in range(n):
                    v.tensor_reduce(
                        sm[:, src_cols[i]:src_cols[i] + 1],
                        bass.AP(ag_sb, i, [[2 * NCORES, 1], [n, NCORES]]),
                        AX.X, ALU.max)
            pr.vec(f_ve)
            return pr.cnt["ve"]

        ve_r1 = collective_round(0, [0], ve_mx)

        def r1_ve(v):
            pr.wait(v, "ve", ve_r1)
            v.tensor_scalar(sm[:, 1:2], sm[:, 0:1], 1.0 / 32767.0, 1e-12,
                            ALU.mult, ALU.max)
            v.reciprocal(sm[:, 2:3], sm[:, 1:2])
        pr.vec(r1_ve)
        ve_r1b = pr.cnt["ve"]

        def bcast(cols_src, cols_dst, wait_ve):
            n = len(cols_src)
            assert cols_src == list(range(cols_src[0], cols_src[0] + n))

            def f_te(t_e):
                pr.wait(t_e, "ve", wait_ve)
                pr.inc(t_e.matmul(P6[:, 0:n], ONESF[:],
                                  sm[0:1, cols_src[0]:cols_src[0] + n],
                                  start=True, stop=True), "te")
            block.tensor(f_te)
            te_t = pr.cnt["te"]

            def f_ve(v):
                pr.wait(v, "te", te_t)
                for i, cd in enumerate(cols_dst):
                    v.tensor_copy(bc[:, cd:cd + 1], P6[:, i:i + 1])
            pr.vec(f_ve)
            return pr.cnt["ve"]

        ve_bc1 = bcast([1, 2], [C_SX, C_RSX], ve_r1b)

        if stop_after < 7:
            early_out()
            return nc, list(ap_in.keys())

        y6 = ctxF1.enter_context(nc.sbuf_tensor("y6", [128, 2 * E], F32))
        Xq = ctxF1.enter_context(nc.sbuf_tensor("Xq", [128, TT_ * E], BF16))

        ctxR3 = ExitStack()
        open_stacks.append(ctxR3)
        XqT = ctxR3.enter_context(nc.sbuf_tensor("XqT", [128, EC * S], BF16,
                                                 side="right"))

        se_b6 = pr.cnt["se"]
        ve_b6 = pr.cnt["ve"]

        def s6_se(s):
            pr.wait(s, "ve", ve_bc1)
            for t in range(TT_):
                if t >= 2:
                    s.wait_ge(pr.sems["ve"], ve_b6 + (t - 1))
                s.activation(y6[:, (t % 2) * E:(t % 2 + 1) * E],
                             xln2[:, t * E:(t + 1) * E], AF.Copy,
                             scale=bc[:, C_RSX:C_RSX + 1])
        pr.sca(s6_se)

        def s6_ve(v):
            for t in range(TT_):
                v.wait_ge(pr.sems["se"], se_b6 + t + 1)
                v.tensor_scalar(Xq[:, t * E:(t + 1) * E],
                                y6[:, (t % 2) * E:(t % 2 + 1) * E],
                                MAGIC, MAGIC, ALU.add, ALU.subtract)
            assert pr.cnt["ve"] == ve_b6 + TT_
        pr.vec(s6_ve)
        ve_xq = pr.cnt["ve"]

        te_b6 = pr.cnt["te"]
        ve_b6b = pr.cnt["ve"]

        def s6_te(t_e):
            pr.wait(t_e, "ve", ve_xq)
            for gi in range(ngrp):
                if gi > 0:
                    t_e.wait_ge(pr.sems["ve"], ve_b6b + 8 * gi)
                for k in range(8):
                    t, c = tr_jobs[gi * 8 + k]
                    ins = t_e.transpose(
                        PT[:, k * 128:(k + 1) * 128],
                        Xq[:, t * E + c * 128:t * E + (c + 1) * 128], IDT[:])
                pr.inc(ins, "te")
        block.tensor(s6_te)

        def s6_ve2(v):
            for gi in range(ngrp):
                v.wait_ge(pr.sems["te"], te_b6 + gi + 1)
                for k in range(8):
                    t, c = tr_jobs[gi * 8 + k]
                    v.tensor_copy(
                        XqT[:, c * S + t * 128:c * S + (t + 1) * 128],
                        PT[:, k * 128:(k + 1) * 128])
        pr.vec(s6_ve2)
        ve_xqT = pr.cnt["ve"]

        ctxF1.close()
        open_stacks.remove(ctxF1)
        ctxF2 = ExitStack()
        open_stacks.append(ctxF2)

        if stop_after < 8:
            early_out()
            return nc, list(ap_in.keys())

        G = ctxF2.enter_context(nc.sbuf_tensor("G", [128, IC * S], BF16))
        ctxPB = ExitStack()
        open_stacks.append(ctxPB)
        yscr = ctxPB.enter_context(nc.sbuf_tensor("yscr", [128, 2 * S], F32))
        sigscr = ctxPB.enter_context(nc.sbuf_tensor("sigscr", [128, S], BF16))
        upc = ctxPB.enter_context(nc.sbuf_tensor("upc", [128, 2 * S], BF16))
        uqscr = ctxPB.enter_context(nc.sbuf_tensor("uqscr", [128, S], BF16))
        t2b = ctxPB.enter_context(nc.sbuf_tensor("t2b", [128, S], BF16))
        ctxUG = ExitStack()
        open_stacks.append(ctxUG)
        wde = ctxUG.enter_context(nc.sbuf_tensor("wde", [128, 2 * E], F32))
        tqscr = ctxUG.enter_context(nc.sbuf_tensor("tqscr", [128, E], F32))
        qwscr = ctxUG.enter_context(nc.sbuf_tensor("qwscr", [128, E], BF16))
        bmscr = ctxUG.enter_context(nc.sbuf_tensor("bmscr", [128, NBU], F32))
        sWscr = ctxUG.enter_context(nc.sbuf_tensor("sWscr", [128, NBU], F32))
        rsWscr = ctxUG.enter_context(nc.sbuf_tensor("rsWscr", [128, NBU], F32))
        wqT = ctxUG.enter_context(nc.sbuf_tensor("wqT", [128, 2 * E], BF16))
        upst = ctxUG.enter_context(nc.sbuf_tensor("upst", [128, S], BF16))

        def dequant_chunk(bufs, nb, width, ld_val, slot):
            wd, tq, qw, bm, sW, rsW, wrow, nbmax = bufs

            def f_ve(v):
                ST["wde"].wait(v, ld_val)
                v.tensor_reduce(
                    bm[:, 0:nb],
                    bass.AP(wd, slot * width, [[wrow, 128], [32, nb], [1, 32]]),
                    AX.X, ALU.max, apply_absolute_value=True)
                v.tensor_scalar(sW[:, 0:nb], bm[:, 0:nb], 1.0 / 7.0,
                                1e-12, ALU.mult, ALU.max)
                v.reciprocal(rsW[:, 0:nb], sW[:, 0:nb])
                v.tensor_tensor(
                    tq[:, 0:width],
                    bass.AP(wd, slot * width, [[wrow, 128], [32, nb], [1, 32]]),
                    bass.AP(rsW, 0, [[nbmax, 128], [1, nb], [0, 32]]),
                    ALU.mult)
                v.tensor_scalar(tq[:, 0:width], tq[:, 0:width],
                                MAGIC, MAGIC, ALU.add, ALU.subtract)
                v.tensor_tensor(
                    qw[:, 0:width],
                    bass.AP(tq, 0, [[width, 128], [32, nb], [1, 32]]),
                    bass.AP(sW, 0, [[nbmax, 128], [1, nb], [0, 32]]),
                    ALU.mult)
            pr.vec(f_ve)
            return pr.cnt["ve"]

        ug_bufs = (wde, tqscr, qwscr, bmscr, sWscr, rsWscr, 2 * E, NBU)

        dq_hist = {}
        wqT_hist = {}
        for mi, (w_src, is_up) in enumerate([(wup_in, True),
                                             (wgate_in, False)]):
            for c in range(IC):
                ci = mi * IC + c

                def f_ld(sync, w_src=w_src, c=c, mi=mi, ci=ci):
                    if mi == 0 and c == 0:
                        ST["x2sp"].wait(sync)
                        pr.wait(sync, "ve")
                        pr.wait(sync, "se")
                        pr.wait(sync, "te")
                    if ci >= 2:
                        pr.wait(sync, "ve", dq_hist[ci - 2])
                    ST["wde"].dma(sync,
                                  wde[:, (c % 2) * E:(c % 2 + 1) * E],
                                  w_src[c * 128:(c + 1) * 128, :])
                block.sync(f_ld)
                ld_w = ST["wde"].cnt

                ve_dq = dequant_chunk(ug_bufs, NBU, E, ld_w, c % 2)
                dq_hist[ci] = ve_dq

                def f_tr(t_e, ve_dq=ve_dq, c=c, mi=mi, ci=ci):
                    pr.wait(t_e, "ve", ve_dq)
                    if ci >= 1:
                        pr.wait(t_e, "ve", wqT_hist[ci - 1])
                    for k in range(8):
                        ins = t_e.transpose(PT[:, k * 128:(k + 1) * 128],
                                            qwscr[:, k * 128:(k + 1) * 128],
                                            IDT[:])
                    pr.inc(ins, "te")
                block.tensor(f_tr)
                te_tr = pr.cnt["te"]

                def f_ev(v, te_tr=te_tr, c=c):
                    pr.wait(v, "te", te_tr)
                    v.tensor_copy(wqT[:, (c % 2) * E:(c % 2 + 1) * E], PT[:])
                pr.vec(f_ev)
                ve_wqT = pr.cnt["ve"]
                wqT_hist[ci] = ve_wqT

                def f_mm(t_e, ve_wqT=ve_wqT, c=c, mi=mi):
                    pr.wait(t_e, "ve", ve_wqT)
                    if mi == 0 and c == 0:
                        pr.wait(t_e, "ve", ve_xqT)
                    for kc in range(EC):
                        for ts_ in range(NT):
                            ins = t_e.matmul(
                                P[ts_][:],
                                wqT[:, (c % 2) * E + kc * 128:
                                    (c % 2) * E + (kc + 1) * 128],
                                XqT[:, kc * S + ts_ * 512:
                                    kc * S + (ts_ + 1) * 512],
                                start=(kc == 0), stop=(kc == EC - 1))
                    pr.inc(ins, "te")
                block.tensor(f_mm)
                te_mm = pr.cnt["te"]

                def f_evac(v, te_mm=te_mm, c=c, is_up=is_up):
                    pr.wait(v, "te", te_mm)
                    if is_up:
                        ST["upw"].wait(v)
                    for ts_ in range(NT):
                        if is_up:
                            o = upst[:, ts_ * 512:(ts_ + 1) * 512]
                        else:
                            o = G[:, c * S + ts_ * 512:c * S + (ts_ + 1) * 512]
                        v.tensor_scalar(o, P[ts_][:], bc[:, C_SX:C_SX + 1],
                                        None, ALU.mult)
                    src = upst[:] if is_up else G[:, c * S:(c + 1) * S]
                    v.tensor_reduce(redcols[:, mi * IC + c:mi * IC + c + 1],
                                    src, AX.X, ALU.max,
                                    apply_absolute_value=True)
                pr.vec(f_evac)
                ve_ev = pr.cnt["ve"]

                if is_up:
                    def f_sp(g, c=c, ve_ev=ve_ev):
                        pr.wait(g, "ve", ve_ev)
                        ST["upw"].dma(g, up_spill.ap()[:, c * S:(c + 1) * S],
                                      upst[:])
                    block.gpsimd(f_sp)

        ve_s7 = pr.cnt["ve"]

        def s7_red(v):
            pr.wait(v, "ve", ve_s7)
            v.tensor_reduce(redsm[:, 0:1], redcols[:, 0:IC], AX.X, ALU.max)
            v.tensor_reduce(redsm[:, 1:2], redcols[:, IC:2 * IC], AX.X,
                            ALU.max)
            v.tensor_single_scalar(redb[:, 0:2], redsm[:, 0:2], 1.0, ALU.mult)
        pr.vec(s7_red)
        ve_r2a = pr.cnt["ve"]

        def s7_tr(t_e):
            pr.wait(t_e, "ve", ve_r2a)
            t_e.transpose(PT[0:1, 0:128], redb[:, 0:1], IDT[:])
            pr.inc(t_e.transpose(PT[0:1, 128:256], redb[:, 1:2], IDT[:]), "te")
        block.tensor(s7_tr)
        te_r2 = pr.cnt["te"]

        def s7_fin(v):
            pr.wait(v, "te", te_r2)
            v.tensor_reduce(sm[:, 4:5], PT[0:1, 0:128], AX.X, ALU.max)
            v.tensor_reduce(sm[:, 5:6], PT[0:1, 128:256], AX.X, ALU.max)
        pr.vec(s7_fin)
        ve_r2b = pr.cnt["ve"]

        ve_r2 = collective_round(1, [4, 5], ve_r2b)

        def r2_ve(v):
            pr.wait(v, "ve", ve_r2)
            v.tensor_scalar(sm[:, 6:7], sm[:, 4:5], 1.0 / 32767.0, 1e-12,
                            ALU.mult, ALU.max)
            v.tensor_scalar(sm[:, 7:8], sm[:, 5:6], 1.0 / 32767.0, 1e-12,
                            ALU.mult, ALU.max)
            v.reciprocal(sm[:, 8:9], sm[:, 6:7])
            v.reciprocal(sm[:, 9:10], sm[:, 7:8])
        pr.vec(r2_ve)
        ve_r2c = pr.cnt["ve"]
        ve_bc2 = bcast([7, 8, 9], [C_SG, C_RSUP, C_RSG], ve_r2c)

        if stop_after < 9:
            early_out()
            return nc, list(ap_in.keys())

        ctxUG.close()
        open_stacks.remove(ctxUG)
        ctxR3.close()
        open_stacks.remove(ctxR3)

        se_b8 = pr.cnt["se"]
        ve_b8 = pr.cnt["ve"]
        for c in range(IC):
            def a_se1(s, c=c):
                if c == 0:
                    pr.wait(s, "ve", ve_bc2)
                else:
                    s.wait_ge(pr.sems["ve"], ve_b8 + 5 * (c - 1) + 3)
                s.activation(yscr[:, 0:S], G[:, c * S:(c + 1) * S],
                             AF.Copy, scale=bc[:, C_RSG:C_RSG + 1])
            pr.sca(a_se1)

            def a_ve1(v, c=c):
                v.wait_ge(pr.sems["se"], se_b8 + 2 * c + 1)
                v.tensor_scalar(G[:, c * S:(c + 1) * S], yscr[:, 0:S],
                                MAGIC, MAGIC, ALU.add, ALU.subtract)
            pr.vec(a_ve1)

            def a_se2(s, c=c):
                s.wait_ge(pr.sems["ve"], ve_b8 + 5 * c + 1)
                s.activation(yscr[:, 0:S], G[:, c * S:(c + 1) * S],
                             AF.Sigmoid, scale=bc[:, C_SG:C_SG + 1])
            pr.sca(a_se2)

            def a_ve2(v, c=c):
                v.wait_ge(pr.sems["se"], se_b8 + 2 * c + 2)
                v.tensor_scalar(yscr[:, 0:S], yscr[:, 0:S], 65536.0, MAGIC,
                                ALU.mult, ALU.add)
                v.tensor_scalar(sigscr[:], yscr[:, 0:S], MAGIC, 65535.0,
                                ALU.subtract, ALU.min)
                v.tensor_tensor(G[:, c * S:(c + 1) * S],
                                G[:, c * S:(c + 1) * S], sigscr[:], ALU.mult)
                v.tensor_reduce(redcols[:, 32 + c:33 + c],
                                G[:, c * S:(c + 1) * S], AX.X, ALU.max,
                                apply_absolute_value=True)
            pr.vec(a_ve2)
        ve_s8a = pr.cnt["ve"]

        def s8_red(v):
            pr.wait(v, "ve", ve_s8a)
            v.tensor_reduce(redsm[:, 2:3], redcols[:, 32:32 + IC], AX.X,
                            ALU.max)
            v.tensor_single_scalar(redb[:, 2:3], redsm[:, 2:3], 1.0, ALU.mult)
        pr.vec(s8_red)
        ve_r3a = pr.cnt["ve"]

        def s8_tr(t_e):
            pr.wait(t_e, "ve", ve_r3a)
            pr.inc(t_e.transpose(PT[0:1, 0:128], redb[:, 2:3], IDT[:]), "te")
        block.tensor(s8_tr)
        te_r3 = pr.cnt["te"]

        def s8_fin(v):
            pr.wait(v, "te", te_r3)
            v.tensor_reduce(sm[:, 10:11], PT[0:1, 0:128], AX.X, ALU.max)
        pr.vec(s8_fin)
        ve_r3b = pr.cnt["ve"]

        ve_r3 = collective_round(2, [10], ve_r3b)

        def r3_ve(v):
            pr.wait(v, "ve", ve_r3)
            v.tensor_tensor(sm[:, 11:12], sm[:, 10:11], sm[:, 7:8], ALU.mult)
            v.tensor_scalar(sm[:, 11:12], sm[:, 11:12],
                            1.0 / (65536.0 * 32767.0), 1e-12,
                            ALU.mult, ALU.max)
            v.reciprocal(sm[:, 12:13], sm[:, 11:12])
            v.tensor_tensor(sm[:, 12:13], sm[:, 12:13], sm[:, 7:8], ALU.mult)
            v.tensor_single_scalar(sm[:, 12:13], sm[:, 12:13],
                                   1.0 / 65536.0, ALU.mult)
        pr.vec(r3_ve)
        ve_r3c = pr.cnt["ve"]
        ve_bc3 = bcast([12], [C_RST], ve_r3c)

        se_b8b = pr.cnt["se"]
        ve_b8b = pr.cnt["ve"]
        for c in range(IC):
            def b_se(s, c=c):
                if c == 0:
                    pr.wait(s, "ve", ve_bc3)
                else:
                    s.wait_ge(pr.sems["ve"], ve_b8b + c)
                s.activation(yscr[:, 0:S], G[:, c * S:(c + 1) * S],
                             AF.Copy, scale=bc[:, C_RST:C_RST + 1])
            pr.sca(b_se)

            def b_ve(v, c=c):
                v.wait_ge(pr.sems["se"], se_b8b + c + 1)
                v.tensor_scalar(G[:, c * S:(c + 1) * S], yscr[:, 0:S],
                                MAGIC, MAGIC, ALU.add, ALU.subtract)
            pr.vec(b_ve)
        ve_g2 = pr.cnt["ve"]

        if stop_after < 10:
            early_out()
            return nc, list(ap_in.keys())

        se_b8c = pr.cnt["se"]
        ve_b8c = pr.cnt["ve"]

        for c in range(IC):
            def pb_g(g, c=c):
                if c == 0:
                    pr.wait(g, "ve", ve_g2)
                    ST["upw"].wait(g)
                if c >= 2:
                    g.wait_ge(pr.sems["se"], se_b8c + (c - 1))
                ST["ub"].dma(g, upc[:, (c % 2) * S:(c % 2 + 1) * S],
                             up_spill.ap()[:, c * S:(c + 1) * S])
            block.gpsimd(pb_g)

            def pb_se(s, c=c):
                ST["ub"].wait(s, 16 * (c + 1))
                if c > 0:
                    s.wait_ge(pr.sems["ve"], ve_b8c + 3 * (c - 1) + 1)
                s.activation(yscr[:, 0:S], upc[:, (c % 2) * S:(c % 2 + 1) * S],
                             AF.Copy, scale=bc[:, C_RSUP:C_RSUP + 1])
            pr.sca(pb_se)

            def pb_ve(v, c=c):
                v.wait_ge(pr.sems["se"], se_b8c + c + 1)
                v.tensor_scalar(uqscr[:], yscr[:, 0:S], MAGIC, MAGIC,
                                ALU.add, ALU.subtract)
                v.tensor_tensor(t2b[:], G[:, c * S:(c + 1) * S], uqscr[:],
                                ALU.mult)
                v.tensor_reduce(redcols[:, 32 + c:33 + c], t2b[:], AX.X,
                                ALU.max, apply_absolute_value=True)
            pr.vec(pb_ve)
        ve_s8b = pr.cnt["ve"]

        def s8_red2(v):
            pr.wait(v, "ve", ve_s8b)
            v.tensor_reduce(redsm[:, 3:4], redcols[:, 32:32 + IC], AX.X,
                            ALU.max)
            v.tensor_single_scalar(redb[:, 3:4], redsm[:, 3:4], 1.0, ALU.mult)
        pr.vec(s8_red2)
        ve_r4a = pr.cnt["ve"]

        def s8_tr2(t_e):
            pr.wait(t_e, "ve", ve_r4a)
            pr.inc(t_e.transpose(PT[0:1, 0:128], redb[:, 3:4], IDT[:]), "te")
        block.tensor(s8_tr2)
        te_r4 = pr.cnt["te"]

        def s8_fin2(v):
            pr.wait(v, "te", te_r4)
            v.tensor_reduce(sm[:, 13:14], PT[0:1, 0:128], AX.X, ALU.max)
        pr.vec(s8_fin2)
        ve_r4b = pr.cnt["ve"]

        ve_r4 = collective_round(3, [13], ve_r4b)

        def r4_ve(v):
            pr.wait(v, "ve", ve_r4)
            v.tensor_tensor(sm[:, 14:15], sm[:, 13:14], sm[:, 11:12],
                            ALU.mult)
            v.tensor_tensor(sm[:, 14:15], sm[:, 14:15], sm[:, 6:7], ALU.mult)
            v.tensor_scalar(sm[:, 14:15], sm[:, 14:15], 1.0 / 32767.0,
                            1e-12, ALU.mult, ALU.max)
            v.reciprocal(sm[:, 15:16], sm[:, 14:15])
            v.tensor_tensor(sm[:, 15:16], sm[:, 15:16], sm[:, 11:12],
                            ALU.mult)
            v.tensor_tensor(sm[:, 15:16], sm[:, 15:16], sm[:, 6:7], ALU.mult)
        pr.vec(r4_ve)
        ve_r4c = pr.cnt["ve"]
        ve_bc4 = bcast([14, 15], [C_SO, C_RSO], ve_r4c)

        if stop_after < 11:
            early_out()
            return nc, list(ap_in.keys())

        se_b8d = pr.cnt["se"]
        ve_b8d = pr.cnt["ve"]
        for c in range(IC):
            def pc_g(g, c=c):
                if c == 0:
                    pr.wait(g, "ve", ve_bc4)
                if c >= 2:
                    g.wait_ge(pr.sems["se"], se_b8d + 2 * (c - 1) + 1)
                ST["uc"].dma(g, upc[:, (c % 2) * S:(c % 2 + 1) * S],
                             up_spill.ap()[:, c * S:(c + 1) * S])
            block.gpsimd(pc_g)

            def pc_se1(s, c=c):
                ST["uc"].wait(s, 16 * (c + 1))
                if c > 0:
                    s.wait_ge(pr.sems["ve"], ve_b8d + 3 * c)
                s.activation(yscr[:, 0:S], upc[:, (c % 2) * S:(c % 2 + 1) * S],
                             AF.Copy, scale=bc[:, C_RSUP:C_RSUP + 1])
            pr.sca(pc_se1)

            def pc_ve1(v, c=c):
                v.wait_ge(pr.sems["se"], se_b8d + 2 * c + 1)
                v.tensor_scalar(uqscr[:], yscr[:, 0:S], MAGIC, MAGIC,
                                ALU.add, ALU.subtract)
                v.tensor_tensor(t2b[:], G[:, c * S:(c + 1) * S], uqscr[:],
                                ALU.mult)
            pr.vec(pc_ve1)

            def pc_se2(s, c=c):
                s.wait_ge(pr.sems["ve"], ve_b8d + 3 * c + 2)
                s.activation(yscr[:, S:2 * S], t2b[:], AF.Copy,
                             scale=bc[:, C_RSO:C_RSO + 1])
            pr.sca(pc_se2)

            def pc_ve2(v, c=c):
                v.wait_ge(pr.sems["se"], se_b8d + 2 * c + 2)
                v.tensor_scalar(G[:, c * S:(c + 1) * S], yscr[:, S:2 * S],
                                MAGIC, MAGIC, ALU.add, ALU.subtract)
            pr.vec(pc_ve2)
        ve_oint = pr.cnt["ve"]

        if stop_after < 12:
            early_out()
            return nc, list(ap_in.keys())

        ctxPB.close()
        open_stacks.remove(ctxPB)
        wdeD = ctxF2.enter_context(nc.sbuf_tensor("wdeD", [128, I], F32))
        tqD = ctxF2.enter_context(nc.sbuf_tensor("tqD", [128, I], F32))
        qwD = ctxF2.enter_context(nc.sbuf_tensor("qwD", [128, I], BF16))
        bmD = ctxF2.enter_context(nc.sbuf_tensor("bmD", [128, NBD], F32))
        sWD = ctxF2.enter_context(nc.sbuf_tensor("sWD", [128, NBD], F32))
        rsWD = ctxF2.enter_context(nc.sbuf_tensor("rsWD", [128, NBD], F32))
        wqdT = ctxF2.enter_context(nc.sbuf_tensor("wqdT", [128, IC * E],
                                                  BF16))
        d_bufs = (wdeD, tqD, qwD, bmD, sWD, rsWD, I, NBD)

        n_dchunks = 1 if stop_after == 121 else (2 if stop_after == 123 else EC)
        skip_dtr = (stop_after == 122)
        for c in range(n_dchunks):
            def d_ld(sync, c=c):
                pr.wait(sync, "ve", ve_oint) if c < 1 else pr.wait(sync, "ve")
                ST["wde"].dma(sync, wdeD[:],
                              wdown_in[c * 128:(c + 1) * 128, :])
            block.sync(d_ld)
            ld_w = ST["wde"].cnt

            ve_dq = dequant_chunk(d_bufs, NBD, I, ld_w, 0)

            for tg in range(0 if skip_dtr else (IC // 8)):
                def d_tr(t_e, ve_dq=ve_dq, c=c, tg=tg):
                    pr.wait(t_e, "ve", ve_dq)
                    if not (c == 0 and tg == 0):
                        pr.wait(t_e, "ve")
                    for k in range(8):
                        jc = tg * 8 + k
                        ins = t_e.transpose(PT[:, k * 128:(k + 1) * 128],
                                            qwD[:, jc * 128:(jc + 1) * 128],
                                            IDT[:])
                    pr.inc(ins, "te")
                block.tensor(d_tr)
                te_tr = pr.cnt["te"]

                def d_ev(v, te_tr=te_tr, c=c, tg=tg):
                    pr.wait(v, "te", te_tr)
                    for k in range(8):
                        jc = tg * 8 + k
                        v.tensor_copy(
                            wqdT[:, jc * E + c * 128:jc * E + (c + 1) * 128],
                            PT[:, k * 128:(k + 1) * 128])
                pr.vec(d_ev)
        ve_wqd = pr.cnt["ve"]

        if stop_after < 13 or stop_after in (121, 122, 123):
            early_out()
            return nc, list(ap_in.keys())

        xres = ctxF2.enter_context(nc.sbuf_tensor("xres", [128, 2 * E], F32))
        xout = ctxF2.enter_context(nc.sbuf_tensor("xout", [128, 2 * E], F32))

        te_b9 = pr.cnt["te"]
        ve_b9 = pr.cnt["ve"]

        only_g = (stop_after == 126)
        only_gte = (stop_after == 127)

        def s9_g(g):
            pr.wait(g, "ve", ve_wqd)
            ST["x2sp"].wait(g)
            for t in range(TT_):
                if t >= 2 and not (only_g or only_gte):
                    g.wait_ge(pr.sems["ve"], ve_b9 + 2 * NO * (t - 1))
                ST["xr"].dma(g, xres[:, (t % 2) * E:(t % 2 + 1) * E],
                             x2_spill.ap()[:, t * E:(t + 1) * E])
        block.gpsimd(s9_g)

        if only_g:
            early_out()
            return nc, list(ap_in.keys())

        jc_rng = (range(0, 14) if stop_after == 124 else
                  (range(14, IC) if stop_after == 125 else range(IC)))

        def s9_te(t_e):
            pr.wait(t_e, "ve", ve_wqd)
            for t in range(TT_):
                if t >= 2 and not only_gte:
                    t_e.wait_ge(pr.sems["ve"], ve_b9 + 2 * NO * (t - 1))
                for jc in jc_rng:
                    for no in range(NO):
                        ins = t_e.matmul(
                            P[(t % 2) * 2 + no][:],
                            G[:, jc * S + t * 128:jc * S + (t + 1) * 128],
                            wqdT[:, jc * E + no * 512:jc * E + (no + 1) * 512],
                            start=(jc == jc_rng[0] if hasattr(jc_rng, "__getitem__") else jc == 0),
                            stop=(jc == list(jc_rng)[-1]))
                pr.inc(ins, "te")
        block.tensor(s9_te)

        if only_gte:
            early_out()
            return nc, list(ap_in.keys())

        def s9_ve(v):
            for t in range(TT_):
                v.wait_ge(pr.sems["te"], te_b9 + t + 1)
                ST["xr"].wait(v, 16 * (t + 1))
                if t >= 2:
                    ST["out"].wait(v, 16 * (t - 1))
                for no in range(NO):
                    v.tensor_scalar(sqscr[:, no * 512:(no + 1) * 512],
                                    P[(t % 2) * 2 + no][:],
                                    bc[:, C_SO:C_SO + 1], None, ALU.mult)
                for no in range(NO):
                    v.tensor_tensor(
                        xout[:, (t % 2) * E + no * 512:
                             (t % 2) * E + (no + 1) * 512],
                        sqscr[:, no * 512:(no + 1) * 512],
                        xres[:, (t % 2) * E + no * 512:
                             (t % 2) * E + (no + 1) * 512],
                        ALU.add)
        pr.vec(s9_ve)

        if stop_after == 128:
            early_out()
            return nc, list(ap_in.keys())

        def s9_st(g):
            for t in range(TT_):
                g.wait_ge(pr.sems["ve"], ve_b9 + 2 * NO * (t + 1))
                ST["out"].dma(g, out_d[t * 128:(t + 1) * 128, :],
                              xout[:, (t % 2) * E:(t % 2 + 1) * E])
            ST["out"].wait(g)
        block.sync(s9_st)

        ctxF2.close()
        open_stacks.remove(ctxF2)

    return nc, list(ap_in.keys())


_CACHE = {}


def _get_prog(S, E, H, I):
    key = (S, E, H, I)
    if key not in _CACHE:
        _CACHE[key] = build_program(S, E, H, I)
    return _CACHE[key]


def kernel(L0_attn_x, L0_attn_Wq, L0_attn_Wk, L0_attn_Wv, L0_attn_Wg,
           L0_ffn_w_gate, L0_ffn_w_up, L0_ffn_w_down):
    x = np.ascontiguousarray(np.asarray(L0_attn_x, np.float32))
    B, S, E = x.shape
    H = 16
    I = L0_ffn_w_up.shape[0]
    consts = make_consts(S, E, H)
    nc, _ = _get_prog(S, E, H, I)
    common = dict(
        wq=np.ascontiguousarray(np.asarray(L0_attn_Wq, np.float32)),
        wk=np.ascontiguousarray(np.asarray(L0_attn_Wk, np.float32)),
        wv=np.ascontiguousarray(np.asarray(L0_attn_Wv, np.float32)),
        wg=np.ascontiguousarray(np.asarray(L0_attn_Wg, np.float32)),
        wup=np.ascontiguousarray(np.asarray(L0_ffn_w_up, np.float32)),
        wgate=np.ascontiguousarray(np.asarray(L0_ffn_w_gate, np.float32)),
        wdown=np.ascontiguousarray(np.asarray(L0_ffn_w_down, np.float32)),
        **consts)
    in_maps = [dict(x=np.ascontiguousarray(x[c]), **common) for c in range(B)]
    res = run_bass_kernel_spmd(nc, in_maps, core_ids=list(range(NCORES)))
    out = np.stack([np.asarray(res.results[c]["out"]) for c in range(B)], 0)
    return out.astype(np.float32)


# version 07
# speedup vs baseline: 1.1651x; 1.0437x over previous
import numpy as np
import ml_dtypes
from contextlib import ExitStack

import concourse.bass as bass
import concourse.mybir as mybir
from concourse.bass_utils import run_bass_kernel_spmd

F32 = mybir.dt.float32
BF16 = mybir.dt.bfloat16
AF = mybir.ActivationFunctionType
ALU = mybir.AluOpType
AX = mybir.AxisListType
BF = ml_dtypes.bfloat16
MAGIC = 12582912.0
NCORES = 8

C_SX, C_RSX, C_RSUP, C_RSG, C_SG, C_RST, C_RSO, C_SO = range(8)
K_LNEPS, K_EIGHTH = 0, 1

_CHAIN_OPS = frozenset((
    "tensor_copy", "tensor_tensor", "tensor_add", "tensor_sub", "tensor_mul",
    "tensor_max", "tensor_scalar", "tensor_single_scalar", "tensor_scalar_add",
    "tensor_scalar_sub", "tensor_scalar_mul", "tensor_scalar_max",
    "tensor_scalar_min", "tensor_reduce", "reduce_max", "reduce_sum",
    "scalar_tensor_tensor", "reciprocal", "memset", "memzero", "select",
    "activation", "copy", "mul", "add", "square", "sqrt", "sign",
))


class Chained:

    def __init__(self, pr, eng, sem_name):
        self._pr = pr
        self._e = eng
        self._s = sem_name

    def __getattr__(self, n):
        a = getattr(self._e, n)
        if n not in _CHAIN_OPS:
            return a
        pr, e, sn = self._pr, self._e, self._s

        def w(*args, **kw):
            if pr.cnt[sn] > 0:
                e.wait_ge(pr.sems[sn], pr.cnt[sn])
            ins = a(*args, **kw)
            ins.then_inc(pr.sems[sn], 1)
            pr.cnt[sn] += 1
            return ins
        return w


class Stream:

    def __init__(self, nc, name, ctx):
        self.sem = ctx.enter_context(nc.semaphore(name))
        self.cnt = 0

    def dma(self, g, out, in_):
        if self.cnt > 0:
            g.wait_ge(self.sem, self.cnt)
        g.dma_start(out=out, in_=in_).then_inc(self.sem, 16)
        self.cnt += 16
        return self.cnt

    def wait(self, eng, val=None):
        v = self.cnt if val is None else val
        if v > 0:
            e = eng._e if isinstance(eng, Chained) else eng
            e.wait_ge(self.sem, v)


class Prog:
    def __init__(self, nc, block, ctx):
        self.nc = nc
        self.block = block
        self.cnt = {}
        self.sems = {}
        for s in ("te", "ve", "se", "cc"):
            self.sems[s] = ctx.enter_context(nc.semaphore(s))
            self.cnt[s] = 0

    def vec(self, f):
        self.block.vector(lambda v: f(Chained(self, v, "ve")))

    def sca(self, f):
        self.block.scalar(lambda s: f(Chained(self, s, "se")))

    def inc(self, ins, name, n=1):
        ins.then_inc(self.sems[name], n)
        self.cnt[name] += n
        return self.cnt[name]

    def wait(self, eng, name, val=None):
        v = self.cnt[name] if val is None else val
        if v > 0:
            e = eng._e if isinstance(eng, Chained) else eng
            e.wait_ge(self.sems[name], v)


def make_consts(S, E, H):
    D = E // H
    idx = np.arange(S, dtype=np.float64)
    a_ang = 1.0 / 10000.0 ** np.linspace(0.0, 1.0, D // 2)
    angle = np.repeat(a_ang, 2)
    sin_t = np.sin(idx[:, None] * angle[None, :])
    cos_t = np.cos(idx[:, None] * angle[None, :])
    gamma = 1.0 - 2.0 ** (-5.0 - np.arange(H, dtype=np.float64))
    r = (1.0 - gamma[:, None] ** (idx[None, :] + 1.0)) / (1.0 - gamma[:, None])
    a_hq = gamma[:, None] ** idx[None, :] / np.sqrt(r)
    b_hk = np.exp(-idx[None, :] * np.log(gamma)[:, None])
    scaling = float(E) ** (-0.5)

    rows = np.arange(E)
    h_r = 2 * (rows // 128) + (rows % 128) // D
    d_r = rows % D
    QCOS = (cos_t[:, d_r].T * a_hq[h_r]).astype(np.float32)
    QSIN = (sin_t[:, d_r].T * a_hq[h_r]).astype(np.float32)
    KCOS = (cos_t[:, d_r].T * b_hk[h_r] * scaling).astype(np.float32)
    KSIN = (sin_t[:, d_r].T * b_hk[h_r] * scaling).astype(np.float32)

    IDT = np.eye(128, dtype=np.float32).astype(BF)
    PROT = np.zeros((128, 128), np.float32)
    for i in range(64):
        PROT[2 * i + 1, 2 * i] = -1.0
        PROT[2 * i, 2 * i + 1] = 1.0
    PROT = PROT.astype(BF)
    ki = np.arange(128)
    TRI = (ki[:, None] <= ki[None, :]).astype(np.float32)
    ONES1 = np.ones((128, 1), np.float32).astype(BF)
    ONESF = np.ones((1, 128), np.float32)
    CONSTS = np.zeros((128, 8), np.float32)
    CONSTS[:, K_LNEPS] = 1e-5
    CONSTS[:, K_EIGHTH] = 0.125
    return dict(qcos=QCOS, qsin=QSIN, kcos=KCOS, ksin=KSIN, idt=IDT, prot=PROT,
                tri=TRI, ones1=ONES1, onesf=ONESF, consts=CONSTS)


def build_program(S=2048, E=1024, H=16, I=3072, stop_after=99):
    D = E // H
    TT_ = S // 128
    EC = E // 128
    IC = I // 128
    NQ = S // 512
    NP = H // 2
    NT = S // 512
    NO = E // 512
    NBU = E // 32
    NBD = I // 32

    nc = bass.Bass("TRN2", target_bir_lowering=False, debug=False)
    ap_in = {}

    def din(name, shape, dt=F32):
        ap_in[name] = nc.dram_tensor(name, list(shape), dt,
                                     kind="ExternalInput").ap()
        return ap_in[name]

    x_in = din("x", (S, E))
    wq_in = din("wq", (E, E)); wk_in = din("wk", (E, E))
    wv_in = din("wv", (E, E)); wg_in = din("wg", (E, E))
    wup_in = din("wup", (I, E)); wgate_in = din("wgate", (I, E))
    wdown_in = din("wdown", (E, I))
    qcos_in = din("qcos", (E, S)); qsin_in = din("qsin", (E, S))
    kcos_in = din("kcos", (E, S)); ksin_in = din("ksin", (E, S))
    idt_in = din("idt", (128, 128), BF16)
    prot_in = din("prot", (128, 128), BF16)
    tri_in = din("tri", (128, 128))
    ones1_in = din("ones1", (128, 1), BF16)
    onesf_in = din("onesf", (1, 128))
    consts_in = din("consts", (128, 8))

    out_d = nc.dram_tensor("out", [S, E], F32, kind="ExternalOutput").ap()

    g_spill = nc.dram_tensor("g_spill", [128, TT_ * E], BF16)
    x_spill = nc.dram_tensor("x_spill", [128, TT_ * E], F32)
    x2_spill = nc.dram_tensor("x2_spill", [128, TT_ * E], F32)
    up_spill = nc.dram_tensor("up_spill", [128, IC * S], BF16)
    cc_i = [nc.dram_tensor(f"cc_i{k}", [1, n], F32)
            for k, n in enumerate((1, 2, 1, 1))]
    cc_o = [nc.dram_tensor(f"cc_o{k}", [1, n * NCORES], F32)
            for k, n in enumerate((1, 2, 1, 1))]

    ctx0 = ExitStack()
    with ctx0:
        sb0 = lambda *a, **kw: ctx0.enter_context(nc.sbuf_tensor(*a, **kw))
        psa = lambda *a: ctx0.enter_context(nc.psum_tensor(*a))

        IDT = sb0("IDT", [128, 128], BF16)
        PROT = sb0("PROT", [128, 128], BF16)
        TRI = sb0("TRI", [128, 128], F32)
        ONES1 = sb0("ONES1", [128, 1], BF16)
        ONESF = sb0("ONESF", [1, 128], F32)
        CONSTS = sb0("CONSTS", [128, 8], F32)
        s1c = sb0("s1c", [128, TT_], F32)
        s2c = sb0("s2c", [128, TT_], F32)
        meanc = sb0("meanc", [128, TT_], F32)
        m2c = sb0("m2c", [128, TT_], F32)
        varc = sb0("varc", [128, TT_], F32)
        rootc = sb0("rootc", [128, TT_], F32)
        rstdc = sb0("rstdc", [128, TT_], F32)
        den_tm = sb0("den_tm", [128, TT_ * H], F32)
        redcols = sb0("redcols", [128, 64], F32)
        redsm = sb0("redsm", [128, 8], F32)
        redb = sb0("redb", [128, 8], BF16)
        sm = sb0("sm", [1, 16], F32)
        ag_sb = sb0("ag_sb", [1, 2 * NCORES], F32)
        bc = sb0("bc", [128, 16], F32)
        sqscr = sb0("sqscr", [128, E], F32)
        d2scr = sb0("d2scr", [128, 8], F32)
        msqscr = sb0("msqscr", [128, 8], F32)
        argscr = sb0("argscr", [128, 8], F32)
        rootscr = sb0("rootscr", [128, 8], F32)
        sclscr = sb0("sclscr", [128, 8], F32)

        P = [psa(f"P{i}", [128, 512], F32) for i in range(6)]
        P6 = psa("P6", [128, 512], F32)
        PT = psa("PT", [128, 1024], BF16)

        block = ctx0.enter_context(nc.Block())
        pr = Prog(nc, block, ctx0)
        ST = {n: Stream(nc, "st_" + n, ctx0) for n in
              ("cs", "x1", "xsp", "wq", "tA", "tB", "gw", "x2l", "gp",
               "x2sp", "xr2", "ccb", "wde", "upw", "ub", "uc", "xr", "out")}
        outb0 = sb0("outb0", [128, E], F32)
        open_stacks = []

        def early_out():
            def f(v):
                pr.wait(v, "te")
                pr.wait(v, "se")
                v.memset(outb0[:], 0.0)
            pr.vec(f)

            def fg(g):
                pr.wait(g, "ve")
                for t in range(TT_):
                    ST["out"].dma(g, out_d[t * 128:(t + 1) * 128, :], outb0[:])
                ST["out"].wait(g)
            block.gpsimd(fg)
            for st_ in reversed(open_stacks):
                st_.close()

        def s0(sync):
            for dst, src in ((TRI, tri_in), (ONES1, ones1_in),
                             (ONESF, onesf_in), (CONSTS, consts_in),
                             (IDT, idt_in), (PROT, prot_in)):
                ST["cs"].dma(sync, dst[:], src)
        block.sync(s0)

        ctxR1 = ExitStack()
        open_stacks.append(ctxR1)
        X1 = ctxR1.enter_context(nc.sbuf_tensor("X1", [128, TT_ * E], F32,
                                                side="right"))

        def s0b(sync):
            for t in range(TT_):
                ST["x1"].dma(sync, X1[:, t * E:(t + 1) * E],
                             x_in[t * 128:(t + 1) * 128, :])
        block.sync(s0b)

        def ln_stats(Xbuf, wait_specs):
            def f_se(s):
                for wfn in wait_specs:
                    wfn(s)
                for t in range(TT_):
                    s.activation(sqscr[:], Xbuf[:, t * E:(t + 1) * E],
                                 AF.Square, accum_out=s2c[:, t:t + 1])
            pr.sca(f_se)
            se_sq = pr.cnt["se"]

            def f_ve(v):
                for wfn in wait_specs:
                    wfn(v)
                for t in range(TT_):
                    v.tensor_reduce(s1c[:, t:t + 1], Xbuf[:, t * E:(t + 1) * E],
                                    AX.X, ALU.add)
                pr.wait(v, "se", se_sq)
                v.tensor_single_scalar(meanc[:, 0:TT_], s1c[:, 0:TT_],
                                       1.0 / E, ALU.mult)
                v.tensor_tensor(m2c[:, 0:TT_], meanc[:, 0:TT_],
                                meanc[:, 0:TT_], ALU.mult)
                v.scalar_tensor_tensor(varc[:, 0:TT_], s2c[:, 0:TT_],
                                       1.0 / E, m2c[:, 0:TT_],
                                       ALU.mult, ALU.subtract)
            pr.vec(f_ve)
            ve_var = pr.cnt["ve"]

            def f_se2(s):
                pr.wait(s, "ve", ve_var)
                s.activation(rootc[:, 0:TT_], varc[:, 0:TT_], AF.Sqrt,
                             bias=CONSTS[:, K_LNEPS:K_LNEPS + 1])
            pr.sca(f_se2)
            se_root = pr.cnt["se"]

            def f_ve2(v):
                pr.wait(v, "se", se_root)
                v.reciprocal(rstdc[:, 0:TT_], rootc[:, 0:TT_])
            pr.vec(f_ve2)

        ln_stats(X1, [lambda e: ST["cs"].wait(e), lambda e: ST["x1"].wait(e)])

        xln = ctxR1.enter_context(nc.sbuf_tensor("xln", [128, TT_ * E], BF16,
                                                 side="right"))

        def s1_ve(v):
            for t in range(TT_):
                v.tensor_scalar(xln[:, t * E:(t + 1) * E],
                                X1[:, t * E:(t + 1) * E],
                                meanc[:, t:t + 1], rstdc[:, t:t + 1],
                                ALU.subtract, ALU.mult)
        pr.vec(s1_ve)
        ve_ln1 = pr.cnt["ve"]

        def s1_g(g):
            pr.wait(g, "ve", ve_ln1)
            ST["xsp"].dma(g, x_spill.ap(), X1[:])
        block.gpsimd(s1_g)

        if stop_after < 2:
            early_out()
            return nc, list(ap_in.keys())

        ctxQK = ExitStack()
        open_stacks.append(ctxQK)
        xlnT = ctxQK.enter_context(nc.sbuf_tensor("xlnT", [128, EC * S], BF16))

        tr_jobs = [(t, c) for t in range(TT_) for c in range(EC)]
        ngrp = len(tr_jobs) // 8
        te_b2 = pr.cnt["te"]
        ve_b2 = pr.cnt["ve"]

        def s2_te(t_e):
            pr.wait(t_e, "ve", ve_ln1)
            ST["xsp"].wait(t_e)
            for gi in range(ngrp):
                if gi > 0:
                    t_e.wait_ge(pr.sems["ve"], ve_b2 + 8 * gi)
                for k in range(8):
                    t, c = tr_jobs[gi * 8 + k]
                    ins = t_e.transpose(
                        PT[:, k * 128:(k + 1) * 128],
                        xln[:, t * E + c * 128:t * E + (c + 1) * 128], IDT[:])
                pr.inc(ins, "te")
        block.tensor(s2_te)

        def s2_ve(v):
            for gi in range(ngrp):
                pr.wait(v, "te", te_b2 + gi + 1)
                for k in range(8):
                    t, c = tr_jobs[gi * 8 + k]
                    v.tensor_copy(
                        xlnT[:, c * S + t * 128:c * S + (t + 1) * 128],
                        PT[:, k * 128:(k + 1) * 128])
            assert pr.cnt["ve"] == ve_b2 + 8 * ngrp
        pr.vec(s2_ve)
        ve_xlnT = pr.cnt["ve"]

        ctxR1.close()
        open_stacks.remove(ctxR1)

        if stop_after < 3:
            early_out()
            return nc, list(ap_in.keys())

        wbuf = ctxQK.enter_context(nc.sbuf_tensor("wbuf", [128, EC * E], BF16))
        wbuf2 = ctxQK.enter_context(nc.sbuf_tensor("wbuf2", [128, EC * E], BF16))
        tabc = ctxQK.enter_context(nc.sbuf_tensor("tabc", [128, 2 * S], BF16))
        tabc2 = ctxQK.enter_context(nc.sbuf_tensor("tabc2", [128, 2 * S], BF16))
        q_sb = ctxQK.enter_context(nc.sbuf_tensor("q_sb", [128, 4 * 512], BF16))
        t1scr = ctxQK.enter_context(nc.sbuf_tensor("t1scr", [128, 512], F32))
        t2scr = ctxQK.enter_context(nc.sbuf_tensor("t2scr", [128, 512], F32))

        ctxR2 = ExitStack()
        open_stacks.append(ctxR2)
        qr = ctxR2.enter_context(nc.sbuf_tensor("qr", [128, EC * S], BF16,
                                                side="right"))
        kr = ctxR2.enter_context(nc.sbuf_tensor("kr", [128, EC * S], BF16,
                                                side="right"))

        for wi, (w_ap, cos_ap, sin_ap, dst, wb) in enumerate([
                (wq_in, qcos_in, qsin_in, qr, wbuf),
                (wk_in, ksin_in, ksin_in, kr, wbuf2)]) if False else \
                enumerate([(wq_in, qcos_in, qsin_in, qr, wbuf),
                           (wk_in, kcos_in, ksin_in, kr, wbuf2)]):
            niter = EC * NT
            te_b = pr.cnt["te"]
            ve_b = pr.cnt["ve"]
            tab_cnt = {}

            def f_g(g, w_ap=w_ap, cos_ap=cos_ap, sin_ap=sin_ap, wb=wb,
                    ve_b=ve_b, wi=wi, tab_cnt=tab_cnt):
                ST["wq"].dma(g, wb[:].rearrange("p (c o) -> p c o", o=E),
                             w_ap.rearrange("(c p) o -> p c o", p=128))
                for oc in range(EC):
                    st_t = ST["tA"] if oc % 2 == 0 else ST["tB"]
                    buf = tabc if oc % 2 == 0 else tabc2
                    if oc >= 2:
                        g.wait_ge(pr.sems["ve"], ve_b + 4 * NT * (oc - 1))
                    elif wi == 1 and ve_b > 0:
                        g.wait_ge(pr.sems["ve"], ve_b)
                    st_t.dma(g, buf[:, 0:S],
                             cos_ap[oc * 128:(oc + 1) * 128, :])
                    st_t.dma(g, buf[:, S:2 * S],
                             sin_ap[oc * 128:(oc + 1) * 128, :])
                    tab_cnt[oc] = st_t.cnt
            block.gpsimd(f_g)
            wq_cnt = ST["wq"].cnt

            def f_te(t_e, wb=wb, te_b=te_b, ve_b=ve_b, wq_cnt=wq_cnt):
                pr.wait(t_e, "ve", ve_b)
                ST["wq"].wait(t_e, wq_cnt)
                for it in range(niter):
                    oc, ts_ = divmod(it, NT)
                    bank = P[it % 4]
                    if it >= 4:
                        t_e.wait_ge(pr.sems["ve"], ve_b + 4 * (it - 4) + 1)
                    if it >= 2:
                        t_e.wait_ge(pr.sems["ve"], ve_b + 4 * (it - 2) + 3)
                    for kc in range(EC):
                        ins = t_e.matmul(
                            bank[:],
                            wb[:, kc * E + oc * 128:kc * E + (oc + 1) * 128],
                            xlnT[:, kc * S + ts_ * 512:kc * S + (ts_ + 1) * 512],
                            start=(kc == 0), stop=(kc == EC - 1))
                    pr.inc(ins, "te")
                    t_e.wait_ge(pr.sems["ve"], ve_b + 4 * it + 1)
                    pr.inc(t_e.matmul(P[4 + it % 2][:], PROT[:],
                                      q_sb[:, (it % 4) * 512:(it % 4 + 1) * 512],
                                      start=True, stop=True), "te")
                assert pr.cnt["te"] == te_b + 2 * niter
            block.tensor(f_te)

            def f_ve(v, dst=dst, te_b=te_b, ve_b=ve_b, tab_cnt=tab_cnt):
                for it in range(niter):
                    oc, ts_ = divmod(it, NT)
                    v.wait_ge(pr.sems["te"], te_b + 2 * it + 1)
                    v.tensor_copy(q_sb[:, (it % 4) * 512:(it % 4 + 1) * 512],
                                  P[it % 4][:])
                    v.wait_ge(pr.sems["te"], te_b + 2 * it + 2)
                    (ST["tA"] if oc % 2 == 0 else ST["tB"]).wait(
                        v, tab_cnt[oc])
                    buf = tabc if oc % 2 == 0 else tabc2
                    sl = slice(ts_ * 512, (ts_ + 1) * 512)
                    v.tensor_tensor(t1scr[:],
                                    q_sb[:, (it % 4) * 512:(it % 4 + 1) * 512],
                                    buf[:, sl], ALU.mult)
                    v.tensor_tensor(t2scr[:], P[4 + it % 2][:],
                                    buf[:, S + ts_ * 512:S + (ts_ + 1) * 512],
                                    ALU.mult)
                    v.tensor_tensor(
                        dst[:, oc * S + ts_ * 512:oc * S + (ts_ + 1) * 512],
                        t1scr[:], t2scr[:], ALU.add)
                assert pr.cnt["ve"] == ve_b + 4 * niter
            pr.vec(f_ve)

        te_qk = pr.cnt["te"]
        ve_qk = pr.cnt["ve"]

        if stop_after < 4:
            early_out()
            return nc, list(ap_in.keys())

        vtm = ctxR2.enter_context(nc.sbuf_tensor("vtm", [128, TT_ * E], BF16,
                                                 side="right"))
        g_stage = ctxQK.enter_context(nc.sbuf_tensor("g_stage", [128, 2 * E],
                                                     BF16))

        def s3b_g(g):
            pr.wait(g, "te", te_qk)
            ST["wq"].dma(g, wbuf[:].rearrange("p (c o) -> p c o", o=E),
                         wv_in.rearrange("(c p) o -> p c o", p=128))
            ST["wq"].dma(g, wbuf2[:].rearrange("p (c o) -> p c o", o=E),
                         wg_in.rearrange("(c p) o -> p c o", p=128))
        block.gpsimd(s3b_g)
        wq_vg = ST["wq"].cnt

        te_b3 = pr.cnt["te"]
        ve_b3 = pr.cnt["ve"]
        se_b3 = pr.cnt["se"]

        def s3b_te(t_e):
            ST["wq"].wait(t_e, wq_vg)
            pr.wait(t_e, "ve", ve_qk)
            for t in range(TT_):
                if t > 0:
                    t_e.wait_ge(pr.sems["ve"], ve_b3 + 2 * NO * t)
                for kc in range(EC):
                    for no in range(NO):
                        ins = t_e.matmul(
                            P[no][:],
                            xlnT[:, kc * S + t * 128:kc * S + (t + 1) * 128],
                            wbuf[:, kc * E + no * 512:kc * E + (no + 1) * 512],
                            start=(kc == 0), stop=(kc == EC - 1))
                    for no in range(NO):
                        ins = t_e.matmul(
                            P[2 + no][:],
                            xlnT[:, kc * S + t * 128:kc * S + (t + 1) * 128],
                            wbuf2[:, kc * E + no * 512:
                                  kc * E + (no + 1) * 512],
                            start=(kc == 0), stop=(kc == EC - 1))
                pr.inc(ins, "te")
        block.tensor(s3b_te)

        def s3b_ve(v):
            for t in range(TT_):
                v.wait_ge(pr.sems["te"], te_b3 + t + 1)
                if t >= 2:
                    ST["gw"].wait(v, 16 * (t - 1))
                for no in range(NO):
                    v.tensor_copy(
                        vtm[:, t * E + no * 512:t * E + (no + 1) * 512],
                        P[no][:])
                for no in range(NO):
                    v.tensor_copy(
                        g_stage[:, (t % 2) * E + no * 512:
                                (t % 2) * E + (no + 1) * 512],
                        P[2 + no][:])
            assert pr.cnt["ve"] == ve_b3 + 2 * NO * TT_
        pr.vec(s3b_ve)

        def s3b_gs(g):
            for t in range(TT_):
                g.wait_ge(pr.sems["ve"], ve_b3 + 2 * NO * (t + 1))
                ST["gw"].dma(g, g_spill.ap()[:, t * E:(t + 1) * E],
                             g_stage[:, (t % 2) * E:(t % 2 + 1) * E])
        block.gpsimd(s3b_gs)
        te_vg = pr.cnt["te"]
        ve_vg = pr.cnt["ve"]
        se_vg = pr.cnt["se"]

        ctxQK.close()
        open_stacks.remove(ctxQK)

        if stop_after < 5 or stop_after in (41, 42):
            early_out()
            return nc, list(ap_in.keys())

        ctxS4 = ExitStack()
        open_stacks.append(ctxS4)
        SA = ctxS4.enter_context(nc.sbuf_tensor("SA", [128, 512], BF16))
        SB = ctxS4.enter_context(nc.sbuf_tensor("SB", [128, 512], BF16))
        ABSA = ctxS4.enter_context(nc.sbuf_tensor("ABSA", [128, 512], BF16))
        ABSB = ctxS4.enter_context(nc.sbuf_tensor("ABSB", [128, 512], BF16))
        SA2 = ctxS4.enter_context(nc.sbuf_tensor("SA2", [128, 512], BF16))
        SB2 = ctxS4.enter_context(nc.sbuf_tensor("SB2", [128, 512], BF16))
        ABSA2 = ctxS4.enter_context(nc.sbuf_tensor("ABSA2", [128, 512], BF16))
        ABSB2 = ctxS4.enter_context(nc.sbuf_tensor("ABSB2", [128, 512], BF16))
        avfm = ctxS4.enter_context(nc.sbuf_tensor("avfm", [128, 512], BF16))
        avtm = ctxS4.enter_context(nc.sbuf_tensor("avtm", [128, 512], BF16))
        den_sb = ctxS4.enter_context(nc.sbuf_tensor("den_sb", [128, 512], BF16))
        gsil = ctxS4.enter_context(nc.sbuf_tensor("gsil", [128, 512], BF16))
        tmpp = ctxS4.enter_context(nc.sbuf_tensor("tmpp", [128, 128], BF16))
        g_pair = ctxS4.enter_context(
            nc.sbuf_tensor("g_pair", [128, TT_ * 128], BF16))
        X2 = ctxS4.enter_context(nc.sbuf_tensor("X2", [128, TT_ * E], F32))

        def s4_x2(g):
            pr.wait(g, "te", te_vg)
            pr.wait(g, "ve", ve_vg)
            pr.wait(g, "se", se_vg)
            ST["x2l"].dma(g, X2[:], x_spill.ap())
        block.gpsimd(s4_x2)

        for p in range(NP):
            def s4_gp(g, p=p):
                if p == 0:
                    ST["gw"].wait(g)
                else:
                    pr.wait(g, "se")
                for q4 in range(TT_ // 4):
                    ST["gp"].dma(
                        g,
                        g_pair[:, q4 * 512:(q4 + 1) * 512].rearrange(
                            "q (t c) -> q t c", c=128),
                        bass.AP(g_spill, p * 128 + q4 * 4 * E,
                                [[TT_ * E, 128], [E, 4], [1, 128]]))
            block.gpsimd(s4_gp)
            gp_cnt = ST["gp"].cnt

            for qc in range(NQ):
                nk = 4 * qc + 4
                te_b4 = pr.cnt["te"]
                ve_b4 = pr.cnt["ve"]
                ve_cum = [0] * (nk + 1)
                for kt in range(nk):
                    j = kt - 4 * qc
                    ve_cum[kt + 1] = ve_cum[kt] + (
                        2 if j < 0 else (2 if j == 3 else 4))
                se_b4 = pr.cnt["se"]

                def s_banks(kt):
                    return (P[0], P[1]) if kt % 2 == 0 else (P[5], P6)

                def emit_scores(t_e, kt, p, qc):
                    j = kt - 4 * qc
                    off = max(j, 0) * 128
                    qs0 = qc * 512
                    bA, bB = s_banks(kt)
                    if kt >= 2:
                        t_e.wait_ge(pr.sems["ve"], ve_b4 + ve_cum[kt - 1])
                    t_e.matmul(
                        bA[:, off:512],
                        kr[0:64, p * S + kt * 128:p * S + (kt + 1) * 128],
                        qr[0:64, p * S + qs0 + off:p * S + qs0 + 512],
                        start=True, stop=True)
                    ins = t_e.matmul(
                        bB[:, off:512],
                        kr[64:128, p * S + kt * 128:p * S + (kt + 1) * 128],
                        qr[64:128, p * S + qs0 + off:p * S + qs0 + 512],
                        start=True, stop=True, tile_position=(64, 0))
                    pr.inc(ins, "te")

                def emit_avden(t_e, kt, p, qc, nk):
                    j = kt - 4 * qc
                    off = max(j, 0) * 128
                    t_e.wait_ge(pr.sems["ve"], ve_b4 + ve_cum[kt + 1])
                    t_e.wait_ge(pr.sems["se"], se_b4 + 2 * (kt + 1))
                    first = (kt == 0)
                    last = (kt == nk - 1)
                    sa = SA if kt % 2 == 0 else SA2
                    sb2 = SB if kt % 2 == 0 else SB2
                    aa = ABSA if kt % 2 == 0 else ABSA2
                    ab = ABSB if kt % 2 == 0 else ABSB2
                    t_e.matmul(P[2][0:64, off:512],
                               vtm[:, kt * E + (2 * p) * D:
                                   kt * E + (2 * p) * D + D],
                               sa[:, off:512], start=first, stop=last)
                    t_e.matmul(P[3][64:128, off:512],
                               vtm[:, kt * E + (2 * p + 1) * D:
                                   kt * E + (2 * p + 1) * D + D],
                               sb2[:, off:512], start=first, stop=last,
                               tile_position=(0, 64), skip_group_check=True)
                    t_e.matmul(P[4][0:1, off:512], ONES1[:],
                               aa[:, off:512], start=first, stop=last)
                    ins = t_e.matmul(P[4][64:65, off:512], ONES1[:],
                                     ab[:, off:512], start=first,
                                     stop=last, tile_position=(0, 64),
                                     skip_group_check=True)
                    pr.inc(ins, "te")

                def f_te(t_e, p=p, qc=qc, nk=nk, te_b4=te_b4,
                         ve_b4=ve_b4, ve_cum=ve_cum):
                    pr.wait(t_e, "ve", ve_vg)
                    for kt in range(nk):
                        emit_scores(t_e, kt, p, qc)
                        if kt >= 1:
                            emit_avden(t_e, kt - 1, p, qc, nk)
                    emit_avden(t_e, nk - 1, p, qc, nk)
                block.tensor(f_te)

                def f_ve(v, p=p, qc=qc, nk=nk, te_b4=te_b4):
                    for kt in range(nk):
                        j = kt - 4 * qc
                        v.wait_ge(pr.sems["te"],
                                  te_b4 + (1 if kt == 0 else 2 * kt))
                        bA, bB = s_banks(kt)
                        sa = SA if kt % 2 == 0 else SA2
                        sb2 = SB if kt % 2 == 0 else SB2
                        if j >= 0:
                            v.tensor_tensor(sa[:, j * 128:(j + 1) * 128],
                                            bA[:, j * 128:(j + 1) * 128],
                                            TRI[:], ALU.mult)
                            v.tensor_tensor(sb2[:, j * 128:(j + 1) * 128],
                                            bB[:, j * 128:(j + 1) * 128],
                                            TRI[:], ALU.mult)
                        if j < 3:
                            off2 = (j + 1) * 128 if j >= 0 else 0
                            v.tensor_copy(sa[:, off2:512], bA[:, off2:512])
                            v.tensor_copy(sb2[:, off2:512], bB[:, off2:512])
                pr.vec(f_ve)

                def f_se(sx, p=p, qc=qc, nk=nk, te_b4=te_b4, ve_b4=ve_b4,
                         ve_cum=ve_cum, se_b4=se_b4):
                    for kt in range(nk):
                        j = kt - 4 * qc
                        off = max(j, 0) * 128
                        sx.wait_ge(pr.sems["ve"], ve_b4 + ve_cum[kt + 1])
                        if kt >= 2:
                            sx.wait_ge(pr.sems["te"], te_b4 + 2 * kt - 1)
                        sa = SA if kt % 2 == 0 else SA2
                        sb2 = SB if kt % 2 == 0 else SB2
                        aa = ABSA if kt % 2 == 0 else ABSA2
                        ab = ABSB if kt % 2 == 0 else ABSB2
                        sx.activation(aa[:, off:512], sa[:, off:512], AF.Abs)
                        sx.activation(ab[:, off:512], sb2[:, off:512], AF.Abs)
                pr.sca(f_se)

                te_av = pr.cnt["te"]

                def e_ve1(v, te_av=te_av):
                    pr.wait(v, "te", te_av)
                    v.tensor_copy(avfm[0:64, :], P[2][0:64, :])
                    v.tensor_copy(avfm[64:128, :], P[3][64:128, :])
                    v.tensor_copy(den_sb[0:1, :], P[4][0:1, :])
                    v.tensor_copy(den_sb[64:65, :], P[4][64:65, :])
                pr.vec(e_ve1)
                ve_ep1 = pr.cnt["ve"]

                def e_te(t_e, ve_ep1=ve_ep1):
                    pr.wait(t_e, "ve", ve_ep1)
                    for k in range(4):
                        t_e.transpose(PT[:, k * 128:(k + 1) * 128],
                                      avfm[:, k * 128:(k + 1) * 128], IDT[:])
                    for k in range(4):
                        ins = t_e.transpose(
                            PT[:, 512 + k * 128:512 + (k + 1) * 128],
                            den_sb[:, k * 128:(k + 1) * 128], IDT[:])
                    pr.inc(ins, "te")
                block.tensor(e_te)
                te_ep = pr.cnt["te"]

                def e_ve2(v, p=p, qc=qc, te_ep=te_ep):
                    pr.wait(v, "te", te_ep)
                    v.tensor_copy(avtm[:], PT[:, 0:512])
                    for tau in range(4):
                        tg = qc * 4 + tau
                        v.tensor_copy(
                            den_tm[:, tg * H + 2 * p:tg * H + 2 * p + 2],
                            bass.AP(PT, 512 + tau * 128,
                                    [[1024, 128], [64, 2]]))
                    dv = bass.AP(den_tm, (qc * 4) * H + 2 * p,
                                 [[TT_ * H, 128], [H, 4], [1, 2]])
                    v.tensor_scalar(dv, dv, 1.0, 50000.0, ALU.max, ALU.min)
                    v.tensor_tensor(d2scr[:], dv, dv, ALU.mult)
                pr.vec(e_ve2)
                ve_ep2 = pr.cnt["ve"]

                def e_se1(s, ve_ep2=ve_ep2):
                    pr.wait(s, "ve", ve_ep2)
                    s.activation(sqscr[:, 0:512], avtm[:], AF.Square,
                                 scale=CONSTS[:, K_EIGHTH:K_EIGHTH + 1])
                pr.sca(e_se1)
                se_ep1 = pr.cnt["se"]

                def e_ve3(v, se_ep1=se_ep1):
                    pr.wait(v, "se", se_ep1)
                    v.tensor_reduce(
                        msqscr[:],
                        bass.AP(sqscr, 0, [[E, 128], [64, 8], [1, 64]]),
                        AX.X, ALU.add)
                    v.scalar_tensor_tensor(argscr[:], d2scr[:], 1e-6,
                                           msqscr[:], ALU.mult, ALU.add)
                pr.vec(e_ve3)
                ve_ep3 = pr.cnt["ve"]

                def e_se2(s, p=p, qc=qc, ve_ep3=ve_ep3, gp_cnt=gp_cnt):
                    pr.wait(s, "ve", ve_ep3)
                    s.activation(rootscr[:], argscr[:], AF.Sqrt)
                    ST["gp"].wait(s, gp_cnt)
                    s.activation(gsil[:], g_pair[:, qc * 512:(qc + 1) * 512],
                                 AF.Silu)
                pr.sca(e_se2)
                se_ep2 = pr.cnt["se"]

                def e_ve4(v, p=p, qc=qc, se_ep2=se_ep2):
                    pr.wait(v, "se", se_ep2)
                    ST["x2l"].wait(v)
                    v.reciprocal(sclscr[:], rootscr[:])
                    for tau in range(4):
                        tg = qc * 4 + tau
                        for h in range(2):
                            v.scalar_tensor_tensor(
                                tmpp[:, h * D:(h + 1) * D],
                                avtm[:, tau * 128 + h * D:
                                     tau * 128 + (h + 1) * D],
                                sclscr[:, tau * 2 + h:tau * 2 + h + 1],
                                gsil[:, tau * 128 + h * D:
                                     tau * 128 + (h + 1) * D],
                                ALU.mult, ALU.mult)
                        v.tensor_tensor(
                            X2[:, tg * E + p * 128:tg * E + (p + 1) * 128],
                            X2[:, tg * E + p * 128:tg * E + (p + 1) * 128],
                            tmpp[:], ALU.add)
                pr.vec(e_ve4)

        ve_s4 = pr.cnt["ve"]
        te_s4 = pr.cnt["te"]

        if stop_after < 6:
            early_out()
            return nc, list(ap_in.keys())

        ln_stats(X2, [lambda e, _v=ve_s4: pr.wait(e, "ve", _v)])

        def s5_sp(g):
            pr.wait(g, "ve")
            pr.wait(g, "se")
            ST["x2sp"].dma(g, x2_spill.ap(), X2[:])
        block.gpsimd(s5_sp)
        te_preF = pr.cnt["te"]
        ctxS4.close()
        open_stacks.remove(ctxS4)
        ctxR2.close()
        open_stacks.remove(ctxR2)

        ctxF1 = ExitStack()
        open_stacks.append(ctxF1)
        xln2 = ctxF1.enter_context(nc.sbuf_tensor("xln2", [128, TT_ * E],
                                                  BF16))
        xres2 = ctxF1.enter_context(nc.sbuf_tensor("xres2", [128, 2 * E], F32))

        ve_b5 = pr.cnt["ve"]

        def s5_g2(g):
            pr.wait(g, "te", te_preF)
            ST["x2sp"].wait(g)
            for t in range(TT_):
                if t >= 2:
                    g.wait_ge(pr.sems["ve"], ve_b5 + (t - 1))
                ST["xr2"].dma(g, xres2[:, (t % 2) * E:(t % 2 + 1) * E],
                              x2_spill.ap()[:, t * E:(t + 1) * E])
        block.gpsimd(s5_g2)

        def s5_ve(v):
            pr.wait(v, "te", te_preF)
            for t in range(TT_):
                ST["xr2"].wait(v, 16 * (t + 1))
                v.tensor_scalar(xln2[:, t * E:(t + 1) * E],
                                xres2[:, (t % 2) * E:(t % 2 + 1) * E],
                                meanc[:, t:t + 1], rstdc[:, t:t + 1],
                                ALU.subtract, ALU.mult)
            v.tensor_reduce(redcols[:, 0:1], xln2[:], AX.X, ALU.max,
                            apply_absolute_value=True)
            v.tensor_single_scalar(redb[:, 0:1], redcols[:, 0:1], 1.0,
                                   ALU.mult)
        pr.vec(s5_ve)
        ve_ln2 = pr.cnt["ve"]

        def part_max_to_sm(col_redb, sm_col, wait_ve):
            def f_te(t_e):
                pr.wait(t_e, "ve", wait_ve)
                pr.inc(t_e.transpose(PT[0:1, 0:128],
                                     redb[:, col_redb:col_redb + 1], IDT[:]),
                       "te")
            block.tensor(f_te)
            te_t = pr.cnt["te"]

            def f_ve(v):
                pr.wait(v, "te", te_t)
                v.tensor_reduce(sm[:, sm_col:sm_col + 1], PT[0:1, 0:128],
                                AX.X, ALU.max)
            pr.vec(f_ve)
            return pr.cnt["ve"]

        ve_mx = part_max_to_sm(0, 0, ve_ln2)

        def collective_round(k, src_cols, wait_ve):
            n = len(src_cols)

            def f_g(g):
                pr.wait(g, "ve", wait_ve)
                ap_src = sm[0:1, src_cols[0]:src_cols[0] + n]
                ST["ccb"].dma(g, cc_i[k].ap(), ap_src)
                ST["ccb"].wait(g)
                pr.inc(g.collective_compute(
                    "AllGather", ALU.bypass,
                    replica_groups=[list(range(NCORES))],
                    ins=[cc_i[k].ap().opt()], outs=[cc_o[k].ap().opt()]),
                    "cc", 1)
                pr.wait(g, "cc")
                ST["ccb"].dma(g, ag_sb[:, 0:n * NCORES], cc_o[k].ap())
            block.gpsimd(f_g)
            cc_done = ST["ccb"].cnt

            def f_ve(v):
                ST["ccb"].wait(v, cc_done)
                for i in range(n):
                    v.tensor_reduce(
                        sm[:, src_cols[i]:src_cols[i] + 1],
                        bass.AP(ag_sb, i, [[2 * NCORES, 1], [n, NCORES]]),
                        AX.X, ALU.max)
            pr.vec(f_ve)
            return pr.cnt["ve"]

        ve_r1 = collective_round(0, [0], ve_mx)

        def r1_ve(v):
            pr.wait(v, "ve", ve_r1)
            v.tensor_scalar(sm[:, 1:2], sm[:, 0:1], 1.0 / 32767.0, 1e-12,
                            ALU.mult, ALU.max)
            v.reciprocal(sm[:, 2:3], sm[:, 1:2])
        pr.vec(r1_ve)
        ve_r1b = pr.cnt["ve"]

        def bcast(cols_src, cols_dst, wait_ve):
            n = len(cols_src)
            assert cols_src == list(range(cols_src[0], cols_src[0] + n))

            def f_te(t_e):
                pr.wait(t_e, "ve", wait_ve)
                pr.inc(t_e.matmul(P6[:, 0:n], ONESF[:],
                                  sm[0:1, cols_src[0]:cols_src[0] + n],
                                  start=True, stop=True), "te")
            block.tensor(f_te)
            te_t = pr.cnt["te"]

            def f_ve(v):
                pr.wait(v, "te", te_t)
                for i, cd in enumerate(cols_dst):
                    v.tensor_copy(bc[:, cd:cd + 1], P6[:, i:i + 1])
            pr.vec(f_ve)
            return pr.cnt["ve"]

        ve_bc1 = bcast([1, 2], [C_SX, C_RSX], ve_r1b)

        if stop_after < 7:
            early_out()
            return nc, list(ap_in.keys())

        y6 = ctxF1.enter_context(nc.sbuf_tensor("y6", [128, 2 * E], F32))
        Xq = ctxF1.enter_context(nc.sbuf_tensor("Xq", [128, TT_ * E], BF16))

        ctxR3 = ExitStack()
        open_stacks.append(ctxR3)
        XqT = ctxR3.enter_context(nc.sbuf_tensor("XqT", [128, EC * S], BF16,
                                                 side="right"))

        se_b6 = pr.cnt["se"]
        ve_b6 = pr.cnt["ve"]

        def s6_se(s):
            pr.wait(s, "ve", ve_bc1)
            for t in range(TT_):
                if t >= 2:
                    s.wait_ge(pr.sems["ve"], ve_b6 + (t - 1))
                s.activation(y6[:, (t % 2) * E:(t % 2 + 1) * E],
                             xln2[:, t * E:(t + 1) * E], AF.Copy,
                             scale=bc[:, C_RSX:C_RSX + 1])
        pr.sca(s6_se)

        def s6_ve(v):
            for t in range(TT_):
                v.wait_ge(pr.sems["se"], se_b6 + t + 1)
                v.tensor_scalar(Xq[:, t * E:(t + 1) * E],
                                y6[:, (t % 2) * E:(t % 2 + 1) * E],
                                MAGIC, MAGIC, ALU.add, ALU.subtract)
            assert pr.cnt["ve"] == ve_b6 + TT_
        pr.vec(s6_ve)
        ve_xq = pr.cnt["ve"]

        te_b6 = pr.cnt["te"]
        ve_b6b = pr.cnt["ve"]

        def s6_te(t_e):
            pr.wait(t_e, "ve", ve_xq)
            for gi in range(ngrp):
                if gi > 0:
                    t_e.wait_ge(pr.sems["ve"], ve_b6b + 8 * gi)
                for k in range(8):
                    t, c = tr_jobs[gi * 8 + k]
                    ins = t_e.transpose(
                        PT[:, k * 128:(k + 1) * 128],
                        Xq[:, t * E + c * 128:t * E + (c + 1) * 128], IDT[:])
                pr.inc(ins, "te")
        block.tensor(s6_te)

        def s6_ve2(v):
            for gi in range(ngrp):
                v.wait_ge(pr.sems["te"], te_b6 + gi + 1)
                for k in range(8):
                    t, c = tr_jobs[gi * 8 + k]
                    v.tensor_copy(
                        XqT[:, c * S + t * 128:c * S + (t + 1) * 128],
                        PT[:, k * 128:(k + 1) * 128])
        pr.vec(s6_ve2)
        ve_xqT = pr.cnt["ve"]

        ctxF1.close()
        open_stacks.remove(ctxF1)
        ctxF2 = ExitStack()
        open_stacks.append(ctxF2)

        if stop_after < 8:
            early_out()
            return nc, list(ap_in.keys())

        G = ctxF2.enter_context(nc.sbuf_tensor("G", [128, IC * S], BF16))
        ctxPB = ExitStack()
        open_stacks.append(ctxPB)
        yscr = ctxPB.enter_context(nc.sbuf_tensor("yscr", [128, 2 * S], F32))
        sigscr = ctxPB.enter_context(nc.sbuf_tensor("sigscr", [128, S], BF16))
        upc = ctxPB.enter_context(nc.sbuf_tensor("upc", [128, 2 * S], BF16))
        uqscr = ctxPB.enter_context(nc.sbuf_tensor("uqscr", [128, S], BF16))
        t2b = ctxPB.enter_context(nc.sbuf_tensor("t2b", [128, S], BF16))
        ctxUG = ExitStack()
        open_stacks.append(ctxUG)
        wde = ctxUG.enter_context(nc.sbuf_tensor("wde", [128, 2 * E], F32))
        tqscr = ctxUG.enter_context(nc.sbuf_tensor("tqscr", [128, E], F32))
        qwscr = ctxUG.enter_context(nc.sbuf_tensor("qwscr", [128, E], BF16))
        bmscr = ctxUG.enter_context(nc.sbuf_tensor("bmscr", [128, NBU], F32))
        sWscr = ctxUG.enter_context(nc.sbuf_tensor("sWscr", [128, NBU], F32))
        rsWscr = ctxUG.enter_context(nc.sbuf_tensor("rsWscr", [128, NBU], F32))
        wqT = ctxUG.enter_context(nc.sbuf_tensor("wqT", [128, 2 * E], BF16))
        upst = ctxUG.enter_context(nc.sbuf_tensor("upst", [128, S], BF16))

        def dequant_chunk(bufs, nb, width, ld_val, slot):
            wd, tq, qw, bm, sW, rsW, wrow, nbmax = bufs

            def f_ve(v):
                ST["wde"].wait(v, ld_val)
                v.tensor_reduce(
                    bm[:, 0:nb],
                    bass.AP(wd, slot * width, [[wrow, 128], [32, nb], [1, 32]]),
                    AX.X, ALU.max, apply_absolute_value=True)
                v.tensor_scalar(sW[:, 0:nb], bm[:, 0:nb], 1.0 / 7.0,
                                1e-12, ALU.mult, ALU.max)
                v.reciprocal(rsW[:, 0:nb], sW[:, 0:nb])
                v.tensor_tensor(
                    tq[:, 0:width],
                    bass.AP(wd, slot * width, [[wrow, 128], [32, nb], [1, 32]]),
                    bass.AP(rsW, 0, [[nbmax, 128], [1, nb], [0, 32]]),
                    ALU.mult)
                v.tensor_scalar(tq[:, 0:width], tq[:, 0:width],
                                MAGIC, MAGIC, ALU.add, ALU.subtract)
                v.tensor_tensor(
                    qw[:, 0:width],
                    bass.AP(tq, 0, [[width, 128], [32, nb], [1, 32]]),
                    bass.AP(sW, 0, [[nbmax, 128], [1, nb], [0, 32]]),
                    ALU.mult)
            pr.vec(f_ve)
            return pr.cnt["ve"]

        ug_bufs = (wde, tqscr, qwscr, bmscr, sWscr, rsWscr, 2 * E, NBU)

        dq_hist = {}
        wqT_hist = {}
        for mi, (w_src, is_up) in enumerate([(wup_in, True),
                                             (wgate_in, False)]):
            for c in range(IC):
                ci = mi * IC + c

                def f_ld(sync, w_src=w_src, c=c, mi=mi, ci=ci):
                    if mi == 0 and c == 0:
                        ST["x2sp"].wait(sync)
                        pr.wait(sync, "ve")
                        pr.wait(sync, "se")
                        pr.wait(sync, "te")
                    if ci >= 2:
                        pr.wait(sync, "ve", dq_hist[ci - 2])
                    ST["wde"].dma(sync,
                                  wde[:, (c % 2) * E:(c % 2 + 1) * E],
                                  w_src[c * 128:(c + 1) * 128, :])
                block.sync(f_ld)
                ld_w = ST["wde"].cnt

                ve_dq = dequant_chunk(ug_bufs, NBU, E, ld_w, c % 2)
                dq_hist[ci] = ve_dq

                def f_tr(t_e, ve_dq=ve_dq, c=c, mi=mi, ci=ci):
                    pr.wait(t_e, "ve", ve_dq)
                    if ci >= 1:
                        pr.wait(t_e, "ve", wqT_hist[ci - 1])
                    for k in range(8):
                        ins = t_e.transpose(PT[:, k * 128:(k + 1) * 128],
                                            qwscr[:, k * 128:(k + 1) * 128],
                                            IDT[:])
                    pr.inc(ins, "te")
                block.tensor(f_tr)
                te_tr = pr.cnt["te"]

                def f_ev(v, te_tr=te_tr, c=c):
                    pr.wait(v, "te", te_tr)
                    v.tensor_copy(wqT[:, (c % 2) * E:(c % 2 + 1) * E], PT[:])
                pr.vec(f_ev)
                ve_wqT = pr.cnt["ve"]
                wqT_hist[ci] = ve_wqT

                def f_mm(t_e, ve_wqT=ve_wqT, c=c, mi=mi):
                    pr.wait(t_e, "ve", ve_wqT)
                    if mi == 0 and c == 0:
                        pr.wait(t_e, "ve", ve_xqT)
                    for kc in range(EC):
                        for ts_ in range(NT):
                            ins = t_e.matmul(
                                P[ts_][:],
                                wqT[:, (c % 2) * E + kc * 128:
                                    (c % 2) * E + (kc + 1) * 128],
                                XqT[:, kc * S + ts_ * 512:
                                    kc * S + (ts_ + 1) * 512],
                                start=(kc == 0), stop=(kc == EC - 1))
                    pr.inc(ins, "te")
                block.tensor(f_mm)
                te_mm = pr.cnt["te"]

                def f_evac(v, te_mm=te_mm, c=c, is_up=is_up):
                    pr.wait(v, "te", te_mm)
                    if is_up:
                        ST["upw"].wait(v)
                    for ts_ in range(NT):
                        if is_up:
                            o = upst[:, ts_ * 512:(ts_ + 1) * 512]
                        else:
                            o = G[:, c * S + ts_ * 512:c * S + (ts_ + 1) * 512]
                        v.tensor_scalar(o, P[ts_][:], bc[:, C_SX:C_SX + 1],
                                        None, ALU.mult)
                    src = upst[:] if is_up else G[:, c * S:(c + 1) * S]
                    v.tensor_reduce(redcols[:, mi * IC + c:mi * IC + c + 1],
                                    src, AX.X, ALU.max,
                                    apply_absolute_value=True)
                pr.vec(f_evac)
                ve_ev = pr.cnt["ve"]

                if is_up:
                    def f_sp(g, c=c, ve_ev=ve_ev):
                        pr.wait(g, "ve", ve_ev)
                        ST["upw"].dma(g, up_spill.ap()[:, c * S:(c + 1) * S],
                                      upst[:])
                    block.gpsimd(f_sp)

        ve_s7 = pr.cnt["ve"]

        def s7_red(v):
            pr.wait(v, "ve", ve_s7)
            v.tensor_reduce(redsm[:, 0:1], redcols[:, 0:IC], AX.X, ALU.max)
            v.tensor_reduce(redsm[:, 1:2], redcols[:, IC:2 * IC], AX.X,
                            ALU.max)
            v.tensor_single_scalar(redb[:, 0:2], redsm[:, 0:2], 1.0, ALU.mult)
        pr.vec(s7_red)
        ve_r2a = pr.cnt["ve"]

        def s7_tr(t_e):
            pr.wait(t_e, "ve", ve_r2a)
            t_e.transpose(PT[0:1, 0:128], redb[:, 0:1], IDT[:])
            pr.inc(t_e.transpose(PT[0:1, 128:256], redb[:, 1:2], IDT[:]), "te")
        block.tensor(s7_tr)
        te_r2 = pr.cnt["te"]

        def s7_fin(v):
            pr.wait(v, "te", te_r2)
            v.tensor_reduce(sm[:, 4:5], PT[0:1, 0:128], AX.X, ALU.max)
            v.tensor_reduce(sm[:, 5:6], PT[0:1, 128:256], AX.X, ALU.max)
        pr.vec(s7_fin)
        ve_r2b = pr.cnt["ve"]

        ve_r2 = collective_round(1, [4, 5], ve_r2b)

        def r2_ve(v):
            pr.wait(v, "ve", ve_r2)
            v.tensor_scalar(sm[:, 6:7], sm[:, 4:5], 1.0 / 32767.0, 1e-12,
                            ALU.mult, ALU.max)
            v.tensor_scalar(sm[:, 7:8], sm[:, 5:6], 1.0 / 32767.0, 1e-12,
                            ALU.mult, ALU.max)
            v.reciprocal(sm[:, 8:9], sm[:, 6:7])
            v.reciprocal(sm[:, 9:10], sm[:, 7:8])
        pr.vec(r2_ve)
        ve_r2c = pr.cnt["ve"]
        ve_bc2 = bcast([7, 8, 9], [C_SG, C_RSUP, C_RSG], ve_r2c)

        if stop_after < 9:
            early_out()
            return nc, list(ap_in.keys())

        ctxUG.close()
        open_stacks.remove(ctxUG)
        ctxR3.close()
        open_stacks.remove(ctxR3)

        se_b8 = pr.cnt["se"]
        ve_b8 = pr.cnt["ve"]
        for c in range(IC):
            def a_se1(s, c=c):
                if c == 0:
                    pr.wait(s, "ve", ve_bc2)
                else:
                    s.wait_ge(pr.sems["ve"], ve_b8 + 5 * (c - 1) + 3)
                s.activation(yscr[:, 0:S], G[:, c * S:(c + 1) * S],
                             AF.Copy, scale=bc[:, C_RSG:C_RSG + 1])
            pr.sca(a_se1)

            def a_ve1(v, c=c):
                v.wait_ge(pr.sems["se"], se_b8 + 2 * c + 1)
                v.tensor_scalar(G[:, c * S:(c + 1) * S], yscr[:, 0:S],
                                MAGIC, MAGIC, ALU.add, ALU.subtract)
            pr.vec(a_ve1)

            def a_se2(s, c=c):
                s.wait_ge(pr.sems["ve"], ve_b8 + 5 * c + 1)
                s.activation(yscr[:, 0:S], G[:, c * S:(c + 1) * S],
                             AF.Sigmoid, scale=bc[:, C_SG:C_SG + 1])
            pr.sca(a_se2)

            def a_ve2(v, c=c):
                v.wait_ge(pr.sems["se"], se_b8 + 2 * c + 2)
                v.tensor_scalar(yscr[:, 0:S], yscr[:, 0:S], 65536.0, MAGIC,
                                ALU.mult, ALU.add)
                v.tensor_scalar(sigscr[:], yscr[:, 0:S], MAGIC, 65535.0,
                                ALU.subtract, ALU.min)
                v.tensor_tensor(G[:, c * S:(c + 1) * S],
                                G[:, c * S:(c + 1) * S], sigscr[:], ALU.mult)
                v.tensor_reduce(redcols[:, 32 + c:33 + c],
                                G[:, c * S:(c + 1) * S], AX.X, ALU.max,
                                apply_absolute_value=True)
            pr.vec(a_ve2)
        ve_s8a = pr.cnt["ve"]

        def s8_red(v):
            pr.wait(v, "ve", ve_s8a)
            v.tensor_reduce(redsm[:, 2:3], redcols[:, 32:32 + IC], AX.X,
                            ALU.max)
            v.tensor_single_scalar(redb[:, 2:3], redsm[:, 2:3], 1.0, ALU.mult)
        pr.vec(s8_red)
        ve_r3a = pr.cnt["ve"]

        def s8_tr(t_e):
            pr.wait(t_e, "ve", ve_r3a)
            pr.inc(t_e.transpose(PT[0:1, 0:128], redb[:, 2:3], IDT[:]), "te")
        block.tensor(s8_tr)
        te_r3 = pr.cnt["te"]

        def s8_fin(v):
            pr.wait(v, "te", te_r3)
            v.tensor_reduce(sm[:, 10:11], PT[0:1, 0:128], AX.X, ALU.max)
        pr.vec(s8_fin)
        ve_r3b = pr.cnt["ve"]

        ve_r3 = collective_round(2, [10], ve_r3b)

        def r3_ve(v):
            pr.wait(v, "ve", ve_r3)
            v.tensor_tensor(sm[:, 11:12], sm[:, 10:11], sm[:, 7:8], ALU.mult)
            v.tensor_scalar(sm[:, 11:12], sm[:, 11:12],
                            1.0 / (65536.0 * 32767.0), 1e-12,
                            ALU.mult, ALU.max)
            v.reciprocal(sm[:, 12:13], sm[:, 11:12])
            v.tensor_tensor(sm[:, 12:13], sm[:, 12:13], sm[:, 7:8], ALU.mult)
            v.tensor_single_scalar(sm[:, 12:13], sm[:, 12:13],
                                   1.0 / 65536.0, ALU.mult)
        pr.vec(r3_ve)
        ve_r3c = pr.cnt["ve"]
        ve_bc3 = bcast([12], [C_RST], ve_r3c)

        se_b8b = pr.cnt["se"]
        ve_b8b = pr.cnt["ve"]
        for c in range(IC):
            def b_se(s, c=c):
                if c == 0:
                    pr.wait(s, "ve", ve_bc3)
                else:
                    s.wait_ge(pr.sems["ve"], ve_b8b + c)
                s.activation(yscr[:, 0:S], G[:, c * S:(c + 1) * S],
                             AF.Copy, scale=bc[:, C_RST:C_RST + 1])
            pr.sca(b_se)

            def b_ve(v, c=c):
                v.wait_ge(pr.sems["se"], se_b8b + c + 1)
                v.tensor_scalar(G[:, c * S:(c + 1) * S], yscr[:, 0:S],
                                MAGIC, MAGIC, ALU.add, ALU.subtract)
            pr.vec(b_ve)
        ve_g2 = pr.cnt["ve"]

        if stop_after < 10:
            early_out()
            return nc, list(ap_in.keys())

        se_b8c = pr.cnt["se"]
        ve_b8c = pr.cnt["ve"]

        for c in range(IC):
            def pb_g(g, c=c):
                if c == 0:
                    pr.wait(g, "ve", ve_g2)
                    ST["upw"].wait(g)
                if c >= 2:
                    g.wait_ge(pr.sems["se"], se_b8c + (c - 1))
                ST["ub"].dma(g, upc[:, (c % 2) * S:(c % 2 + 1) * S],
                             up_spill.ap()[:, c * S:(c + 1) * S])
            block.gpsimd(pb_g)

            def pb_se(s, c=c):
                ST["ub"].wait(s, 16 * (c + 1))
                if c > 0:
                    s.wait_ge(pr.sems["ve"], ve_b8c + 3 * (c - 1) + 1)
                s.activation(yscr[:, 0:S], upc[:, (c % 2) * S:(c % 2 + 1) * S],
                             AF.Copy, scale=bc[:, C_RSUP:C_RSUP + 1])
            pr.sca(pb_se)

            def pb_ve(v, c=c):
                v.wait_ge(pr.sems["se"], se_b8c + c + 1)
                v.tensor_scalar(uqscr[:], yscr[:, 0:S], MAGIC, MAGIC,
                                ALU.add, ALU.subtract)
                v.tensor_tensor(t2b[:], G[:, c * S:(c + 1) * S], uqscr[:],
                                ALU.mult)
                v.tensor_reduce(redcols[:, 32 + c:33 + c], t2b[:], AX.X,
                                ALU.max, apply_absolute_value=True)
            pr.vec(pb_ve)
        ve_s8b = pr.cnt["ve"]

        def s8_red2(v):
            pr.wait(v, "ve", ve_s8b)
            v.tensor_reduce(redsm[:, 3:4], redcols[:, 32:32 + IC], AX.X,
                            ALU.max)
            v.tensor_single_scalar(redb[:, 3:4], redsm[:, 3:4], 1.0, ALU.mult)
        pr.vec(s8_red2)
        ve_r4a = pr.cnt["ve"]

        def s8_tr2(t_e):
            pr.wait(t_e, "ve", ve_r4a)
            pr.inc(t_e.transpose(PT[0:1, 0:128], redb[:, 3:4], IDT[:]), "te")
        block.tensor(s8_tr2)
        te_r4 = pr.cnt["te"]

        def s8_fin2(v):
            pr.wait(v, "te", te_r4)
            v.tensor_reduce(sm[:, 13:14], PT[0:1, 0:128], AX.X, ALU.max)
        pr.vec(s8_fin2)
        ve_r4b = pr.cnt["ve"]

        ve_r4 = collective_round(3, [13], ve_r4b)

        def r4_ve(v):
            pr.wait(v, "ve", ve_r4)
            v.tensor_tensor(sm[:, 14:15], sm[:, 13:14], sm[:, 11:12],
                            ALU.mult)
            v.tensor_tensor(sm[:, 14:15], sm[:, 14:15], sm[:, 6:7], ALU.mult)
            v.tensor_scalar(sm[:, 14:15], sm[:, 14:15], 1.0 / 32767.0,
                            1e-12, ALU.mult, ALU.max)
            v.reciprocal(sm[:, 15:16], sm[:, 14:15])
            v.tensor_tensor(sm[:, 15:16], sm[:, 15:16], sm[:, 11:12],
                            ALU.mult)
            v.tensor_tensor(sm[:, 15:16], sm[:, 15:16], sm[:, 6:7], ALU.mult)
        pr.vec(r4_ve)
        ve_r4c = pr.cnt["ve"]
        ve_bc4 = bcast([14, 15], [C_SO, C_RSO], ve_r4c)

        if stop_after < 11:
            early_out()
            return nc, list(ap_in.keys())

        se_b8d = pr.cnt["se"]
        ve_b8d = pr.cnt["ve"]
        for c in range(IC):
            def pc_g(g, c=c):
                if c == 0:
                    pr.wait(g, "ve", ve_bc4)
                if c >= 2:
                    g.wait_ge(pr.sems["se"], se_b8d + 2 * (c - 1) + 1)
                ST["uc"].dma(g, upc[:, (c % 2) * S:(c % 2 + 1) * S],
                             up_spill.ap()[:, c * S:(c + 1) * S])
            block.gpsimd(pc_g)

            def pc_se1(s, c=c):
                ST["uc"].wait(s, 16 * (c + 1))
                if c > 0:
                    s.wait_ge(pr.sems["ve"], ve_b8d + 3 * c)
                s.activation(yscr[:, 0:S], upc[:, (c % 2) * S:(c % 2 + 1) * S],
                             AF.Copy, scale=bc[:, C_RSUP:C_RSUP + 1])
            pr.sca(pc_se1)

            def pc_ve1(v, c=c):
                v.wait_ge(pr.sems["se"], se_b8d + 2 * c + 1)
                v.tensor_scalar(uqscr[:], yscr[:, 0:S], MAGIC, MAGIC,
                                ALU.add, ALU.subtract)
                v.tensor_tensor(t2b[:], G[:, c * S:(c + 1) * S], uqscr[:],
                                ALU.mult)
            pr.vec(pc_ve1)

            def pc_se2(s, c=c):
                s.wait_ge(pr.sems["ve"], ve_b8d + 3 * c + 2)
                s.activation(yscr[:, S:2 * S], t2b[:], AF.Copy,
                             scale=bc[:, C_RSO:C_RSO + 1])
            pr.sca(pc_se2)

            def pc_ve2(v, c=c):
                v.wait_ge(pr.sems["se"], se_b8d + 2 * c + 2)
                v.tensor_scalar(G[:, c * S:(c + 1) * S], yscr[:, S:2 * S],
                                MAGIC, MAGIC, ALU.add, ALU.subtract)
            pr.vec(pc_ve2)
        ve_oint = pr.cnt["ve"]

        if stop_after < 12:
            early_out()
            return nc, list(ap_in.keys())

        ctxPB.close()
        open_stacks.remove(ctxPB)
        wdeD = ctxF2.enter_context(nc.sbuf_tensor("wdeD", [128, I], F32))
        tqD = ctxF2.enter_context(nc.sbuf_tensor("tqD", [128, I], F32))
        qwD = ctxF2.enter_context(nc.sbuf_tensor("qwD", [128, I], BF16))
        bmD = ctxF2.enter_context(nc.sbuf_tensor("bmD", [128, NBD], F32))
        sWD = ctxF2.enter_context(nc.sbuf_tensor("sWD", [128, NBD], F32))
        rsWD = ctxF2.enter_context(nc.sbuf_tensor("rsWD", [128, NBD], F32))
        wqdT = ctxF2.enter_context(nc.sbuf_tensor("wqdT", [128, IC * E],
                                                  BF16))
        d_bufs = (wdeD, tqD, qwD, bmD, sWD, rsWD, I, NBD)

        n_dchunks = 1 if stop_after == 121 else (2 if stop_after == 123 else EC)
        skip_dtr = (stop_after == 122)
        for c in range(n_dchunks):
            def d_ld(sync, c=c):
                pr.wait(sync, "ve", ve_oint) if c < 1 else pr.wait(sync, "ve")
                ST["wde"].dma(sync, wdeD[:],
                              wdown_in[c * 128:(c + 1) * 128, :])
            block.sync(d_ld)
            ld_w = ST["wde"].cnt

            ve_dq = dequant_chunk(d_bufs, NBD, I, ld_w, 0)

            for tg in range(0 if skip_dtr else (IC // 8)):
                def d_tr(t_e, ve_dq=ve_dq, c=c, tg=tg):
                    pr.wait(t_e, "ve", ve_dq)
                    if not (c == 0 and tg == 0):
                        pr.wait(t_e, "ve")
                    for k in range(8):
                        jc = tg * 8 + k
                        ins = t_e.transpose(PT[:, k * 128:(k + 1) * 128],
                                            qwD[:, jc * 128:(jc + 1) * 128],
                                            IDT[:])
                    pr.inc(ins, "te")
                block.tensor(d_tr)
                te_tr = pr.cnt["te"]

                def d_ev(v, te_tr=te_tr, c=c, tg=tg):
                    pr.wait(v, "te", te_tr)
                    for k in range(8):
                        jc = tg * 8 + k
                        v.tensor_copy(
                            wqdT[:, jc * E + c * 128:jc * E + (c + 1) * 128],
                            PT[:, k * 128:(k + 1) * 128])
                pr.vec(d_ev)
        ve_wqd = pr.cnt["ve"]

        if stop_after < 13 or stop_after in (121, 122, 123):
            early_out()
            return nc, list(ap_in.keys())

        xres = ctxF2.enter_context(nc.sbuf_tensor("xres", [128, 2 * E], F32))
        xout = ctxF2.enter_context(nc.sbuf_tensor("xout", [128, 2 * E], F32))

        te_b9 = pr.cnt["te"]
        ve_b9 = pr.cnt["ve"]

        only_g = (stop_after == 126)
        only_gte = (stop_after == 127)

        def s9_g(g):
            pr.wait(g, "ve", ve_wqd)
            ST["x2sp"].wait(g)
            for t in range(TT_):
                if t >= 2 and not (only_g or only_gte):
                    g.wait_ge(pr.sems["ve"], ve_b9 + 2 * NO * (t - 1))
                ST["xr"].dma(g, xres[:, (t % 2) * E:(t % 2 + 1) * E],
                             x2_spill.ap()[:, t * E:(t + 1) * E])
        block.gpsimd(s9_g)

        if only_g:
            early_out()
            return nc, list(ap_in.keys())

        jc_rng = (range(0, 14) if stop_after == 124 else
                  (range(14, IC) if stop_after == 125 else range(IC)))

        def s9_te(t_e):
            pr.wait(t_e, "ve", ve_wqd)
            for t in range(TT_):
                if t >= 2 and not only_gte:
                    t_e.wait_ge(pr.sems["ve"], ve_b9 + 2 * NO * (t - 1))
                for jc in jc_rng:
                    for no in range(NO):
                        ins = t_e.matmul(
                            P[(t % 2) * 2 + no][:],
                            G[:, jc * S + t * 128:jc * S + (t + 1) * 128],
                            wqdT[:, jc * E + no * 512:jc * E + (no + 1) * 512],
                            start=(jc == jc_rng[0] if hasattr(jc_rng, "__getitem__") else jc == 0),
                            stop=(jc == list(jc_rng)[-1]))
                pr.inc(ins, "te")
        block.tensor(s9_te)

        if only_gte:
            early_out()
            return nc, list(ap_in.keys())

        def s9_ve(v):
            for t in range(TT_):
                v.wait_ge(pr.sems["te"], te_b9 + t + 1)
                ST["xr"].wait(v, 16 * (t + 1))
                if t >= 2:
                    ST["out"].wait(v, 16 * (t - 1))
                for no in range(NO):
                    v.tensor_scalar(sqscr[:, no * 512:(no + 1) * 512],
                                    P[(t % 2) * 2 + no][:],
                                    bc[:, C_SO:C_SO + 1], None, ALU.mult)
                for no in range(NO):
                    v.tensor_tensor(
                        xout[:, (t % 2) * E + no * 512:
                             (t % 2) * E + (no + 1) * 512],
                        sqscr[:, no * 512:(no + 1) * 512],
                        xres[:, (t % 2) * E + no * 512:
                             (t % 2) * E + (no + 1) * 512],
                        ALU.add)
        pr.vec(s9_ve)

        if stop_after == 128:
            early_out()
            return nc, list(ap_in.keys())

        def s9_st(g):
            for t in range(TT_):
                g.wait_ge(pr.sems["ve"], ve_b9 + 2 * NO * (t + 1))
                ST["out"].dma(g, out_d[t * 128:(t + 1) * 128, :],
                              xout[:, (t % 2) * E:(t % 2 + 1) * E])
            ST["out"].wait(g)
        block.sync(s9_st)

        ctxF2.close()
        open_stacks.remove(ctxF2)

    return nc, list(ap_in.keys())


_CACHE = {}


def _get_prog(S, E, H, I):
    key = (S, E, H, I)
    if key not in _CACHE:
        _CACHE[key] = build_program(S, E, H, I)
    return _CACHE[key]


def kernel(L0_attn_x, L0_attn_Wq, L0_attn_Wk, L0_attn_Wv, L0_attn_Wg,
           L0_ffn_w_gate, L0_ffn_w_up, L0_ffn_w_down):
    x = np.ascontiguousarray(np.asarray(L0_attn_x, np.float32))
    B, S, E = x.shape
    H = 16
    I = L0_ffn_w_up.shape[0]
    consts = make_consts(S, E, H)
    nc, _ = _get_prog(S, E, H, I)
    common = dict(
        wq=np.ascontiguousarray(np.asarray(L0_attn_Wq, np.float32)),
        wk=np.ascontiguousarray(np.asarray(L0_attn_Wk, np.float32)),
        wv=np.ascontiguousarray(np.asarray(L0_attn_Wv, np.float32)),
        wg=np.ascontiguousarray(np.asarray(L0_attn_Wg, np.float32)),
        wup=np.ascontiguousarray(np.asarray(L0_ffn_w_up, np.float32)),
        wgate=np.ascontiguousarray(np.asarray(L0_ffn_w_gate, np.float32)),
        wdown=np.ascontiguousarray(np.asarray(L0_ffn_w_down, np.float32)),
        **consts)
    in_maps = [dict(x=np.ascontiguousarray(x[c]), **common) for c in range(B)]
    res = run_bass_kernel_spmd(nc, in_maps, core_ids=list(range(NCORES)))
    out = np.stack([np.asarray(res.results[c]["out"]) for c in range(B)], 0)
    return out.astype(np.float32)


# version 08
# speedup vs baseline: 1.1794x; 1.0123x over previous
import numpy as np
import ml_dtypes
from contextlib import ExitStack

import concourse.bass as bass
import concourse.mybir as mybir
from concourse.bass_utils import run_bass_kernel_spmd

F32 = mybir.dt.float32
BF16 = mybir.dt.bfloat16
AF = mybir.ActivationFunctionType
ALU = mybir.AluOpType
AX = mybir.AxisListType
BF = ml_dtypes.bfloat16
MAGIC = 12582912.0
NCORES = 8

C_SX, C_RSX, C_RSUP, C_RSG, C_SG, C_RST, C_RSO, C_SO = range(8)
K_LNEPS, K_EIGHTH = 0, 1

_CHAIN_OPS = frozenset((
    "tensor_copy", "tensor_tensor", "tensor_add", "tensor_sub", "tensor_mul",
    "tensor_max", "tensor_scalar", "tensor_single_scalar", "tensor_scalar_add",
    "tensor_scalar_sub", "tensor_scalar_mul", "tensor_scalar_max",
    "tensor_scalar_min", "tensor_reduce", "reduce_max", "reduce_sum",
    "scalar_tensor_tensor", "reciprocal", "memset", "memzero", "select",
    "activation", "copy", "mul", "add", "square", "sqrt", "sign",
))


class Chained:

    def __init__(self, pr, eng, sem_name):
        self._pr = pr
        self._e = eng
        self._s = sem_name

    def __getattr__(self, n):
        a = getattr(self._e, n)
        if n not in _CHAIN_OPS:
            return a
        pr, e, sn = self._pr, self._e, self._s

        def w(*args, **kw):
            if pr.cnt[sn] > 0:
                e.wait_ge(pr.sems[sn], pr.cnt[sn])
            ins = a(*args, **kw)
            ins.then_inc(pr.sems[sn], 1)
            pr.cnt[sn] += 1
            return ins
        return w


class Stream:

    def __init__(self, nc, name, ctx):
        self.sem = ctx.enter_context(nc.semaphore(name))
        self.cnt = 0

    def dma(self, g, out, in_):
        if self.cnt > 0:
            g.wait_ge(self.sem, self.cnt)
        g.dma_start(out=out, in_=in_).then_inc(self.sem, 16)
        self.cnt += 16
        return self.cnt

    def wait(self, eng, val=None):
        v = self.cnt if val is None else val
        if v > 0:
            e = eng._e if isinstance(eng, Chained) else eng
            e.wait_ge(self.sem, v)


class Prog:
    def __init__(self, nc, block, ctx):
        self.nc = nc
        self.block = block
        self.cnt = {}
        self.sems = {}
        for s in ("te", "ve", "se", "cc"):
            self.sems[s] = ctx.enter_context(nc.semaphore(s))
            self.cnt[s] = 0

    def vec(self, f):
        self.block.vector(lambda v: f(Chained(self, v, "ve")))

    def sca(self, f):
        self.block.scalar(lambda s: f(Chained(self, s, "se")))

    def inc(self, ins, name, n=1):
        ins.then_inc(self.sems[name], n)
        self.cnt[name] += n
        return self.cnt[name]

    def wait(self, eng, name, val=None):
        v = self.cnt[name] if val is None else val
        if v > 0:
            e = eng._e if isinstance(eng, Chained) else eng
            e.wait_ge(self.sems[name], v)


def make_consts(S, E, H):
    D = E // H
    idx = np.arange(S, dtype=np.float64)
    a_ang = 1.0 / 10000.0 ** np.linspace(0.0, 1.0, D // 2)
    angle = np.repeat(a_ang, 2)
    sin_t = np.sin(idx[:, None] * angle[None, :])
    cos_t = np.cos(idx[:, None] * angle[None, :])
    gamma = 1.0 - 2.0 ** (-5.0 - np.arange(H, dtype=np.float64))
    r = (1.0 - gamma[:, None] ** (idx[None, :] + 1.0)) / (1.0 - gamma[:, None])
    a_hq = gamma[:, None] ** idx[None, :] / np.sqrt(r)
    b_hk = np.exp(-idx[None, :] * np.log(gamma)[:, None])
    scaling = float(E) ** (-0.5)

    rows = np.arange(E)
    h_r = 2 * (rows // 128) + (rows % 128) // D
    d_r = rows % D
    QCOS = (cos_t[:, d_r].T * a_hq[h_r]).astype(np.float32)
    QSIN = (sin_t[:, d_r].T * a_hq[h_r]).astype(np.float32)
    KCOS = (cos_t[:, d_r].T * b_hk[h_r] * scaling).astype(np.float32)
    KSIN = (sin_t[:, d_r].T * b_hk[h_r] * scaling).astype(np.float32)

    IDT = np.eye(128, dtype=np.float32).astype(BF)
    PROT = np.zeros((128, 128), np.float32)
    for i in range(64):
        PROT[2 * i + 1, 2 * i] = -1.0
        PROT[2 * i, 2 * i + 1] = 1.0
    PROT = PROT.astype(BF)
    ki = np.arange(128)
    TRI = (ki[:, None] <= ki[None, :]).astype(np.float32)
    ONES1 = np.ones((128, 1), np.float32).astype(BF)
    ONESF = np.ones((1, 128), np.float32)
    CONSTS = np.zeros((128, 8), np.float32)
    CONSTS[:, K_LNEPS] = 1e-5
    CONSTS[:, K_EIGHTH] = 0.125
    return dict(qcos=QCOS, qsin=QSIN, kcos=KCOS, ksin=KSIN, idt=IDT, prot=PROT,
                tri=TRI, ones1=ONES1, onesf=ONESF, consts=CONSTS)


def build_program(S=2048, E=1024, H=16, I=3072, stop_after=99):
    D = E // H
    TT_ = S // 128
    EC = E // 128
    IC = I // 128
    NQ = S // 512
    NP = H // 2
    NT = S // 512
    NO = E // 512
    NBU = E // 32
    NBD = I // 32

    nc = bass.Bass("TRN2", target_bir_lowering=False, debug=False)
    ap_in = {}

    def din(name, shape, dt=F32):
        ap_in[name] = nc.dram_tensor(name, list(shape), dt,
                                     kind="ExternalInput").ap()
        return ap_in[name]

    x_in = din("x", (S, E))
    wq_in = din("wq", (E, E)); wk_in = din("wk", (E, E))
    wv_in = din("wv", (E, E)); wg_in = din("wg", (E, E))
    wup_in = din("wup", (I, E)); wgate_in = din("wgate", (I, E))
    wdown_in = din("wdown", (E, I))
    qcos_in = din("qcos", (E, S)); qsin_in = din("qsin", (E, S))
    kcos_in = din("kcos", (E, S)); ksin_in = din("ksin", (E, S))
    idt_in = din("idt", (128, 128), BF16)
    prot_in = din("prot", (128, 128), BF16)
    tri_in = din("tri", (128, 128))
    ones1_in = din("ones1", (128, 1), BF16)
    onesf_in = din("onesf", (1, 128))
    consts_in = din("consts", (128, 8))

    out_d = nc.dram_tensor("out", [S, E], F32, kind="ExternalOutput").ap()

    g_spill = nc.dram_tensor("g_spill", [128, TT_ * E], BF16)
    x_spill = nc.dram_tensor("x_spill", [128, TT_ * E], F32)
    x2_spill = nc.dram_tensor("x2_spill", [128, TT_ * E], F32)
    up_spill = nc.dram_tensor("up_spill", [128, IC * S], BF16)
    cc_i = [nc.dram_tensor(f"cc_i{k}", [1, n], F32)
            for k, n in enumerate((1, 2, 1, 1))]
    cc_o = [nc.dram_tensor(f"cc_o{k}", [1, n * NCORES], F32)
            for k, n in enumerate((1, 2, 1, 1))]

    ctx0 = ExitStack()
    with ctx0:
        sb0 = lambda *a, **kw: ctx0.enter_context(nc.sbuf_tensor(*a, **kw))
        psa = lambda *a: ctx0.enter_context(nc.psum_tensor(*a))

        IDT = sb0("IDT", [128, 128], BF16)
        PROT = sb0("PROT", [128, 128], BF16)
        TRI = sb0("TRI", [128, 128], F32)
        ONES1 = sb0("ONES1", [128, 1], BF16)
        ONESF = sb0("ONESF", [1, 128], F32)
        CONSTS = sb0("CONSTS", [128, 8], F32)
        s1c = sb0("s1c", [128, TT_], F32)
        s2c = sb0("s2c", [128, TT_], F32)
        meanc = sb0("meanc", [128, TT_], F32)
        m2c = sb0("m2c", [128, TT_], F32)
        varc = sb0("varc", [128, TT_], F32)
        rootc = sb0("rootc", [128, TT_], F32)
        rstdc = sb0("rstdc", [128, TT_], F32)
        den_tm = sb0("den_tm", [128, TT_ * H], F32)
        redcols = sb0("redcols", [128, 64], F32)
        redsm = sb0("redsm", [128, 8], F32)
        redb = sb0("redb", [128, 8], BF16)
        sm = sb0("sm", [1, 16], F32)
        ag_sb = sb0("ag_sb", [1, 2 * NCORES], F32)
        bc = sb0("bc", [128, 16], F32)
        sqscr = sb0("sqscr", [128, E], F32)
        d2scr = sb0("d2scr", [128, 8], F32)
        msqscr = sb0("msqscr", [128, 8], F32)
        argscr = sb0("argscr", [128, 8], F32)
        rootscr = sb0("rootscr", [128, 8], F32)
        sclscr = sb0("sclscr", [128, 8], F32)

        P = [psa(f"P{i}", [128, 512], F32) for i in range(6)]
        P6 = psa("P6", [128, 512], F32)
        PT = psa("PT", [128, 1024], BF16)

        block = ctx0.enter_context(nc.Block())
        pr = Prog(nc, block, ctx0)
        ST = {n: Stream(nc, "st_" + n, ctx0) for n in
              ("cs", "x1", "xsp", "wq", "tA", "tB", "gw", "x2l", "gp",
               "x2sp", "xr2", "ccb", "wde", "upw", "ub", "uc", "xr", "out")}
        outb0 = sb0("outb0", [128, E], F32)
        open_stacks = []

        def early_out():
            def f(v):
                pr.wait(v, "te")
                pr.wait(v, "se")
                v.memset(outb0[:], 0.0)
            pr.vec(f)

            def fg(g):
                pr.wait(g, "ve")
                for t in range(TT_):
                    ST["out"].dma(g, out_d[t * 128:(t + 1) * 128, :], outb0[:])
                ST["out"].wait(g)
            block.gpsimd(fg)
            for st_ in reversed(open_stacks):
                st_.close()

        def s0(sync):
            for dst, src in ((TRI, tri_in), (ONES1, ones1_in),
                             (ONESF, onesf_in), (CONSTS, consts_in),
                             (IDT, idt_in), (PROT, prot_in)):
                ST["cs"].dma(sync, dst[:], src)
        block.sync(s0)

        ctxR1 = ExitStack()
        open_stacks.append(ctxR1)
        X1 = ctxR1.enter_context(nc.sbuf_tensor("X1", [128, TT_ * E], F32,
                                                side="right"))

        def s0b(sync):
            for t in range(TT_):
                ST["x1"].dma(sync, X1[:, t * E:(t + 1) * E],
                             x_in[t * 128:(t + 1) * 128, :])
        block.sync(s0b)

        def ln_stats(Xbuf, wait_specs):
            def f_se(s):
                for wfn in wait_specs:
                    wfn(s)
                for t in range(TT_):
                    s.activation(sqscr[:], Xbuf[:, t * E:(t + 1) * E],
                                 AF.Square, accum_out=s2c[:, t:t + 1])
            pr.sca(f_se)
            se_sq = pr.cnt["se"]

            def f_ve(v):
                for wfn in wait_specs:
                    wfn(v)
                for t in range(TT_):
                    v.tensor_reduce(s1c[:, t:t + 1], Xbuf[:, t * E:(t + 1) * E],
                                    AX.X, ALU.add)
                pr.wait(v, "se", se_sq)
                v.tensor_single_scalar(meanc[:, 0:TT_], s1c[:, 0:TT_],
                                       1.0 / E, ALU.mult)
                v.tensor_tensor(m2c[:, 0:TT_], meanc[:, 0:TT_],
                                meanc[:, 0:TT_], ALU.mult)
                v.scalar_tensor_tensor(varc[:, 0:TT_], s2c[:, 0:TT_],
                                       1.0 / E, m2c[:, 0:TT_],
                                       ALU.mult, ALU.subtract)
            pr.vec(f_ve)
            ve_var = pr.cnt["ve"]

            def f_se2(s):
                pr.wait(s, "ve", ve_var)
                s.activation(rootc[:, 0:TT_], varc[:, 0:TT_], AF.Sqrt,
                             bias=CONSTS[:, K_LNEPS:K_LNEPS + 1])
            pr.sca(f_se2)
            se_root = pr.cnt["se"]

            def f_ve2(v):
                pr.wait(v, "se", se_root)
                v.reciprocal(rstdc[:, 0:TT_], rootc[:, 0:TT_])
            pr.vec(f_ve2)

        ln_stats(X1, [lambda e: ST["cs"].wait(e), lambda e: ST["x1"].wait(e)])

        xln = ctxR1.enter_context(nc.sbuf_tensor("xln", [128, TT_ * E], BF16,
                                                 side="right"))

        def s1_ve(v):
            for t in range(TT_):
                v.tensor_scalar(xln[:, t * E:(t + 1) * E],
                                X1[:, t * E:(t + 1) * E],
                                meanc[:, t:t + 1], rstdc[:, t:t + 1],
                                ALU.subtract, ALU.mult)
        pr.vec(s1_ve)
        ve_ln1 = pr.cnt["ve"]

        def s1_g(g):
            pr.wait(g, "ve", ve_ln1)
            ST["xsp"].dma(g, x_spill.ap(), X1[:])
        block.gpsimd(s1_g)

        if stop_after < 2:
            early_out()
            return nc, list(ap_in.keys())

        ctxQK = ExitStack()
        open_stacks.append(ctxQK)
        xlnT = ctxQK.enter_context(nc.sbuf_tensor("xlnT", [128, EC * S], BF16))

        tr_jobs = [(t, c) for t in range(TT_) for c in range(EC)]
        ngrp = len(tr_jobs) // 8
        te_b2 = pr.cnt["te"]
        ve_b2 = pr.cnt["ve"]

        def s2_te(t_e):
            pr.wait(t_e, "ve", ve_ln1)
            ST["xsp"].wait(t_e)
            for gi in range(ngrp):
                if gi > 0:
                    t_e.wait_ge(pr.sems["ve"], ve_b2 + 8 * gi)
                for k in range(8):
                    t, c = tr_jobs[gi * 8 + k]
                    ins = t_e.transpose(
                        PT[:, k * 128:(k + 1) * 128],
                        xln[:, t * E + c * 128:t * E + (c + 1) * 128], IDT[:])
                pr.inc(ins, "te")
        block.tensor(s2_te)

        def s2_ve(v):
            for gi in range(ngrp):
                pr.wait(v, "te", te_b2 + gi + 1)
                for k in range(8):
                    t, c = tr_jobs[gi * 8 + k]
                    v.tensor_copy(
                        xlnT[:, c * S + t * 128:c * S + (t + 1) * 128],
                        PT[:, k * 128:(k + 1) * 128])
            assert pr.cnt["ve"] == ve_b2 + 8 * ngrp
        pr.vec(s2_ve)
        ve_xlnT = pr.cnt["ve"]

        ctxR1.close()
        open_stacks.remove(ctxR1)

        if stop_after < 3:
            early_out()
            return nc, list(ap_in.keys())

        wbuf = ctxQK.enter_context(nc.sbuf_tensor("wbuf", [128, EC * E], BF16))
        wbuf2 = ctxQK.enter_context(nc.sbuf_tensor("wbuf2", [128, EC * E], BF16))
        tabc = ctxQK.enter_context(nc.sbuf_tensor("tabc", [128, 2 * S], BF16))
        tabc2 = ctxQK.enter_context(nc.sbuf_tensor("tabc2", [128, 2 * S], BF16))
        q_sb = ctxQK.enter_context(nc.sbuf_tensor("q_sb", [128, 4 * 512], BF16))
        t1scr = ctxQK.enter_context(nc.sbuf_tensor("t1scr", [128, 512], F32))
        t2scr = ctxQK.enter_context(nc.sbuf_tensor("t2scr", [128, 512], F32))

        ctxR2 = ExitStack()
        open_stacks.append(ctxR2)
        qr = ctxR2.enter_context(nc.sbuf_tensor("qr", [128, EC * S], BF16,
                                                side="right"))
        kr = ctxR2.enter_context(nc.sbuf_tensor("kr", [128, EC * S], BF16,
                                                side="right"))

        for wi, (w_ap, cos_ap, sin_ap, dst, wb) in enumerate([
                (wq_in, qcos_in, qsin_in, qr, wbuf),
                (wk_in, ksin_in, ksin_in, kr, wbuf2)]) if False else \
                enumerate([(wq_in, qcos_in, qsin_in, qr, wbuf),
                           (wk_in, kcos_in, ksin_in, kr, wbuf2)]):
            niter = EC * NT
            te_b = pr.cnt["te"]
            ve_b = pr.cnt["ve"]
            tab_cnt = {}

            def f_g(g, w_ap=w_ap, cos_ap=cos_ap, sin_ap=sin_ap, wb=wb,
                    ve_b=ve_b, wi=wi, tab_cnt=tab_cnt):
                ST["wq"].dma(g, wb[:].rearrange("p (c o) -> p c o", o=E),
                             w_ap.rearrange("(c p) o -> p c o", p=128))
                for oc in range(EC):
                    st_t = ST["tA"] if oc % 2 == 0 else ST["tB"]
                    buf = tabc if oc % 2 == 0 else tabc2
                    if oc >= 2:
                        g.wait_ge(pr.sems["ve"], ve_b + 4 * NT * (oc - 1))
                    elif wi == 1 and ve_b > 0:
                        g.wait_ge(pr.sems["ve"], ve_b)
                    st_t.dma(g, buf[:, 0:S],
                             cos_ap[oc * 128:(oc + 1) * 128, :])
                    st_t.dma(g, buf[:, S:2 * S],
                             sin_ap[oc * 128:(oc + 1) * 128, :])
                    tab_cnt[oc] = st_t.cnt
            block.gpsimd(f_g)
            wq_cnt = ST["wq"].cnt

            def f_te(t_e, wb=wb, te_b=te_b, ve_b=ve_b, wq_cnt=wq_cnt):
                pr.wait(t_e, "ve", ve_b)
                ST["wq"].wait(t_e, wq_cnt)
                for it in range(niter):
                    oc, ts_ = divmod(it, NT)
                    bank = P[it % 4]
                    if it >= 4:
                        t_e.wait_ge(pr.sems["ve"], ve_b + 4 * (it - 4) + 1)
                    if it >= 2:
                        t_e.wait_ge(pr.sems["ve"], ve_b + 4 * (it - 2) + 3)
                    for kc in range(EC):
                        ins = t_e.matmul(
                            bank[:],
                            wb[:, kc * E + oc * 128:kc * E + (oc + 1) * 128],
                            xlnT[:, kc * S + ts_ * 512:kc * S + (ts_ + 1) * 512],
                            start=(kc == 0), stop=(kc == EC - 1))
                    pr.inc(ins, "te")
                    t_e.wait_ge(pr.sems["ve"], ve_b + 4 * it + 1)
                    pr.inc(t_e.matmul(P[4 + it % 2][:], PROT[:],
                                      q_sb[:, (it % 4) * 512:(it % 4 + 1) * 512],
                                      start=True, stop=True), "te")
                assert pr.cnt["te"] == te_b + 2 * niter
            block.tensor(f_te)

            def f_ve(v, dst=dst, te_b=te_b, ve_b=ve_b, tab_cnt=tab_cnt):
                for it in range(niter):
                    oc, ts_ = divmod(it, NT)
                    v.wait_ge(pr.sems["te"], te_b + 2 * it + 1)
                    v.tensor_copy(q_sb[:, (it % 4) * 512:(it % 4 + 1) * 512],
                                  P[it % 4][:])
                    v.wait_ge(pr.sems["te"], te_b + 2 * it + 2)
                    (ST["tA"] if oc % 2 == 0 else ST["tB"]).wait(
                        v, tab_cnt[oc])
                    buf = tabc if oc % 2 == 0 else tabc2
                    sl = slice(ts_ * 512, (ts_ + 1) * 512)
                    v.tensor_tensor(t1scr[:],
                                    q_sb[:, (it % 4) * 512:(it % 4 + 1) * 512],
                                    buf[:, sl], ALU.mult)
                    v.tensor_tensor(t2scr[:], P[4 + it % 2][:],
                                    buf[:, S + ts_ * 512:S + (ts_ + 1) * 512],
                                    ALU.mult)
                    v.tensor_tensor(
                        dst[:, oc * S + ts_ * 512:oc * S + (ts_ + 1) * 512],
                        t1scr[:], t2scr[:], ALU.add)
                assert pr.cnt["ve"] == ve_b + 4 * niter
            pr.vec(f_ve)

        te_qk = pr.cnt["te"]
        ve_qk = pr.cnt["ve"]

        if stop_after < 4:
            early_out()
            return nc, list(ap_in.keys())

        vtm = ctxR2.enter_context(nc.sbuf_tensor("vtm", [128, TT_ * E], BF16,
                                                 side="right"))
        g_stage = ctxQK.enter_context(nc.sbuf_tensor("g_stage", [128, 2 * E],
                                                     BF16))

        def s3b_g(g):
            pr.wait(g, "te", te_qk)
            ST["wq"].dma(g, wbuf[:].rearrange("p (c o) -> p c o", o=E),
                         wv_in.rearrange("(c p) o -> p c o", p=128))
            ST["wq"].dma(g, wbuf2[:].rearrange("p (c o) -> p c o", o=E),
                         wg_in.rearrange("(c p) o -> p c o", p=128))
        block.gpsimd(s3b_g)
        wq_vg = ST["wq"].cnt

        te_b3 = pr.cnt["te"]
        ve_b3 = pr.cnt["ve"]
        se_b3 = pr.cnt["se"]

        def s3b_te(t_e):
            ST["wq"].wait(t_e, wq_vg)
            pr.wait(t_e, "ve", ve_qk)
            for t in range(TT_):
                if t > 0:
                    t_e.wait_ge(pr.sems["ve"], ve_b3 + 2 * NO * t)
                for kc in range(EC):
                    for no in range(NO):
                        ins = t_e.matmul(
                            P[no][:],
                            xlnT[:, kc * S + t * 128:kc * S + (t + 1) * 128],
                            wbuf[:, kc * E + no * 512:kc * E + (no + 1) * 512],
                            start=(kc == 0), stop=(kc == EC - 1))
                    for no in range(NO):
                        ins = t_e.matmul(
                            P[2 + no][:],
                            xlnT[:, kc * S + t * 128:kc * S + (t + 1) * 128],
                            wbuf2[:, kc * E + no * 512:
                                  kc * E + (no + 1) * 512],
                            start=(kc == 0), stop=(kc == EC - 1))
                pr.inc(ins, "te")
        block.tensor(s3b_te)

        def s3b_ve(v):
            for t in range(TT_):
                v.wait_ge(pr.sems["te"], te_b3 + t + 1)
                if t >= 2:
                    ST["gw"].wait(v, 16 * (t - 1))
                for no in range(NO):
                    v.tensor_copy(
                        vtm[:, t * E + no * 512:t * E + (no + 1) * 512],
                        P[no][:])
                for no in range(NO):
                    v.tensor_copy(
                        g_stage[:, (t % 2) * E + no * 512:
                                (t % 2) * E + (no + 1) * 512],
                        P[2 + no][:])
            assert pr.cnt["ve"] == ve_b3 + 2 * NO * TT_
        pr.vec(s3b_ve)

        def s3b_gs(g):
            for t in range(TT_):
                g.wait_ge(pr.sems["ve"], ve_b3 + 2 * NO * (t + 1))
                ST["gw"].dma(g, g_spill.ap()[:, t * E:(t + 1) * E],
                             g_stage[:, (t % 2) * E:(t % 2 + 1) * E])
        block.gpsimd(s3b_gs)
        te_vg = pr.cnt["te"]
        ve_vg = pr.cnt["ve"]
        se_vg = pr.cnt["se"]

        ctxQK.close()
        open_stacks.remove(ctxQK)

        if stop_after < 5 or stop_after in (41, 42):
            early_out()
            return nc, list(ap_in.keys())

        ctxS4 = ExitStack()
        open_stacks.append(ctxS4)
        SA = ctxS4.enter_context(nc.sbuf_tensor("SA", [128, 512], BF16))
        SB = ctxS4.enter_context(nc.sbuf_tensor("SB", [128, 512], BF16))
        ABSA = ctxS4.enter_context(nc.sbuf_tensor("ABSA", [128, 512], BF16))
        ABSB = ctxS4.enter_context(nc.sbuf_tensor("ABSB", [128, 512], BF16))
        SA2 = ctxS4.enter_context(nc.sbuf_tensor("SA2", [128, 512], BF16))
        SB2 = ctxS4.enter_context(nc.sbuf_tensor("SB2", [128, 512], BF16))
        ABSA2 = ctxS4.enter_context(nc.sbuf_tensor("ABSA2", [128, 512], BF16))
        ABSB2 = ctxS4.enter_context(nc.sbuf_tensor("ABSB2", [128, 512], BF16))
        avfm = ctxS4.enter_context(nc.sbuf_tensor("avfm", [128, 512], BF16))
        avtm = ctxS4.enter_context(nc.sbuf_tensor("avtm", [128, 512], BF16))
        den_sb = ctxS4.enter_context(nc.sbuf_tensor("den_sb", [128, 512], BF16))
        gsil = ctxS4.enter_context(nc.sbuf_tensor("gsil", [128, 512], BF16))
        tmpp = ctxS4.enter_context(nc.sbuf_tensor("tmpp", [128, 128], BF16))
        g_pair = ctxS4.enter_context(
            nc.sbuf_tensor("g_pair", [128, TT_ * 128], BF16))
        X2 = ctxS4.enter_context(nc.sbuf_tensor("X2", [128, TT_ * E], F32))

        def s4_x2(g):
            pr.wait(g, "te", te_vg)
            pr.wait(g, "ve", ve_vg)
            pr.wait(g, "se", se_vg)
            ST["x2l"].dma(g, X2[:], x_spill.ap())
        block.gpsimd(s4_x2)

        for p in range(NP):
            def s4_gp(g, p=p):
                if p == 0:
                    ST["gw"].wait(g)
                else:
                    pr.wait(g, "se")
                for q4 in range(TT_ // 4):
                    ST["gp"].dma(
                        g,
                        g_pair[:, q4 * 512:(q4 + 1) * 512].rearrange(
                            "q (t c) -> q t c", c=128),
                        bass.AP(g_spill, p * 128 + q4 * 4 * E,
                                [[TT_ * E, 128], [E, 4], [1, 128]]))
            block.gpsimd(s4_gp)
            gp_cnt = ST["gp"].cnt

            for qc in range(NQ):
                nk = 4 * qc + 4
                te_b4 = pr.cnt["te"]
                ve_b4 = pr.cnt["ve"]
                ve_cum = [0] * (nk + 1)
                for kt in range(nk):
                    j = kt - 4 * qc
                    ve_cum[kt + 1] = ve_cum[kt] + (
                        2 if j < 0 else (2 if j == 3 else 4))
                se_b4 = pr.cnt["se"]

                def s_banks(kt):
                    return (P[0], P[1]) if kt % 2 == 0 else (P[5], P6)

                def emit_scores(t_e, kt, p, qc):
                    j = kt - 4 * qc
                    off = max(j, 0) * 128
                    qs0 = qc * 512
                    bA, bB = s_banks(kt)
                    if kt >= 2:
                        t_e.wait_ge(pr.sems["ve"], ve_b4 + ve_cum[kt - 1])
                    t_e.matmul(
                        bA[:, off:512],
                        kr[0:64, p * S + kt * 128:p * S + (kt + 1) * 128],
                        qr[0:64, p * S + qs0 + off:p * S + qs0 + 512],
                        start=True, stop=True)
                    ins = t_e.matmul(
                        bB[:, off:512],
                        kr[64:128, p * S + kt * 128:p * S + (kt + 1) * 128],
                        qr[64:128, p * S + qs0 + off:p * S + qs0 + 512],
                        start=True, stop=True, tile_position=(64, 0))
                    pr.inc(ins, "te")

                def emit_avden(t_e, kt, p, qc, nk):
                    j = kt - 4 * qc
                    off = max(j, 0) * 128
                    t_e.wait_ge(pr.sems["ve"], ve_b4 + ve_cum[kt + 1])
                    t_e.wait_ge(pr.sems["se"], se_b4 + 2 * (kt + 1))
                    first = (kt == 0)
                    last = (kt == nk - 1)
                    sa = SA if kt % 2 == 0 else SA2
                    sb2 = SB if kt % 2 == 0 else SB2
                    aa = ABSA if kt % 2 == 0 else ABSA2
                    ab = ABSB if kt % 2 == 0 else ABSB2
                    t_e.matmul(P[2][0:64, off:512],
                               vtm[:, kt * E + (2 * p) * D:
                                   kt * E + (2 * p) * D + D],
                               sa[:, off:512], start=first, stop=last)
                    t_e.matmul(P[3][64:128, off:512],
                               vtm[:, kt * E + (2 * p + 1) * D:
                                   kt * E + (2 * p + 1) * D + D],
                               sb2[:, off:512], start=first, stop=last,
                               tile_position=(0, 64), skip_group_check=True)
                    t_e.matmul(P[4][0:1, off:512], ONES1[:],
                               aa[:, off:512], start=first, stop=last)
                    ins = t_e.matmul(P[4][64:65, off:512], ONES1[:],
                                     ab[:, off:512], start=first,
                                     stop=last, tile_position=(0, 64),
                                     skip_group_check=True)
                    pr.inc(ins, "te")

                def f_te(t_e, p=p, qc=qc, nk=nk, te_b4=te_b4,
                         ve_b4=ve_b4, ve_cum=ve_cum):
                    pr.wait(t_e, "ve", ve_vg)
                    for kt in range(nk):
                        emit_scores(t_e, kt, p, qc)
                        if kt >= 1:
                            emit_avden(t_e, kt - 1, p, qc, nk)
                    emit_avden(t_e, nk - 1, p, qc, nk)
                block.tensor(f_te)

                def f_ve(v, p=p, qc=qc, nk=nk, te_b4=te_b4):
                    for kt in range(nk):
                        j = kt - 4 * qc
                        v.wait_ge(pr.sems["te"],
                                  te_b4 + (1 if kt == 0 else 2 * kt))
                        bA, bB = s_banks(kt)
                        sa = SA if kt % 2 == 0 else SA2
                        sb2 = SB if kt % 2 == 0 else SB2
                        if j >= 0:
                            v.tensor_tensor(sa[:, j * 128:(j + 1) * 128],
                                            bA[:, j * 128:(j + 1) * 128],
                                            TRI[:], ALU.mult)
                            v.tensor_tensor(sb2[:, j * 128:(j + 1) * 128],
                                            bB[:, j * 128:(j + 1) * 128],
                                            TRI[:], ALU.mult)
                        if j < 3:
                            off2 = (j + 1) * 128 if j >= 0 else 0
                            v.tensor_copy(sa[:, off2:512], bA[:, off2:512])
                            v.tensor_copy(sb2[:, off2:512], bB[:, off2:512])
                pr.vec(f_ve)

                def f_se(sx, p=p, qc=qc, nk=nk, te_b4=te_b4, ve_b4=ve_b4,
                         ve_cum=ve_cum, se_b4=se_b4):
                    for kt in range(nk):
                        j = kt - 4 * qc
                        off = max(j, 0) * 128
                        sx.wait_ge(pr.sems["ve"], ve_b4 + ve_cum[kt + 1])
                        if kt >= 2:
                            sx.wait_ge(pr.sems["te"], te_b4 + 2 * kt - 1)
                        sa = SA if kt % 2 == 0 else SA2
                        sb2 = SB if kt % 2 == 0 else SB2
                        aa = ABSA if kt % 2 == 0 else ABSA2
                        ab = ABSB if kt % 2 == 0 else ABSB2
                        sx.activation(aa[:, off:512], sa[:, off:512], AF.Abs)
                        sx.activation(ab[:, off:512], sb2[:, off:512], AF.Abs)
                pr.sca(f_se)

                te_av = pr.cnt["te"]

                def e_ve1(v, te_av=te_av):
                    pr.wait(v, "te", te_av)
                    v.tensor_copy(avfm[0:64, :], P[2][0:64, :])
                    v.tensor_copy(avfm[64:128, :], P[3][64:128, :])
                    v.tensor_copy(den_sb[0:1, :], P[4][0:1, :])
                    v.tensor_copy(den_sb[64:65, :], P[4][64:65, :])
                pr.vec(e_ve1)
                ve_ep1 = pr.cnt["ve"]

                def e_te(t_e, ve_ep1=ve_ep1):
                    pr.wait(t_e, "ve", ve_ep1)
                    for k in range(4):
                        t_e.transpose(PT[:, k * 128:(k + 1) * 128],
                                      avfm[:, k * 128:(k + 1) * 128], IDT[:])
                    for k in range(4):
                        ins = t_e.transpose(
                            PT[:, 512 + k * 128:512 + (k + 1) * 128],
                            den_sb[:, k * 128:(k + 1) * 128], IDT[:])
                    pr.inc(ins, "te")
                block.tensor(e_te)
                te_ep = pr.cnt["te"]

                def e_ve2(v, p=p, qc=qc, te_ep=te_ep):
                    pr.wait(v, "te", te_ep)
                    v.tensor_copy(avtm[:], PT[:, 0:512])
                    for tau in range(4):
                        tg = qc * 4 + tau
                        v.tensor_copy(
                            den_tm[:, tg * H + 2 * p:tg * H + 2 * p + 2],
                            bass.AP(PT, 512 + tau * 128,
                                    [[1024, 128], [64, 2]]))
                    dv = bass.AP(den_tm, (qc * 4) * H + 2 * p,
                                 [[TT_ * H, 128], [H, 4], [1, 2]])
                    v.tensor_scalar(dv, dv, 1.0, 50000.0, ALU.max, ALU.min)
                    v.tensor_tensor(d2scr[:], dv, dv, ALU.mult)
                pr.vec(e_ve2)
                ve_ep2 = pr.cnt["ve"]

                def e_se1(s, ve_ep2=ve_ep2):
                    pr.wait(s, "ve", ve_ep2)
                    s.activation(sqscr[:, 0:512], avtm[:], AF.Square,
                                 scale=CONSTS[:, K_EIGHTH:K_EIGHTH + 1])
                pr.sca(e_se1)
                se_ep1 = pr.cnt["se"]

                def e_ve3(v, se_ep1=se_ep1):
                    pr.wait(v, "se", se_ep1)
                    v.tensor_reduce(
                        msqscr[:],
                        bass.AP(sqscr, 0, [[E, 128], [64, 8], [1, 64]]),
                        AX.X, ALU.add)
                    v.scalar_tensor_tensor(argscr[:], d2scr[:], 1e-6,
                                           msqscr[:], ALU.mult, ALU.add)
                pr.vec(e_ve3)
                ve_ep3 = pr.cnt["ve"]

                def e_se2(s, p=p, qc=qc, ve_ep3=ve_ep3, gp_cnt=gp_cnt):
                    pr.wait(s, "ve", ve_ep3)
                    s.activation(rootscr[:], argscr[:], AF.Sqrt)
                    ST["gp"].wait(s, gp_cnt)
                    s.activation(gsil[:], g_pair[:, qc * 512:(qc + 1) * 512],
                                 AF.Silu)
                pr.sca(e_se2)
                se_ep2 = pr.cnt["se"]

                def e_ve4(v, p=p, qc=qc, se_ep2=se_ep2):
                    pr.wait(v, "se", se_ep2)
                    ST["x2l"].wait(v)
                    v.reciprocal(sclscr[:], rootscr[:])
                    for tau in range(4):
                        tg = qc * 4 + tau
                        for h in range(2):
                            v.scalar_tensor_tensor(
                                tmpp[:, h * D:(h + 1) * D],
                                avtm[:, tau * 128 + h * D:
                                     tau * 128 + (h + 1) * D],
                                sclscr[:, tau * 2 + h:tau * 2 + h + 1],
                                gsil[:, tau * 128 + h * D:
                                     tau * 128 + (h + 1) * D],
                                ALU.mult, ALU.mult)
                        v.tensor_tensor(
                            X2[:, tg * E + p * 128:tg * E + (p + 1) * 128],
                            X2[:, tg * E + p * 128:tg * E + (p + 1) * 128],
                            tmpp[:], ALU.add)
                pr.vec(e_ve4)

        ve_s4 = pr.cnt["ve"]
        te_s4 = pr.cnt["te"]

        if stop_after < 6:
            early_out()
            return nc, list(ap_in.keys())

        ln_stats(X2, [lambda e, _v=ve_s4: pr.wait(e, "ve", _v)])

        def s5_sp(g):
            pr.wait(g, "ve")
            pr.wait(g, "se")
            ST["x2sp"].dma(g, x2_spill.ap(), X2[:])
        block.gpsimd(s5_sp)
        te_preF = pr.cnt["te"]
        ctxS4.close()
        open_stacks.remove(ctxS4)
        ctxR2.close()
        open_stacks.remove(ctxR2)

        ctxF1 = ExitStack()
        open_stacks.append(ctxF1)
        xln2 = ctxF1.enter_context(nc.sbuf_tensor("xln2", [128, TT_ * E],
                                                  BF16))
        xres2 = ctxF1.enter_context(nc.sbuf_tensor("xres2", [128, 2 * E], F32))

        ve_b5 = pr.cnt["ve"]

        def s5_g2(g):
            pr.wait(g, "te", te_preF)
            ST["x2sp"].wait(g)
            for t in range(TT_):
                if t >= 2:
                    g.wait_ge(pr.sems["ve"], ve_b5 + (t - 1))
                ST["xr2"].dma(g, xres2[:, (t % 2) * E:(t % 2 + 1) * E],
                              x2_spill.ap()[:, t * E:(t + 1) * E])
        block.gpsimd(s5_g2)

        def s5_ve(v):
            pr.wait(v, "te", te_preF)
            for t in range(TT_):
                ST["xr2"].wait(v, 16 * (t + 1))
                v.tensor_scalar(xln2[:, t * E:(t + 1) * E],
                                xres2[:, (t % 2) * E:(t % 2 + 1) * E],
                                meanc[:, t:t + 1], rstdc[:, t:t + 1],
                                ALU.subtract, ALU.mult)
            v.tensor_reduce(redcols[:, 0:1], xln2[:], AX.X, ALU.max,
                            apply_absolute_value=True)
            v.tensor_single_scalar(redb[:, 0:1], redcols[:, 0:1], 1.0,
                                   ALU.mult)
        pr.vec(s5_ve)
        ve_ln2 = pr.cnt["ve"]

        def part_max_to_sm(col_redb, sm_col, wait_ve):
            def f_te(t_e):
                pr.wait(t_e, "ve", wait_ve)
                pr.inc(t_e.transpose(PT[0:1, 0:128],
                                     redb[:, col_redb:col_redb + 1], IDT[:]),
                       "te")
            block.tensor(f_te)
            te_t = pr.cnt["te"]

            def f_ve(v):
                pr.wait(v, "te", te_t)
                v.tensor_reduce(sm[:, sm_col:sm_col + 1], PT[0:1, 0:128],
                                AX.X, ALU.max)
            pr.vec(f_ve)
            return pr.cnt["ve"]

        ve_mx = part_max_to_sm(0, 0, ve_ln2)

        def collective_round(k, src_cols, wait_ve):
            n = len(src_cols)

            def f_g(g):
                pr.wait(g, "ve", wait_ve)
                ap_src = sm[0:1, src_cols[0]:src_cols[0] + n]
                ST["ccb"].dma(g, cc_i[k].ap(), ap_src)
                ST["ccb"].wait(g)
                pr.inc(g.collective_compute(
                    "AllGather", ALU.bypass,
                    replica_groups=[list(range(NCORES))],
                    ins=[cc_i[k].ap().opt()], outs=[cc_o[k].ap().opt()]),
                    "cc", 1)
                pr.wait(g, "cc")
                ST["ccb"].dma(g, ag_sb[:, 0:n * NCORES], cc_o[k].ap())
            block.gpsimd(f_g)
            cc_done = ST["ccb"].cnt

            def f_ve(v):
                ST["ccb"].wait(v, cc_done)
                for i in range(n):
                    v.tensor_reduce(
                        sm[:, src_cols[i]:src_cols[i] + 1],
                        bass.AP(ag_sb, i, [[2 * NCORES, 1], [n, NCORES]]),
                        AX.X, ALU.max)
            pr.vec(f_ve)
            return pr.cnt["ve"]

        ve_r1 = collective_round(0, [0], ve_mx)

        def r1_ve(v):
            pr.wait(v, "ve", ve_r1)
            v.tensor_scalar(sm[:, 1:2], sm[:, 0:1], 1.0 / 32767.0, 1e-12,
                            ALU.mult, ALU.max)
            v.reciprocal(sm[:, 2:3], sm[:, 1:2])
        pr.vec(r1_ve)
        ve_r1b = pr.cnt["ve"]

        def bcast(cols_src, cols_dst, wait_ve):
            n = len(cols_src)
            assert cols_src == list(range(cols_src[0], cols_src[0] + n))

            def f_te(t_e):
                pr.wait(t_e, "ve", wait_ve)
                pr.inc(t_e.matmul(P6[:, 0:n], ONESF[:],
                                  sm[0:1, cols_src[0]:cols_src[0] + n],
                                  start=True, stop=True), "te")
            block.tensor(f_te)
            te_t = pr.cnt["te"]

            def f_ve(v):
                pr.wait(v, "te", te_t)
                for i, cd in enumerate(cols_dst):
                    v.tensor_copy(bc[:, cd:cd + 1], P6[:, i:i + 1])
            pr.vec(f_ve)
            return pr.cnt["ve"]

        ve_bc1 = bcast([1, 2], [C_SX, C_RSX], ve_r1b)

        if stop_after < 7:
            early_out()
            return nc, list(ap_in.keys())

        y6 = ctxF1.enter_context(nc.sbuf_tensor("y6", [128, 2 * E], F32))
        Xq = ctxF1.enter_context(nc.sbuf_tensor("Xq", [128, TT_ * E], BF16))

        ctxR3 = ExitStack()
        open_stacks.append(ctxR3)
        XqT = ctxR3.enter_context(nc.sbuf_tensor("XqT", [128, EC * S], BF16,
                                                 side="right"))

        se_b6 = pr.cnt["se"]
        ve_b6 = pr.cnt["ve"]

        def s6_se(s):
            pr.wait(s, "ve", ve_bc1)
            for t in range(TT_):
                if t >= 2:
                    s.wait_ge(pr.sems["ve"], ve_b6 + (t - 1))
                s.activation(y6[:, (t % 2) * E:(t % 2 + 1) * E],
                             xln2[:, t * E:(t + 1) * E], AF.Copy,
                             scale=bc[:, C_RSX:C_RSX + 1])
        pr.sca(s6_se)

        def s6_ve(v):
            for t in range(TT_):
                v.wait_ge(pr.sems["se"], se_b6 + t + 1)
                v.tensor_scalar(Xq[:, t * E:(t + 1) * E],
                                y6[:, (t % 2) * E:(t % 2 + 1) * E],
                                MAGIC, MAGIC, ALU.add, ALU.subtract)
            assert pr.cnt["ve"] == ve_b6 + TT_
        pr.vec(s6_ve)
        ve_xq = pr.cnt["ve"]

        te_b6 = pr.cnt["te"]
        ve_b6b = pr.cnt["ve"]

        def s6_te(t_e):
            pr.wait(t_e, "ve", ve_xq)
            for gi in range(ngrp):
                if gi > 0:
                    t_e.wait_ge(pr.sems["ve"], ve_b6b + 8 * gi)
                for k in range(8):
                    t, c = tr_jobs[gi * 8 + k]
                    ins = t_e.transpose(
                        PT[:, k * 128:(k + 1) * 128],
                        Xq[:, t * E + c * 128:t * E + (c + 1) * 128], IDT[:])
                pr.inc(ins, "te")
        block.tensor(s6_te)

        def s6_ve2(v):
            for gi in range(ngrp):
                v.wait_ge(pr.sems["te"], te_b6 + gi + 1)
                for k in range(8):
                    t, c = tr_jobs[gi * 8 + k]
                    v.tensor_copy(
                        XqT[:, c * S + t * 128:c * S + (t + 1) * 128],
                        PT[:, k * 128:(k + 1) * 128])
        pr.vec(s6_ve2)
        ve_xqT = pr.cnt["ve"]

        ctxF1.close()
        open_stacks.remove(ctxF1)
        ctxF2 = ExitStack()
        open_stacks.append(ctxF2)

        if stop_after < 8:
            early_out()
            return nc, list(ap_in.keys())

        G = ctxF2.enter_context(nc.sbuf_tensor("G", [128, IC * S], BF16))
        ctxPB = ExitStack()
        open_stacks.append(ctxPB)
        yscr = ctxPB.enter_context(nc.sbuf_tensor("yscr", [128, 2 * S], F32))
        sigscr = ctxPB.enter_context(nc.sbuf_tensor("sigscr", [128, S], BF16))
        upc = ctxPB.enter_context(nc.sbuf_tensor("upc", [128, 2 * S], BF16))
        uqscr = ctxPB.enter_context(nc.sbuf_tensor("uqscr", [128, S], BF16))
        t2b = ctxPB.enter_context(nc.sbuf_tensor("t2b", [128, S], BF16))
        ctxUG = ExitStack()
        open_stacks.append(ctxUG)
        wde = ctxUG.enter_context(nc.sbuf_tensor("wde", [128, 2 * E], F32))
        tqscr = ctxUG.enter_context(nc.sbuf_tensor("tqscr", [128, E], F32))
        qwscr = ctxUG.enter_context(nc.sbuf_tensor("qwscr", [128, E], BF16))
        bmscr = ctxUG.enter_context(nc.sbuf_tensor("bmscr", [128, NBU], F32))
        sWscr = ctxUG.enter_context(nc.sbuf_tensor("sWscr", [128, NBU], F32))
        rsWscr = ctxUG.enter_context(nc.sbuf_tensor("rsWscr", [128, NBU], F32))
        wqT = ctxUG.enter_context(nc.sbuf_tensor("wqT", [128, 2 * E], BF16))
        upst = ctxUG.enter_context(nc.sbuf_tensor("upst", [128, S], BF16))

        def dequant_chunk(bufs, nb, width, ld_val, slot):
            wd, tq, qw, bm, sW, rsW, wrow, nbmax = bufs

            def f_ve(v):
                ST["wde"].wait(v, ld_val)
                v.tensor_reduce(
                    bm[:, 0:nb],
                    bass.AP(wd, slot * width, [[wrow, 128], [32, nb], [1, 32]]),
                    AX.X, ALU.max, apply_absolute_value=True)
                v.tensor_scalar(sW[:, 0:nb], bm[:, 0:nb], 1.0 / 7.0,
                                1e-12, ALU.mult, ALU.max)
                v.reciprocal(rsW[:, 0:nb], sW[:, 0:nb])
                v.tensor_tensor(
                    tq[:, 0:width],
                    bass.AP(wd, slot * width, [[wrow, 128], [32, nb], [1, 32]]),
                    bass.AP(rsW, 0, [[nbmax, 128], [1, nb], [0, 32]]),
                    ALU.mult)
                v.tensor_scalar(tq[:, 0:width], tq[:, 0:width],
                                MAGIC, MAGIC, ALU.add, ALU.subtract)
                v.tensor_tensor(
                    qw[:, 0:width],
                    bass.AP(tq, 0, [[width, 128], [32, nb], [1, 32]]),
                    bass.AP(sW, 0, [[nbmax, 128], [1, nb], [0, 32]]),
                    ALU.mult)
            pr.vec(f_ve)
            return pr.cnt["ve"]

        ug_bufs = (wde, tqscr, qwscr, bmscr, sWscr, rsWscr, 2 * E, NBU)

        dq_hist = {}
        wqT_hist = {}
        for mi, (w_src, is_up) in enumerate([(wup_in, True),
                                             (wgate_in, False)]):
            for c in range(IC):
                ci = mi * IC + c

                def f_ld(sync, w_src=w_src, c=c, mi=mi, ci=ci):
                    if mi == 0 and c == 0:
                        ST["x2sp"].wait(sync)
                        pr.wait(sync, "ve")
                        pr.wait(sync, "se")
                        pr.wait(sync, "te")
                    if ci >= 2:
                        pr.wait(sync, "ve", dq_hist[ci - 2])
                    ST["wde"].dma(sync,
                                  wde[:, (c % 2) * E:(c % 2 + 1) * E],
                                  w_src[c * 128:(c + 1) * 128, :])
                block.sync(f_ld)
                ld_w = ST["wde"].cnt

                ve_dq = dequant_chunk(ug_bufs, NBU, E, ld_w, c % 2)
                dq_hist[ci] = ve_dq

                def f_tr(t_e, ve_dq=ve_dq, c=c, mi=mi, ci=ci):
                    pr.wait(t_e, "ve", ve_dq)
                    if ci >= 1:
                        pr.wait(t_e, "ve", wqT_hist[ci - 1])
                    for k in range(8):
                        ins = t_e.transpose(PT[:, k * 128:(k + 1) * 128],
                                            qwscr[:, k * 128:(k + 1) * 128],
                                            IDT[:])
                    pr.inc(ins, "te")
                block.tensor(f_tr)
                te_tr = pr.cnt["te"]

                def f_ev(v, te_tr=te_tr, c=c):
                    pr.wait(v, "te", te_tr)
                    v.tensor_copy(wqT[:, (c % 2) * E:(c % 2 + 1) * E], PT[:])
                pr.vec(f_ev)
                ve_wqT = pr.cnt["ve"]
                wqT_hist[ci] = ve_wqT

                def f_mm(t_e, ve_wqT=ve_wqT, c=c, mi=mi):
                    pr.wait(t_e, "ve", ve_wqT)
                    if mi == 0 and c == 0:
                        pr.wait(t_e, "ve", ve_xqT)
                    for kc in range(EC):
                        for ts_ in range(NT):
                            ins = t_e.matmul(
                                P[ts_][:],
                                wqT[:, (c % 2) * E + kc * 128:
                                    (c % 2) * E + (kc + 1) * 128],
                                XqT[:, kc * S + ts_ * 512:
                                    kc * S + (ts_ + 1) * 512],
                                start=(kc == 0), stop=(kc == EC - 1))
                    pr.inc(ins, "te")
                block.tensor(f_mm)
                te_mm = pr.cnt["te"]

                def f_evac(v, te_mm=te_mm, c=c, is_up=is_up):
                    pr.wait(v, "te", te_mm)
                    if is_up:
                        ST["upw"].wait(v)
                    for ts_ in range(NT):
                        if is_up:
                            o = upst[:, ts_ * 512:(ts_ + 1) * 512]
                        else:
                            o = G[:, c * S + ts_ * 512:c * S + (ts_ + 1) * 512]
                        v.tensor_scalar(o, P[ts_][:], bc[:, C_SX:C_SX + 1],
                                        None, ALU.mult)
                    src = upst[:] if is_up else G[:, c * S:(c + 1) * S]
                    v.tensor_reduce(redcols[:, mi * IC + c:mi * IC + c + 1],
                                    src, AX.X, ALU.max,
                                    apply_absolute_value=True)
                pr.vec(f_evac)
                ve_ev = pr.cnt["ve"]

                if is_up:
                    def f_sp(g, c=c, ve_ev=ve_ev):
                        pr.wait(g, "ve", ve_ev)
                        ST["upw"].dma(g, up_spill.ap()[:, c * S:(c + 1) * S],
                                      upst[:])
                    block.gpsimd(f_sp)

        ve_s7 = pr.cnt["ve"]

        def s7_red(v):
            pr.wait(v, "ve", ve_s7)
            v.tensor_reduce(redsm[:, 0:1], redcols[:, 0:IC], AX.X, ALU.max)
            v.tensor_reduce(redsm[:, 1:2], redcols[:, IC:2 * IC], AX.X,
                            ALU.max)
            v.tensor_single_scalar(redb[:, 0:2], redsm[:, 0:2], 1.0, ALU.mult)
        pr.vec(s7_red)
        ve_r2a = pr.cnt["ve"]

        def s7_tr(t_e):
            pr.wait(t_e, "ve", ve_r2a)
            t_e.transpose(PT[0:1, 0:128], redb[:, 0:1], IDT[:])
            pr.inc(t_e.transpose(PT[0:1, 128:256], redb[:, 1:2], IDT[:]), "te")
        block.tensor(s7_tr)
        te_r2 = pr.cnt["te"]

        def s7_fin(v):
            pr.wait(v, "te", te_r2)
            v.tensor_reduce(sm[:, 4:5], PT[0:1, 0:128], AX.X, ALU.max)
            v.tensor_reduce(sm[:, 5:6], PT[0:1, 128:256], AX.X, ALU.max)
        pr.vec(s7_fin)
        ve_r2b = pr.cnt["ve"]

        ve_r2 = collective_round(1, [4, 5], ve_r2b)

        def r2_ve(v):
            pr.wait(v, "ve", ve_r2)
            v.tensor_scalar(sm[:, 6:7], sm[:, 4:5], 1.0 / 32767.0, 1e-12,
                            ALU.mult, ALU.max)
            v.tensor_scalar(sm[:, 7:8], sm[:, 5:6], 1.0 / 32767.0, 1e-12,
                            ALU.mult, ALU.max)
            v.reciprocal(sm[:, 8:9], sm[:, 6:7])
            v.reciprocal(sm[:, 9:10], sm[:, 7:8])
        pr.vec(r2_ve)
        ve_r2c = pr.cnt["ve"]
        ve_bc2 = bcast([7, 8, 9], [C_SG, C_RSUP, C_RSG], ve_r2c)

        if stop_after < 9:
            early_out()
            return nc, list(ap_in.keys())

        ctxUG.close()
        open_stacks.remove(ctxUG)
        ctxR3.close()
        open_stacks.remove(ctxR3)

        se_b8 = pr.cnt["se"]
        ve_b8 = pr.cnt["ve"]
        for c in range(IC):
            def a_se1(s, c=c):
                if c == 0:
                    pr.wait(s, "ve", ve_bc2)
                elif c >= 2:
                    s.wait_ge(pr.sems["ve"], ve_b8 + c - 1)
                s.activation(yscr[:, (c % 2) * S:(c % 2 + 1) * S],
                             G[:, c * S:(c + 1) * S],
                             AF.Copy, scale=bc[:, C_RSG:C_RSG + 1])
            pr.sca(a_se1)

            def a_ve1(v, c=c):
                v.wait_ge(pr.sems["se"], se_b8 + c + 1)
                v.tensor_scalar(G[:, c * S:(c + 1) * S],
                                yscr[:, (c % 2) * S:(c % 2 + 1) * S],
                                MAGIC, MAGIC, ALU.add, ALU.subtract)
            pr.vec(a_ve1)
        se_m8 = pr.cnt["se"]
        ve_m8 = pr.cnt["ve"]
        for c in range(IC):
            def a_se2(s, c=c):
                if c < 2:
                    s.wait_ge(pr.sems["ve"], ve_m8)
                else:
                    s.wait_ge(pr.sems["ve"], ve_m8 + 4 * (c - 1))
                s.activation(yscr[:, (c % 2) * S:(c % 2 + 1) * S],
                             G[:, c * S:(c + 1) * S],
                             AF.Sigmoid, scale=bc[:, C_SG:C_SG + 1])
            pr.sca(a_se2)

            def a_ve2(v, c=c):
                v.wait_ge(pr.sems["se"], se_m8 + c + 1)
                yb = yscr[:, (c % 2) * S:(c % 2 + 1) * S]
                v.tensor_scalar(yb, yb, 65536.0, MAGIC,
                                ALU.mult, ALU.add)
                v.tensor_scalar(sigscr[:], yb, MAGIC, 65535.0,
                                ALU.subtract, ALU.min)
                v.tensor_tensor(G[:, c * S:(c + 1) * S],
                                G[:, c * S:(c + 1) * S], sigscr[:], ALU.mult)
                v.tensor_reduce(redcols[:, 32 + c:33 + c],
                                G[:, c * S:(c + 1) * S], AX.X, ALU.max,
                                apply_absolute_value=True)
            pr.vec(a_ve2)
        ve_s8a = pr.cnt["ve"]

        def s8_red(v):
            pr.wait(v, "ve", ve_s8a)
            v.tensor_reduce(redsm[:, 2:3], redcols[:, 32:32 + IC], AX.X,
                            ALU.max)
            v.tensor_single_scalar(redb[:, 2:3], redsm[:, 2:3], 1.0, ALU.mult)
        pr.vec(s8_red)
        ve_r3a = pr.cnt["ve"]

        def s8_tr(t_e):
            pr.wait(t_e, "ve", ve_r3a)
            pr.inc(t_e.transpose(PT[0:1, 0:128], redb[:, 2:3], IDT[:]), "te")
        block.tensor(s8_tr)
        te_r3 = pr.cnt["te"]

        def s8_fin(v):
            pr.wait(v, "te", te_r3)
            v.tensor_reduce(sm[:, 10:11], PT[0:1, 0:128], AX.X, ALU.max)
        pr.vec(s8_fin)
        ve_r3b = pr.cnt["ve"]

        ve_r3 = collective_round(2, [10], ve_r3b)

        def r3_ve(v):
            pr.wait(v, "ve", ve_r3)
            v.tensor_tensor(sm[:, 11:12], sm[:, 10:11], sm[:, 7:8], ALU.mult)
            v.tensor_scalar(sm[:, 11:12], sm[:, 11:12],
                            1.0 / (65536.0 * 32767.0), 1e-12,
                            ALU.mult, ALU.max)
            v.reciprocal(sm[:, 12:13], sm[:, 11:12])
            v.tensor_tensor(sm[:, 12:13], sm[:, 12:13], sm[:, 7:8], ALU.mult)
            v.tensor_single_scalar(sm[:, 12:13], sm[:, 12:13],
                                   1.0 / 65536.0, ALU.mult)
        pr.vec(r3_ve)
        ve_r3c = pr.cnt["ve"]
        ve_bc3 = bcast([12], [C_RST], ve_r3c)

        se_b8b = pr.cnt["se"]
        ve_b8b = pr.cnt["ve"]
        for c in range(IC):
            def b_se(s, c=c):
                if c == 0:
                    pr.wait(s, "ve", ve_bc3)
                elif c >= 2:
                    s.wait_ge(pr.sems["ve"], ve_b8b + c - 1)
                s.activation(yscr[:, (c % 2) * S:(c % 2 + 1) * S],
                             G[:, c * S:(c + 1) * S],
                             AF.Copy, scale=bc[:, C_RST:C_RST + 1])
            pr.sca(b_se)

            def b_ve(v, c=c):
                v.wait_ge(pr.sems["se"], se_b8b + c + 1)
                v.tensor_scalar(G[:, c * S:(c + 1) * S],
                                yscr[:, (c % 2) * S:(c % 2 + 1) * S],
                                MAGIC, MAGIC, ALU.add, ALU.subtract)
            pr.vec(b_ve)
        ve_g2 = pr.cnt["ve"]

        if stop_after < 10:
            early_out()
            return nc, list(ap_in.keys())

        se_b8c = pr.cnt["se"]
        ve_b8c = pr.cnt["ve"]

        for c in range(IC):
            def pb_g(g, c=c):
                if c == 0:
                    pr.wait(g, "ve", ve_g2)
                    ST["upw"].wait(g)
                if c >= 2:
                    g.wait_ge(pr.sems["se"], se_b8c + (c - 1))
                ST["ub"].dma(g, upc[:, (c % 2) * S:(c % 2 + 1) * S],
                             up_spill.ap()[:, c * S:(c + 1) * S])
            block.gpsimd(pb_g)

            def pb_se(s, c=c):
                ST["ub"].wait(s, 16 * (c + 1))
                if c >= 2:
                    s.wait_ge(pr.sems["ve"], ve_b8c + 3 * (c - 2) + 1)
                s.activation(yscr[:, (c % 2) * S:(c % 2 + 1) * S],
                             upc[:, (c % 2) * S:(c % 2 + 1) * S],
                             AF.Copy, scale=bc[:, C_RSUP:C_RSUP + 1])
            pr.sca(pb_se)

            def pb_ve(v, c=c):
                v.wait_ge(pr.sems["se"], se_b8c + c + 1)
                v.tensor_scalar(uqscr[:],
                                yscr[:, (c % 2) * S:(c % 2 + 1) * S],
                                MAGIC, MAGIC, ALU.add, ALU.subtract)
                v.tensor_tensor(t2b[:], G[:, c * S:(c + 1) * S], uqscr[:],
                                ALU.mult)
                v.tensor_reduce(redcols[:, 32 + c:33 + c], t2b[:], AX.X,
                                ALU.max, apply_absolute_value=True)
            pr.vec(pb_ve)
        ve_s8b = pr.cnt["ve"]

        def s8_red2(v):
            pr.wait(v, "ve", ve_s8b)
            v.tensor_reduce(redsm[:, 3:4], redcols[:, 32:32 + IC], AX.X,
                            ALU.max)
            v.tensor_single_scalar(redb[:, 3:4], redsm[:, 3:4], 1.0, ALU.mult)
        pr.vec(s8_red2)
        ve_r4a = pr.cnt["ve"]

        def s8_tr2(t_e):
            pr.wait(t_e, "ve", ve_r4a)
            pr.inc(t_e.transpose(PT[0:1, 0:128], redb[:, 3:4], IDT[:]), "te")
        block.tensor(s8_tr2)
        te_r4 = pr.cnt["te"]

        def s8_fin2(v):
            pr.wait(v, "te", te_r4)
            v.tensor_reduce(sm[:, 13:14], PT[0:1, 0:128], AX.X, ALU.max)
        pr.vec(s8_fin2)
        ve_r4b = pr.cnt["ve"]

        ve_r4 = collective_round(3, [13], ve_r4b)

        def r4_ve(v):
            pr.wait(v, "ve", ve_r4)
            v.tensor_tensor(sm[:, 14:15], sm[:, 13:14], sm[:, 11:12],
                            ALU.mult)
            v.tensor_tensor(sm[:, 14:15], sm[:, 14:15], sm[:, 6:7], ALU.mult)
            v.tensor_scalar(sm[:, 14:15], sm[:, 14:15], 1.0 / 32767.0,
                            1e-12, ALU.mult, ALU.max)
            v.reciprocal(sm[:, 15:16], sm[:, 14:15])
            v.tensor_tensor(sm[:, 15:16], sm[:, 15:16], sm[:, 11:12],
                            ALU.mult)
            v.tensor_tensor(sm[:, 15:16], sm[:, 15:16], sm[:, 6:7], ALU.mult)
        pr.vec(r4_ve)
        ve_r4c = pr.cnt["ve"]
        ve_bc4 = bcast([14, 15], [C_SO, C_RSO], ve_r4c)

        if stop_after < 11:
            early_out()
            return nc, list(ap_in.keys())

        se_b8d = pr.cnt["se"]
        ve_b8d = pr.cnt["ve"]
        for c in range(IC):
            def pc_g(g, c=c):
                if c == 0:
                    pr.wait(g, "ve", ve_bc4)
                if c >= 2:
                    g.wait_ge(pr.sems["se"], se_b8d + 2 * (c - 1) + 1)
                ST["uc"].dma(g, upc[:, (c % 2) * S:(c % 2 + 1) * S],
                             up_spill.ap()[:, c * S:(c + 1) * S])
            block.gpsimd(pc_g)

            def pc_se1(s, c=c):
                ST["uc"].wait(s, 16 * (c + 1))
                if c > 0:
                    s.wait_ge(pr.sems["ve"], ve_b8d + 3 * c)
                s.activation(yscr[:, 0:S], upc[:, (c % 2) * S:(c % 2 + 1) * S],
                             AF.Copy, scale=bc[:, C_RSUP:C_RSUP + 1])
            pr.sca(pc_se1)

            def pc_ve1(v, c=c):
                v.wait_ge(pr.sems["se"], se_b8d + 2 * c + 1)
                v.tensor_scalar(uqscr[:], yscr[:, 0:S], MAGIC, MAGIC,
                                ALU.add, ALU.subtract)
                v.tensor_tensor(t2b[:], G[:, c * S:(c + 1) * S], uqscr[:],
                                ALU.mult)
            pr.vec(pc_ve1)

            def pc_se2(s, c=c):
                s.wait_ge(pr.sems["ve"], ve_b8d + 3 * c + 2)
                s.activation(yscr[:, S:2 * S], t2b[:], AF.Copy,
                             scale=bc[:, C_RSO:C_RSO + 1])
            pr.sca(pc_se2)

            def pc_ve2(v, c=c):
                v.wait_ge(pr.sems["se"], se_b8d + 2 * c + 2)
                v.tensor_scalar(G[:, c * S:(c + 1) * S], yscr[:, S:2 * S],
                                MAGIC, MAGIC, ALU.add, ALU.subtract)
            pr.vec(pc_ve2)
        ve_oint = pr.cnt["ve"]

        if stop_after < 12:
            early_out()
            return nc, list(ap_in.keys())

        ctxPB.close()
        open_stacks.remove(ctxPB)
        wdeD = ctxF2.enter_context(nc.sbuf_tensor("wdeD", [128, I], F32))
        tqD = ctxF2.enter_context(nc.sbuf_tensor("tqD", [128, I], F32))
        qwD = ctxF2.enter_context(nc.sbuf_tensor("qwD", [128, I], BF16))
        bmD = ctxF2.enter_context(nc.sbuf_tensor("bmD", [128, NBD], F32))
        sWD = ctxF2.enter_context(nc.sbuf_tensor("sWD", [128, NBD], F32))
        rsWD = ctxF2.enter_context(nc.sbuf_tensor("rsWD", [128, NBD], F32))
        wqdT = ctxF2.enter_context(nc.sbuf_tensor("wqdT", [128, IC * E],
                                                  BF16))
        d_bufs = (wdeD, tqD, qwD, bmD, sWD, rsWD, I, NBD)

        n_dchunks = 1 if stop_after == 121 else (2 if stop_after == 123 else EC)
        skip_dtr = (stop_after == 122)
        for c in range(n_dchunks):
            def d_ld(sync, c=c):
                pr.wait(sync, "ve", ve_oint) if c < 1 else pr.wait(sync, "ve")
                ST["wde"].dma(sync, wdeD[:],
                              wdown_in[c * 128:(c + 1) * 128, :])
            block.sync(d_ld)
            ld_w = ST["wde"].cnt

            ve_dq = dequant_chunk(d_bufs, NBD, I, ld_w, 0)

            for tg in range(0 if skip_dtr else (IC // 8)):
                def d_tr(t_e, ve_dq=ve_dq, c=c, tg=tg):
                    pr.wait(t_e, "ve", ve_dq)
                    if not (c == 0 and tg == 0):
                        pr.wait(t_e, "ve")
                    for k in range(8):
                        jc = tg * 8 + k
                        ins = t_e.transpose(PT[:, k * 128:(k + 1) * 128],
                                            qwD[:, jc * 128:(jc + 1) * 128],
                                            IDT[:])
                    pr.inc(ins, "te")
                block.tensor(d_tr)
                te_tr = pr.cnt["te"]

                def d_ev(v, te_tr=te_tr, c=c, tg=tg):
                    pr.wait(v, "te", te_tr)
                    for k in range(8):
                        jc = tg * 8 + k
                        v.tensor_copy(
                            wqdT[:, jc * E + c * 128:jc * E + (c + 1) * 128],
                            PT[:, k * 128:(k + 1) * 128])
                pr.vec(d_ev)
        ve_wqd = pr.cnt["ve"]

        if stop_after < 13 or stop_after in (121, 122, 123):
            early_out()
            return nc, list(ap_in.keys())

        xres = ctxF2.enter_context(nc.sbuf_tensor("xres", [128, 2 * E], F32))
        xout = ctxF2.enter_context(nc.sbuf_tensor("xout", [128, 2 * E], F32))

        te_b9 = pr.cnt["te"]
        ve_b9 = pr.cnt["ve"]

        only_g = (stop_after == 126)
        only_gte = (stop_after == 127)

        def s9_g(g):
            pr.wait(g, "ve", ve_wqd)
            ST["x2sp"].wait(g)
            for t in range(TT_):
                if t >= 2 and not (only_g or only_gte):
                    g.wait_ge(pr.sems["ve"], ve_b9 + 2 * NO * (t - 1))
                ST["xr"].dma(g, xres[:, (t % 2) * E:(t % 2 + 1) * E],
                             x2_spill.ap()[:, t * E:(t + 1) * E])
        block.gpsimd(s9_g)

        if only_g:
            early_out()
            return nc, list(ap_in.keys())

        jc_rng = (range(0, 14) if stop_after == 124 else
                  (range(14, IC) if stop_after == 125 else range(IC)))

        def s9_te(t_e):
            pr.wait(t_e, "ve", ve_wqd)
            for t in range(TT_):
                if t >= 2 and not only_gte:
                    t_e.wait_ge(pr.sems["ve"], ve_b9 + 2 * NO * (t - 1))
                for jc in jc_rng:
                    for no in range(NO):
                        ins = t_e.matmul(
                            P[(t % 2) * 2 + no][:],
                            G[:, jc * S + t * 128:jc * S + (t + 1) * 128],
                            wqdT[:, jc * E + no * 512:jc * E + (no + 1) * 512],
                            start=(jc == jc_rng[0] if hasattr(jc_rng, "__getitem__") else jc == 0),
                            stop=(jc == list(jc_rng)[-1]))
                pr.inc(ins, "te")
        block.tensor(s9_te)

        if only_gte:
            early_out()
            return nc, list(ap_in.keys())

        def s9_ve(v):
            for t in range(TT_):
                v.wait_ge(pr.sems["te"], te_b9 + t + 1)
                ST["xr"].wait(v, 16 * (t + 1))
                if t >= 2:
                    ST["out"].wait(v, 16 * (t - 1))
                for no in range(NO):
                    v.tensor_scalar(sqscr[:, no * 512:(no + 1) * 512],
                                    P[(t % 2) * 2 + no][:],
                                    bc[:, C_SO:C_SO + 1], None, ALU.mult)
                for no in range(NO):
                    v.tensor_tensor(
                        xout[:, (t % 2) * E + no * 512:
                             (t % 2) * E + (no + 1) * 512],
                        sqscr[:, no * 512:(no + 1) * 512],
                        xres[:, (t % 2) * E + no * 512:
                             (t % 2) * E + (no + 1) * 512],
                        ALU.add)
        pr.vec(s9_ve)

        if stop_after == 128:
            early_out()
            return nc, list(ap_in.keys())

        def s9_st(g):
            for t in range(TT_):
                g.wait_ge(pr.sems["ve"], ve_b9 + 2 * NO * (t + 1))
                ST["out"].dma(g, out_d[t * 128:(t + 1) * 128, :],
                              xout[:, (t % 2) * E:(t % 2 + 1) * E])
            ST["out"].wait(g)
        block.sync(s9_st)

        ctxF2.close()
        open_stacks.remove(ctxF2)

    return nc, list(ap_in.keys())


_CACHE = {}


def _get_prog(S, E, H, I):
    key = (S, E, H, I)
    if key not in _CACHE:
        _CACHE[key] = build_program(S, E, H, I)
    return _CACHE[key]


def kernel(L0_attn_x, L0_attn_Wq, L0_attn_Wk, L0_attn_Wv, L0_attn_Wg,
           L0_ffn_w_gate, L0_ffn_w_up, L0_ffn_w_down):
    x = np.ascontiguousarray(np.asarray(L0_attn_x, np.float32))
    B, S, E = x.shape
    H = 16
    I = L0_ffn_w_up.shape[0]
    consts = make_consts(S, E, H)
    nc, _ = _get_prog(S, E, H, I)
    common = dict(
        wq=np.ascontiguousarray(np.asarray(L0_attn_Wq, np.float32)),
        wk=np.ascontiguousarray(np.asarray(L0_attn_Wk, np.float32)),
        wv=np.ascontiguousarray(np.asarray(L0_attn_Wv, np.float32)),
        wg=np.ascontiguousarray(np.asarray(L0_attn_Wg, np.float32)),
        wup=np.ascontiguousarray(np.asarray(L0_ffn_w_up, np.float32)),
        wgate=np.ascontiguousarray(np.asarray(L0_ffn_w_gate, np.float32)),
        wdown=np.ascontiguousarray(np.asarray(L0_ffn_w_down, np.float32)),
        **consts)
    in_maps = [dict(x=np.ascontiguousarray(x[c]), **common) for c in range(B)]
    res = run_bass_kernel_spmd(nc, in_maps, core_ids=list(range(NCORES)))
    out = np.stack([np.asarray(res.results[c]["out"]) for c in range(B)], 0)
    return out.astype(np.float32)


# version 09
# speedup vs baseline: 1.2002x; 1.0176x over previous
import numpy as np
import ml_dtypes
from contextlib import ExitStack

import concourse.bass as bass
import concourse.mybir as mybir
from concourse.bass_utils import run_bass_kernel_spmd

F32 = mybir.dt.float32
BF16 = mybir.dt.bfloat16
AF = mybir.ActivationFunctionType
ALU = mybir.AluOpType
AX = mybir.AxisListType
BF = ml_dtypes.bfloat16
MAGIC = 12582912.0
NCORES = 8

C_SX, C_RSX, C_RSUP, C_RSG, C_SG, C_RST, C_RSO, C_SO = range(8)
K_LNEPS, K_EIGHTH = 0, 1

_CHAIN_OPS = frozenset((
    "tensor_copy", "tensor_tensor", "tensor_add", "tensor_sub", "tensor_mul",
    "tensor_max", "tensor_scalar", "tensor_single_scalar", "tensor_scalar_add",
    "tensor_scalar_sub", "tensor_scalar_mul", "tensor_scalar_max",
    "tensor_scalar_min", "tensor_reduce", "reduce_max", "reduce_sum",
    "scalar_tensor_tensor", "reciprocal", "memset", "memzero", "select",
    "activation", "copy", "mul", "add", "square", "sqrt", "sign",
))


class Chained:

    def __init__(self, pr, eng, sem_name):
        self._pr = pr
        self._e = eng
        self._s = sem_name

    def __getattr__(self, n):
        a = getattr(self._e, n)
        if n not in _CHAIN_OPS:
            return a
        pr, e, sn = self._pr, self._e, self._s

        def w(*args, **kw):
            if pr.cnt[sn] > 0:
                e.wait_ge(pr.sems[sn], pr.cnt[sn])
            ins = a(*args, **kw)
            ins.then_inc(pr.sems[sn], 1)
            pr.cnt[sn] += 1
            return ins
        return w


class Stream:

    def __init__(self, nc, name, ctx):
        self.sem = ctx.enter_context(nc.semaphore(name))
        self.cnt = 0

    def dma(self, g, out, in_):
        if self.cnt > 0:
            g.wait_ge(self.sem, self.cnt)
        g.dma_start(out=out, in_=in_).then_inc(self.sem, 16)
        self.cnt += 16
        return self.cnt

    def wait(self, eng, val=None):
        v = self.cnt if val is None else val
        if v > 0:
            e = eng._e if isinstance(eng, Chained) else eng
            e.wait_ge(self.sem, v)


class Prog:
    def __init__(self, nc, block, ctx):
        self.nc = nc
        self.block = block
        self.cnt = {}
        self.sems = {}
        for s in ("te", "ve", "se", "cc"):
            self.sems[s] = ctx.enter_context(nc.semaphore(s))
            self.cnt[s] = 0

    def vec(self, f):
        self.block.vector(lambda v: f(Chained(self, v, "ve")))

    def sca(self, f):
        self.block.scalar(lambda s: f(Chained(self, s, "se")))

    def inc(self, ins, name, n=1):
        ins.then_inc(self.sems[name], n)
        self.cnt[name] += n
        return self.cnt[name]

    def wait(self, eng, name, val=None):
        v = self.cnt[name] if val is None else val
        if v > 0:
            e = eng._e if isinstance(eng, Chained) else eng
            e.wait_ge(self.sems[name], v)


def make_consts(S, E, H):
    D = E // H
    idx = np.arange(S, dtype=np.float64)
    a_ang = 1.0 / 10000.0 ** np.linspace(0.0, 1.0, D // 2)
    angle = np.repeat(a_ang, 2)
    sin_t = np.sin(idx[:, None] * angle[None, :])
    cos_t = np.cos(idx[:, None] * angle[None, :])
    gamma = 1.0 - 2.0 ** (-5.0 - np.arange(H, dtype=np.float64))
    r = (1.0 - gamma[:, None] ** (idx[None, :] + 1.0)) / (1.0 - gamma[:, None])
    a_hq = gamma[:, None] ** idx[None, :] / np.sqrt(r)
    b_hk = np.exp(-idx[None, :] * np.log(gamma)[:, None])
    scaling = float(E) ** (-0.5)

    rows = np.arange(E)
    h_r = 2 * (rows // 128) + (rows % 128) // D
    d_r = rows % D
    QCOS = (cos_t[:, d_r].T * a_hq[h_r]).astype(np.float32)
    QSIN = (sin_t[:, d_r].T * a_hq[h_r]).astype(np.float32)
    KCOS = (cos_t[:, d_r].T * b_hk[h_r] * scaling).astype(np.float32)
    KSIN = (sin_t[:, d_r].T * b_hk[h_r] * scaling).astype(np.float32)

    IDT = np.eye(128, dtype=np.float32).astype(BF)
    PROT = np.zeros((128, 128), np.float32)
    for i in range(64):
        PROT[2 * i + 1, 2 * i] = -1.0
        PROT[2 * i, 2 * i + 1] = 1.0
    PROT = PROT.astype(BF)
    ki = np.arange(128)
    TRI = (ki[:, None] <= ki[None, :]).astype(np.float32)
    ONES1 = np.ones((128, 1), np.float32).astype(BF)
    ONESF = np.ones((1, 128), np.float32)
    CONSTS = np.zeros((128, 8), np.float32)
    CONSTS[:, K_LNEPS] = 1e-5
    CONSTS[:, K_EIGHTH] = 0.125
    return dict(qcos=QCOS, qsin=QSIN, kcos=KCOS, ksin=KSIN, idt=IDT, prot=PROT,
                tri=TRI, ones1=ONES1, onesf=ONESF, consts=CONSTS)


def build_program(S=2048, E=1024, H=16, I=3072, stop_after=99):
    D = E // H
    TT_ = S // 128
    EC = E // 128
    IC = I // 128
    NQ = S // 512
    NP = H // 2
    NT = S // 512
    NO = E // 512
    NBU = E // 32
    NBD = I // 32

    nc = bass.Bass("TRN2", target_bir_lowering=False, debug=False)
    ap_in = {}

    def din(name, shape, dt=F32):
        ap_in[name] = nc.dram_tensor(name, list(shape), dt,
                                     kind="ExternalInput").ap()
        return ap_in[name]

    x_in = din("x", (S, E))
    wq_in = din("wq", (E, E)); wk_in = din("wk", (E, E))
    wv_in = din("wv", (E, E)); wg_in = din("wg", (E, E))
    wup_in = din("wup", (I, E)); wgate_in = din("wgate", (I, E))
    wdown_in = din("wdown", (E, I))
    qcos_in = din("qcos", (E, S)); qsin_in = din("qsin", (E, S))
    kcos_in = din("kcos", (E, S)); ksin_in = din("ksin", (E, S))
    idt_in = din("idt", (128, 128), BF16)
    prot_in = din("prot", (128, 128), BF16)
    tri_in = din("tri", (128, 128))
    ones1_in = din("ones1", (128, 1), BF16)
    onesf_in = din("onesf", (1, 128))
    consts_in = din("consts", (128, 8))

    out_d = nc.dram_tensor("out", [S, E], F32, kind="ExternalOutput").ap()

    g_spill = nc.dram_tensor("g_spill", [128, TT_ * E], BF16)
    x_spill = nc.dram_tensor("x_spill", [128, TT_ * E], F32)
    x2_spill = nc.dram_tensor("x2_spill", [128, TT_ * E], F32)
    up_spill = nc.dram_tensor("up_spill", [128, IC * S], BF16)
    cc_i = [nc.dram_tensor(f"cc_i{k}", [1, n], F32)
            for k, n in enumerate((1, 2, 1, 1))]
    cc_o = [nc.dram_tensor(f"cc_o{k}", [1, n * NCORES], F32)
            for k, n in enumerate((1, 2, 1, 1))]

    ctx0 = ExitStack()
    with ctx0:
        sb0 = lambda *a, **kw: ctx0.enter_context(nc.sbuf_tensor(*a, **kw))
        psa = lambda *a: ctx0.enter_context(nc.psum_tensor(*a))

        IDT = sb0("IDT", [128, 128], BF16)
        PROT = sb0("PROT", [128, 128], BF16)
        TRI = sb0("TRI", [128, 128], F32)
        ONES1 = sb0("ONES1", [128, 1], BF16)
        ONESF = sb0("ONESF", [1, 128], F32)
        CONSTS = sb0("CONSTS", [128, 8], F32)
        s1c = sb0("s1c", [128, TT_], F32)
        s2c = sb0("s2c", [128, TT_], F32)
        meanc = sb0("meanc", [128, TT_], F32)
        m2c = sb0("m2c", [128, TT_], F32)
        varc = sb0("varc", [128, TT_], F32)
        rootc = sb0("rootc", [128, TT_], F32)
        rstdc = sb0("rstdc", [128, TT_], F32)
        den_tm = sb0("den_tm", [128, TT_ * H], F32)
        redcols = sb0("redcols", [128, 64], F32)
        redsm = sb0("redsm", [128, 8], F32)
        redb = sb0("redb", [128, 8], BF16)
        sm = sb0("sm", [1, 16], F32)
        ag_sb = sb0("ag_sb", [1, 2 * NCORES], F32)
        bc = sb0("bc", [128, 16], F32)
        sqscr = sb0("sqscr", [128, E], F32)
        d2scr = sb0("d2scr", [128, 8], F32)
        msqscr = sb0("msqscr", [128, 8], F32)
        argscr = sb0("argscr", [128, 8], F32)
        rootscr = sb0("rootscr", [128, 8], F32)
        sclscr = sb0("sclscr", [128, 8], F32)

        P = [psa(f"P{i}", [128, 512], F32) for i in range(6)]
        P6 = psa("P6", [128, 512], F32)
        PT = psa("PT", [128, 1024], BF16)

        block = ctx0.enter_context(nc.Block())
        pr = Prog(nc, block, ctx0)
        ST = {n: Stream(nc, "st_" + n, ctx0) for n in
              ("cs", "x1", "xsp", "wq", "tA", "tB", "gw", "x2l", "gp",
               "x2sp", "xr2", "ccb", "wde", "upw", "ub", "uc", "xr", "out")}
        outb0 = sb0("outb0", [128, E], F32)
        open_stacks = []

        def early_out():
            def f(v):
                pr.wait(v, "te")
                pr.wait(v, "se")
                v.memset(outb0[:], 0.0)
            pr.vec(f)

            def fg(g):
                pr.wait(g, "ve")
                for t in range(TT_):
                    ST["out"].dma(g, out_d[t * 128:(t + 1) * 128, :], outb0[:])
                ST["out"].wait(g)
            block.gpsimd(fg)
            for st_ in reversed(open_stacks):
                st_.close()

        def s0(sync):
            for dst, src in ((TRI, tri_in), (ONES1, ones1_in),
                             (ONESF, onesf_in), (CONSTS, consts_in),
                             (IDT, idt_in), (PROT, prot_in)):
                ST["cs"].dma(sync, dst[:], src)
        block.sync(s0)

        ctxR1 = ExitStack()
        open_stacks.append(ctxR1)
        X1 = ctxR1.enter_context(nc.sbuf_tensor("X1", [128, TT_ * E], F32,
                                                side="right"))

        def s0b(sync):
            for t in range(TT_):
                ST["x1"].dma(sync, X1[:, t * E:(t + 1) * E],
                             x_in[t * 128:(t + 1) * 128, :])
        block.sync(s0b)

        def ln_stats(Xbuf, wait_specs):
            def f_se(s):
                for wfn in wait_specs:
                    wfn(s)
                for t in range(TT_):
                    s.activation(sqscr[:], Xbuf[:, t * E:(t + 1) * E],
                                 AF.Square, accum_out=s2c[:, t:t + 1])
            pr.sca(f_se)
            se_sq = pr.cnt["se"]

            def f_ve(v):
                for wfn in wait_specs:
                    wfn(v)
                for t in range(TT_):
                    v.tensor_reduce(s1c[:, t:t + 1], Xbuf[:, t * E:(t + 1) * E],
                                    AX.X, ALU.add)
                pr.wait(v, "se", se_sq)
                v.tensor_single_scalar(meanc[:, 0:TT_], s1c[:, 0:TT_],
                                       1.0 / E, ALU.mult)
                v.tensor_tensor(m2c[:, 0:TT_], meanc[:, 0:TT_],
                                meanc[:, 0:TT_], ALU.mult)
                v.scalar_tensor_tensor(varc[:, 0:TT_], s2c[:, 0:TT_],
                                       1.0 / E, m2c[:, 0:TT_],
                                       ALU.mult, ALU.subtract)
            pr.vec(f_ve)
            ve_var = pr.cnt["ve"]

            def f_se2(s):
                pr.wait(s, "ve", ve_var)
                s.activation(rootc[:, 0:TT_], varc[:, 0:TT_], AF.Sqrt,
                             bias=CONSTS[:, K_LNEPS:K_LNEPS + 1])
            pr.sca(f_se2)
            se_root = pr.cnt["se"]

            def f_ve2(v):
                pr.wait(v, "se", se_root)
                v.reciprocal(rstdc[:, 0:TT_], rootc[:, 0:TT_])
            pr.vec(f_ve2)

        ln_stats(X1, [lambda e: ST["cs"].wait(e), lambda e: ST["x1"].wait(e)])

        xln = ctxR1.enter_context(nc.sbuf_tensor("xln", [128, TT_ * E], BF16,
                                                 side="right"))

        def s1_ve(v):
            for t in range(TT_):
                v.tensor_scalar(xln[:, t * E:(t + 1) * E],
                                X1[:, t * E:(t + 1) * E],
                                meanc[:, t:t + 1], rstdc[:, t:t + 1],
                                ALU.subtract, ALU.mult)
        pr.vec(s1_ve)
        ve_ln1 = pr.cnt["ve"]

        def s1_g(g):
            pr.wait(g, "ve", ve_ln1)
            ST["xsp"].dma(g, x_spill.ap(), X1[:])
        block.gpsimd(s1_g)

        if stop_after < 2:
            early_out()
            return nc, list(ap_in.keys())

        ctxQK = ExitStack()
        open_stacks.append(ctxQK)
        xlnT = ctxQK.enter_context(nc.sbuf_tensor("xlnT", [128, EC * S], BF16))

        tr_jobs = [(t, c) for t in range(TT_) for c in range(EC)]
        ngrp = len(tr_jobs) // 8
        te_b2 = pr.cnt["te"]
        ve_b2 = pr.cnt["ve"]

        def s2_te(t_e):
            pr.wait(t_e, "ve", ve_ln1)
            ST["xsp"].wait(t_e)
            for gi in range(ngrp):
                if gi > 0:
                    t_e.wait_ge(pr.sems["ve"], ve_b2 + 8 * gi)
                for k in range(8):
                    t, c = tr_jobs[gi * 8 + k]
                    ins = t_e.transpose(
                        PT[:, k * 128:(k + 1) * 128],
                        xln[:, t * E + c * 128:t * E + (c + 1) * 128], IDT[:])
                pr.inc(ins, "te")
        block.tensor(s2_te)

        def s2_ve(v):
            for gi in range(ngrp):
                pr.wait(v, "te", te_b2 + gi + 1)
                for k in range(8):
                    t, c = tr_jobs[gi * 8 + k]
                    v.tensor_copy(
                        xlnT[:, c * S + t * 128:c * S + (t + 1) * 128],
                        PT[:, k * 128:(k + 1) * 128])
            assert pr.cnt["ve"] == ve_b2 + 8 * ngrp
        pr.vec(s2_ve)
        ve_xlnT = pr.cnt["ve"]

        ctxR1.close()
        open_stacks.remove(ctxR1)

        if stop_after < 3:
            early_out()
            return nc, list(ap_in.keys())

        wbuf = ctxQK.enter_context(nc.sbuf_tensor("wbuf", [128, EC * E], BF16))
        wbuf2 = ctxQK.enter_context(nc.sbuf_tensor("wbuf2", [128, EC * E], BF16))
        tabc = ctxQK.enter_context(nc.sbuf_tensor("tabc", [128, 2 * S], BF16))
        tabc2 = ctxQK.enter_context(nc.sbuf_tensor("tabc2", [128, 2 * S], BF16))
        q_sb = ctxQK.enter_context(nc.sbuf_tensor("q_sb", [128, 4 * 512], BF16))
        t1scr = ctxQK.enter_context(nc.sbuf_tensor("t1scr", [128, 512], F32))
        t2scr = ctxQK.enter_context(nc.sbuf_tensor("t2scr", [128, 512], F32))

        ctxR2 = ExitStack()
        open_stacks.append(ctxR2)
        qr = ctxR2.enter_context(nc.sbuf_tensor("qr", [128, EC * S], BF16,
                                                side="right"))
        kr = ctxR2.enter_context(nc.sbuf_tensor("kr", [128, EC * S], BF16,
                                                side="right"))

        for wi, (w_ap, cos_ap, sin_ap, dst, wb) in enumerate([
                (wq_in, qcos_in, qsin_in, qr, wbuf),
                (wk_in, ksin_in, ksin_in, kr, wbuf2)]) if False else \
                enumerate([(wq_in, qcos_in, qsin_in, qr, wbuf),
                           (wk_in, kcos_in, ksin_in, kr, wbuf2)]):
            niter = EC * NT
            te_b = pr.cnt["te"]
            ve_b = pr.cnt["ve"]
            tab_cnt = {}

            def f_g(g, w_ap=w_ap, cos_ap=cos_ap, sin_ap=sin_ap, wb=wb,
                    ve_b=ve_b, wi=wi, tab_cnt=tab_cnt):
                ST["wq"].dma(g, wb[:].rearrange("p (c o) -> p c o", o=E),
                             w_ap.rearrange("(c p) o -> p c o", p=128))
                for oc in range(EC):
                    st_t = ST["tA"] if oc % 2 == 0 else ST["tB"]
                    buf = tabc if oc % 2 == 0 else tabc2
                    if oc >= 2:
                        g.wait_ge(pr.sems["ve"], ve_b + 4 * NT * (oc - 1))
                    elif wi == 1 and ve_b > 0:
                        g.wait_ge(pr.sems["ve"], ve_b)
                    st_t.dma(g, buf[:, 0:S],
                             cos_ap[oc * 128:(oc + 1) * 128, :])
                    st_t.dma(g, buf[:, S:2 * S],
                             sin_ap[oc * 128:(oc + 1) * 128, :])
                    tab_cnt[oc] = st_t.cnt
            block.gpsimd(f_g)
            wq_cnt = ST["wq"].cnt

            def f_te(t_e, wb=wb, te_b=te_b, ve_b=ve_b, wq_cnt=wq_cnt):
                pr.wait(t_e, "ve", ve_b)
                ST["wq"].wait(t_e, wq_cnt)
                for it in range(niter):
                    oc, ts_ = divmod(it, NT)
                    bank = P[it % 4]
                    if it >= 4:
                        t_e.wait_ge(pr.sems["ve"], ve_b + 4 * (it - 4) + 1)
                    if it >= 2:
                        t_e.wait_ge(pr.sems["ve"], ve_b + 4 * (it - 2) + 3)
                    for kc in range(EC):
                        ins = t_e.matmul(
                            bank[:],
                            wb[:, kc * E + oc * 128:kc * E + (oc + 1) * 128],
                            xlnT[:, kc * S + ts_ * 512:kc * S + (ts_ + 1) * 512],
                            start=(kc == 0), stop=(kc == EC - 1))
                    pr.inc(ins, "te")
                    t_e.wait_ge(pr.sems["ve"], ve_b + 4 * it + 1)
                    pr.inc(t_e.matmul(P[4 + it % 2][:], PROT[:],
                                      q_sb[:, (it % 4) * 512:(it % 4 + 1) * 512],
                                      start=True, stop=True), "te")
                assert pr.cnt["te"] == te_b + 2 * niter
            block.tensor(f_te)

            def f_ve(v, dst=dst, te_b=te_b, ve_b=ve_b, tab_cnt=tab_cnt):
                for it in range(niter):
                    oc, ts_ = divmod(it, NT)
                    v.wait_ge(pr.sems["te"], te_b + 2 * it + 1)
                    v.tensor_copy(q_sb[:, (it % 4) * 512:(it % 4 + 1) * 512],
                                  P[it % 4][:])
                    v.wait_ge(pr.sems["te"], te_b + 2 * it + 2)
                    (ST["tA"] if oc % 2 == 0 else ST["tB"]).wait(
                        v, tab_cnt[oc])
                    buf = tabc if oc % 2 == 0 else tabc2
                    sl = slice(ts_ * 512, (ts_ + 1) * 512)
                    v.tensor_tensor(t1scr[:],
                                    q_sb[:, (it % 4) * 512:(it % 4 + 1) * 512],
                                    buf[:, sl], ALU.mult)
                    v.tensor_tensor(t2scr[:], P[4 + it % 2][:],
                                    buf[:, S + ts_ * 512:S + (ts_ + 1) * 512],
                                    ALU.mult)
                    v.tensor_tensor(
                        dst[:, oc * S + ts_ * 512:oc * S + (ts_ + 1) * 512],
                        t1scr[:], t2scr[:], ALU.add)
                assert pr.cnt["ve"] == ve_b + 4 * niter
            pr.vec(f_ve)

        te_qk = pr.cnt["te"]
        ve_qk = pr.cnt["ve"]

        if stop_after < 4:
            early_out()
            return nc, list(ap_in.keys())

        vtm = ctxR2.enter_context(nc.sbuf_tensor("vtm", [128, TT_ * E], BF16,
                                                 side="right"))
        g_stage = ctxQK.enter_context(nc.sbuf_tensor("g_stage", [128, 2 * E],
                                                     BF16))

        def s3b_g(g):
            pr.wait(g, "te", te_qk)
            ST["wq"].dma(g, wbuf[:].rearrange("p (c o) -> p c o", o=E),
                         wv_in.rearrange("(c p) o -> p c o", p=128))
            ST["wq"].dma(g, wbuf2[:].rearrange("p (c o) -> p c o", o=E),
                         wg_in.rearrange("(c p) o -> p c o", p=128))
        block.gpsimd(s3b_g)
        wq_vg = ST["wq"].cnt

        te_b3 = pr.cnt["te"]
        ve_b3 = pr.cnt["ve"]
        se_b3 = pr.cnt["se"]

        def s3b_te(t_e):
            ST["wq"].wait(t_e, wq_vg)
            pr.wait(t_e, "ve", ve_qk)
            for t in range(TT_):
                if t > 0:
                    t_e.wait_ge(pr.sems["ve"], ve_b3 + 2 * NO * t)
                for kc in range(EC):
                    for no in range(NO):
                        ins = t_e.matmul(
                            P[no][:],
                            xlnT[:, kc * S + t * 128:kc * S + (t + 1) * 128],
                            wbuf[:, kc * E + no * 512:kc * E + (no + 1) * 512],
                            start=(kc == 0), stop=(kc == EC - 1))
                    for no in range(NO):
                        ins = t_e.matmul(
                            P[2 + no][:],
                            xlnT[:, kc * S + t * 128:kc * S + (t + 1) * 128],
                            wbuf2[:, kc * E + no * 512:
                                  kc * E + (no + 1) * 512],
                            start=(kc == 0), stop=(kc == EC - 1))
                pr.inc(ins, "te")
        block.tensor(s3b_te)

        def s3b_ve(v):
            for t in range(TT_):
                v.wait_ge(pr.sems["te"], te_b3 + t + 1)
                if t >= 2:
                    ST["gw"].wait(v, 16 * (t - 1))
                for no in range(NO):
                    v.tensor_copy(
                        vtm[:, t * E + no * 512:t * E + (no + 1) * 512],
                        P[no][:])
                for no in range(NO):
                    v.tensor_copy(
                        g_stage[:, (t % 2) * E + no * 512:
                                (t % 2) * E + (no + 1) * 512],
                        P[2 + no][:])
            assert pr.cnt["ve"] == ve_b3 + 2 * NO * TT_
        pr.vec(s3b_ve)

        def s3b_gs(g):
            for t in range(TT_):
                g.wait_ge(pr.sems["ve"], ve_b3 + 2 * NO * (t + 1))
                ST["gw"].dma(g, g_spill.ap()[:, t * E:(t + 1) * E],
                             g_stage[:, (t % 2) * E:(t % 2 + 1) * E])
        block.gpsimd(s3b_gs)
        te_vg = pr.cnt["te"]
        ve_vg = pr.cnt["ve"]
        se_vg = pr.cnt["se"]

        ctxQK.close()
        open_stacks.remove(ctxQK)

        if stop_after < 5 or stop_after in (41, 42):
            early_out()
            return nc, list(ap_in.keys())

        ctxS4 = ExitStack()
        open_stacks.append(ctxS4)
        SA = ctxS4.enter_context(nc.sbuf_tensor("SA", [128, 512], BF16))
        SB = ctxS4.enter_context(nc.sbuf_tensor("SB", [128, 512], BF16))
        ABSA = ctxS4.enter_context(nc.sbuf_tensor("ABSA", [128, 512], BF16))
        ABSB = ctxS4.enter_context(nc.sbuf_tensor("ABSB", [128, 512], BF16))
        SA2 = ctxS4.enter_context(nc.sbuf_tensor("SA2", [128, 512], BF16))
        SB2 = ctxS4.enter_context(nc.sbuf_tensor("SB2", [128, 512], BF16))
        ABSA2 = ctxS4.enter_context(nc.sbuf_tensor("ABSA2", [128, 512], BF16))
        ABSB2 = ctxS4.enter_context(nc.sbuf_tensor("ABSB2", [128, 512], BF16))
        avfm = ctxS4.enter_context(nc.sbuf_tensor("avfm", [128, 512], BF16))
        avtm = ctxS4.enter_context(nc.sbuf_tensor("avtm", [128, 512], BF16))
        den_sb = ctxS4.enter_context(nc.sbuf_tensor("den_sb", [128, 512], BF16))
        gsil = ctxS4.enter_context(nc.sbuf_tensor("gsil", [128, 512], BF16))
        tmpp = ctxS4.enter_context(nc.sbuf_tensor("tmpp", [128, 128], BF16))
        g_pair = ctxS4.enter_context(
            nc.sbuf_tensor("g_pair", [128, TT_ * 128], BF16))
        X2 = ctxS4.enter_context(nc.sbuf_tensor("X2", [128, TT_ * E], F32))

        def s4_x2(g):
            pr.wait(g, "te", te_vg)
            pr.wait(g, "ve", ve_vg)
            pr.wait(g, "se", se_vg)
            ST["x2l"].dma(g, X2[:], x_spill.ap())
        block.gpsimd(s4_x2)

        for p in range(NP):
            def s4_gp(g, p=p):
                if p == 0:
                    ST["gw"].wait(g)
                else:
                    pr.wait(g, "se")
                for q4 in range(TT_ // 4):
                    ST["gp"].dma(
                        g,
                        g_pair[:, q4 * 512:(q4 + 1) * 512].rearrange(
                            "q (t c) -> q t c", c=128),
                        bass.AP(g_spill, p * 128 + q4 * 4 * E,
                                [[TT_ * E, 128], [E, 4], [1, 128]]))
            block.gpsimd(s4_gp)
            gp_cnt = ST["gp"].cnt

            for qc in range(NQ):
                nk = 4 * qc + 4
                te_b4 = pr.cnt["te"]
                ve_b4 = pr.cnt["ve"]
                ve_cum = [0] * (nk + 1)
                for kt in range(nk):
                    j = kt - 4 * qc
                    ve_cum[kt + 1] = ve_cum[kt] + (
                        2 if j < 0 else (2 if j == 3 else 4))
                se_b4 = pr.cnt["se"]

                def s_banks(kt):
                    return (P[0], P[1]) if kt % 2 == 0 else (P[5], P6)

                def emit_scores(t_e, kt, p, qc):
                    j = kt - 4 * qc
                    off = max(j, 0) * 128
                    qs0 = qc * 512
                    bA, bB = s_banks(kt)
                    if kt >= 2:
                        t_e.wait_ge(pr.sems["ve"], ve_b4 + ve_cum[kt - 1])
                    t_e.matmul(
                        bA[:, off:512],
                        kr[0:64, p * S + kt * 128:p * S + (kt + 1) * 128],
                        qr[0:64, p * S + qs0 + off:p * S + qs0 + 512],
                        start=True, stop=True)
                    ins = t_e.matmul(
                        bB[:, off:512],
                        kr[64:128, p * S + kt * 128:p * S + (kt + 1) * 128],
                        qr[64:128, p * S + qs0 + off:p * S + qs0 + 512],
                        start=True, stop=True, tile_position=(64, 0))
                    pr.inc(ins, "te")

                def emit_avden(t_e, kt, p, qc, nk):
                    j = kt - 4 * qc
                    off = max(j, 0) * 128
                    t_e.wait_ge(pr.sems["ve"], ve_b4 + ve_cum[kt + 1])
                    t_e.wait_ge(pr.sems["se"], se_b4 + 2 * (kt + 1))
                    first = (kt == 0)
                    last = (kt == nk - 1)
                    sa = SA if kt % 2 == 0 else SA2
                    sb2 = SB if kt % 2 == 0 else SB2
                    aa = ABSA if kt % 2 == 0 else ABSA2
                    ab = ABSB if kt % 2 == 0 else ABSB2
                    t_e.matmul(P[2][0:64, off:512],
                               vtm[:, kt * E + (2 * p) * D:
                                   kt * E + (2 * p) * D + D],
                               sa[:, off:512], start=first, stop=last)
                    t_e.matmul(P[3][64:128, off:512],
                               vtm[:, kt * E + (2 * p + 1) * D:
                                   kt * E + (2 * p + 1) * D + D],
                               sb2[:, off:512], start=first, stop=last,
                               tile_position=(0, 64), skip_group_check=True)
                    t_e.matmul(P[4][0:1, off:512], ONES1[:],
                               aa[:, off:512], start=first, stop=last)
                    ins = t_e.matmul(P[4][64:65, off:512], ONES1[:],
                                     ab[:, off:512], start=first,
                                     stop=last, tile_position=(0, 64),
                                     skip_group_check=True)
                    pr.inc(ins, "te")

                def f_te(t_e, p=p, qc=qc, nk=nk, te_b4=te_b4,
                         ve_b4=ve_b4, ve_cum=ve_cum):
                    pr.wait(t_e, "ve", ve_vg)
                    for kt in range(nk):
                        emit_scores(t_e, kt, p, qc)
                        if kt >= 1:
                            emit_avden(t_e, kt - 1, p, qc, nk)
                    emit_avden(t_e, nk - 1, p, qc, nk)
                block.tensor(f_te)

                def f_ve(v, p=p, qc=qc, nk=nk, te_b4=te_b4):
                    for kt in range(nk):
                        j = kt - 4 * qc
                        v.wait_ge(pr.sems["te"],
                                  te_b4 + (1 if kt == 0 else 2 * kt))
                        bA, bB = s_banks(kt)
                        sa = SA if kt % 2 == 0 else SA2
                        sb2 = SB if kt % 2 == 0 else SB2
                        if j >= 0:
                            v.tensor_tensor(sa[:, j * 128:(j + 1) * 128],
                                            bA[:, j * 128:(j + 1) * 128],
                                            TRI[:], ALU.mult)
                            v.tensor_tensor(sb2[:, j * 128:(j + 1) * 128],
                                            bB[:, j * 128:(j + 1) * 128],
                                            TRI[:], ALU.mult)
                        if j < 3:
                            off2 = (j + 1) * 128 if j >= 0 else 0
                            v.tensor_copy(sa[:, off2:512], bA[:, off2:512])
                            v.tensor_copy(sb2[:, off2:512], bB[:, off2:512])
                pr.vec(f_ve)

                def f_se(sx, p=p, qc=qc, nk=nk, te_b4=te_b4, ve_b4=ve_b4,
                         ve_cum=ve_cum, se_b4=se_b4):
                    for kt in range(nk):
                        j = kt - 4 * qc
                        off = max(j, 0) * 128
                        sx.wait_ge(pr.sems["ve"], ve_b4 + ve_cum[kt + 1])
                        if kt >= 2:
                            sx.wait_ge(pr.sems["te"], te_b4 + 2 * kt - 1)
                        sa = SA if kt % 2 == 0 else SA2
                        sb2 = SB if kt % 2 == 0 else SB2
                        aa = ABSA if kt % 2 == 0 else ABSA2
                        ab = ABSB if kt % 2 == 0 else ABSB2
                        sx.activation(aa[:, off:512], sa[:, off:512], AF.Abs)
                        sx.activation(ab[:, off:512], sb2[:, off:512], AF.Abs)
                pr.sca(f_se)

                te_av = pr.cnt["te"]

                def e_ve1(v, te_av=te_av):
                    pr.wait(v, "te", te_av)
                    v.tensor_copy(avfm[0:64, :], P[2][0:64, :])
                    v.tensor_copy(avfm[64:128, :], P[3][64:128, :])
                    v.tensor_copy(den_sb[0:1, :], P[4][0:1, :])
                    v.tensor_copy(den_sb[64:65, :], P[4][64:65, :])
                pr.vec(e_ve1)
                ve_ep1 = pr.cnt["ve"]

                def e_te(t_e, ve_ep1=ve_ep1):
                    pr.wait(t_e, "ve", ve_ep1)
                    for k in range(4):
                        t_e.transpose(PT[:, k * 128:(k + 1) * 128],
                                      avfm[:, k * 128:(k + 1) * 128], IDT[:])
                    for k in range(4):
                        ins = t_e.transpose(
                            PT[:, 512 + k * 128:512 + (k + 1) * 128],
                            den_sb[:, k * 128:(k + 1) * 128], IDT[:])
                    pr.inc(ins, "te")
                block.tensor(e_te)
                te_ep = pr.cnt["te"]

                def e_ve2(v, p=p, qc=qc, te_ep=te_ep):
                    pr.wait(v, "te", te_ep)
                    v.tensor_copy(avtm[:], PT[:, 0:512])
                    for tau in range(4):
                        tg = qc * 4 + tau
                        v.tensor_copy(
                            den_tm[:, tg * H + 2 * p:tg * H + 2 * p + 2],
                            bass.AP(PT, 512 + tau * 128,
                                    [[1024, 128], [64, 2]]))
                    dv = bass.AP(den_tm, (qc * 4) * H + 2 * p,
                                 [[TT_ * H, 128], [H, 4], [1, 2]])
                    v.tensor_scalar(dv, dv, 1.0, 50000.0, ALU.max, ALU.min)
                    v.tensor_tensor(d2scr[:], dv, dv, ALU.mult)
                pr.vec(e_ve2)
                ve_ep2 = pr.cnt["ve"]

                def e_se1(s, ve_ep2=ve_ep2):
                    pr.wait(s, "ve", ve_ep2)
                    s.activation(sqscr[:, 0:512], avtm[:], AF.Square,
                                 scale=CONSTS[:, K_EIGHTH:K_EIGHTH + 1])
                pr.sca(e_se1)
                se_ep1 = pr.cnt["se"]

                def e_ve3(v, se_ep1=se_ep1):
                    pr.wait(v, "se", se_ep1)
                    v.tensor_reduce(
                        msqscr[:],
                        bass.AP(sqscr, 0, [[E, 128], [64, 8], [1, 64]]),
                        AX.X, ALU.add)
                    v.scalar_tensor_tensor(argscr[:], d2scr[:], 1e-6,
                                           msqscr[:], ALU.mult, ALU.add)
                pr.vec(e_ve3)
                ve_ep3 = pr.cnt["ve"]

                def e_se2(s, p=p, qc=qc, ve_ep3=ve_ep3, gp_cnt=gp_cnt):
                    pr.wait(s, "ve", ve_ep3)
                    s.activation(rootscr[:], argscr[:], AF.Sqrt)
                    ST["gp"].wait(s, gp_cnt)
                    s.activation(gsil[:], g_pair[:, qc * 512:(qc + 1) * 512],
                                 AF.Silu)
                pr.sca(e_se2)
                se_ep2 = pr.cnt["se"]

                def e_ve4(v, p=p, qc=qc, se_ep2=se_ep2):
                    pr.wait(v, "se", se_ep2)
                    ST["x2l"].wait(v)
                    v.reciprocal(sclscr[:], rootscr[:])
                    for tau in range(4):
                        tg = qc * 4 + tau
                        for h in range(2):
                            v.scalar_tensor_tensor(
                                tmpp[:, h * D:(h + 1) * D],
                                avtm[:, tau * 128 + h * D:
                                     tau * 128 + (h + 1) * D],
                                sclscr[:, tau * 2 + h:tau * 2 + h + 1],
                                gsil[:, tau * 128 + h * D:
                                     tau * 128 + (h + 1) * D],
                                ALU.mult, ALU.mult)
                        v.tensor_tensor(
                            X2[:, tg * E + p * 128:tg * E + (p + 1) * 128],
                            X2[:, tg * E + p * 128:tg * E + (p + 1) * 128],
                            tmpp[:], ALU.add)
                pr.vec(e_ve4)

        ve_s4 = pr.cnt["ve"]
        te_s4 = pr.cnt["te"]

        if stop_after < 6:
            early_out()
            return nc, list(ap_in.keys())

        ln_stats(X2, [lambda e, _v=ve_s4: pr.wait(e, "ve", _v)])

        def s5_sp(g):
            pr.wait(g, "ve")
            pr.wait(g, "se")
            ST["x2sp"].dma(g, x2_spill.ap(), X2[:])
        block.gpsimd(s5_sp)
        te_preF = pr.cnt["te"]
        ctxS4.close()
        open_stacks.remove(ctxS4)
        ctxR2.close()
        open_stacks.remove(ctxR2)

        ctxF1 = ExitStack()
        open_stacks.append(ctxF1)
        xln2 = ctxF1.enter_context(nc.sbuf_tensor("xln2", [128, TT_ * E],
                                                  BF16))
        xres2 = ctxF1.enter_context(nc.sbuf_tensor("xres2", [128, 2 * E], F32))

        ve_b5 = pr.cnt["ve"]

        def s5_g2(g):
            pr.wait(g, "te", te_preF)
            ST["x2sp"].wait(g)
            for t in range(TT_):
                if t >= 2:
                    g.wait_ge(pr.sems["ve"], ve_b5 + (t - 1))
                ST["xr2"].dma(g, xres2[:, (t % 2) * E:(t % 2 + 1) * E],
                              x2_spill.ap()[:, t * E:(t + 1) * E])
        block.gpsimd(s5_g2)

        def s5_ve(v):
            pr.wait(v, "te", te_preF)
            for t in range(TT_):
                ST["xr2"].wait(v, 16 * (t + 1))
                v.tensor_scalar(xln2[:, t * E:(t + 1) * E],
                                xres2[:, (t % 2) * E:(t % 2 + 1) * E],
                                meanc[:, t:t + 1], rstdc[:, t:t + 1],
                                ALU.subtract, ALU.mult)
            v.tensor_reduce(redcols[:, 0:1], xln2[:], AX.X, ALU.max,
                            apply_absolute_value=True)
            v.tensor_single_scalar(redb[:, 0:1], redcols[:, 0:1], 1.0,
                                   ALU.mult)
        pr.vec(s5_ve)
        ve_ln2 = pr.cnt["ve"]

        def part_max_to_sm(col_redb, sm_col, wait_ve):
            def f_te(t_e):
                pr.wait(t_e, "ve", wait_ve)
                pr.inc(t_e.transpose(PT[0:1, 0:128],
                                     redb[:, col_redb:col_redb + 1], IDT[:]),
                       "te")
            block.tensor(f_te)
            te_t = pr.cnt["te"]

            def f_ve(v):
                pr.wait(v, "te", te_t)
                v.tensor_reduce(sm[:, sm_col:sm_col + 1], PT[0:1, 0:128],
                                AX.X, ALU.max)
            pr.vec(f_ve)
            return pr.cnt["ve"]

        ve_mx = part_max_to_sm(0, 0, ve_ln2)

        def collective_round(k, src_cols, wait_ve):
            n = len(src_cols)

            def f_g(g):
                pr.wait(g, "ve", wait_ve)
                ap_src = sm[0:1, src_cols[0]:src_cols[0] + n]
                ST["ccb"].dma(g, cc_i[k].ap(), ap_src)
                ST["ccb"].wait(g)
                pr.inc(g.collective_compute(
                    "AllGather", ALU.bypass,
                    replica_groups=[list(range(NCORES))],
                    ins=[cc_i[k].ap().opt()], outs=[cc_o[k].ap().opt()]),
                    "cc", 1)
                pr.wait(g, "cc")
                ST["ccb"].dma(g, ag_sb[:, 0:n * NCORES], cc_o[k].ap())
            block.gpsimd(f_g)
            cc_done = ST["ccb"].cnt

            def f_ve(v):
                ST["ccb"].wait(v, cc_done)
                for i in range(n):
                    v.tensor_reduce(
                        sm[:, src_cols[i]:src_cols[i] + 1],
                        bass.AP(ag_sb, i, [[2 * NCORES, 1], [n, NCORES]]),
                        AX.X, ALU.max)
            pr.vec(f_ve)
            return pr.cnt["ve"]

        ve_r1 = collective_round(0, [0], ve_mx)

        def r1_ve(v):
            pr.wait(v, "ve", ve_r1)
            v.tensor_scalar(sm[:, 1:2], sm[:, 0:1], 1.0 / 32767.0, 1e-12,
                            ALU.mult, ALU.max)
            v.reciprocal(sm[:, 2:3], sm[:, 1:2])
        pr.vec(r1_ve)
        ve_r1b = pr.cnt["ve"]

        def bcast(cols_src, cols_dst, wait_ve):
            n = len(cols_src)
            assert cols_src == list(range(cols_src[0], cols_src[0] + n))

            def f_te(t_e):
                pr.wait(t_e, "ve", wait_ve)
                pr.inc(t_e.matmul(P6[:, 0:n], ONESF[:],
                                  sm[0:1, cols_src[0]:cols_src[0] + n],
                                  start=True, stop=True), "te")
            block.tensor(f_te)
            te_t = pr.cnt["te"]

            def f_ve(v):
                pr.wait(v, "te", te_t)
                for i, cd in enumerate(cols_dst):
                    v.tensor_copy(bc[:, cd:cd + 1], P6[:, i:i + 1])
            pr.vec(f_ve)
            return pr.cnt["ve"]

        ve_bc1 = bcast([1, 2], [C_SX, C_RSX], ve_r1b)

        if stop_after < 7:
            early_out()
            return nc, list(ap_in.keys())

        y6 = ctxF1.enter_context(nc.sbuf_tensor("y6", [128, 2 * E], F32))
        Xq = ctxF1.enter_context(nc.sbuf_tensor("Xq", [128, TT_ * E], BF16))

        ctxR3 = ExitStack()
        open_stacks.append(ctxR3)
        XqT = ctxR3.enter_context(nc.sbuf_tensor("XqT", [128, EC * S], BF16,
                                                 side="right"))

        se_b6 = pr.cnt["se"]
        ve_b6 = pr.cnt["ve"]

        def s6_se(s):
            pr.wait(s, "ve", ve_bc1)
            for t in range(TT_):
                if t >= 2:
                    s.wait_ge(pr.sems["ve"], ve_b6 + (t - 1))
                s.activation(y6[:, (t % 2) * E:(t % 2 + 1) * E],
                             xln2[:, t * E:(t + 1) * E], AF.Copy,
                             scale=bc[:, C_RSX:C_RSX + 1])
        pr.sca(s6_se)

        def s6_ve(v):
            for t in range(TT_):
                v.wait_ge(pr.sems["se"], se_b6 + t + 1)
                v.tensor_scalar(Xq[:, t * E:(t + 1) * E],
                                y6[:, (t % 2) * E:(t % 2 + 1) * E],
                                MAGIC, MAGIC, ALU.add, ALU.subtract)
            assert pr.cnt["ve"] == ve_b6 + TT_
        pr.vec(s6_ve)
        ve_xq = pr.cnt["ve"]

        te_b6 = pr.cnt["te"]
        ve_b6b = pr.cnt["ve"]

        def s6_te(t_e):
            pr.wait(t_e, "ve", ve_xq)
            for gi in range(ngrp):
                if gi > 0:
                    t_e.wait_ge(pr.sems["ve"], ve_b6b + 8 * gi)
                for k in range(8):
                    t, c = tr_jobs[gi * 8 + k]
                    ins = t_e.transpose(
                        PT[:, k * 128:(k + 1) * 128],
                        Xq[:, t * E + c * 128:t * E + (c + 1) * 128], IDT[:])
                pr.inc(ins, "te")
        block.tensor(s6_te)

        def s6_ve2(v):
            for gi in range(ngrp):
                v.wait_ge(pr.sems["te"], te_b6 + gi + 1)
                for k in range(8):
                    t, c = tr_jobs[gi * 8 + k]
                    v.tensor_copy(
                        XqT[:, c * S + t * 128:c * S + (t + 1) * 128],
                        PT[:, k * 128:(k + 1) * 128])
        pr.vec(s6_ve2)
        ve_xqT = pr.cnt["ve"]

        ctxF1.close()
        open_stacks.remove(ctxF1)
        ctxF2 = ExitStack()
        open_stacks.append(ctxF2)

        if stop_after < 8:
            early_out()
            return nc, list(ap_in.keys())

        G = ctxF2.enter_context(nc.sbuf_tensor("G", [128, IC * S], BF16))
        ctxPB = ExitStack()
        open_stacks.append(ctxPB)
        yscr = ctxPB.enter_context(nc.sbuf_tensor("yscr", [128, 2 * S], F32))
        sigscr = ctxPB.enter_context(nc.sbuf_tensor("sigscr", [128, S], BF16))
        upc = ctxPB.enter_context(nc.sbuf_tensor("upc", [128, 2 * S], BF16))
        uqscr = ctxPB.enter_context(nc.sbuf_tensor("uqscr", [128, S], BF16))
        t2b = ctxPB.enter_context(nc.sbuf_tensor("t2b", [128, S], BF16))
        ctxUG = ExitStack()
        open_stacks.append(ctxUG)
        wde = ctxUG.enter_context(nc.sbuf_tensor("wde", [128, 2 * E], F32))
        tqscr = ctxUG.enter_context(nc.sbuf_tensor("tqscr", [128, E], F32))
        qwscr = ctxUG.enter_context(nc.sbuf_tensor("qwscr", [128, E], BF16))
        bmscr = ctxUG.enter_context(nc.sbuf_tensor("bmscr", [128, NBU], F32))
        sWscr = ctxUG.enter_context(nc.sbuf_tensor("sWscr", [128, NBU], F32))
        rsWscr = ctxUG.enter_context(nc.sbuf_tensor("rsWscr", [128, NBU], F32))
        wqT = ctxUG.enter_context(nc.sbuf_tensor("wqT", [128, 2 * E], BF16))
        upst = ctxUG.enter_context(nc.sbuf_tensor("upst", [128, S], BF16))

        def dequant_chunk(bufs, nb, width, ld_val, slot):
            wd, tq, qw, bm, sW, rsW, wrow, nbmax = bufs

            def f_ve(v):
                ST["wde"].wait(v, ld_val)
                v.tensor_reduce(
                    bm[:, 0:nb],
                    bass.AP(wd, slot * width, [[wrow, 128], [32, nb], [1, 32]]),
                    AX.X, ALU.max, apply_absolute_value=True)
                v.tensor_scalar(sW[:, 0:nb], bm[:, 0:nb], 1.0 / 7.0,
                                1e-12, ALU.mult, ALU.max)
                v.reciprocal(rsW[:, 0:nb], sW[:, 0:nb])
                v.tensor_tensor(
                    tq[:, 0:width],
                    bass.AP(wd, slot * width, [[wrow, 128], [32, nb], [1, 32]]),
                    bass.AP(rsW, 0, [[nbmax, 128], [1, nb], [0, 32]]),
                    ALU.mult)
                v.tensor_scalar(tq[:, 0:width], tq[:, 0:width],
                                MAGIC, MAGIC, ALU.add, ALU.subtract)
                v.tensor_tensor(
                    qw[:, 0:width],
                    bass.AP(tq, 0, [[width, 128], [32, nb], [1, 32]]),
                    bass.AP(sW, 0, [[nbmax, 128], [1, nb], [0, 32]]),
                    ALU.mult)
            pr.vec(f_ve)
            return pr.cnt["ve"]

        ug_bufs = (wde, tqscr, qwscr, bmscr, sWscr, rsWscr, 2 * E, NBU)

        dq_hist = {}
        wqT_hist = {}
        for mi, (w_src, is_up) in enumerate([(wup_in, True),
                                             (wgate_in, False)]):
            for c in range(IC):
                ci = mi * IC + c

                def f_ld(sync, w_src=w_src, c=c, mi=mi, ci=ci):
                    if mi == 0 and c == 0:
                        ST["x2sp"].wait(sync)
                        pr.wait(sync, "ve")
                        pr.wait(sync, "se")
                        pr.wait(sync, "te")
                    if ci >= 2:
                        pr.wait(sync, "ve", dq_hist[ci - 2])
                    ST["wde"].dma(sync,
                                  wde[:, (c % 2) * E:(c % 2 + 1) * E],
                                  w_src[c * 128:(c + 1) * 128, :])
                block.sync(f_ld)
                ld_w = ST["wde"].cnt

                ve_dq = dequant_chunk(ug_bufs, NBU, E, ld_w, c % 2)
                dq_hist[ci] = ve_dq

                def f_tr(t_e, ve_dq=ve_dq, c=c, mi=mi, ci=ci):
                    pr.wait(t_e, "ve", ve_dq)
                    if ci >= 1:
                        pr.wait(t_e, "ve", wqT_hist[ci - 1])
                    for k in range(8):
                        ins = t_e.transpose(PT[:, k * 128:(k + 1) * 128],
                                            qwscr[:, k * 128:(k + 1) * 128],
                                            IDT[:])
                    pr.inc(ins, "te")
                block.tensor(f_tr)
                te_tr = pr.cnt["te"]

                def f_ev(v, te_tr=te_tr, c=c):
                    pr.wait(v, "te", te_tr)
                    v.tensor_copy(wqT[:, (c % 2) * E:(c % 2 + 1) * E], PT[:])
                pr.vec(f_ev)
                ve_wqT = pr.cnt["ve"]
                wqT_hist[ci] = ve_wqT

                def f_mm(t_e, ve_wqT=ve_wqT, c=c, mi=mi):
                    pr.wait(t_e, "ve", ve_wqT)
                    if mi == 0 and c == 0:
                        pr.wait(t_e, "ve", ve_xqT)
                    for kc in range(EC):
                        for ts_ in range(NT):
                            ins = t_e.matmul(
                                P[ts_][:],
                                wqT[:, (c % 2) * E + kc * 128:
                                    (c % 2) * E + (kc + 1) * 128],
                                XqT[:, kc * S + ts_ * 512:
                                    kc * S + (ts_ + 1) * 512],
                                start=(kc == 0), stop=(kc == EC - 1))
                    pr.inc(ins, "te")
                block.tensor(f_mm)
                te_mm = pr.cnt["te"]

                def f_evac(v, te_mm=te_mm, c=c, is_up=is_up):
                    pr.wait(v, "te", te_mm)
                    if is_up:
                        ST["upw"].wait(v)
                    for ts_ in range(NT):
                        if is_up:
                            o = upst[:, ts_ * 512:(ts_ + 1) * 512]
                        else:
                            o = G[:, c * S + ts_ * 512:c * S + (ts_ + 1) * 512]
                        v.tensor_scalar(o, P[ts_][:], bc[:, C_SX:C_SX + 1],
                                        None, ALU.mult)
                    src = upst[:] if is_up else G[:, c * S:(c + 1) * S]
                    v.tensor_reduce(redcols[:, mi * IC + c:mi * IC + c + 1],
                                    src, AX.X, ALU.max,
                                    apply_absolute_value=True)
                pr.vec(f_evac)
                ve_ev = pr.cnt["ve"]

                if is_up:
                    def f_sp(g, c=c, ve_ev=ve_ev):
                        pr.wait(g, "ve", ve_ev)
                        ST["upw"].dma(g, up_spill.ap()[:, c * S:(c + 1) * S],
                                      upst[:])
                    block.gpsimd(f_sp)

        ve_s7 = pr.cnt["ve"]

        def s7_red(v):
            pr.wait(v, "ve", ve_s7)
            v.tensor_reduce(redsm[:, 0:1], redcols[:, 0:IC], AX.X, ALU.max)
            v.tensor_reduce(redsm[:, 1:2], redcols[:, IC:2 * IC], AX.X,
                            ALU.max)
            v.tensor_single_scalar(redb[:, 0:2], redsm[:, 0:2], 1.0, ALU.mult)
        pr.vec(s7_red)
        ve_r2a = pr.cnt["ve"]

        def s7_tr(t_e):
            pr.wait(t_e, "ve", ve_r2a)
            t_e.transpose(PT[0:1, 0:128], redb[:, 0:1], IDT[:])
            pr.inc(t_e.transpose(PT[0:1, 128:256], redb[:, 1:2], IDT[:]), "te")
        block.tensor(s7_tr)
        te_r2 = pr.cnt["te"]

        def s7_fin(v):
            pr.wait(v, "te", te_r2)
            v.tensor_reduce(sm[:, 4:5], PT[0:1, 0:128], AX.X, ALU.max)
            v.tensor_reduce(sm[:, 5:6], PT[0:1, 128:256], AX.X, ALU.max)
        pr.vec(s7_fin)
        ve_r2b = pr.cnt["ve"]

        ve_r2 = collective_round(1, [4, 5], ve_r2b)

        def r2_ve(v):
            pr.wait(v, "ve", ve_r2)
            v.tensor_scalar(sm[:, 6:7], sm[:, 4:5], 1.0 / 32767.0, 1e-12,
                            ALU.mult, ALU.max)
            v.tensor_scalar(sm[:, 7:8], sm[:, 5:6], 1.0 / 32767.0, 1e-12,
                            ALU.mult, ALU.max)
            v.reciprocal(sm[:, 8:9], sm[:, 6:7])
            v.reciprocal(sm[:, 9:10], sm[:, 7:8])
        pr.vec(r2_ve)
        ve_r2c = pr.cnt["ve"]
        ve_bc2 = bcast([7, 8, 9], [C_SG, C_RSUP, C_RSG], ve_r2c)

        if stop_after < 9:
            early_out()
            return nc, list(ap_in.keys())

        ctxUG.close()
        open_stacks.remove(ctxUG)
        ctxR3.close()
        open_stacks.remove(ctxR3)

        se_b8 = pr.cnt["se"]
        ve_b8 = pr.cnt["ve"]
        for c in range(IC):
            def a_se1(s, c=c):
                if c == 0:
                    pr.wait(s, "ve", ve_bc2)
                elif c >= 2:
                    s.wait_ge(pr.sems["ve"], ve_b8 + c - 1)
                s.activation(yscr[:, (c % 2) * S:(c % 2 + 1) * S],
                             G[:, c * S:(c + 1) * S],
                             AF.Copy, scale=bc[:, C_RSG:C_RSG + 1])
            pr.sca(a_se1)

            def a_ve1(v, c=c):
                v.wait_ge(pr.sems["se"], se_b8 + c + 1)
                v.tensor_scalar(G[:, c * S:(c + 1) * S],
                                yscr[:, (c % 2) * S:(c % 2 + 1) * S],
                                MAGIC, MAGIC, ALU.add, ALU.subtract)
            pr.vec(a_ve1)
        se_m8 = pr.cnt["se"]
        ve_m8 = pr.cnt["ve"]
        for c in range(IC):
            def a_se2(s, c=c):
                if c < 2:
                    s.wait_ge(pr.sems["ve"], ve_m8)
                else:
                    s.wait_ge(pr.sems["ve"], ve_m8 + 4 * (c - 1))
                s.activation(yscr[:, (c % 2) * S:(c % 2 + 1) * S],
                             G[:, c * S:(c + 1) * S],
                             AF.Sigmoid, scale=bc[:, C_SG:C_SG + 1])
            pr.sca(a_se2)

            def a_ve2(v, c=c):
                v.wait_ge(pr.sems["se"], se_m8 + c + 1)
                yb = yscr[:, (c % 2) * S:(c % 2 + 1) * S]
                v.tensor_scalar(yb, yb, 65536.0, MAGIC,
                                ALU.mult, ALU.add)
                v.tensor_scalar(sigscr[:], yb, MAGIC, 65535.0,
                                ALU.subtract, ALU.min)
                v.tensor_tensor(G[:, c * S:(c + 1) * S],
                                G[:, c * S:(c + 1) * S], sigscr[:], ALU.mult)
                v.tensor_reduce(redcols[:, 32 + c:33 + c],
                                G[:, c * S:(c + 1) * S], AX.X, ALU.max,
                                apply_absolute_value=True)
            pr.vec(a_ve2)
        ve_s8a = pr.cnt["ve"]

        def s8_red(v):
            pr.wait(v, "ve", ve_s8a)
            v.tensor_reduce(redsm[:, 2:3], redcols[:, 32:32 + IC], AX.X,
                            ALU.max)
            v.tensor_single_scalar(redb[:, 2:3], redsm[:, 2:3], 1.0, ALU.mult)
        pr.vec(s8_red)
        ve_r3a = pr.cnt["ve"]

        def s8_tr(t_e):
            pr.wait(t_e, "ve", ve_r3a)
            pr.inc(t_e.transpose(PT[0:1, 0:128], redb[:, 2:3], IDT[:]), "te")
        block.tensor(s8_tr)
        te_r3 = pr.cnt["te"]

        def s8_fin(v):
            pr.wait(v, "te", te_r3)
            v.tensor_reduce(sm[:, 10:11], PT[0:1, 0:128], AX.X, ALU.max)
        pr.vec(s8_fin)
        ve_r3b = pr.cnt["ve"]

        ve_r3 = collective_round(2, [10], ve_r3b)

        def r3_ve(v):
            pr.wait(v, "ve", ve_r3)
            v.tensor_tensor(sm[:, 11:12], sm[:, 10:11], sm[:, 7:8], ALU.mult)
            v.tensor_scalar(sm[:, 11:12], sm[:, 11:12],
                            1.0 / (65536.0 * 32767.0), 1e-12,
                            ALU.mult, ALU.max)
            v.reciprocal(sm[:, 12:13], sm[:, 11:12])
            v.tensor_tensor(sm[:, 12:13], sm[:, 12:13], sm[:, 7:8], ALU.mult)
            v.tensor_single_scalar(sm[:, 12:13], sm[:, 12:13],
                                   1.0 / 65536.0, ALU.mult)
        pr.vec(r3_ve)
        ve_r3c = pr.cnt["ve"]
        ve_bc3 = bcast([12], [C_RST], ve_r3c)

        se_b8b = pr.cnt["se"]
        ve_b8b = pr.cnt["ve"]
        for c in range(IC):
            def b_se(s, c=c):
                if c == 0:
                    pr.wait(s, "ve", ve_bc3)
                elif c >= 2:
                    s.wait_ge(pr.sems["ve"], ve_b8b + c - 1)
                s.activation(yscr[:, (c % 2) * S:(c % 2 + 1) * S],
                             G[:, c * S:(c + 1) * S],
                             AF.Copy, scale=bc[:, C_RST:C_RST + 1])
            pr.sca(b_se)

            def b_ve(v, c=c):
                v.wait_ge(pr.sems["se"], se_b8b + c + 1)
                v.tensor_scalar(G[:, c * S:(c + 1) * S],
                                yscr[:, (c % 2) * S:(c % 2 + 1) * S],
                                MAGIC, MAGIC, ALU.add, ALU.subtract)
            pr.vec(b_ve)
        ve_g2 = pr.cnt["ve"]

        if stop_after < 10:
            early_out()
            return nc, list(ap_in.keys())

        se_b8c = pr.cnt["se"]
        ve_b8c = pr.cnt["ve"]

        for c in range(IC):
            def pb_g(g, c=c):
                if c == 0:
                    pr.wait(g, "ve", ve_g2)
                    ST["upw"].wait(g)
                if c >= 2:
                    g.wait_ge(pr.sems["se"], se_b8c + (c - 1))
                ST["ub"].dma(g, upc[:, (c % 2) * S:(c % 2 + 1) * S],
                             up_spill.ap()[:, c * S:(c + 1) * S])
            block.gpsimd(pb_g)

            def pb_se(s, c=c):
                ST["ub"].wait(s, 16 * (c + 1))
                if c >= 2:
                    s.wait_ge(pr.sems["ve"], ve_b8c + 3 * (c - 2) + 1)
                s.activation(yscr[:, (c % 2) * S:(c % 2 + 1) * S],
                             upc[:, (c % 2) * S:(c % 2 + 1) * S],
                             AF.Copy, scale=bc[:, C_RSUP:C_RSUP + 1])
            pr.sca(pb_se)

            def pb_ve(v, c=c):
                v.wait_ge(pr.sems["se"], se_b8c + c + 1)
                v.tensor_scalar(uqscr[:],
                                yscr[:, (c % 2) * S:(c % 2 + 1) * S],
                                MAGIC, MAGIC, ALU.add, ALU.subtract)
                v.tensor_tensor(t2b[:], G[:, c * S:(c + 1) * S], uqscr[:],
                                ALU.mult)
                v.tensor_reduce(redcols[:, 32 + c:33 + c], t2b[:], AX.X,
                                ALU.max, apply_absolute_value=True)
            pr.vec(pb_ve)
        ve_s8b = pr.cnt["ve"]

        def s8_red2(v):
            pr.wait(v, "ve", ve_s8b)
            v.tensor_reduce(redsm[:, 3:4], redcols[:, 32:32 + IC], AX.X,
                            ALU.max)
            v.tensor_single_scalar(redb[:, 3:4], redsm[:, 3:4], 1.0, ALU.mult)
        pr.vec(s8_red2)
        ve_r4a = pr.cnt["ve"]

        def s8_tr2(t_e):
            pr.wait(t_e, "ve", ve_r4a)
            pr.inc(t_e.transpose(PT[0:1, 0:128], redb[:, 3:4], IDT[:]), "te")
        block.tensor(s8_tr2)
        te_r4 = pr.cnt["te"]

        def s8_fin2(v):
            pr.wait(v, "te", te_r4)
            v.tensor_reduce(sm[:, 13:14], PT[0:1, 0:128], AX.X, ALU.max)
        pr.vec(s8_fin2)
        ve_r4b = pr.cnt["ve"]

        ve_r4 = collective_round(3, [13], ve_r4b)

        def r4_ve(v):
            pr.wait(v, "ve", ve_r4)
            v.tensor_tensor(sm[:, 14:15], sm[:, 13:14], sm[:, 11:12],
                            ALU.mult)
            v.tensor_tensor(sm[:, 14:15], sm[:, 14:15], sm[:, 6:7], ALU.mult)
            v.tensor_scalar(sm[:, 14:15], sm[:, 14:15], 1.0 / 32767.0,
                            1e-12, ALU.mult, ALU.max)
            v.reciprocal(sm[:, 15:16], sm[:, 14:15])
            v.tensor_tensor(sm[:, 15:16], sm[:, 15:16], sm[:, 11:12],
                            ALU.mult)
            v.tensor_tensor(sm[:, 15:16], sm[:, 15:16], sm[:, 6:7], ALU.mult)
        pr.vec(r4_ve)
        ve_r4c = pr.cnt["ve"]
        ve_bc4 = bcast([14, 15], [C_SO, C_RSO], ve_r4c)

        if stop_after < 11:
            early_out()
            return nc, list(ap_in.keys())

        se_b8d = pr.cnt["se"]
        ve_b8d = pr.cnt["ve"]
        for c in range(IC):
            def pc_g(g, c=c):
                if c == 0:
                    pr.wait(g, "ve", ve_bc4)
                if c >= 2:
                    g.wait_ge(pr.sems["se"], se_b8d + (c - 1))
                ST["uc"].dma(g, upc[:, (c % 2) * S:(c % 2 + 1) * S],
                             up_spill.ap()[:, c * S:(c + 1) * S])
            block.gpsimd(pc_g)

            def pc_se1(s, c=c):
                ST["uc"].wait(s, 16 * (c + 1))
                if c >= 2:
                    s.wait_ge(pr.sems["ve"], ve_b8d + 2 * (c - 2) + 1)
                s.activation(yscr[:, (c % 2) * S:(c % 2 + 1) * S],
                             upc[:, (c % 2) * S:(c % 2 + 1) * S],
                             AF.Copy, scale=bc[:, C_RSUP:C_RSUP + 1])
            pr.sca(pc_se1)

            def pc_ve1(v, c=c):
                v.wait_ge(pr.sems["se"], se_b8d + c + 1)
                v.tensor_scalar(uqscr[:],
                                yscr[:, (c % 2) * S:(c % 2 + 1) * S],
                                MAGIC, MAGIC, ALU.add, ALU.subtract)
                v.tensor_tensor(G[:, c * S:(c + 1) * S],
                                G[:, c * S:(c + 1) * S], uqscr[:],
                                ALU.mult)
            pr.vec(pc_ve1)
        se_m8d = pr.cnt["se"]
        ve_m8d = pr.cnt["ve"]
        for c in range(IC):
            def pc_se2(s, c=c):
                if c < 2:
                    s.wait_ge(pr.sems["ve"], ve_m8d)
                else:
                    s.wait_ge(pr.sems["ve"], ve_m8d + c - 1)
                s.activation(yscr[:, (c % 2) * S:(c % 2 + 1) * S],
                             G[:, c * S:(c + 1) * S], AF.Copy,
                             scale=bc[:, C_RSO:C_RSO + 1])
            pr.sca(pc_se2)

            def pc_ve2(v, c=c):
                v.wait_ge(pr.sems["se"], se_m8d + c + 1)
                v.tensor_scalar(G[:, c * S:(c + 1) * S],
                                yscr[:, (c % 2) * S:(c % 2 + 1) * S],
                                MAGIC, MAGIC, ALU.add, ALU.subtract)
            pr.vec(pc_ve2)
        ve_oint = pr.cnt["ve"]

        if stop_after < 12:
            early_out()
            return nc, list(ap_in.keys())

        ctxPB.close()
        open_stacks.remove(ctxPB)
        wdeD = ctxF2.enter_context(nc.sbuf_tensor("wdeD", [128, I], F32))
        tqD = ctxF2.enter_context(nc.sbuf_tensor("tqD", [128, I], F32))
        qwD = ctxF2.enter_context(nc.sbuf_tensor("qwD", [128, I], BF16))
        bmD = ctxF2.enter_context(nc.sbuf_tensor("bmD", [128, NBD], F32))
        sWD = ctxF2.enter_context(nc.sbuf_tensor("sWD", [128, NBD], F32))
        rsWD = ctxF2.enter_context(nc.sbuf_tensor("rsWD", [128, NBD], F32))
        wqdT = ctxF2.enter_context(nc.sbuf_tensor("wqdT", [128, IC * E],
                                                  BF16))
        d_bufs = (wdeD, tqD, qwD, bmD, sWD, rsWD, I, NBD)

        n_dchunks = 1 if stop_after == 121 else (2 if stop_after == 123 else EC)
        skip_dtr = (stop_after == 122)
        for c in range(n_dchunks):
            def d_ld(sync, c=c):
                pr.wait(sync, "ve", ve_oint) if c < 1 else pr.wait(sync, "ve")
                ST["wde"].dma(sync, wdeD[:],
                              wdown_in[c * 128:(c + 1) * 128, :])
            block.sync(d_ld)
            ld_w = ST["wde"].cnt

            ve_dq = dequant_chunk(d_bufs, NBD, I, ld_w, 0)

            for tg in range(0 if skip_dtr else (IC // 8)):
                def d_tr(t_e, ve_dq=ve_dq, c=c, tg=tg):
                    pr.wait(t_e, "ve", ve_dq)
                    if not (c == 0 and tg == 0):
                        pr.wait(t_e, "ve")
                    for k in range(8):
                        jc = tg * 8 + k
                        ins = t_e.transpose(PT[:, k * 128:(k + 1) * 128],
                                            qwD[:, jc * 128:(jc + 1) * 128],
                                            IDT[:])
                    pr.inc(ins, "te")
                block.tensor(d_tr)
                te_tr = pr.cnt["te"]

                def d_ev(v, te_tr=te_tr, c=c, tg=tg):
                    pr.wait(v, "te", te_tr)
                    for k in range(8):
                        jc = tg * 8 + k
                        v.tensor_copy(
                            wqdT[:, jc * E + c * 128:jc * E + (c + 1) * 128],
                            PT[:, k * 128:(k + 1) * 128])
                pr.vec(d_ev)
        ve_wqd = pr.cnt["ve"]

        if stop_after < 13 or stop_after in (121, 122, 123):
            early_out()
            return nc, list(ap_in.keys())

        xres = ctxF2.enter_context(nc.sbuf_tensor("xres", [128, 2 * E], F32))
        xout = ctxF2.enter_context(nc.sbuf_tensor("xout", [128, 2 * E], F32))

        te_b9 = pr.cnt["te"]
        ve_b9 = pr.cnt["ve"]

        only_g = (stop_after == 126)
        only_gte = (stop_after == 127)

        def s9_g(g):
            pr.wait(g, "ve", ve_wqd)
            ST["x2sp"].wait(g)
            for t in range(TT_):
                if t >= 2 and not (only_g or only_gte):
                    g.wait_ge(pr.sems["ve"], ve_b9 + 2 * NO * (t - 1))
                ST["xr"].dma(g, xres[:, (t % 2) * E:(t % 2 + 1) * E],
                             x2_spill.ap()[:, t * E:(t + 1) * E])
        block.gpsimd(s9_g)

        if only_g:
            early_out()
            return nc, list(ap_in.keys())

        jc_rng = (range(0, 14) if stop_after == 124 else
                  (range(14, IC) if stop_after == 125 else range(IC)))

        def s9_te(t_e):
            pr.wait(t_e, "ve", ve_wqd)
            for t in range(TT_):
                if t >= 2 and not only_gte:
                    t_e.wait_ge(pr.sems["ve"], ve_b9 + 2 * NO * (t - 1))
                for jc in jc_rng:
                    for no in range(NO):
                        ins = t_e.matmul(
                            P[(t % 2) * 2 + no][:],
                            G[:, jc * S + t * 128:jc * S + (t + 1) * 128],
                            wqdT[:, jc * E + no * 512:jc * E + (no + 1) * 512],
                            start=(jc == jc_rng[0] if hasattr(jc_rng, "__getitem__") else jc == 0),
                            stop=(jc == list(jc_rng)[-1]))
                pr.inc(ins, "te")
        block.tensor(s9_te)

        if only_gte:
            early_out()
            return nc, list(ap_in.keys())

        def s9_ve(v):
            for t in range(TT_):
                v.wait_ge(pr.sems["te"], te_b9 + t + 1)
                ST["xr"].wait(v, 16 * (t + 1))
                if t >= 2:
                    ST["out"].wait(v, 16 * (t - 1))
                for no in range(NO):
                    v.tensor_scalar(sqscr[:, no * 512:(no + 1) * 512],
                                    P[(t % 2) * 2 + no][:],
                                    bc[:, C_SO:C_SO + 1], None, ALU.mult)
                for no in range(NO):
                    v.tensor_tensor(
                        xout[:, (t % 2) * E + no * 512:
                             (t % 2) * E + (no + 1) * 512],
                        sqscr[:, no * 512:(no + 1) * 512],
                        xres[:, (t % 2) * E + no * 512:
                             (t % 2) * E + (no + 1) * 512],
                        ALU.add)
        pr.vec(s9_ve)

        if stop_after == 128:
            early_out()
            return nc, list(ap_in.keys())

        def s9_st(g):
            for t in range(TT_):
                g.wait_ge(pr.sems["ve"], ve_b9 + 2 * NO * (t + 1))
                ST["out"].dma(g, out_d[t * 128:(t + 1) * 128, :],
                              xout[:, (t % 2) * E:(t % 2 + 1) * E])
            ST["out"].wait(g)
        block.sync(s9_st)

        ctxF2.close()
        open_stacks.remove(ctxF2)

    return nc, list(ap_in.keys())


_CACHE = {}


def _get_prog(S, E, H, I):
    key = (S, E, H, I)
    if key not in _CACHE:
        _CACHE[key] = build_program(S, E, H, I)
    return _CACHE[key]


def kernel(L0_attn_x, L0_attn_Wq, L0_attn_Wk, L0_attn_Wv, L0_attn_Wg,
           L0_ffn_w_gate, L0_ffn_w_up, L0_ffn_w_down):
    x = np.ascontiguousarray(np.asarray(L0_attn_x, np.float32))
    B, S, E = x.shape
    H = 16
    I = L0_ffn_w_up.shape[0]
    consts = make_consts(S, E, H)
    nc, _ = _get_prog(S, E, H, I)
    common = dict(
        wq=np.ascontiguousarray(np.asarray(L0_attn_Wq, np.float32)),
        wk=np.ascontiguousarray(np.asarray(L0_attn_Wk, np.float32)),
        wv=np.ascontiguousarray(np.asarray(L0_attn_Wv, np.float32)),
        wg=np.ascontiguousarray(np.asarray(L0_attn_Wg, np.float32)),
        wup=np.ascontiguousarray(np.asarray(L0_ffn_w_up, np.float32)),
        wgate=np.ascontiguousarray(np.asarray(L0_ffn_w_gate, np.float32)),
        wdown=np.ascontiguousarray(np.asarray(L0_ffn_w_down, np.float32)),
        **consts)
    in_maps = [dict(x=np.ascontiguousarray(x[c]), **common) for c in range(B)]
    res = run_bass_kernel_spmd(nc, in_maps, core_ids=list(range(NCORES)))
    out = np.stack([np.asarray(res.results[c]["out"]) for c in range(B)], 0)
    return out.astype(np.float32)


# version 10 (final)
# speedup vs baseline: 1.2143x; 1.0118x over previous
import numpy as np
import ml_dtypes
from contextlib import ExitStack

import concourse.bass as bass
import concourse.mybir as mybir
from concourse.bass_utils import run_bass_kernel_spmd

F32 = mybir.dt.float32
BF16 = mybir.dt.bfloat16
AF = mybir.ActivationFunctionType
ALU = mybir.AluOpType
AX = mybir.AxisListType
BF = ml_dtypes.bfloat16
MAGIC = 12582912.0
NCORES = 8

C_SX, C_RSX, C_RSUP, C_RSG, C_SG, C_RST, C_RSO, C_SO = range(8)
K_LNEPS, K_EIGHTH = 0, 1

_CHAIN_OPS = frozenset((
    "tensor_copy", "tensor_tensor", "tensor_add", "tensor_sub", "tensor_mul",
    "tensor_max", "tensor_scalar", "tensor_single_scalar", "tensor_scalar_add",
    "tensor_scalar_sub", "tensor_scalar_mul", "tensor_scalar_max",
    "tensor_scalar_min", "tensor_reduce", "reduce_max", "reduce_sum",
    "scalar_tensor_tensor", "reciprocal", "memset", "memzero", "select",
    "activation", "copy", "mul", "add", "square", "sqrt", "sign",
))


class Chained:

    def __init__(self, pr, eng, sem_name):
        self._pr = pr
        self._e = eng
        self._s = sem_name

    def __getattr__(self, n):
        a = getattr(self._e, n)
        if n not in _CHAIN_OPS:
            return a
        pr, e, sn = self._pr, self._e, self._s

        def w(*args, **kw):
            if pr.cnt[sn] > 0:
                e.wait_ge(pr.sems[sn], pr.cnt[sn])
            ins = a(*args, **kw)
            ins.then_inc(pr.sems[sn], 1)
            pr.cnt[sn] += 1
            return ins
        return w


class Stream:

    def __init__(self, nc, name, ctx):
        self.sem = ctx.enter_context(nc.semaphore(name))
        self.cnt = 0

    def dma(self, g, out, in_):
        if self.cnt > 0:
            g.wait_ge(self.sem, self.cnt)
        g.dma_start(out=out, in_=in_).then_inc(self.sem, 16)
        self.cnt += 16
        return self.cnt

    def wait(self, eng, val=None):
        v = self.cnt if val is None else val
        if v > 0:
            e = eng._e if isinstance(eng, Chained) else eng
            e.wait_ge(self.sem, v)


class Prog:
    def __init__(self, nc, block, ctx):
        self.nc = nc
        self.block = block
        self.cnt = {}
        self.sems = {}
        for s in ("te", "ve", "se", "cc"):
            self.sems[s] = ctx.enter_context(nc.semaphore(s))
            self.cnt[s] = 0

    def vec(self, f):
        self.block.vector(lambda v: f(Chained(self, v, "ve")))

    def sca(self, f):
        self.block.scalar(lambda s: f(Chained(self, s, "se")))

    def inc(self, ins, name, n=1):
        ins.then_inc(self.sems[name], n)
        self.cnt[name] += n
        return self.cnt[name]

    def wait(self, eng, name, val=None):
        v = self.cnt[name] if val is None else val
        if v > 0:
            e = eng._e if isinstance(eng, Chained) else eng
            e.wait_ge(self.sems[name], v)


def make_consts(S, E, H):
    D = E // H
    idx = np.arange(S, dtype=np.float64)
    a_ang = 1.0 / 10000.0 ** np.linspace(0.0, 1.0, D // 2)
    angle = np.repeat(a_ang, 2)
    sin_t = np.sin(idx[:, None] * angle[None, :])
    cos_t = np.cos(idx[:, None] * angle[None, :])
    gamma = 1.0 - 2.0 ** (-5.0 - np.arange(H, dtype=np.float64))
    r = (1.0 - gamma[:, None] ** (idx[None, :] + 1.0)) / (1.0 - gamma[:, None])
    a_hq = gamma[:, None] ** idx[None, :] / np.sqrt(r)
    b_hk = np.exp(-idx[None, :] * np.log(gamma)[:, None])
    scaling = float(E) ** (-0.5)

    rows = np.arange(E)
    h_r = 2 * (rows // 128) + (rows % 128) // D
    d_r = rows % D
    QCOS = (cos_t[:, d_r].T * a_hq[h_r]).astype(np.float32)
    QSIN = (sin_t[:, d_r].T * a_hq[h_r]).astype(np.float32)
    KCOS = (cos_t[:, d_r].T * b_hk[h_r] * scaling).astype(np.float32)
    KSIN = (sin_t[:, d_r].T * b_hk[h_r] * scaling).astype(np.float32)

    IDT = np.eye(128, dtype=np.float32).astype(BF)
    PROT = np.zeros((128, 128), np.float32)
    for i in range(64):
        PROT[2 * i + 1, 2 * i] = -1.0
        PROT[2 * i, 2 * i + 1] = 1.0
    PROT = PROT.astype(BF)
    ki = np.arange(128)
    TRI = (ki[:, None] <= ki[None, :]).astype(np.float32)
    ONES1 = np.ones((128, 1), np.float32).astype(BF)
    ONESF = np.ones((1, 128), np.float32)
    CONSTS = np.zeros((128, 8), np.float32)
    CONSTS[:, K_LNEPS] = 1e-5
    CONSTS[:, K_EIGHTH] = 0.125
    return dict(qcos=QCOS, qsin=QSIN, kcos=KCOS, ksin=KSIN, idt=IDT, prot=PROT,
                tri=TRI, ones1=ONES1, onesf=ONESF, consts=CONSTS)


def build_program(S=2048, E=1024, H=16, I=3072, stop_after=99):
    D = E // H
    TT_ = S // 128
    EC = E // 128
    IC = I // 128
    NQ = S // 512
    NP = H // 2
    NT = S // 512
    NO = E // 512
    NBU = E // 32
    NBD = I // 32

    nc = bass.Bass("TRN2", target_bir_lowering=False, debug=False)
    ap_in = {}

    def din(name, shape, dt=F32):
        ap_in[name] = nc.dram_tensor(name, list(shape), dt,
                                     kind="ExternalInput").ap()
        return ap_in[name]

    x_in = din("x", (S, E))
    wq_in = din("wq", (E, E)); wk_in = din("wk", (E, E))
    wv_in = din("wv", (E, E)); wg_in = din("wg", (E, E))
    wup_in = din("wup", (I, E)); wgate_in = din("wgate", (I, E))
    wdown_in = din("wdown", (E, I))
    qcos_in = din("qcos", (E, S)); qsin_in = din("qsin", (E, S))
    kcos_in = din("kcos", (E, S)); ksin_in = din("ksin", (E, S))
    idt_in = din("idt", (128, 128), BF16)
    prot_in = din("prot", (128, 128), BF16)
    tri_in = din("tri", (128, 128))
    ones1_in = din("ones1", (128, 1), BF16)
    onesf_in = din("onesf", (1, 128))
    consts_in = din("consts", (128, 8))

    out_d = nc.dram_tensor("out", [S, E], F32, kind="ExternalOutput").ap()

    g_spill = nc.dram_tensor("g_spill", [128, TT_ * E], BF16)
    x_spill = nc.dram_tensor("x_spill", [128, TT_ * E], F32)
    x2_spill = nc.dram_tensor("x2_spill", [128, TT_ * E], F32)
    up_spill = nc.dram_tensor("up_spill", [128, IC * S], BF16)
    cc_i = [nc.dram_tensor(f"cc_i{k}", [1, n], F32)
            for k, n in enumerate((1, 2, 1, 1))]
    cc_o = [nc.dram_tensor(f"cc_o{k}", [1, n * NCORES], F32)
            for k, n in enumerate((1, 2, 1, 1))]

    ctx0 = ExitStack()
    with ctx0:
        sb0 = lambda *a, **kw: ctx0.enter_context(nc.sbuf_tensor(*a, **kw))
        psa = lambda *a: ctx0.enter_context(nc.psum_tensor(*a))

        IDT = sb0("IDT", [128, 128], BF16)
        PROT = sb0("PROT", [128, 128], BF16)
        TRI = sb0("TRI", [128, 128], F32)
        ONES1 = sb0("ONES1", [128, 1], BF16)
        ONESF = sb0("ONESF", [1, 128], F32)
        CONSTS = sb0("CONSTS", [128, 8], F32)
        s1c = sb0("s1c", [128, TT_], F32)
        s2c = sb0("s2c", [128, TT_], F32)
        meanc = sb0("meanc", [128, TT_], F32)
        m2c = sb0("m2c", [128, TT_], F32)
        varc = sb0("varc", [128, TT_], F32)
        rootc = sb0("rootc", [128, TT_], F32)
        rstdc = sb0("rstdc", [128, TT_], F32)
        den_tm = sb0("den_tm", [128, TT_ * H], F32)
        redcols = sb0("redcols", [128, 64], F32)
        redsm = sb0("redsm", [128, 8], F32)
        redb = sb0("redb", [128, 8], BF16)
        sm = sb0("sm", [1, 16], F32)
        ag_sb = sb0("ag_sb", [1, 2 * NCORES], F32)
        bc = sb0("bc", [128, 16], F32)
        sqscr = sb0("sqscr", [128, E], F32)
        d2scr = sb0("d2scr", [128, 8], F32)
        msqscr = sb0("msqscr", [128, 8], F32)
        argscr = sb0("argscr", [128, 8], F32)
        rootscr = sb0("rootscr", [128, 8], F32)
        sclscr = sb0("sclscr", [128, 8], F32)

        P = [psa(f"P{i}", [128, 512], F32) for i in range(6)]
        P6 = psa("P6", [128, 512], F32)
        PT = psa("PT", [128, 1024], BF16)

        block = ctx0.enter_context(nc.Block())
        pr = Prog(nc, block, ctx0)
        ST = {n: Stream(nc, "st_" + n, ctx0) for n in
              ("cs", "x1", "xsp", "wq", "tA", "tB", "gw", "x2l", "gp",
               "x2sp", "xr2", "ccb", "wde", "upw", "ub", "uc", "xr", "out")}
        outb0 = sb0("outb0", [128, E], F32)
        open_stacks = []

        def early_out():
            def f(v):
                pr.wait(v, "te")
                pr.wait(v, "se")
                v.memset(outb0[:], 0.0)
            pr.vec(f)

            def fg(g):
                pr.wait(g, "ve")
                for t in range(TT_):
                    ST["out"].dma(g, out_d[t * 128:(t + 1) * 128, :], outb0[:])
                ST["out"].wait(g)
            block.gpsimd(fg)
            for st_ in reversed(open_stacks):
                st_.close()

        def s0(sync):
            for dst, src in ((TRI, tri_in), (ONES1, ones1_in),
                             (ONESF, onesf_in), (CONSTS, consts_in),
                             (IDT, idt_in), (PROT, prot_in)):
                ST["cs"].dma(sync, dst[:], src)
        block.sync(s0)

        ctxR1 = ExitStack()
        open_stacks.append(ctxR1)
        X1 = ctxR1.enter_context(nc.sbuf_tensor("X1", [128, TT_ * E], F32,
                                                side="right"))

        def s0b(sync):
            for t in range(TT_):
                ST["x1"].dma(sync, X1[:, t * E:(t + 1) * E],
                             x_in[t * 128:(t + 1) * 128, :])
        block.sync(s0b)

        def ln_stats(Xbuf, wait_specs):
            def f_se(s):
                for wfn in wait_specs:
                    wfn(s)
                for t in range(TT_):
                    s.activation(sqscr[:], Xbuf[:, t * E:(t + 1) * E],
                                 AF.Square, accum_out=s2c[:, t:t + 1])
            pr.sca(f_se)
            se_sq = pr.cnt["se"]

            def f_ve(v):
                for wfn in wait_specs:
                    wfn(v)
                for t in range(TT_):
                    v.tensor_reduce(s1c[:, t:t + 1], Xbuf[:, t * E:(t + 1) * E],
                                    AX.X, ALU.add)
                pr.wait(v, "se", se_sq)
                v.tensor_single_scalar(meanc[:, 0:TT_], s1c[:, 0:TT_],
                                       1.0 / E, ALU.mult)
                v.tensor_tensor(m2c[:, 0:TT_], meanc[:, 0:TT_],
                                meanc[:, 0:TT_], ALU.mult)
                v.scalar_tensor_tensor(varc[:, 0:TT_], s2c[:, 0:TT_],
                                       1.0 / E, m2c[:, 0:TT_],
                                       ALU.mult, ALU.subtract)
            pr.vec(f_ve)
            ve_var = pr.cnt["ve"]

            def f_se2(s):
                pr.wait(s, "ve", ve_var)
                s.activation(rootc[:, 0:TT_], varc[:, 0:TT_], AF.Sqrt,
                             bias=CONSTS[:, K_LNEPS:K_LNEPS + 1])
            pr.sca(f_se2)
            se_root = pr.cnt["se"]

            def f_ve2(v):
                pr.wait(v, "se", se_root)
                v.reciprocal(rstdc[:, 0:TT_], rootc[:, 0:TT_])
            pr.vec(f_ve2)

        ln_stats(X1, [lambda e: ST["cs"].wait(e), lambda e: ST["x1"].wait(e)])

        xln = ctxR1.enter_context(nc.sbuf_tensor("xln", [128, TT_ * E], BF16,
                                                 side="right"))

        def s1_ve(v):
            for t in range(TT_):
                v.tensor_scalar(xln[:, t * E:(t + 1) * E],
                                X1[:, t * E:(t + 1) * E],
                                meanc[:, t:t + 1], rstdc[:, t:t + 1],
                                ALU.subtract, ALU.mult)
        pr.vec(s1_ve)
        ve_ln1 = pr.cnt["ve"]

        def s1_g(g):
            pr.wait(g, "ve", ve_ln1)
            ST["xsp"].dma(g, x_spill.ap(), X1[:])
        block.gpsimd(s1_g)

        if stop_after < 2:
            early_out()
            return nc, list(ap_in.keys())

        ctxQK = ExitStack()
        open_stacks.append(ctxQK)
        xlnT = ctxQK.enter_context(nc.sbuf_tensor("xlnT", [128, EC * S], BF16))

        tr_jobs = [(t, c) for t in range(TT_) for c in range(EC)]
        ngrp = len(tr_jobs) // 8
        te_b2 = pr.cnt["te"]
        ve_b2 = pr.cnt["ve"]

        def s2_te(t_e):
            pr.wait(t_e, "ve", ve_ln1)
            ST["xsp"].wait(t_e)
            for gi in range(ngrp):
                if gi > 0:
                    t_e.wait_ge(pr.sems["ve"], ve_b2 + 8 * gi)
                for k in range(8):
                    t, c = tr_jobs[gi * 8 + k]
                    ins = t_e.transpose(
                        PT[:, k * 128:(k + 1) * 128],
                        xln[:, t * E + c * 128:t * E + (c + 1) * 128], IDT[:])
                pr.inc(ins, "te")
        block.tensor(s2_te)

        def s2_ve(v):
            for gi in range(ngrp):
                pr.wait(v, "te", te_b2 + gi + 1)
                for k in range(8):
                    t, c = tr_jobs[gi * 8 + k]
                    v.tensor_copy(
                        xlnT[:, c * S + t * 128:c * S + (t + 1) * 128],
                        PT[:, k * 128:(k + 1) * 128])
            assert pr.cnt["ve"] == ve_b2 + 8 * ngrp
        pr.vec(s2_ve)
        ve_xlnT = pr.cnt["ve"]

        ctxR1.close()
        open_stacks.remove(ctxR1)

        if stop_after < 3:
            early_out()
            return nc, list(ap_in.keys())

        wbuf = ctxQK.enter_context(nc.sbuf_tensor("wbuf", [128, EC * E], BF16))
        wbuf2 = ctxQK.enter_context(nc.sbuf_tensor("wbuf2", [128, EC * E], BF16))
        tabc = ctxQK.enter_context(nc.sbuf_tensor("tabc", [128, 2 * S], BF16))
        tabc2 = ctxQK.enter_context(nc.sbuf_tensor("tabc2", [128, 2 * S], BF16))
        q_sb = ctxQK.enter_context(nc.sbuf_tensor("q_sb", [128, 4 * 512], BF16))
        t1scr = ctxQK.enter_context(nc.sbuf_tensor("t1scr", [128, 512], F32))
        t2scr = ctxQK.enter_context(nc.sbuf_tensor("t2scr", [128, 512], F32))

        ctxR2 = ExitStack()
        open_stacks.append(ctxR2)
        qr = ctxR2.enter_context(nc.sbuf_tensor("qr", [128, EC * S], BF16,
                                                side="right"))
        kr = ctxR2.enter_context(nc.sbuf_tensor("kr", [128, EC * S], BF16,
                                                side="right"))

        for wi, (w_ap, cos_ap, sin_ap, dst, wb) in enumerate([
                (wq_in, qcos_in, qsin_in, qr, wbuf),
                (wk_in, ksin_in, ksin_in, kr, wbuf2)]) if False else \
                enumerate([(wq_in, qcos_in, qsin_in, qr, wbuf),
                           (wk_in, kcos_in, ksin_in, kr, wbuf2)]):
            niter = EC * NT
            te_b = pr.cnt["te"]
            ve_b = pr.cnt["ve"]
            tab_cnt = {}

            def f_g(g, w_ap=w_ap, cos_ap=cos_ap, sin_ap=sin_ap, wb=wb,
                    ve_b=ve_b, wi=wi, tab_cnt=tab_cnt):
                ST["wq"].dma(g, wb[:].rearrange("p (c o) -> p c o", o=E),
                             w_ap.rearrange("(c p) o -> p c o", p=128))
                for oc in range(EC):
                    st_t = ST["tA"] if oc % 2 == 0 else ST["tB"]
                    buf = tabc if oc % 2 == 0 else tabc2
                    if oc >= 2:
                        g.wait_ge(pr.sems["ve"], ve_b + 4 * NT * (oc - 1))
                    elif wi == 1 and ve_b > 0:
                        g.wait_ge(pr.sems["ve"], ve_b)
                    st_t.dma(g, buf[:, 0:S],
                             cos_ap[oc * 128:(oc + 1) * 128, :])
                    st_t.dma(g, buf[:, S:2 * S],
                             sin_ap[oc * 128:(oc + 1) * 128, :])
                    tab_cnt[oc] = st_t.cnt
            block.gpsimd(f_g)
            wq_cnt = ST["wq"].cnt

            def f_te(t_e, wb=wb, te_b=te_b, ve_b=ve_b, wq_cnt=wq_cnt):
                pr.wait(t_e, "ve", ve_b)
                ST["wq"].wait(t_e, wq_cnt)
                for it in range(niter):
                    oc, ts_ = divmod(it, NT)
                    bank = P[it % 4]
                    if it >= 4:
                        t_e.wait_ge(pr.sems["ve"], ve_b + 4 * (it - 4) + 1)
                    if it >= 2:
                        t_e.wait_ge(pr.sems["ve"], ve_b + 4 * (it - 2) + 3)
                    for kc in range(EC):
                        ins = t_e.matmul(
                            bank[:],
                            wb[:, kc * E + oc * 128:kc * E + (oc + 1) * 128],
                            xlnT[:, kc * S + ts_ * 512:kc * S + (ts_ + 1) * 512],
                            start=(kc == 0), stop=(kc == EC - 1))
                    pr.inc(ins, "te")
                    t_e.wait_ge(pr.sems["ve"], ve_b + 4 * it + 1)
                    pr.inc(t_e.matmul(P[4 + it % 2][:], PROT[:],
                                      q_sb[:, (it % 4) * 512:(it % 4 + 1) * 512],
                                      start=True, stop=True), "te")
                assert pr.cnt["te"] == te_b + 2 * niter
            block.tensor(f_te)

            def f_ve(v, dst=dst, te_b=te_b, ve_b=ve_b, tab_cnt=tab_cnt):
                for it in range(niter):
                    oc, ts_ = divmod(it, NT)
                    v.wait_ge(pr.sems["te"], te_b + 2 * it + 1)
                    v.tensor_copy(q_sb[:, (it % 4) * 512:(it % 4 + 1) * 512],
                                  P[it % 4][:])
                    v.wait_ge(pr.sems["te"], te_b + 2 * it + 2)
                    (ST["tA"] if oc % 2 == 0 else ST["tB"]).wait(
                        v, tab_cnt[oc])
                    buf = tabc if oc % 2 == 0 else tabc2
                    sl = slice(ts_ * 512, (ts_ + 1) * 512)
                    v.tensor_tensor(t1scr[:],
                                    q_sb[:, (it % 4) * 512:(it % 4 + 1) * 512],
                                    buf[:, sl], ALU.mult)
                    v.tensor_tensor(t2scr[:], P[4 + it % 2][:],
                                    buf[:, S + ts_ * 512:S + (ts_ + 1) * 512],
                                    ALU.mult)
                    v.tensor_tensor(
                        dst[:, oc * S + ts_ * 512:oc * S + (ts_ + 1) * 512],
                        t1scr[:], t2scr[:], ALU.add)
                assert pr.cnt["ve"] == ve_b + 4 * niter
            pr.vec(f_ve)

        te_qk = pr.cnt["te"]
        ve_qk = pr.cnt["ve"]

        if stop_after < 4:
            early_out()
            return nc, list(ap_in.keys())

        vtm = ctxR2.enter_context(nc.sbuf_tensor("vtm", [128, TT_ * E], BF16,
                                                 side="right"))
        g_stage = ctxQK.enter_context(nc.sbuf_tensor("g_stage", [128, 2 * E],
                                                     BF16))

        def s3b_g(g):
            pr.wait(g, "te", te_qk)
            ST["wq"].dma(g, wbuf[:].rearrange("p (c o) -> p c o", o=E),
                         wv_in.rearrange("(c p) o -> p c o", p=128))
            ST["wq"].dma(g, wbuf2[:].rearrange("p (c o) -> p c o", o=E),
                         wg_in.rearrange("(c p) o -> p c o", p=128))
        block.gpsimd(s3b_g)
        wq_vg = ST["wq"].cnt

        te_b3 = pr.cnt["te"]
        ve_b3 = pr.cnt["ve"]
        se_b3 = pr.cnt["se"]

        def s3b_te(t_e):
            ST["wq"].wait(t_e, wq_vg)
            pr.wait(t_e, "ve", ve_qk)
            for t in range(TT_):
                if t > 0:
                    t_e.wait_ge(pr.sems["ve"], ve_b3 + 2 * NO * t)
                for kc in range(EC):
                    for no in range(NO):
                        ins = t_e.matmul(
                            P[no][:],
                            xlnT[:, kc * S + t * 128:kc * S + (t + 1) * 128],
                            wbuf[:, kc * E + no * 512:kc * E + (no + 1) * 512],
                            start=(kc == 0), stop=(kc == EC - 1))
                    for no in range(NO):
                        ins = t_e.matmul(
                            P[2 + no][:],
                            xlnT[:, kc * S + t * 128:kc * S + (t + 1) * 128],
                            wbuf2[:, kc * E + no * 512:
                                  kc * E + (no + 1) * 512],
                            start=(kc == 0), stop=(kc == EC - 1))
                pr.inc(ins, "te")
        block.tensor(s3b_te)

        def s3b_ve(v):
            for t in range(TT_):
                v.wait_ge(pr.sems["te"], te_b3 + t + 1)
                if t >= 2:
                    ST["gw"].wait(v, 16 * (t - 1))
                for no in range(NO):
                    v.tensor_copy(
                        vtm[:, t * E + no * 512:t * E + (no + 1) * 512],
                        P[no][:])
                for no in range(NO):
                    v.tensor_copy(
                        g_stage[:, (t % 2) * E + no * 512:
                                (t % 2) * E + (no + 1) * 512],
                        P[2 + no][:])
            assert pr.cnt["ve"] == ve_b3 + 2 * NO * TT_
        pr.vec(s3b_ve)

        def s3b_gs(g):
            for t in range(TT_):
                g.wait_ge(pr.sems["ve"], ve_b3 + 2 * NO * (t + 1))
                ST["gw"].dma(g, g_spill.ap()[:, t * E:(t + 1) * E],
                             g_stage[:, (t % 2) * E:(t % 2 + 1) * E])
        block.gpsimd(s3b_gs)
        te_vg = pr.cnt["te"]
        ve_vg = pr.cnt["ve"]
        se_vg = pr.cnt["se"]

        ctxQK.close()
        open_stacks.remove(ctxQK)

        if stop_after < 5 or stop_after in (41, 42):
            early_out()
            return nc, list(ap_in.keys())

        ctxS4 = ExitStack()
        open_stacks.append(ctxS4)
        SA = ctxS4.enter_context(nc.sbuf_tensor("SA", [128, 512], BF16))
        SB = ctxS4.enter_context(nc.sbuf_tensor("SB", [128, 512], BF16))
        ABSA = ctxS4.enter_context(nc.sbuf_tensor("ABSA", [128, 512], BF16))
        ABSB = ctxS4.enter_context(nc.sbuf_tensor("ABSB", [128, 512], BF16))
        SA2 = ctxS4.enter_context(nc.sbuf_tensor("SA2", [128, 512], BF16))
        SB2 = ctxS4.enter_context(nc.sbuf_tensor("SB2", [128, 512], BF16))
        ABSA2 = ctxS4.enter_context(nc.sbuf_tensor("ABSA2", [128, 512], BF16))
        ABSB2 = ctxS4.enter_context(nc.sbuf_tensor("ABSB2", [128, 512], BF16))
        avfm = ctxS4.enter_context(nc.sbuf_tensor("avfm", [128, 512], BF16))
        avtm = ctxS4.enter_context(nc.sbuf_tensor("avtm", [128, 512], BF16))
        den_sb = ctxS4.enter_context(nc.sbuf_tensor("den_sb", [128, 512], BF16))
        gsil = ctxS4.enter_context(nc.sbuf_tensor("gsil", [128, 512], BF16))
        tmpp = ctxS4.enter_context(nc.sbuf_tensor("tmpp", [128, 128], BF16))
        g_pair = ctxS4.enter_context(
            nc.sbuf_tensor("g_pair", [128, TT_ * 128], BF16))
        X2 = ctxS4.enter_context(nc.sbuf_tensor("X2", [128, TT_ * E], F32))

        def s4_x2(g):
            pr.wait(g, "te", te_vg)
            pr.wait(g, "ve", ve_vg)
            pr.wait(g, "se", se_vg)
            ST["x2l"].dma(g, X2[:], x_spill.ap())
        block.gpsimd(s4_x2)

        for p in range(NP):
            def s4_gp(g, p=p):
                if p == 0:
                    ST["gw"].wait(g)
                else:
                    pr.wait(g, "se")
                for q4 in range(TT_ // 4):
                    ST["gp"].dma(
                        g,
                        g_pair[:, q4 * 512:(q4 + 1) * 512].rearrange(
                            "q (t c) -> q t c", c=128),
                        bass.AP(g_spill, p * 128 + q4 * 4 * E,
                                [[TT_ * E, 128], [E, 4], [1, 128]]))
            block.gpsimd(s4_gp)
            gp_cnt = ST["gp"].cnt

            for qc in range(NQ):
                nk = 4 * qc + 4
                te_b4 = pr.cnt["te"]
                ve_b4 = pr.cnt["ve"]
                ve_cum = [0] * (nk + 1)
                for kt in range(nk):
                    j = kt - 4 * qc
                    ve_cum[kt + 1] = ve_cum[kt] + (
                        2 if j < 0 else (2 if j == 3 else 4))
                se_b4 = pr.cnt["se"]

                def s_banks(kt):
                    return (P[0], P[1]) if kt % 2 == 0 else (P[5], P6)

                def emit_scores(t_e, kt, p, qc):
                    j = kt - 4 * qc
                    off = max(j, 0) * 128
                    qs0 = qc * 512
                    bA, bB = s_banks(kt)
                    if kt >= 2:
                        t_e.wait_ge(pr.sems["ve"], ve_b4 + ve_cum[kt - 1])
                    t_e.matmul(
                        bA[:, off:512],
                        kr[0:64, p * S + kt * 128:p * S + (kt + 1) * 128],
                        qr[0:64, p * S + qs0 + off:p * S + qs0 + 512],
                        start=True, stop=True)
                    ins = t_e.matmul(
                        bB[:, off:512],
                        kr[64:128, p * S + kt * 128:p * S + (kt + 1) * 128],
                        qr[64:128, p * S + qs0 + off:p * S + qs0 + 512],
                        start=True, stop=True, tile_position=(64, 0))
                    pr.inc(ins, "te")

                def emit_avden(t_e, kt, p, qc, nk):
                    j = kt - 4 * qc
                    off = max(j, 0) * 128
                    t_e.wait_ge(pr.sems["ve"], ve_b4 + ve_cum[kt + 1])
                    first = (kt == 0)
                    last = (kt == nk - 1)
                    sa = SA if kt % 2 == 0 else SA2
                    sb2 = SB if kt % 2 == 0 else SB2
                    aa = ABSA if kt % 2 == 0 else ABSA2
                    ab = ABSB if kt % 2 == 0 else ABSB2
                    t_e.matmul(P[2][0:64, off:512],
                               vtm[:, kt * E + (2 * p) * D:
                                   kt * E + (2 * p) * D + D],
                               sa[:, off:512], start=first, stop=last)
                    t_e.matmul(P[3][64:128, off:512],
                               vtm[:, kt * E + (2 * p + 1) * D:
                                   kt * E + (2 * p + 1) * D + D],
                               sb2[:, off:512], start=first, stop=last,
                               tile_position=(0, 64), skip_group_check=True)
                    t_e.wait_ge(pr.sems["se"], se_b4 + 2 * (kt + 1))
                    t_e.matmul(P[4][0:1, off:512], ONES1[:],
                               aa[:, off:512], start=first, stop=last)
                    ins = t_e.matmul(P[4][64:65, off:512], ONES1[:],
                                     ab[:, off:512], start=first,
                                     stop=last, tile_position=(0, 64),
                                     skip_group_check=True)
                    pr.inc(ins, "te")

                def f_te(t_e, p=p, qc=qc, nk=nk, te_b4=te_b4,
                         ve_b4=ve_b4, ve_cum=ve_cum):
                    pr.wait(t_e, "ve", ve_vg)
                    for kt in range(nk):
                        emit_scores(t_e, kt, p, qc)
                        if kt >= 1:
                            emit_avden(t_e, kt - 1, p, qc, nk)
                    emit_avden(t_e, nk - 1, p, qc, nk)
                block.tensor(f_te)

                def f_ve(v, p=p, qc=qc, nk=nk, te_b4=te_b4):
                    for kt in range(nk):
                        j = kt - 4 * qc
                        v.wait_ge(pr.sems["te"],
                                  te_b4 + (1 if kt == 0 else 2 * kt))
                        bA, bB = s_banks(kt)
                        sa = SA if kt % 2 == 0 else SA2
                        sb2 = SB if kt % 2 == 0 else SB2
                        if j >= 0:
                            v.tensor_tensor(sa[:, j * 128:(j + 1) * 128],
                                            bA[:, j * 128:(j + 1) * 128],
                                            TRI[:], ALU.mult)
                            v.tensor_tensor(sb2[:, j * 128:(j + 1) * 128],
                                            bB[:, j * 128:(j + 1) * 128],
                                            TRI[:], ALU.mult)
                        if j < 3:
                            off2 = (j + 1) * 128 if j >= 0 else 0
                            v.tensor_copy(sa[:, off2:512], bA[:, off2:512])
                            v.tensor_copy(sb2[:, off2:512], bB[:, off2:512])
                pr.vec(f_ve)

                def f_se(sx, p=p, qc=qc, nk=nk, te_b4=te_b4, ve_b4=ve_b4,
                         ve_cum=ve_cum, se_b4=se_b4):
                    for kt in range(nk):
                        j = kt - 4 * qc
                        off = max(j, 0) * 128
                        sx.wait_ge(pr.sems["ve"], ve_b4 + ve_cum[kt + 1])
                        if kt >= 2:
                            sx.wait_ge(pr.sems["te"], te_b4 + 2 * kt - 1)
                        sa = SA if kt % 2 == 0 else SA2
                        sb2 = SB if kt % 2 == 0 else SB2
                        aa = ABSA if kt % 2 == 0 else ABSA2
                        ab = ABSB if kt % 2 == 0 else ABSB2
                        sx.activation(aa[:, off:512], sa[:, off:512], AF.Abs)
                        sx.activation(ab[:, off:512], sb2[:, off:512], AF.Abs)
                pr.sca(f_se)

                te_av = pr.cnt["te"]

                def e_ve1(v, te_av=te_av):
                    pr.wait(v, "te", te_av)
                    v.tensor_copy(avfm[0:64, :], P[2][0:64, :])
                    v.tensor_copy(avfm[64:128, :], P[3][64:128, :])
                    v.tensor_copy(den_sb[0:1, :], P[4][0:1, :])
                    v.tensor_copy(den_sb[64:65, :], P[4][64:65, :])
                pr.vec(e_ve1)
                ve_ep1 = pr.cnt["ve"]

                def e_te(t_e, ve_ep1=ve_ep1):
                    pr.wait(t_e, "ve", ve_ep1)
                    for k in range(4):
                        t_e.transpose(PT[:, k * 128:(k + 1) * 128],
                                      avfm[:, k * 128:(k + 1) * 128], IDT[:])
                    for k in range(4):
                        ins = t_e.transpose(
                            PT[:, 512 + k * 128:512 + (k + 1) * 128],
                            den_sb[:, k * 128:(k + 1) * 128], IDT[:])
                    pr.inc(ins, "te")
                block.tensor(e_te)
                te_ep = pr.cnt["te"]

                def e_ve2(v, p=p, qc=qc, te_ep=te_ep):
                    pr.wait(v, "te", te_ep)
                    v.tensor_copy(avtm[:], PT[:, 0:512])
                    for tau in range(4):
                        tg = qc * 4 + tau
                        v.tensor_copy(
                            den_tm[:, tg * H + 2 * p:tg * H + 2 * p + 2],
                            bass.AP(PT, 512 + tau * 128,
                                    [[1024, 128], [64, 2]]))
                    dv = bass.AP(den_tm, (qc * 4) * H + 2 * p,
                                 [[TT_ * H, 128], [H, 4], [1, 2]])
                    v.tensor_scalar(dv, dv, 1.0, 50000.0, ALU.max, ALU.min)
                    v.tensor_tensor(d2scr[:], dv, dv, ALU.mult)
                pr.vec(e_ve2)
                ve_ep2 = pr.cnt["ve"]

                def e_se1(s, ve_ep2=ve_ep2):
                    pr.wait(s, "ve", ve_ep2)
                    s.activation(sqscr[:, 0:512], avtm[:], AF.Square,
                                 scale=CONSTS[:, K_EIGHTH:K_EIGHTH + 1])
                pr.sca(e_se1)
                se_ep1 = pr.cnt["se"]

                def e_ve3(v, se_ep1=se_ep1):
                    pr.wait(v, "se", se_ep1)
                    v.tensor_reduce(
                        msqscr[:],
                        bass.AP(sqscr, 0, [[E, 128], [64, 8], [1, 64]]),
                        AX.X, ALU.add)
                    v.scalar_tensor_tensor(argscr[:], d2scr[:], 1e-6,
                                           msqscr[:], ALU.mult, ALU.add)
                pr.vec(e_ve3)
                ve_ep3 = pr.cnt["ve"]

                def e_se2(s, p=p, qc=qc, ve_ep3=ve_ep3, gp_cnt=gp_cnt):
                    pr.wait(s, "ve", ve_ep3)
                    s.activation(rootscr[:], argscr[:], AF.Sqrt)
                    ST["gp"].wait(s, gp_cnt)
                    s.activation(gsil[:], g_pair[:, qc * 512:(qc + 1) * 512],
                                 AF.Silu)
                pr.sca(e_se2)
                se_ep2 = pr.cnt["se"]

                def e_ve4(v, p=p, qc=qc, se_ep2=se_ep2):
                    pr.wait(v, "se", se_ep2)
                    ST["x2l"].wait(v)
                    v.reciprocal(sclscr[:], rootscr[:])
                    for tau in range(4):
                        tg = qc * 4 + tau
                        for h in range(2):
                            v.scalar_tensor_tensor(
                                tmpp[:, h * D:(h + 1) * D],
                                avtm[:, tau * 128 + h * D:
                                     tau * 128 + (h + 1) * D],
                                sclscr[:, tau * 2 + h:tau * 2 + h + 1],
                                gsil[:, tau * 128 + h * D:
                                     tau * 128 + (h + 1) * D],
                                ALU.mult, ALU.mult)
                        v.tensor_tensor(
                            X2[:, tg * E + p * 128:tg * E + (p + 1) * 128],
                            X2[:, tg * E + p * 128:tg * E + (p + 1) * 128],
                            tmpp[:], ALU.add)
                pr.vec(e_ve4)

        ve_s4 = pr.cnt["ve"]
        te_s4 = pr.cnt["te"]

        if stop_after < 6:
            early_out()
            return nc, list(ap_in.keys())

        ln_stats(X2, [lambda e, _v=ve_s4: pr.wait(e, "ve", _v)])

        def s5_sp(g):
            pr.wait(g, "ve")
            pr.wait(g, "se")
            ST["x2sp"].dma(g, x2_spill.ap(), X2[:])
        block.gpsimd(s5_sp)
        te_preF = pr.cnt["te"]
        ctxS4.close()
        open_stacks.remove(ctxS4)
        ctxR2.close()
        open_stacks.remove(ctxR2)

        ctxF1 = ExitStack()
        open_stacks.append(ctxF1)
        xln2 = ctxF1.enter_context(nc.sbuf_tensor("xln2", [128, TT_ * E],
                                                  BF16))
        xres2 = ctxF1.enter_context(nc.sbuf_tensor("xres2", [128, 2 * E], F32))

        ve_b5 = pr.cnt["ve"]

        def s5_g2(g):
            pr.wait(g, "te", te_preF)
            ST["x2sp"].wait(g)
            for t in range(TT_):
                if t >= 2:
                    g.wait_ge(pr.sems["ve"], ve_b5 + (t - 1))
                ST["xr2"].dma(g, xres2[:, (t % 2) * E:(t % 2 + 1) * E],
                              x2_spill.ap()[:, t * E:(t + 1) * E])
        block.gpsimd(s5_g2)

        def s5_ve(v):
            pr.wait(v, "te", te_preF)
            for t in range(TT_):
                ST["xr2"].wait(v, 16 * (t + 1))
                v.tensor_scalar(xln2[:, t * E:(t + 1) * E],
                                xres2[:, (t % 2) * E:(t % 2 + 1) * E],
                                meanc[:, t:t + 1], rstdc[:, t:t + 1],
                                ALU.subtract, ALU.mult)
            v.tensor_reduce(redcols[:, 0:1], xln2[:], AX.X, ALU.max,
                            apply_absolute_value=True)
            v.tensor_single_scalar(redb[:, 0:1], redcols[:, 0:1], 1.0,
                                   ALU.mult)
        pr.vec(s5_ve)
        ve_ln2 = pr.cnt["ve"]

        def part_max_to_sm(col_redb, sm_col, wait_ve):
            def f_te(t_e):
                pr.wait(t_e, "ve", wait_ve)
                pr.inc(t_e.transpose(PT[0:1, 0:128],
                                     redb[:, col_redb:col_redb + 1], IDT[:]),
                       "te")
            block.tensor(f_te)
            te_t = pr.cnt["te"]

            def f_ve(v):
                pr.wait(v, "te", te_t)
                v.tensor_reduce(sm[:, sm_col:sm_col + 1], PT[0:1, 0:128],
                                AX.X, ALU.max)
            pr.vec(f_ve)
            return pr.cnt["ve"]

        ve_mx = part_max_to_sm(0, 0, ve_ln2)

        def collective_round(k, src_cols, wait_ve):
            n = len(src_cols)

            def f_g(g):
                pr.wait(g, "ve", wait_ve)
                ap_src = sm[0:1, src_cols[0]:src_cols[0] + n]
                ST["ccb"].dma(g, cc_i[k].ap(), ap_src)
                ST["ccb"].wait(g)
                pr.inc(g.collective_compute(
                    "AllGather", ALU.bypass,
                    replica_groups=[list(range(NCORES))],
                    ins=[cc_i[k].ap().opt()], outs=[cc_o[k].ap().opt()]),
                    "cc", 1)
                pr.wait(g, "cc")
                ST["ccb"].dma(g, ag_sb[:, 0:n * NCORES], cc_o[k].ap())
            block.gpsimd(f_g)
            cc_done = ST["ccb"].cnt

            def f_ve(v):
                ST["ccb"].wait(v, cc_done)
                for i in range(n):
                    v.tensor_reduce(
                        sm[:, src_cols[i]:src_cols[i] + 1],
                        bass.AP(ag_sb, i, [[2 * NCORES, 1], [n, NCORES]]),
                        AX.X, ALU.max)
            pr.vec(f_ve)
            return pr.cnt["ve"]

        ve_r1 = collective_round(0, [0], ve_mx)

        def r1_ve(v):
            pr.wait(v, "ve", ve_r1)
            v.tensor_scalar(sm[:, 1:2], sm[:, 0:1], 1.0 / 32767.0, 1e-12,
                            ALU.mult, ALU.max)
            v.reciprocal(sm[:, 2:3], sm[:, 1:2])
        pr.vec(r1_ve)
        ve_r1b = pr.cnt["ve"]

        def bcast(cols_src, cols_dst, wait_ve):
            n = len(cols_src)
            assert cols_src == list(range(cols_src[0], cols_src[0] + n))

            def f_te(t_e):
                pr.wait(t_e, "ve", wait_ve)
                pr.inc(t_e.matmul(P6[:, 0:n], ONESF[:],
                                  sm[0:1, cols_src[0]:cols_src[0] + n],
                                  start=True, stop=True), "te")
            block.tensor(f_te)
            te_t = pr.cnt["te"]

            def f_ve(v):
                pr.wait(v, "te", te_t)
                for i, cd in enumerate(cols_dst):
                    v.tensor_copy(bc[:, cd:cd + 1], P6[:, i:i + 1])
            pr.vec(f_ve)
            return pr.cnt["ve"]

        ve_bc1 = bcast([1, 2], [C_SX, C_RSX], ve_r1b)

        if stop_after < 7:
            early_out()
            return nc, list(ap_in.keys())

        y6 = ctxF1.enter_context(nc.sbuf_tensor("y6", [128, 2 * E], F32))
        Xq = ctxF1.enter_context(nc.sbuf_tensor("Xq", [128, TT_ * E], BF16))

        ctxR3 = ExitStack()
        open_stacks.append(ctxR3)
        XqT = ctxR3.enter_context(nc.sbuf_tensor("XqT", [128, EC * S], BF16,
                                                 side="right"))

        se_b6 = pr.cnt["se"]
        ve_b6 = pr.cnt["ve"]

        def s6_se(s):
            pr.wait(s, "ve", ve_bc1)
            for t in range(TT_):
                if t >= 2:
                    s.wait_ge(pr.sems["ve"], ve_b6 + (t - 1))
                s.activation(y6[:, (t % 2) * E:(t % 2 + 1) * E],
                             xln2[:, t * E:(t + 1) * E], AF.Copy,
                             scale=bc[:, C_RSX:C_RSX + 1])
        pr.sca(s6_se)

        def s6_ve(v):
            for t in range(TT_):
                v.wait_ge(pr.sems["se"], se_b6 + t + 1)
                v.tensor_scalar(Xq[:, t * E:(t + 1) * E],
                                y6[:, (t % 2) * E:(t % 2 + 1) * E],
                                MAGIC, MAGIC, ALU.add, ALU.subtract)
            assert pr.cnt["ve"] == ve_b6 + TT_
        pr.vec(s6_ve)
        ve_xq = pr.cnt["ve"]

        te_b6 = pr.cnt["te"]
        ve_b6b = pr.cnt["ve"]

        def s6_te(t_e):
            pr.wait(t_e, "ve", ve_xq)
            for gi in range(ngrp):
                if gi > 0:
                    t_e.wait_ge(pr.sems["ve"], ve_b6b + 8 * gi)
                for k in range(8):
                    t, c = tr_jobs[gi * 8 + k]
                    ins = t_e.transpose(
                        PT[:, k * 128:(k + 1) * 128],
                        Xq[:, t * E + c * 128:t * E + (c + 1) * 128], IDT[:])
                pr.inc(ins, "te")
        block.tensor(s6_te)

        def s6_ve2(v):
            for gi in range(ngrp):
                v.wait_ge(pr.sems["te"], te_b6 + gi + 1)
                for k in range(8):
                    t, c = tr_jobs[gi * 8 + k]
                    v.tensor_copy(
                        XqT[:, c * S + t * 128:c * S + (t + 1) * 128],
                        PT[:, k * 128:(k + 1) * 128])
        pr.vec(s6_ve2)
        ve_xqT = pr.cnt["ve"]

        ctxF1.close()
        open_stacks.remove(ctxF1)
        ctxF2 = ExitStack()
        open_stacks.append(ctxF2)

        if stop_after < 8:
            early_out()
            return nc, list(ap_in.keys())

        G = ctxF2.enter_context(nc.sbuf_tensor("G", [128, IC * S], BF16))
        ctxPB = ExitStack()
        open_stacks.append(ctxPB)
        yscr = ctxPB.enter_context(nc.sbuf_tensor("yscr", [128, 2 * S], F32))
        sigscr = ctxPB.enter_context(nc.sbuf_tensor("sigscr", [128, S], BF16))
        upc = ctxPB.enter_context(nc.sbuf_tensor("upc", [128, 2 * S], BF16))
        uqscr = ctxPB.enter_context(nc.sbuf_tensor("uqscr", [128, S], BF16))
        t2b = ctxPB.enter_context(nc.sbuf_tensor("t2b", [128, S], BF16))
        ctxUG = ExitStack()
        open_stacks.append(ctxUG)
        wde = ctxUG.enter_context(nc.sbuf_tensor("wde", [128, 2 * E], F32))
        tqscr = ctxUG.enter_context(nc.sbuf_tensor("tqscr", [128, E], F32))
        qwscr = ctxUG.enter_context(nc.sbuf_tensor("qwscr", [128, E], BF16))
        bmscr = ctxUG.enter_context(nc.sbuf_tensor("bmscr", [128, NBU], F32))
        sWscr = ctxUG.enter_context(nc.sbuf_tensor("sWscr", [128, NBU], F32))
        rsWscr = ctxUG.enter_context(nc.sbuf_tensor("rsWscr", [128, NBU], F32))
        wqT = ctxUG.enter_context(nc.sbuf_tensor("wqT", [128, 2 * E], BF16))
        upst = ctxUG.enter_context(nc.sbuf_tensor("upst", [128, S], BF16))

        def dequant_chunk(bufs, nb, width, ld_val, slot):
            wd, tq, qw, bm, sW, rsW, wrow, nbmax = bufs

            def f_ve(v):
                ST["wde"].wait(v, ld_val)
                v.tensor_reduce(
                    bm[:, 0:nb],
                    bass.AP(wd, slot * width, [[wrow, 128], [32, nb], [1, 32]]),
                    AX.X, ALU.max, apply_absolute_value=True)
                v.tensor_scalar(sW[:, 0:nb], bm[:, 0:nb], 1.0 / 7.0,
                                1e-12, ALU.mult, ALU.max)
                v.reciprocal(rsW[:, 0:nb], sW[:, 0:nb])
                v.tensor_tensor(
                    tq[:, 0:width],
                    bass.AP(wd, slot * width, [[wrow, 128], [32, nb], [1, 32]]),
                    bass.AP(rsW, 0, [[nbmax, 128], [1, nb], [0, 32]]),
                    ALU.mult)
                v.tensor_scalar(tq[:, 0:width], tq[:, 0:width],
                                MAGIC, MAGIC, ALU.add, ALU.subtract)
                v.tensor_tensor(
                    qw[:, 0:width],
                    bass.AP(tq, 0, [[width, 128], [32, nb], [1, 32]]),
                    bass.AP(sW, 0, [[nbmax, 128], [1, nb], [0, 32]]),
                    ALU.mult)
            pr.vec(f_ve)
            return pr.cnt["ve"]

        ug_bufs = (wde, tqscr, qwscr, bmscr, sWscr, rsWscr, 2 * E, NBU)

        dq_hist = {}
        wqT_hist = {}
        for mi, (w_src, is_up) in enumerate([(wup_in, True),
                                             (wgate_in, False)]):
            for c in range(IC):
                ci = mi * IC + c

                def f_ld(sync, w_src=w_src, c=c, mi=mi, ci=ci):
                    if mi == 0 and c == 0:
                        ST["x2sp"].wait(sync)
                        pr.wait(sync, "ve")
                        pr.wait(sync, "se")
                        pr.wait(sync, "te")
                    if ci >= 2:
                        pr.wait(sync, "ve", dq_hist[ci - 2])
                    ST["wde"].dma(sync,
                                  wde[:, (c % 2) * E:(c % 2 + 1) * E],
                                  w_src[c * 128:(c + 1) * 128, :])
                block.sync(f_ld)
                ld_w = ST["wde"].cnt

                ve_dq = dequant_chunk(ug_bufs, NBU, E, ld_w, c % 2)
                dq_hist[ci] = ve_dq

                def f_tr(t_e, ve_dq=ve_dq, c=c, mi=mi, ci=ci):
                    pr.wait(t_e, "ve", ve_dq)
                    if ci >= 1:
                        pr.wait(t_e, "ve", wqT_hist[ci - 1])
                    for k in range(8):
                        ins = t_e.transpose(PT[:, k * 128:(k + 1) * 128],
                                            qwscr[:, k * 128:(k + 1) * 128],
                                            IDT[:])
                    pr.inc(ins, "te")
                block.tensor(f_tr)
                te_tr = pr.cnt["te"]

                def f_ev(v, te_tr=te_tr, c=c):
                    pr.wait(v, "te", te_tr)
                    v.tensor_copy(wqT[:, (c % 2) * E:(c % 2 + 1) * E], PT[:])
                pr.vec(f_ev)
                ve_wqT = pr.cnt["ve"]
                wqT_hist[ci] = ve_wqT

                def f_mm(t_e, ve_wqT=ve_wqT, c=c, mi=mi):
                    pr.wait(t_e, "ve", ve_wqT)
                    if mi == 0 and c == 0:
                        pr.wait(t_e, "ve", ve_xqT)
                    for kc in range(EC):
                        for ts_ in range(NT):
                            ins = t_e.matmul(
                                P[ts_][:],
                                wqT[:, (c % 2) * E + kc * 128:
                                    (c % 2) * E + (kc + 1) * 128],
                                XqT[:, kc * S + ts_ * 512:
                                    kc * S + (ts_ + 1) * 512],
                                start=(kc == 0), stop=(kc == EC - 1))
                    pr.inc(ins, "te")
                block.tensor(f_mm)
                te_mm = pr.cnt["te"]

                def f_evac(v, te_mm=te_mm, c=c, is_up=is_up):
                    pr.wait(v, "te", te_mm)
                    if is_up:
                        ST["upw"].wait(v)
                    for ts_ in range(NT):
                        if is_up:
                            o = upst[:, ts_ * 512:(ts_ + 1) * 512]
                        else:
                            o = G[:, c * S + ts_ * 512:c * S + (ts_ + 1) * 512]
                        v.tensor_scalar(o, P[ts_][:], bc[:, C_SX:C_SX + 1],
                                        None, ALU.mult)
                    src = upst[:] if is_up else G[:, c * S:(c + 1) * S]
                    v.tensor_reduce(redcols[:, mi * IC + c:mi * IC + c + 1],
                                    src, AX.X, ALU.max,
                                    apply_absolute_value=True)
                pr.vec(f_evac)
                ve_ev = pr.cnt["ve"]

                if is_up:
                    def f_sp(g, c=c, ve_ev=ve_ev):
                        pr.wait(g, "ve", ve_ev)
                        ST["upw"].dma(g, up_spill.ap()[:, c * S:(c + 1) * S],
                                      upst[:])
                    block.gpsimd(f_sp)

        ve_s7 = pr.cnt["ve"]

        def s7_red(v):
            pr.wait(v, "ve", ve_s7)
            v.tensor_reduce(redsm[:, 0:1], redcols[:, 0:IC], AX.X, ALU.max)
            v.tensor_reduce(redsm[:, 1:2], redcols[:, IC:2 * IC], AX.X,
                            ALU.max)
            v.tensor_single_scalar(redb[:, 0:2], redsm[:, 0:2], 1.0, ALU.mult)
        pr.vec(s7_red)
        ve_r2a = pr.cnt["ve"]

        def s7_tr(t_e):
            pr.wait(t_e, "ve", ve_r2a)
            t_e.transpose(PT[0:1, 0:128], redb[:, 0:1], IDT[:])
            pr.inc(t_e.transpose(PT[0:1, 128:256], redb[:, 1:2], IDT[:]), "te")
        block.tensor(s7_tr)
        te_r2 = pr.cnt["te"]

        def s7_fin(v):
            pr.wait(v, "te", te_r2)
            v.tensor_reduce(sm[:, 4:5], PT[0:1, 0:128], AX.X, ALU.max)
            v.tensor_reduce(sm[:, 5:6], PT[0:1, 128:256], AX.X, ALU.max)
        pr.vec(s7_fin)
        ve_r2b = pr.cnt["ve"]

        ve_r2 = collective_round(1, [4, 5], ve_r2b)

        def r2_ve(v):
            pr.wait(v, "ve", ve_r2)
            v.tensor_scalar(sm[:, 6:7], sm[:, 4:5], 1.0 / 32767.0, 1e-12,
                            ALU.mult, ALU.max)
            v.tensor_scalar(sm[:, 7:8], sm[:, 5:6], 1.0 / 32767.0, 1e-12,
                            ALU.mult, ALU.max)
            v.reciprocal(sm[:, 8:9], sm[:, 6:7])
            v.reciprocal(sm[:, 9:10], sm[:, 7:8])
        pr.vec(r2_ve)
        ve_r2c = pr.cnt["ve"]
        ve_bc2 = bcast([7, 8, 9], [C_SG, C_RSUP, C_RSG], ve_r2c)

        if stop_after < 9:
            early_out()
            return nc, list(ap_in.keys())

        ctxUG.close()
        open_stacks.remove(ctxUG)
        ctxR3.close()
        open_stacks.remove(ctxR3)

        se_b8 = pr.cnt["se"]
        ve_b8 = pr.cnt["ve"]
        for c in range(IC):
            def a_se1(s, c=c):
                if c == 0:
                    pr.wait(s, "ve", ve_bc2)
                elif c >= 2:
                    s.wait_ge(pr.sems["ve"], ve_b8 + c - 1)
                s.activation(yscr[:, (c % 2) * S:(c % 2 + 1) * S],
                             G[:, c * S:(c + 1) * S],
                             AF.Copy, scale=bc[:, C_RSG:C_RSG + 1])
            pr.sca(a_se1)

            def a_ve1(v, c=c):
                v.wait_ge(pr.sems["se"], se_b8 + c + 1)
                v.tensor_scalar(G[:, c * S:(c + 1) * S],
                                yscr[:, (c % 2) * S:(c % 2 + 1) * S],
                                MAGIC, MAGIC, ALU.add, ALU.subtract)
            pr.vec(a_ve1)
        se_m8 = pr.cnt["se"]
        ve_m8 = pr.cnt["ve"]
        for c in range(IC):
            def a_se2(s, c=c):
                if c < 2:
                    s.wait_ge(pr.sems["ve"], ve_m8)
                else:
                    s.wait_ge(pr.sems["ve"], ve_m8 + 4 * (c - 1))
                s.activation(yscr[:, (c % 2) * S:(c % 2 + 1) * S],
                             G[:, c * S:(c + 1) * S],
                             AF.Sigmoid, scale=bc[:, C_SG:C_SG + 1])
            pr.sca(a_se2)

            def a_ve2(v, c=c):
                v.wait_ge(pr.sems["se"], se_m8 + c + 1)
                yb = yscr[:, (c % 2) * S:(c % 2 + 1) * S]
                v.tensor_scalar(yb, yb, 65536.0, MAGIC,
                                ALU.mult, ALU.add)
                v.tensor_scalar(sigscr[:], yb, MAGIC, 65535.0,
                                ALU.subtract, ALU.min)
                v.tensor_tensor(G[:, c * S:(c + 1) * S],
                                G[:, c * S:(c + 1) * S], sigscr[:], ALU.mult)
                v.tensor_reduce(redcols[:, 32 + c:33 + c],
                                G[:, c * S:(c + 1) * S], AX.X, ALU.max,
                                apply_absolute_value=True)
            pr.vec(a_ve2)
        ve_s8a = pr.cnt["ve"]

        def s8_red(v):
            pr.wait(v, "ve", ve_s8a)
            v.tensor_reduce(redsm[:, 2:3], redcols[:, 32:32 + IC], AX.X,
                            ALU.max)
            v.tensor_single_scalar(redb[:, 2:3], redsm[:, 2:3], 1.0, ALU.mult)
        pr.vec(s8_red)
        ve_r3a = pr.cnt["ve"]

        def s8_tr(t_e):
            pr.wait(t_e, "ve", ve_r3a)
            pr.inc(t_e.transpose(PT[0:1, 0:128], redb[:, 2:3], IDT[:]), "te")
        block.tensor(s8_tr)
        te_r3 = pr.cnt["te"]

        def s8_fin(v):
            pr.wait(v, "te", te_r3)
            v.tensor_reduce(sm[:, 10:11], PT[0:1, 0:128], AX.X, ALU.max)
        pr.vec(s8_fin)
        ve_r3b = pr.cnt["ve"]

        ve_r3 = collective_round(2, [10], ve_r3b)

        def r3_ve(v):
            pr.wait(v, "ve", ve_r3)
            v.tensor_tensor(sm[:, 11:12], sm[:, 10:11], sm[:, 7:8], ALU.mult)
            v.tensor_scalar(sm[:, 11:12], sm[:, 11:12],
                            1.0 / (65536.0 * 32767.0), 1e-12,
                            ALU.mult, ALU.max)
            v.reciprocal(sm[:, 12:13], sm[:, 11:12])
            v.tensor_tensor(sm[:, 12:13], sm[:, 12:13], sm[:, 7:8], ALU.mult)
            v.tensor_single_scalar(sm[:, 12:13], sm[:, 12:13],
                                   1.0 / 65536.0, ALU.mult)
        pr.vec(r3_ve)
        ve_r3c = pr.cnt["ve"]
        ve_bc3 = bcast([12], [C_RST], ve_r3c)

        se_b8b = pr.cnt["se"]
        ve_b8b = pr.cnt["ve"]
        for c in range(IC):
            def b_se(s, c=c):
                if c == 0:
                    pr.wait(s, "ve", ve_bc3)
                elif c >= 2:
                    s.wait_ge(pr.sems["ve"], ve_b8b + c - 1)
                s.activation(yscr[:, (c % 2) * S:(c % 2 + 1) * S],
                             G[:, c * S:(c + 1) * S],
                             AF.Copy, scale=bc[:, C_RST:C_RST + 1])
            pr.sca(b_se)

            def b_ve(v, c=c):
                v.wait_ge(pr.sems["se"], se_b8b + c + 1)
                v.tensor_scalar(G[:, c * S:(c + 1) * S],
                                yscr[:, (c % 2) * S:(c % 2 + 1) * S],
                                MAGIC, MAGIC, ALU.add, ALU.subtract)
            pr.vec(b_ve)
        ve_g2 = pr.cnt["ve"]

        if stop_after < 10:
            early_out()
            return nc, list(ap_in.keys())

        se_b8c = pr.cnt["se"]
        ve_b8c = pr.cnt["ve"]

        for c in range(IC):
            def pb_g(g, c=c):
                if c == 0:
                    pr.wait(g, "ve", ve_g2)
                    ST["upw"].wait(g)
                if c >= 2:
                    g.wait_ge(pr.sems["se"], se_b8c + (c - 1))
                ST["ub"].dma(g, upc[:, (c % 2) * S:(c % 2 + 1) * S],
                             up_spill.ap()[:, c * S:(c + 1) * S])
            block.gpsimd(pb_g)

            def pb_se(s, c=c):
                ST["ub"].wait(s, 16 * (c + 1))
                if c >= 2:
                    s.wait_ge(pr.sems["ve"], ve_b8c + 3 * (c - 2) + 1)
                s.activation(yscr[:, (c % 2) * S:(c % 2 + 1) * S],
                             upc[:, (c % 2) * S:(c % 2 + 1) * S],
                             AF.Copy, scale=bc[:, C_RSUP:C_RSUP + 1])
            pr.sca(pb_se)

            def pb_ve(v, c=c):
                v.wait_ge(pr.sems["se"], se_b8c + c + 1)
                v.tensor_scalar(uqscr[:],
                                yscr[:, (c % 2) * S:(c % 2 + 1) * S],
                                MAGIC, MAGIC, ALU.add, ALU.subtract)
                v.tensor_tensor(t2b[:], G[:, c * S:(c + 1) * S], uqscr[:],
                                ALU.mult)
                v.tensor_reduce(redcols[:, 32 + c:33 + c], t2b[:], AX.X,
                                ALU.max, apply_absolute_value=True)
            pr.vec(pb_ve)
        ve_s8b = pr.cnt["ve"]

        def s8_red2(v):
            pr.wait(v, "ve", ve_s8b)
            v.tensor_reduce(redsm[:, 3:4], redcols[:, 32:32 + IC], AX.X,
                            ALU.max)
            v.tensor_single_scalar(redb[:, 3:4], redsm[:, 3:4], 1.0, ALU.mult)
        pr.vec(s8_red2)
        ve_r4a = pr.cnt["ve"]

        def s8_tr2(t_e):
            pr.wait(t_e, "ve", ve_r4a)
            pr.inc(t_e.transpose(PT[0:1, 0:128], redb[:, 3:4], IDT[:]), "te")
        block.tensor(s8_tr2)
        te_r4 = pr.cnt["te"]

        def s8_fin2(v):
            pr.wait(v, "te", te_r4)
            v.tensor_reduce(sm[:, 13:14], PT[0:1, 0:128], AX.X, ALU.max)
        pr.vec(s8_fin2)
        ve_r4b = pr.cnt["ve"]

        ve_r4 = collective_round(3, [13], ve_r4b)

        def r4_ve(v):
            pr.wait(v, "ve", ve_r4)
            v.tensor_tensor(sm[:, 14:15], sm[:, 13:14], sm[:, 11:12],
                            ALU.mult)
            v.tensor_tensor(sm[:, 14:15], sm[:, 14:15], sm[:, 6:7], ALU.mult)
            v.tensor_scalar(sm[:, 14:15], sm[:, 14:15], 1.0 / 32767.0,
                            1e-12, ALU.mult, ALU.max)
            v.reciprocal(sm[:, 15:16], sm[:, 14:15])
            v.tensor_tensor(sm[:, 15:16], sm[:, 15:16], sm[:, 11:12],
                            ALU.mult)
            v.tensor_tensor(sm[:, 15:16], sm[:, 15:16], sm[:, 6:7], ALU.mult)
        pr.vec(r4_ve)
        ve_r4c = pr.cnt["ve"]
        ve_bc4 = bcast([14, 15], [C_SO, C_RSO], ve_r4c)

        if stop_after < 11:
            early_out()
            return nc, list(ap_in.keys())

        se_b8d = pr.cnt["se"]
        ve_b8d = pr.cnt["ve"]
        for c in range(IC):
            def pc_g(g, c=c):
                if c == 0:
                    pr.wait(g, "ve", ve_bc4)
                if c >= 2:
                    g.wait_ge(pr.sems["se"], se_b8d + (c - 1))
                ST["uc"].dma(g, upc[:, (c % 2) * S:(c % 2 + 1) * S],
                             up_spill.ap()[:, c * S:(c + 1) * S])
            block.gpsimd(pc_g)

            def pc_se1(s, c=c):
                ST["uc"].wait(s, 16 * (c + 1))
                if c >= 2:
                    s.wait_ge(pr.sems["ve"], ve_b8d + 2 * (c - 2) + 1)
                s.activation(yscr[:, (c % 2) * S:(c % 2 + 1) * S],
                             upc[:, (c % 2) * S:(c % 2 + 1) * S],
                             AF.Copy, scale=bc[:, C_RSUP:C_RSUP + 1])
            pr.sca(pc_se1)

            def pc_ve1(v, c=c):
                v.wait_ge(pr.sems["se"], se_b8d + c + 1)
                v.tensor_scalar(uqscr[:],
                                yscr[:, (c % 2) * S:(c % 2 + 1) * S],
                                MAGIC, MAGIC, ALU.add, ALU.subtract)
                v.tensor_tensor(G[:, c * S:(c + 1) * S],
                                G[:, c * S:(c + 1) * S], uqscr[:],
                                ALU.mult)
            pr.vec(pc_ve1)
        se_m8d = pr.cnt["se"]
        ve_m8d = pr.cnt["ve"]
        for c in range(IC):
            def pc_se2(s, c=c):
                if c < 2:
                    s.wait_ge(pr.sems["ve"], ve_m8d)
                else:
                    s.wait_ge(pr.sems["ve"], ve_m8d + c - 1)
                s.activation(yscr[:, (c % 2) * S:(c % 2 + 1) * S],
                             G[:, c * S:(c + 1) * S], AF.Copy,
                             scale=bc[:, C_RSO:C_RSO + 1])
            pr.sca(pc_se2)

            def pc_ve2(v, c=c):
                v.wait_ge(pr.sems["se"], se_m8d + c + 1)
                v.tensor_scalar(G[:, c * S:(c + 1) * S],
                                yscr[:, (c % 2) * S:(c % 2 + 1) * S],
                                MAGIC, MAGIC, ALU.add, ALU.subtract)
            pr.vec(pc_ve2)
        ve_oint = pr.cnt["ve"]

        if stop_after < 12:
            early_out()
            return nc, list(ap_in.keys())

        ctxPB.close()
        open_stacks.remove(ctxPB)
        wdeD = ctxF2.enter_context(nc.sbuf_tensor("wdeD", [128, I], F32))
        tqD = ctxF2.enter_context(nc.sbuf_tensor("tqD", [128, I], F32))
        qwD = ctxF2.enter_context(nc.sbuf_tensor("qwD", [128, I], BF16))
        bmD = ctxF2.enter_context(nc.sbuf_tensor("bmD", [128, NBD], F32))
        sWD = ctxF2.enter_context(nc.sbuf_tensor("sWD", [128, NBD], F32))
        rsWD = ctxF2.enter_context(nc.sbuf_tensor("rsWD", [128, NBD], F32))
        wqdT = ctxF2.enter_context(nc.sbuf_tensor("wqdT", [128, IC * E],
                                                  BF16))
        d_bufs = (wdeD, tqD, qwD, bmD, sWD, rsWD, I, NBD)

        n_dchunks = 1 if stop_after == 121 else (2 if stop_after == 123 else EC)
        skip_dtr = (stop_after == 122)
        for c in range(n_dchunks):
            def d_ld(sync, c=c):
                pr.wait(sync, "ve", ve_oint) if c < 1 else pr.wait(sync, "ve")
                ST["wde"].dma(sync, wdeD[:],
                              wdown_in[c * 128:(c + 1) * 128, :])
            block.sync(d_ld)
            ld_w = ST["wde"].cnt

            ve_dq = dequant_chunk(d_bufs, NBD, I, ld_w, 0)

            for tg in range(0 if skip_dtr else (IC // 8)):
                def d_tr(t_e, ve_dq=ve_dq, c=c, tg=tg):
                    pr.wait(t_e, "ve", ve_dq)
                    if not (c == 0 and tg == 0):
                        pr.wait(t_e, "ve")
                    for k in range(8):
                        jc = tg * 8 + k
                        ins = t_e.transpose(PT[:, k * 128:(k + 1) * 128],
                                            qwD[:, jc * 128:(jc + 1) * 128],
                                            IDT[:])
                    pr.inc(ins, "te")
                block.tensor(d_tr)
                te_tr = pr.cnt["te"]

                def d_ev(v, te_tr=te_tr, c=c, tg=tg):
                    pr.wait(v, "te", te_tr)
                    for k in range(8):
                        jc = tg * 8 + k
                        v.tensor_copy(
                            wqdT[:, jc * E + c * 128:jc * E + (c + 1) * 128],
                            PT[:, k * 128:(k + 1) * 128])
                pr.vec(d_ev)
        ve_wqd = pr.cnt["ve"]

        if stop_after < 13 or stop_after in (121, 122, 123):
            early_out()
            return nc, list(ap_in.keys())

        xres = ctxF2.enter_context(nc.sbuf_tensor("xres", [128, 2 * E], F32))
        xout = ctxF2.enter_context(nc.sbuf_tensor("xout", [128, 2 * E], F32))

        te_b9 = pr.cnt["te"]
        ve_b9 = pr.cnt["ve"]

        only_g = (stop_after == 126)
        only_gte = (stop_after == 127)

        def s9_g(g):
            pr.wait(g, "ve", ve_wqd)
            ST["x2sp"].wait(g)
            for t in range(TT_):
                if t >= 2 and not (only_g or only_gte):
                    g.wait_ge(pr.sems["ve"], ve_b9 + 2 * NO * (t - 1))
                ST["xr"].dma(g, xres[:, (t % 2) * E:(t % 2 + 1) * E],
                             x2_spill.ap()[:, t * E:(t + 1) * E])
        block.gpsimd(s9_g)

        if only_g:
            early_out()
            return nc, list(ap_in.keys())

        jc_rng = (range(0, 14) if stop_after == 124 else
                  (range(14, IC) if stop_after == 125 else range(IC)))

        def s9_te(t_e):
            pr.wait(t_e, "ve", ve_wqd)
            for t in range(TT_):
                if t >= 2 and not only_gte:
                    t_e.wait_ge(pr.sems["ve"], ve_b9 + 2 * NO * (t - 1))
                for jc in jc_rng:
                    for no in range(NO):
                        ins = t_e.matmul(
                            P[(t % 2) * 2 + no][:],
                            G[:, jc * S + t * 128:jc * S + (t + 1) * 128],
                            wqdT[:, jc * E + no * 512:jc * E + (no + 1) * 512],
                            start=(jc == jc_rng[0] if hasattr(jc_rng, "__getitem__") else jc == 0),
                            stop=(jc == list(jc_rng)[-1]))
                pr.inc(ins, "te")
        block.tensor(s9_te)

        if only_gte:
            early_out()
            return nc, list(ap_in.keys())

        def s9_ve(v):
            for t in range(TT_):
                v.wait_ge(pr.sems["te"], te_b9 + t + 1)
                ST["xr"].wait(v, 16 * (t + 1))
                if t >= 2:
                    ST["out"].wait(v, 16 * (t - 1))
                for no in range(NO):
                    v.tensor_scalar(sqscr[:, no * 512:(no + 1) * 512],
                                    P[(t % 2) * 2 + no][:],
                                    bc[:, C_SO:C_SO + 1], None, ALU.mult)
                for no in range(NO):
                    v.tensor_tensor(
                        xout[:, (t % 2) * E + no * 512:
                             (t % 2) * E + (no + 1) * 512],
                        sqscr[:, no * 512:(no + 1) * 512],
                        xres[:, (t % 2) * E + no * 512:
                             (t % 2) * E + (no + 1) * 512],
                        ALU.add)
        pr.vec(s9_ve)

        if stop_after == 128:
            early_out()
            return nc, list(ap_in.keys())

        def s9_st(g):
            for t in range(TT_):
                g.wait_ge(pr.sems["ve"], ve_b9 + 2 * NO * (t + 1))
                ST["out"].dma(g, out_d[t * 128:(t + 1) * 128, :],
                              xout[:, (t % 2) * E:(t % 2 + 1) * E])
            ST["out"].wait(g)
        block.sync(s9_st)

        ctxF2.close()
        open_stacks.remove(ctxF2)

    return nc, list(ap_in.keys())


_CACHE = {}


def _get_prog(S, E, H, I):
    key = (S, E, H, I)
    if key not in _CACHE:
        _CACHE[key] = build_program(S, E, H, I)
    return _CACHE[key]


def kernel(L0_attn_x, L0_attn_Wq, L0_attn_Wk, L0_attn_Wv, L0_attn_Wg,
           L0_ffn_w_gate, L0_ffn_w_up, L0_ffn_w_down):
    x = np.ascontiguousarray(np.asarray(L0_attn_x, np.float32))
    B, S, E = x.shape
    H = 16
    I = L0_ffn_w_up.shape[0]
    consts = make_consts(S, E, H)
    nc, _ = _get_prog(S, E, H, I)
    common = dict(
        wq=np.ascontiguousarray(np.asarray(L0_attn_Wq, np.float32)),
        wk=np.ascontiguousarray(np.asarray(L0_attn_Wk, np.float32)),
        wv=np.ascontiguousarray(np.asarray(L0_attn_Wv, np.float32)),
        wg=np.ascontiguousarray(np.asarray(L0_attn_Wg, np.float32)),
        wup=np.ascontiguousarray(np.asarray(L0_ffn_w_up, np.float32)),
        wgate=np.ascontiguousarray(np.asarray(L0_ffn_w_gate, np.float32)),
        wdown=np.ascontiguousarray(np.asarray(L0_ffn_w_down, np.float32)),
        **consts)
    in_maps = [dict(x=np.ascontiguousarray(x[c]), **common) for c in range(B)]
    res = run_bass_kernel_spmd(nc, in_maps, core_ids=list(range(NCORES)))
    out = np.stack([np.asarray(res.results[c]["out"]) for c in range(B)], 0)
    return out.astype(np.float32)
